# version 07
# baseline (speedup 1.0000x reference)
import math
import contextlib
import numpy as np

import concourse.bass as bass
import concourse.mybir as mybir
from concourse.bass_utils import run_bass_kernel_spmd

F32 = mybir.dt.float32
BF16 = mybir.dt.bfloat16
AF = mybir.ActivationFunctionType
ALU = mybir.AluOpType

MAGIC = 12582912.0
S = 2048
E = 1024
NT = 16
NC_ = 8
NH = 16

_DEBUG_TAPS = []
_BUILD_OPTS = {"n_seq": 2, "stop_after": None}


def _host_prepare(inp):
    x = np.ascontiguousarray(inp["L0_attn_a_x"])
    perm = np.empty(E, dtype=np.int64)
    idx = 0
    for g in range(8):
        half = 0 if g < 4 else 1
        hh = (g % 4) * 4
        for h2 in range(4):
            h = hh + h2
            for d in range(32):
                perm[idx] = half * 512 + h * 32 + d
                idx += 1
    wq = np.ascontiguousarray(inp["L0_attn_a_Wq"])
    wk = np.ascontiguousarray(inp["L0_attn_a_Wk"])
    wqT = np.ascontiguousarray(wq[perm].T)
    wkT = np.ascontiguousarray(wk[perm].T)
    wvT = np.ascontiguousarray(inp["L0_attn_a_Wv"].T)
    wprojT = np.ascontiguousarray(inp["L0_attn_a_Wproj"].T)
    ww = inp["L0_attn_b_W_w"]
    wwqT = np.ascontiguousarray(ww[0:1024].T)
    wwkT = np.ascontiguousarray(ww[1024:2048].T)
    wwvT = np.ascontiguousarray(ww[2048:3072].T)
    projwT = np.ascontiguousarray(inp["L0_attn_b_proj_w"].T)
    w1T = np.ascontiguousarray(inp["L0_ffn_w1"].T)
    w2T = np.ascontiguousarray(inp["L0_ffn_w2"].T)

    wb = inp["L0_attn_b_W_b"]
    def pcol(v):
        v = np.asarray(v).reshape(-1)
        return np.ascontiguousarray(v.reshape(-1, 128).T)
    wbq = pcol(wb[0:1024] * 1.0) if False else pcol(wb[0:1024])
    wbk = pcol(wb[1024:2048])
    wbv_row = np.ascontiguousarray(wb[2048:3072])
    b1 = pcol(inp["L0_ffn_b1"])
    b2 = pcol(inp["L0_ffn_b2"])
    pb = pcol(inp["L0_attn_b_proj_b"])
    lng = pcol(inp["L0_attn_b_ln_g"])
    lnb = pcol(inp["L0_attn_b_ln_b"])

    rpb = np.asarray(inp["L0_attn_b_rpb"])
    pi = np.arange(128)[:, None] - (np.arange(2560)[None, :] - 512) + 2047
    pi = np.clip(pi, 0, 4094)
    ZB = np.ascontiguousarray(rpb[pi, :].transpose(2, 0, 1))

    maskf = (np.arange(128)[:, None] <= (np.arange(2560)[None, :] - 512)).astype(np.float32)
    ident = np.eye(128, dtype=np.float32)
    import ml_dtypes
    i16 = np.arange(16, dtype=np.float64)
    inv_freq = 1.0 / (10000.0 ** (i16 / 256.0))
    t = np.arange(S, dtype=np.float64)
    fr = np.outer(t, inv_freq)
    cosb = np.cos(fr).astype(np.float32).astype(ml_dtypes.bfloat16).astype(np.float32)
    sinb = np.sin(fr).astype(np.float32).astype(ml_dtypes.bfloat16).astype(np.float32)
    def tmtab(tab):
        return np.ascontiguousarray(tab.reshape(NT, 128, 16).transpose(1, 0, 2).reshape(128, NT * 16))
    cosq = tmtab(cosb * 0.125)
    sinq = tmtab(sinb * 0.125)
    cosk = tmtab(cosb)
    sink = tmtab(sinb)

    xv = x.reshape(16, NT, 128, NC_, 128).transpose(0, 1, 4, 3, 2)
    xt_all = np.ascontiguousarray(xv.reshape(16, NT, 128, E))

    per_core = []
    for core in range(8):
        d = {
            "xt": np.ascontiguousarray(xt_all[2 * core:2 * core + 2]),
            "wqT": wqT, "wkT": wkT, "wvT": wvT, "wprojT": wprojT,
            "wwqT": wwqT, "wwkT": wwkT, "wwvT": wwvT, "projwT": projwT,
            "w1T": w1T, "w2T": w2T,
            "wbq": wbq, "wbk": wbk, "wbv_row": wbv_row,
            "b1": b1, "b2": b2, "pbias": pb, "lng": lng, "lnb": lnb,
            "ZB": ZB, "maskf": maskf, "ident": ident,
            "cosq": cosq, "sinq": sinq, "cosk": cosk, "sink": sink,
            "lq1": np.asarray(inp["L0_attn_a_lambda_q1"]).reshape(1, 64),
            "lq2": np.asarray(inp["L0_attn_a_lambda_q2"]).reshape(1, 64),
            "lk1": np.asarray(inp["L0_attn_a_lambda_k1"]).reshape(1, 64),
            "lk2": np.asarray(inp["L0_attn_a_lambda_k2"]).reshape(1, 64),
            "alpha": np.asarray(inp["L0_attn_a_alpha"]).reshape(1, 1),
        }
        per_core.append({k: np.ascontiguousarray(v.astype(np.float32, copy=False)) for k, v in d.items()})
    return per_core


def _host_gather(results):
    out = np.empty((16, S, E), dtype=np.float32)
    for core, r in enumerate(results):
        y = r["y"]
        for s in range(2):
            out[2 * core + s] = y[s].reshape(E, S).T
    return out


class Sem:
    __slots__ = ("h", "n", "name")


class Ctx:
    def __init__(self, nc, es):
        self.nc = nc
        self.es = es
        self._sems = []

    def sem(self, name):
        s = Sem()
        s.h = self.es.enter_context(self.nc.semaphore(name))
        s.n = 0
        s.name = name
        self._sems.append(s)
        return s


def inc(instr, sem, k=1):
    instr.then_inc(sem.h, k)
    sem.n += k
    return sem.n


def wt(eng, sem, v=None):
    eng.wait_ge(sem.h, sem.n if v is None else v)


def build_nc():
    nc = bass.Bass("TRN2", target_bir_lowering=False)
    es = contextlib.ExitStack()
    with es:
        _build_body(nc, es)
    return nc


class NS:
    pass


def _build_body(nc, es):
    dt_in = {}
    def din(name, shape):
        dt_in[name] = nc.dram_tensor(name, shape, F32, kind="ExternalInput").ap()
        return dt_in[name]

    xt = din("xt", [2, NT, 128, E])
    wqT = din("wqT", [E, E]); wkT = din("wkT", [E, E]); wvT = din("wvT", [E, 2 * E])
    wprojT = din("wprojT", [2 * E, 2 * E])
    wwqT = din("wwqT", [E, E]); wwkT = din("wwkT", [E, E]); wwvT = din("wwvT", [E, E])
    projwT = din("projwT", [E, E])
    w1T = din("w1T", [E, 4 * E]); w2T = din("w2T", [4 * E, E])
    wbq = din("wbq", [128, 8]); wbk = din("wbk", [128, 8]); wbv_row = din("wbv_row", [E])
    b1 = din("b1", [128, 32]); b2 = din("b2", [128, 8])
    pbias = din("pbias", [128, 8]); lng = din("lng", [128, 8]); lnb = din("lnb", [128, 8])
    ZBin = din("ZB", [16, 128, 2560])
    maskf = din("maskf", [128, 2560]); identin = din("ident", [128, 128])
    cosq_in = din("cosq", [128, 256]); sinq_in = din("sinq", [128, 256])
    cosk_in = din("cosk", [128, 256]); sink_in = din("sink", [128, 256])
    lq1 = din("lq1", [1, 64]); lq2 = din("lq2", [1, 64])
    lk1 = din("lk1", [1, 64]); lk2 = din("lk2", [1, 64])
    alpha_in = din("alpha", [1, 1])

    yout = nc.dram_tensor("y", [2, NC_, 128, S], F32, kind="ExternalOutput").ap()
    taps = {name: nc.dram_tensor("tap_" + name, shape, dt, kind="ExternalOutput").ap()
            for (name, shape, dt) in _DEBUG_TAPS}

    wq_c = nc.dram_tensor("wq_c", [E, E], BF16).ap()
    wk_c = nc.dram_tensor("wk_c", [E, E], BF16).ap()
    wv_c = nc.dram_tensor("wv_c", [E, 2 * E], BF16).ap()
    wproj_c = nc.dram_tensor("wproj_c", [2 * E, 2 * E], BF16).ap()
    wwq_c = nc.dram_tensor("wwq_c", [E, E], BF16).ap()
    wwk_c = nc.dram_tensor("wwk_c", [E, E], BF16).ap()
    wwv_c = nc.dram_tensor("wwv_c", [E, E], BF16).ap()
    projw_c = nc.dram_tensor("projw_c", [E, E], BF16).ap()
    t1_c = nc.dram_tensor("t1_c", [E, 4 * E], BF16).ap()
    t2_c = nc.dram_tensor("t2_c", [4 * E, E], BF16).ap()
    expb_c = nc.dram_tensor("expb_c", [16, 128, 2560], BF16).ap()
    bounce_c = nc.dram_tensor("bounce_c", [4096], F32).ap()

    ctx = Ctx(nc, es)
    dL = ctx.sem("dL")
    dX0 = ctx.sem("dX0")
    dX1 = ctx.sem("dX1")
    dS = ctx.sem("dS")
    sT = ctx.sem("sT")
    sV = ctx.sem("sV")
    sA = ctx.sem("sA")
    sG = ctx.sem("sG")

    sb = lambda name, shape, dt=F32: es.enter_context(nc.sbuf_tensor(name, shape, dt))

    z = NS()
    z.ps = es.enter_context(nc.psum_tensor("ps", [128, 3584], F32))
    z.psbf = es.enter_context(nc.psum_tensor("psbf", [128, 1024], BF16))
    z.ident = sb("identb", [128, 128], BF16)
    z.ones_f = sb("ones_f", [128, 1], F32)
    z.ones_b = sb("ones_b", [128, 1], BF16)
    z.ones128 = sb("ones128", [128, 128], BF16)
    z.ones128f = sb("ones128f", [128, 128], F32)
    z.identf = sb("identf", [128, 128], F32)
    z.consts = sb("consts", [128, 16], F32)
    z.rows = sb("rows", [1, 1024], F32)
    z.b1 = sb("b1_sb", [128, 32], F32); z.b2 = sb("b2_sb", [128, 8], F32)
    z.wbq = sb("wbq_sb", [128, 8], F32); z.wbk = sb("wbk_sb", [128, 8], F32)
    z.pbias = sb("pbias_sb", [128, 8], F32)
    z.lng = sb("lng_sb", [128, 8], F32); z.lnb = sb("lnb_sb", [128, 8], F32)
    z.consts2 = sb("consts2", [128, 8], F32)
    prep_es = contextlib.ExitStack()
    z.lrows = prep_es.enter_context(nc.sbuf_tensor("lrows", [1, 512], F32))
    z.maskf = prep_es.enter_context(nc.sbuf_tensor("maskf_sb", [128, 2560], F32))
    z.wchunk = prep_es.enter_context(nc.sbuf_tensor("wchunk", [128, 4096], F32))
    z.wchunk2 = prep_es.enter_context(nc.sbuf_tensor("wchunk2", [128, 4096], F32))
    z.bchunk = prep_es.enter_context(nc.sbuf_tensor("bchunk", [128, 4096], BF16))
    z.tpart = prep_es.enter_context(nc.sbuf_tensor("tpart", [1, 4096], F32))

    blk = es.enter_context(nc.Block())
    z.nc = nc; z.blk = blk; z.ctx = ctx
    z.dL, z.dS, z.sT, z.sV, z.sA, z.sG = dL, dS, sT, sV, sA, sG
    z.dX = [dX0, dX1]
    z.taps = taps
    z.yout = yout
    z.xt = xt
    z.dram_tabs = (cosq_in, sinq_in, cosk_in, sink_in)
    z.wbv_row = wbv_row
    z.bounce = bounce_c

    @blk.sync
    def _(q):
        q.dma_start(out=z.maskf[:], in_=maskf).then_inc(dL.h, 16); dL.n += 16
        q.dma_start(out=z.b1[:], in_=b1).then_inc(dL.h, 16); dL.n += 16
        q.dma_start(out=z.b2[:], in_=b2).then_inc(dL.h, 16); dL.n += 16
        q.dma_start(out=z.wbq[:], in_=wbq).then_inc(dL.h, 16); dL.n += 16
        q.dma_start(out=z.wbk[:], in_=wbk).then_inc(dL.h, 16); dL.n += 16
        q.dma_start(out=z.pbias[:], in_=pbias).then_inc(dL.h, 16); dL.n += 16
        q.dma_start(out=z.lng[:], in_=lng).then_inc(dL.h, 16); dL.n += 16
        q.dma_start(out=z.lnb[:], in_=lnb).then_inc(dL.h, 16); dL.n += 16
        q.dma_start(out=z.lrows[0:1, 0:64], in_=lq1).then_inc(dL.h, 16); dL.n += 16
        q.dma_start(out=z.lrows[0:1, 64:128], in_=lk1).then_inc(dL.h, 16); dL.n += 16
        q.dma_start(out=z.lrows[0:1, 128:192], in_=lq2).then_inc(dL.h, 16); dL.n += 16
        q.dma_start(out=z.lrows[0:1, 192:256], in_=lk2).then_inc(dL.h, 16); dL.n += 16
        q.dma_start(out=z.lrows[0:1, 256:257] if False else z.rows[0:1, 0:1], in_=alpha_in).then_inc(dL.h, 16); dL.n += 16
        q.dma_start(out=z.wchunk[:, 0:128], in_=identin).then_inc(dL.h, 16); dL.n += 16
        q.dma_start(out=z.identf[:], in_=identin).then_inc(dL.h, 16); dL.n += 16
    ld_consts = dL.n

    @blk.vector
    def _(v):
        wt(v, dL, ld_consts)
        v.memset(z.consts[:], 0.0)
        v.memset(z.consts2[:], 0.0)
        v.drain()
        v.memset(z.ones_f[:], 1.0)
        v.memset(z.ones_b[:], 1.0)
        v.memset(z.ones128[:], 1.0)
        v.memset(z.ones128f[:], 1.0)
        v.memset(z.consts[:, 0:1], -0.5)
        v.memset(z.consts[:, 1:2], 0.5)
        v.memset(z.consts[:, 2:3], 1e-5)
        v.memset(z.consts[:, 3:4], 1.0 / 1024.0)
        v.memset(z.consts[:, 8:9], 1e-30)
        v.tensor_copy(z.ident[:], z.wchunk[:, 0:128])
        v.drain()
        v.tensor_tensor(z.lrows[0:1, 256:320], z.lrows[0:1, 0:64], z.lrows[0:1, 64:128], ALU.mult)
        v.tensor_tensor(z.lrows[0:1, 320:384], z.lrows[0:1, 128:192], z.lrows[0:1, 192:256], ALU.mult)
        v.drain()
        v.tensor_reduce(z.lrows[0:1, 384:385], z.lrows[0:1, 256:320], mybir.AxisListType.X, ALU.add)
        v.tensor_reduce(z.lrows[0:1, 385:386], z.lrows[0:1, 320:384], mybir.AxisListType.X, ALU.add)
        v.drain().then_inc(sV.h, 1); sV.n += 1
    v_dots = sV.n

    @blk.scalar
    def _(s):
        wt(s, sV, v_dots)
        s.activation(z.lrows[0:1, 386:388], z.lrows[0:1, 384:386], AF.Exp).then_inc(sA.h, 1); sA.n += 1
    a_exps = sA.n

    @blk.vector
    def _(v):
        wt(v, sA, a_exps)
        v.tensor_tensor(z.lrows[0:1, 388:389], z.lrows[0:1, 386:387], z.lrows[0:1, 387:388], ALU.subtract)
        v.drain()
        v.tensor_scalar(z.lrows[0:1, 389:390], z.lrows[0:1, 388:389], 0.2, None, ALU.add)
        v.drain()
        v.tensor_scalar(z.lrows[0:1, 390:391], z.lrows[0:1, 389:390], -1.0, None, ALU.mult)
        v.tensor_scalar(z.rows[0:1, 1:2], z.rows[0:1, 0:1], 2.0, 1.0, ALU.mult, ALU.add)
        v.drain()
        v.tensor_scalar(z.rows[0:1, 2:3], z.rows[0:1, 1:2], 1.0 / math.pi, None, ALU.mult)
        v.scalar_tensor_tensor(z.rows[0:1, 3:4], z.rows[0:1, 1:2], 0.5, z.rows[0:1, 0:1], ALU.mult, ALU.subtract)
        v.drain().then_inc(sV.h, 1); sV.n += 1
    v_consts = sV.n

    @blk.gpsimd
    def _(g):
        wt(g, sV, v_consts)
        g.dma_start(out=z.bounce[0:1], in_=z.lrows[0:1, 389:390]).then_inc(dS.h, 16); dS.n += 16
        g.dma_start(out=z.bounce[1:2], in_=z.lrows[0:1, 390:391]).then_inc(dS.h, 16); dS.n += 16
        g.dma_start(out=z.bounce[2:3], in_=z.rows[0:1, 2:3]).then_inc(dS.h, 16); dS.n += 16
        g.dma_start(out=z.bounce[3:4], in_=z.rows[0:1, 3:4]).then_inc(dS.h, 16); dS.n += 16
    @blk.sync
    def _(q):
        wt(q, dS)
        wt(q, dL)
        q.dma_start(out=z.consts[:, 4:8], in_=bass.AP(z.bounce.tensor, 0, [[0, 128], [1, 4]])).then_inc(dL.h, 16); dL.n += 16
    z.gp_consts = dL.n

    def cache_weight(src, dst, rows, cols):
        for r0 in range(0, rows, 128):
            @blk.sync
            def _(q):
                wt(q, sV)
                wt(q, dL)
                q.dma_start(out=z.wchunk[:, 0:cols], in_=src[r0:r0 + 128, :]).then_inc(dL.h, 16); dL.n += 16
            @blk.vector
            def _(v):
                wt(v, dL)
                wt(v, dS)
                v.tensor_copy(z.bchunk[:, 0:cols], z.wchunk[:, 0:cols]).then_inc(sV.h, 1); sV.n += 1
            @blk.gpsimd
            def _(g):
                wt(g, sV)
                g.dma_start(out=dst[r0:r0 + 128, :], in_=z.bchunk[:, 0:cols]).then_inc(dS.h, 16); dS.n += 16

    cache_weight(wprojT, wproj_c, 2 * E, 2 * E)
    z.wdirect = dict(wqT=wqT, wkT=wkT, wvT=wvT, wwqT=wwqT, wwkT=wwkT, wwvT=wwvT,
                     projwT=projwT)

    def tern_weight(srcT, dstT, rows, cols, mean_count, s_col, inv_col):
        nch = rows // 128
        for r0i in range(nch):
            r0 = r0i * 128
            @blk.sync
            def _(q):
                wt(q, sA)
                wt(q, sV)
                wt(q, dL)
                q.dma_start(out=z.wchunk[:, 0:cols], in_=srcT[r0:r0 + 128, :]).then_inc(dL.h, 16); dL.n += 16
            @blk.scalar
            def _(s):
                wt(s, dL)
                wt(s, sT)
                s.activation(z.wchunk2[:, 0:cols], z.wchunk[:, 0:cols], AF.Abs).then_inc(sA.h, 1); sA.n += 1
            @blk.tensor
            def _(t):
                wt(t, sA)
                for ji, j in enumerate(range(0, cols, 512)):
                    if rows == 1024:
                        k = ji % 4
                        fl = dict(start=(r0i % 4 == 0 and ji < 4), stop=(r0i % 4 == 3 and ji >= 4))
                        if r0i % 4 == 0 and ji < 4:
                            wt(t, sV)
                    else:
                        k = ji * 2 + (r0i // 8) % 2
                        fl = dict(start=(r0i % 8 == 0), stop=(r0i % 8 == 7))
                        if r0i % 16 == 0 and ji == 0:
                            wt(t, sV)
                    t.matmul(z.ps[0:1, k * 512:(k + 1) * 512], z.ones_f[:], z.wchunk2[:, j:j + 512], **fl)
                t.drain().then_inc(sT.h, 1); sT.n += 1
            if r0i == nch // 2 - 1 or r0i == nch - 1:
                half = 0 if r0i == nch // 2 - 1 else 1
                @blk.vector
                def _(v):
                    wt(v, sT)
                    v.tensor_copy(z.tpart[0:1, half * 2048:(half + 1) * 2048], z.ps[0:1, 0:2048])
                    v.drain().then_inc(sV.h, 1); sV.n += 1
        @blk.vector
        def _(v):
            wt(v, sV)
            v.tensor_scalar(z.wchunk[0:1, 0:4096], z.tpart[0:1, :], 128.0, MAGIC, ALU.mult, ALU.add)
            v.drain()
            v.tensor_scalar(z.wchunk[0:1, 0:4096], z.wchunk[0:1, 0:4096], MAGIC, 1.0 / 128.0,
                            ALU.subtract, ALU.mult)
            v.drain()
            v.tensor_tensor(z.wchunk2[0:1, 0:4096], z.tpart[0:1, :], z.wchunk[0:1, 0:4096], ALU.subtract)
            v.drain()
            v.tensor_reduce(z.rows[0:1, 8:9], z.wchunk[0:1, 0:4096], mybir.AxisListType.X, ALU.add)
            v.tensor_reduce(z.rows[0:1, 10:11], z.wchunk2[0:1, 0:4096], mybir.AxisListType.X, ALU.add)
            v.drain()
            v.tensor_tensor(z.rows[0:1, 8:9], z.rows[0:1, 8:9], z.rows[0:1, 10:11], ALU.add)
            v.drain()
            v.tensor_scalar(z.rows[0:1, 8:9], z.rows[0:1, 8:9], 1.0 / mean_count, None, ALU.mult)
            v.drain()
            v.tensor_scalar(z.rows[0:1, 8:9], z.rows[0:1, 8:9], 1e-5, None, ALU.max)
            v.drain()
            v.reciprocal(z.rows[0:1, 9:10], z.rows[0:1, 8:9])
            v.drain().then_inc(sV.h, 1); sV.n += 1
        @blk.gpsimd
        def _(g):
            wt(g, sV)
            g.dma_start(out=z.bounce[0:2], in_=z.rows[0:1, 8:10]).then_inc(dS.h, 16); dS.n += 16
        @blk.sync
        def _(q):
            wt(q, dS)
            wt(q, dL)
            q.dma_start(out=z.consts[:, s_col:s_col + 1], in_=bass.AP(z.bounce.tensor, 0, [[0, 128], [1, 1]])).then_inc(dL.h, 16); dL.n += 16
            q.dma_start(out=z.consts[:, inv_col:inv_col + 1], in_=bass.AP(z.bounce.tensor, 1, [[0, 128], [1, 1]])).then_inc(dL.h, 16); dL.n += 16
        @blk.gpsimd
        def _(g):
            wt(g, dL)
            g.drain().then_inc(sG.h, 1); sG.n += 1
        for r0 in range(0, rows, 128):
            @blk.sync
            def _(q):
                wt(q, sA)
                wt(q, sV)
                wt(q, dL)
                q.dma_start(out=z.wchunk[:, 0:cols], in_=srcT[r0:r0 + 128, :]).then_inc(dL.h, 16); dL.n += 16
            @blk.scalar
            def _(s):
                wt(s, dL)
                wt(s, sG)
                wt(s, sV)
                s.activation(z.wchunk2[:, 0:cols], z.wchunk[:, 0:cols], AF.Sign,
                             bias=z.consts[:, 0:1], scale=z.consts[:, inv_col:inv_col + 1])
                s.drain()
                s.activation(z.wchunk[:, 0:cols], z.wchunk[:, 0:cols], AF.Sign,
                             bias=z.consts[:, 1:2], scale=z.consts[:, inv_col:inv_col + 1])
                s.drain().then_inc(sA.h, 1); sA.n += 1
            @blk.vector
            def _(v):
                wt(v, sA)
                wt(v, dS)
                v.tensor_tensor(z.wchunk2[:, 0:cols], z.wchunk2[:, 0:cols], z.wchunk[:, 0:cols], ALU.add)
                v.drain()
                v.tensor_scalar(z.bchunk[:, 0:cols], z.wchunk2[:, 0:cols], 0.5, None, ALU.mult).then_inc(sV.h, 1); sV.n += 1
            @blk.gpsimd
            def _(g):
                wt(g, sV)
                g.dma_start(out=dstT[r0:r0 + 128, :], in_=z.bchunk[:, 0:cols]).then_inc(dS.h, 16); dS.n += 16

    tern_weight(w1T, t1_c, E, 4 * E, E * 4 * E, 9, 11)
    tern_weight(w2T, t2_c, 4 * E, E, E * 4 * E, 10, 12)

    for h in range(16):
        @blk.sync
        def _(q):
            wt(q, sA)
            wt(q, sV)
            wt(q, dL)
            q.dma_start(out=z.wchunk[:, 0:2560], in_=ZBin[h]).then_inc(dL.h, 16); dL.n += 16
        @blk.scalar
        def _(s):
            wt(s, dL)
            wt(s, sV)
            s.activation(z.wchunk2[:, 0:2560], z.wchunk[:, 0:2560], AF.Exp).then_inc(sA.h, 1); sA.n += 1
        @blk.vector
        def _(v):
            wt(v, sA)
            wt(v, dS)
            v.tensor_tensor(z.bchunk[:, 0:2560], z.wchunk2[:, 0:2560], z.maskf[:], ALU.mult).then_inc(sV.h, 1); sV.n += 1
        @blk.gpsimd
        def _(g):
            wt(g, sV)
            g.dma_start(out=expb_c[h], in_=z.bchunk[:, 0:2560]).then_inc(dS.h, 16); dS.n += 16

    prep_es.close()
    z.weights = dict(wq_c=wq_c, wk_c=wk_c, wv_c=wv_c, wproj_c=wproj_c, wwq_c=wwq_c,
                     wwk_c=wwk_c, wwv_c=wwv_c, projw_c=projw_c, t1_c=t1_c, t2_c=t2_c,
                     expb_c=expb_c)

    for seq in range(_BUILD_OPTS["n_seq"]):
        _build_seq(z, seq)

    @blk.gpsimd
    def _(g):
        wt(g, dS)
        g.nop()

    nc.all_engine_barrier()


def _quiesce(z):
    blk = z.blk
    sems = [z.dL, z.dS, z.sT, z.sV, z.sA, z.sG]
    for opener in (blk.sync, blk.tensor, blk.vector, blk.scalar, blk.gpsimd):
        def body(eng):
            for s in sems:
                eng.wait_ge(s.h, s.n)
        opener(body)


def _phase_a(z, seq):
    nc, blk = z.nc, z.blk
    dL, dS, sT, sV, sA, sG = z.dL, z.dS, z.sT, z.sV, z.sA, z.sG
    W = z.weights
    ps, psbf = z.ps, z.psbf

    pes = contextlib.ExitStack()
    sb = lambda name, shape, dt=F32: pes.enter_context(nc.sbuf_tensor(name + z.sfx, shape, dt))
    xtile = sb("xtile", [128, 2048], F32)
    xsq = sb("xsq", [128, 1024], F32)
    rtmp = sb("rtmp", [128, 4096], F32)
    ropeQ = sb("ropeQ", [128, 1024], BF16)
    ropeK = sb("ropeK", [128, 1024], BF16)
    mu_b = sb("mu_b", [128, 128], F32)
    r_b = sb("r_b", [128, 128], F32)
    wq_sb = sb("wq_sb", [128, 8 * 1024], BF16)
    wk_sb = sb("wk_sb", [128, 8 * 1024], BF16)
    wstg = sb("wstg", [128, 8 * 1024], F32)
    qknorm = sb("qknorm", [128, 64], F32)
    qkmax = sb("qkmax", [128, 64], F32)
    tabs = sb("tabs", [128, 4 * 256], F32)

    @blk.sync
    def _(q):
        wt(q, dL)
        q.dma_start(out=wstg[:], in_=bass.AP(z.wdirect["wqT"].tensor, 0, [[1024, 128], [131072, 8], [1, 1024]])).then_inc(dL.h, 16); dL.n += 16
        for i in range(4):
            q.dma_start(out=tabs[:, i * 256:(i + 1) * 256], in_=z.dram_tabs[i]).then_inc(dL.h, 16); dL.n += 16
    tok_wq = dL.n
    @blk.vector
    def _(v):
        wt(v, dL, tok_wq)
        v.tensor_copy(wq_sb[:], wstg[:]).then_inc(sV.h, 1); sV.n += 1
    tok_wqc = sV.n
    @blk.sync
    def _(q):
        wt(q, sV, tok_wqc)
        wt(q, dL)
        q.dma_start(out=wstg[:], in_=bass.AP(z.wdirect["wkT"].tensor, 0, [[1024, 128], [131072, 8], [1, 1024]])).then_inc(dL.h, 16); dL.n += 16
    tok_wk = dL.n
    @blk.vector
    def _(v):
        wt(v, dL, tok_wk)
        v.tensor_copy(wk_sb[:], wstg[:]).then_inc(sV.h, 1); sV.n += 1
    z.tok["qk_w_ready"] = (sV, sV.n)
    tok_w = dL.n

    tok = z.tok
    tok["xtile_free"] = [(sV, 0), (sV, 0)]
    tok["stats_evd"] = (sV, 0)
    tok["xsq_free"] = (sT, 0)
    tok["rope_done"] = (sV, 0)
    tok["bank7_ev"] = (sV, 0)
    tok["mub_free"] = (sV, 0)

    for tt in range(NT):
        par = tt % 2
        xsl = lambda a, b: xtile[:, par * 1024 + a: par * 1024 + b]
        u1sl = lambda a, b: z.u1all[:, tt * 1024 + a: tt * 1024 + b]

        dX = z.dX[par]
        @blk.sync
        def _(q):
            s_, v_ = tok["xtile_free"][par]
            wt(q, s_, v_)
            q.dma_start(out=xsl(0, 1024), in_=z.xt[seq, tt]).then_inc(dX.h, 16); dX.n += 16
        tok_x = dX.n

        @blk.scalar
        def _(s):
            wt(s, dX, tok_x)
            s_, v_ = tok["xsq_free"]
            wt(s, s_, v_)
            s.activation(xsq[:], xsl(0, 1024), AF.Square).then_inc(sA.h, 1); sA.n += 1
        tok_sq = sA.n

        @blk.tensor
        def _(t):
            wt(t, dX, tok_x)
            wt(t, sA, tok_sq)
            s_, v_ = tok["stats_evd"]
            wt(t, s_, v_)
            t.matmul(ps[:, 2048:2560], z.ones128f[:], xsl(0, 512), start=True, stop=True)
            t.matmul(ps[:, 2560:3072], z.ones128f[:], xsl(512, 1024), start=True, stop=True).then_inc(sT.h, 1); sT.n += 1
        tok_summ = sT.n

        @blk.vector
        def _(v):
            wt(v, sT, tok_summ)
            v.tensor_reduce(mu_b[:], bass.AP(ps, 2048, [[3584, 128], [1, 128], [128, 8]]),
                            mybir.AxisListType.X, ALU.add)
            v.drain()
            v.tensor_scalar(mu_b[:], mu_b[:], 1.0 / 1024.0, None, ALU.mult).then_inc(sV.h, 1); sV.n += 1
        tok_sumred = sV.n

        @blk.tensor
        def _(t):
            wt(t, sV, tok_sumred)
            t.matmul(ps[:, 2048:2560], z.ones128f[:], xsq[:, 0:512], start=True, stop=True)
            t.matmul(ps[:, 2560:3072], z.ones128f[:], xsq[:, 512:1024], start=True, stop=True).then_inc(sT.h, 1); sT.n += 1
        tok_ssqm = sT.n

        @blk.vector
        def _(v):
            wt(v, sT, tok_ssqm)
            v.tensor_reduce(r_b[:], bass.AP(ps, 2048, [[3584, 128], [1, 128], [128, 8]]),
                            mybir.AxisListType.X, ALU.add)
            v.drain()
            v.tensor_scalar(r_b[:], r_b[:], 1.0 / 1024.0, None, ALU.mult)
            v.drain()
            v.tensor_tensor(xsq[:, 0:128], mu_b[:], mu_b[:], ALU.mult)
            v.drain()
            v.tensor_tensor(r_b[:], r_b[:], xsq[:, 0:128], ALU.subtract).then_inc(sV.h, 1); sV.n += 1
        tok_var = sV.n
        tok["stats_evd"] = (sV, tok_var)

        @blk.scalar
        def _(s):
            wt(s, sV, tok_var)
            s.activation(r_b[:], r_b[:], AF.Sqrt, bias=z.consts[:, 2:3]).then_inc(sA.h, 1); sA.n += 1
        tok_std = sA.n

        @blk.vector
        def _(v):
            wt(v, sA, tok_std)
            v.reciprocal(r_b[:], r_b[:])
            v.drain()
            mu_v = bass.AP(mu_b, 0, [[128, 128], [0, 8], [1, 128]])
            r_v = bass.AP(r_b, 0, [[128, 128], [0, 8], [1, 128]])
            v.tensor_tensor(xsq[:], xsl(0, 1024), mu_v, ALU.subtract)
            v.drain()
            v.tensor_tensor(u1sl(0, 1024), xsq[:], r_v, ALU.mult).then_inc(sV.h, 1); sV.n += 1
        tok_u1 = sV.n
        tok["xtile_free"][par] = (sV, tok_u1)
        tok["mub_free"] = (sV, tok_u1)
        tok["xsq_free"] = (sV, tok_u1)

        @blk.tensor
        def _(t):
            wt(t, sV, tok_u1)
            if tt == 0:
                wt(t, dL, tok_w)
                s_, v_ = z.tok["qk_w_ready"]
                wt(t, s_, v_)
            s_, v_ = tok["rope_done"]
            wt(t, s_, v_)
            for c in range(8):
                fl = dict(start=(c == 0), stop=(c == 7))
                u1c = u1sl(c * 128, (c + 1) * 128)
                t.matmul(ps[:, 0:512], u1c, wq_sb[:, c * 1024 + 0:c * 1024 + 512], **fl)
                t.matmul(ps[:, 512:1024], u1c, wq_sb[:, c * 1024 + 512:c * 1024 + 1024], **fl)
                t.matmul(ps[:, 1024:1536], u1c, wk_sb[:, c * 1024 + 0:c * 1024 + 512], **fl)
                last = t.matmul(ps[:, 1536:2048], u1c, wk_sb[:, c * 1024 + 512:c * 1024 + 1024], **fl)
            last.then_inc(sT.h, 1); sT.n += 1
        tok_qk = sT.n

        @blk.vector
        def _(v):
            wt(v, sT, tok_qk)
            vws = []
            for (src_off, dst, cbase, sbase, toff) in ((0, ropeQ, 0, 256, 0),
                                                       (1024, ropeK, 512, 768, 2048)):
                x1v = bass.AP(ps, src_off, [[3584, 128], [128, 8], [32, 4], [1, 16]])
                x2v = bass.AP(ps, src_off + 16, [[3584, 128], [128, 8], [32, 4], [1, 16]])
                cv = bass.AP(tabs, cbase + tt * 16, [[1024, 128], [0, 8], [0, 4], [1, 16]])
                sv_ = bass.AP(tabs, sbase + tt * 16, [[1024, 128], [0, 8], [0, 4], [1, 16]])
                t1v = bass.AP(rtmp, toff, [[4096, 128], [128, 8], [32, 4], [1, 16]])
                t2v = bass.AP(rtmp, toff + 16, [[4096, 128], [128, 8], [32, 4], [1, 16]])
                t3v = bass.AP(rtmp, toff + 1024, [[4096, 128], [128, 8], [32, 4], [1, 16]])
                t4v = bass.AP(rtmp, toff + 1040, [[4096, 128], [128, 8], [32, 4], [1, 16]])
                y1v = bass.AP(dst, 0, [[1024, 128], [128, 8], [32, 4], [1, 16]])
                y2v = bass.AP(dst, 16, [[1024, 128], [128, 8], [32, 4], [1, 16]])
                vws.append((x1v, x2v, cv, sv_, t1v, t2v, t3v, t4v, y1v, y2v))
            for (x1v, x2v, cv, sv_, t1v, t2v, t3v, t4v, y1v, y2v) in vws:
                v.tensor_tensor(t1v, x1v, cv, ALU.mult)
                v.tensor_tensor(t2v, x2v, sv_, ALU.mult)
                v.tensor_tensor(t3v, x2v, cv, ALU.mult)
                v.tensor_tensor(t4v, x1v, sv_, ALU.mult)
            v.drain()
            for (x1v, x2v, cv, sv_, t1v, t2v, t3v, t4v, y1v, y2v) in vws:
                v.tensor_tensor(y1v, t1v, t2v, ALU.subtract)
                last = v.tensor_tensor(y2v, t3v, t4v, ALU.add)
            last.then_inc(sV.h, 1); sV.n += 1
        tok_rope = sV.n
        tok["rope_done"] = (sV, tok_rope)

        @blk.tensor
        def _(t):
            wt(t, sV, tok_rope)
            s_, v_ = tok["bank7_ev"]
            wt(t, s_, v_)
            for g in range(8):
                last = t.transpose(psbf[:, g * 128:(g + 1) * 128], ropeQ[:, g * 128:(g + 1) * 128], z.ident[:])
            last.then_inc(sT.h, 1); sT.n += 1
        tok_trQ = sT.n

        @blk.vector
        def _(v):
            wt(v, sT, tok_trQ)
            v.tensor_copy(bass.AP(z.QT, tt * 128, [[16384, 128], [2048, 8], [1, 128]]), psbf[:, 0:1024]).then_inc(sV.h, 1); sV.n += 1
        tok_qtev = sV.n

        @blk.tensor
        def _(t):
            wt(t, sV, tok_qtev)
            for g in range(8):
                last = t.transpose(psbf[:, g * 128:(g + 1) * 128], ropeK[:, g * 128:(g + 1) * 128], z.ident[:])
            last.then_inc(sT.h, 1); sT.n += 1
        tok_trK = sT.n

        @blk.vector
        def _(v):
            wt(v, sT, tok_trK)
            v.tensor_copy(bass.AP(z.KT, tt * 128, [[16384, 128], [2048, 8], [1, 128]]), psbf[:, 0:1024]).then_inc(sV.h, 1); sV.n += 1
        tok_ktev = sV.n
        tok["bank7_ev"] = (sV, tok_ktev)

        @blk.vector
        def _(v):
            v.tensor_tensor(rtmp[:, 0:1024], ropeQ[:], ropeQ[:], ALU.mult)
            v.tensor_tensor(rtmp[:, 1024:2048], ropeK[:], ropeK[:], ALU.mult)
            v.drain()
            if tt == 0:
                v.tensor_reduce(qknorm[:, 0:32], bass.AP(rtmp, 0, [[4096, 128], [32, 32], [1, 32]]),
                                mybir.AxisListType.X, ALU.add)
                last = v.tensor_reduce(qknorm[:, 32:64], bass.AP(rtmp, 1024, [[4096, 128], [32, 32], [1, 32]]),
                                       mybir.AxisListType.X, ALU.add)
            else:
                v.tensor_reduce(qkmax[:, 0:32], bass.AP(rtmp, 0, [[4096, 128], [32, 32], [1, 32]]),
                                mybir.AxisListType.X, ALU.add)
                v.tensor_reduce(qkmax[:, 32:64], bass.AP(rtmp, 1024, [[4096, 128], [32, 32], [1, 32]]),
                                mybir.AxisListType.X, ALU.add)
                v.drain()
                last = v.tensor_tensor(qknorm[:], qknorm[:], qkmax[:], ALU.max)
            last.then_inc(sV.h, 1); sV.n += 1

    @blk.tensor
    def _(t):
        wt(t, sV)
        t.transpose(ps[0:64, 3072:3200], qknorm[:], z.identf[:])
        t.drain().then_inc(sT.h, 1); sT.n += 1
    tok_btr = z.sT.n

    @blk.vector
    def _(v):
        wt(v, sT, tok_btr)
        v.tensor_reduce(qkmax[0:64, 0:1], ps[0:64, 3072:3200], mybir.AxisListType.X, ALU.max)
        v.drain().then_inc(sV.h, 1); sV.n += 1
    tok_bred = z.sV.n

    @blk.tensor
    def _(t):
        wt(t, sV, tok_bred)
        t.transpose(ps[0:1, 3200:3264], qkmax[0:64, 0:1], z.identf[0:64, 0:64])
        t.drain().then_inc(sT.h, 1); sT.n += 1
    tok_btr2 = z.sT.n

    @blk.vector
    def _(v):
        wt(v, sT, tok_btr2)
        v.tensor_reduce(z.rows[0:1, 640:644], bass.AP(ps, 3200, [[3584, 1], [16, 4], [1, 16]]),
                        mybir.AxisListType.X, ALU.max)
        v.drain()
        v.tensor_tensor(z.rows[0:1, 644:646], z.rows[0:1, 640:642], z.rows[0:1, 642:644], ALU.mult)
        v.drain().then_inc(sV.h, 1); sV.n += 1
    tok_bmul = z.sV.n

    @blk.scalar
    def _(s):
        wt(s, sV, tok_bmul)
        s.activation(z.rows[0:1, 644:646], z.rows[0:1, 644:646], AF.Sqrt).then_inc(sA.h, 1); sA.n += 1
    tok_bsq = z.sA.n

    @blk.vector
    def _(v):
        wt(v, sA, tok_bsq)
        v.tensor_scalar(z.rows[0:1, 646:648], z.rows[0:1, 644:646], -1.0, None, ALU.mult)
        v.drain().then_inc(sV.h, 1); sV.n += 1
    tok_bneg = z.sV.n

    @blk.gpsimd
    def _(g):
        wt(g, sV, tok_bneg)
        wt(g, dL)
        g.dma_start(out=z.bounce[0:2], in_=z.rows[0:1, 646:648]).then_inc(dS.h, 16); dS.n += 16
    @blk.sync
    def _(q):
        wt(q, dS)
        wt(q, dL)
        q.dma_start(out=z.consts2[:, 0:2],
                    in_=bass.AP(z.bounce.tensor, 0, [[0, 128], [1, 2]])).then_inc(dL.h, 16); dL.n += 16
    z.tok["phaseA_done"] = (dL, z.dL.n)
    return pes


def _build_seq(z, seq):
    z.tok = {}
    z.sfx = "_s%d" % seq
    nc = z.nc
    ab_es = contextlib.ExitStack()
    z.QT = ab_es.enter_context(nc.sbuf_tensor("QT" + z.sfx, [128, 8 * 2048], BF16, side="right"))
    z.KT = ab_es.enter_context(nc.sbuf_tensor("KT" + z.sfx, [128, 8 * 2048], BF16, side="right"))
    z.u1all = ab_es.enter_context(nc.sbuf_tensor("u1all" + z.sfx, [128, NT * 1024], BF16, side="right"))
    _quiesce(z)
    pes = _phase_a(z, seq)
    stop = _BUILD_OPTS.get("stop_after")
    if stop == "A":
        if seq == 0:
            _dump_taps(z, dict(QT=z.QT, KT=z.KT, u1all=z.u1all, consts2=z.consts2, consts=z.consts))
        pes.close(); ab_es.close()
        return
    pes.close()
    tt_es = contextlib.ExitStack()
    z.TT = tt_es.enter_context(nc.sbuf_tensor("TT" + z.sfx, [128, 16 * 2048], BF16))
    _quiesce(z)
    pesb = _phase_b(z, seq)
    pesb.close()
    if stop == "B":
        if seq == 0:
            _dump_taps(z, dict(TT=z.TT, consts2=z.consts2))
        ab_es.close(); tt_es.close()
        return
    ab_es.close()
    stream_es = contextlib.ExitStack()
    z.xs = stream_es.enter_context(nc.sbuf_tensor("xs" + z.sfx, [128, 8 * 2048], F32, side="right"))
    _quiesce(z)
    pesc = _phase_c(z, seq)
    pesc.close()
    tt_es.close()
    if stop == "C":
        if seq == 0:
            _dump_taps(z, dict(xs=z.xs))
        stream_es.close()
        return
    _quiesce(z)
    pesd = _phase_ffn(z, seq, "D" + z.sfx)
    pesd.close()
    if stop == "D":
        if seq == 0:
            _dump_taps(z, dict(xs=z.xs))
        stream_es.close()
        return
    _quiesce(z)
    pese = _phase_e(z, seq)
    pese.close()
    if stop == "E":
        if seq == 0:
            _dump_taps(z, dict(xs=z.xs))
        stream_es.close()
        return
    _quiesce(z)
    pesf = _phase_ffn(z, seq, "F" + z.sfx)
    pesf.close()
    blk = z.blk
    @blk.gpsimd
    def _(g):
        wt(g, z.sV)
        for fc in range(8):
            g.dma_start(out=z.yout[seq, fc], in_=z.xs[:, fc * 2048:(fc + 1) * 2048]).then_inc(z.dS.h, 16); z.dS.n += 16
        wt(g, z.dS)
    stream_es.close()


def _dump_taps(z, table):
    if not z.taps:
        return
    blk = z.blk
    _quiesce(z)
    @blk.gpsimd
    def _(g):
        for name, ap in z.taps.items():
            src = table[name]
            g.dma_start(out=ap, in_=src[:]).then_inc(z.dS.h, 16); z.dS.n += 16


def kernel(**inputs):
    per_core = _host_prepare(inputs)
    nc = build_nc()
    res = run_bass_kernel_spmd(nc, per_core, core_ids=list(range(8)))
    return _host_gather(res.results)


def _phase_b(z, seq):
    nc, blk = z.nc, z.blk
    dL, dS, sT, sV, sA, sG = z.dL, z.dS, z.sT, z.sV, z.sA, z.sG
    W = z.weights
    ps, psbf = z.ps, z.psbf

    pes = contextlib.ExitStack()
    sb = lambda name, shape, dt=F32: pes.enter_context(nc.sbuf_tensor(name + z.sfx, shape, dt))
    wvp = sb("wvp", [128, 2048], BF16)
    Vp = sb("Vp", [128, 4096], BF16)
    Ebuf = sb("Ebuf", [128, 2048], BF16)
    scr = sb("scr", [128, 2048], F32)
    OcA = sb("OcA", [128, 2048], F32)
    OcB = sb("OcB", [128, 2048], F32)
    gnmu = sb("gnmu", [128, 2], F32)

    tok = z.tok
    tok["wvp_free"] = [(sV, sV.n), (sV, sV.n)]
    tok["expdone"] = (sA, sA.n)
    tok["ozfree"] = (sV, sV.n)
    tok["bank7_ev"] = (sV, sV.n)
    tok["gnmu_free"] = (sV, sV.n)
    tok["vev"] = [(sV, sV.n), (sV, sV.n)]

    SA, SB = 0, 512
    OA, OB = 1024, 1536
    ZA, ZB = 2048, 2560
    ST6 = 3072

    for pr in range(8):
        hA, hB = 2 * pr, 2 * pr + 1
        g = pr // 2
        r0 = (pr % 2) * 64
        par = pr % 2
        dX = z.dX[par]

        @blk.sync
        def _(q):
            s_, v_ = tok["wvp_free"][0]
            wt(q, s_, v_)
            q.dma_start(out=scr[:],
                        in_=bass.AP(z.wdirect["wvT"].tensor, hA * 128, [[2048, 128], [128 * 2048, 8], [1, 256]])
                        ).then_inc(dX.h, 16); dX.n += 16
        tok_wvf = dX.n
        @blk.vector
        def _(v):
            wt(v, dX, tok_wvf)
            s_, v_ = tok["wvp_free"][1]
            wt(v, s_, v_)
            v.tensor_copy(wvp[:], scr[:]).then_inc(sV.h, 1); sV.n += 1
        tok_wv = sV.n

        for tt in range(NT):
            vb = (tt % 2) * 512
            @blk.tensor
            def _(t):
                if tt == 0:
                    wt(t, sV, tok_wv)
                    s_, v_ = tok["expdone"]
                    wt(t, s_, v_)
                    wt(t, sV)
                else:
                    s_, v_ = tok["vev"][tt % 2]
                    wt(t, s_, v_)
                for c in range(8):
                    last = t.matmul(ps[:, vb:vb + 256], z.u1all[:, tt * 1024 + c * 128:tt * 1024 + (c + 1) * 128],
                             wvp[:, c * 256:(c + 1) * 256],
                             start=(c == 0), stop=(c == 7))
                last.then_inc(sT.h, 1); sT.n += 1
            tok_vmm = sT.n

            @blk.vector
            def _(v):
                wt(v, sT, tok_vmm)
                v.tensor_copy(Vp[:, tt * 256:tt * 256 + 256], ps[:, vb:vb + 256]).then_inc(sV.h, 1); sV.n += 1
            tok["vev"][tt % 2] = (sV, sV.n)
        tok["wvp_free"][0] = (sV, tok_wv)
        tok["wvp_free"][1] = (sT, sT.n)
        tok["ozfree"] = (sV, sV.n)

        for qb in range(4):
            q0 = qb * 512
            for s_i in (1, 2):
                gs = g + 4 * (s_i - 1)
                qcol = gs * 2048 + q0
                sT0 = sT.n
                sA0 = sA.n
                oz_s, oz_v = tok["ozfree"]

                @blk.tensor
                def _(t):
                    for kt in range(NT):
                        kcol = gs * 2048 + kt * 128
                        eb = (kt % 2) * 1024
                        wt(t, sA, sA0 + kt)
                        if kt == 0:
                            wt(t, oz_s, oz_v)
                        t.matmul(ps[:, SA:SA + 512], z.KT[r0:r0 + 32, kcol:kcol + 128],
                                 z.QT[r0:r0 + 32, qcol:qcol + 512], start=True, stop=True,
                                 tile_position=(r0, 0))
                        t.matmul(ps[:, SB:SB + 512], z.KT[r0 + 32:r0 + 64, kcol:kcol + 128],
                                 z.QT[r0 + 32:r0 + 64, qcol:qcol + 512], start=True, stop=True,
                                 tile_position=(r0 + 32, 0)).then_inc(sT.h, 1); sT.n += 1
                        if kt > 0:
                            ep = ((kt - 1) % 2) * 1024
                            fl = dict(start=(kt - 1 == 0), stop=False)
                            t.matmul(ps[:, ZA:ZA + 512], z.ones128[:], Ebuf[:, ep:ep + 512], **fl)
                            t.matmul(ps[:, ZB:ZB + 512], z.ones128[:], Ebuf[:, ep + 512:ep + 1024], **fl)
                            t.matmul(ps[:, OA:OA + 512], Vp[:, (kt - 1) * 256:(kt - 1) * 256 + 128],
                                     Ebuf[:, ep:ep + 512], **fl)
                            t.matmul(ps[:, OB:OB + 512], Vp[:, (kt - 1) * 256 + 128:(kt - 1) * 256 + 256],
                                     Ebuf[:, ep + 512:ep + 1024], **fl)
                    wt(t, sA, sA0 + NT)
                    ep = ((NT - 1) % 2) * 1024
                    fl = dict(start=False, stop=True)
                    t.matmul(ps[:, ZA:ZA + 512], z.ones128[:], Ebuf[:, ep:ep + 512], **fl)
                    t.matmul(ps[:, ZB:ZB + 512], z.ones128[:], Ebuf[:, ep + 512:ep + 1024], **fl)
                    t.matmul(ps[:, OA:OA + 512], Vp[:, (NT - 1) * 256:(NT - 1) * 256 + 128],
                             Ebuf[:, ep:ep + 512], **fl)
                    t.matmul(ps[:, OB:OB + 512], Vp[:, (NT - 1) * 256 + 128:(NT - 1) * 256 + 256],
                             Ebuf[:, ep + 512:ep + 1024], **fl).then_inc(sT.h, 1); sT.n += 1

                @blk.scalar
                def _(s):
                    for kt in range(NT):
                        eb = (kt % 2) * 1024
                        wt(s, sT, sT0 + kt + 1)
                        s.activation(Ebuf[:, eb:eb + 1024], ps[:, 0:1024], AF.Exp,
                                     bias=z.consts2[:, s_i - 1:s_i]).then_inc(sA.h, 1); sA.n += 1
                assert sA.n == sA0 + NT
                tok["expdone"] = (sA, sA.n)
                tok_oz = sT.n

                @blk.vector
                def _(v):
                    wt(v, sT, tok_oz)
                    v.reciprocal(scr[:, 0:512], ps[:, ZA:ZA + 512])
                    v.reciprocal(scr[:, 512:1024], ps[:, ZB:ZB + 512])
                    v.drain()
                    if s_i == 1:
                        v.tensor_tensor(OcA[:, q0:q0 + 512], ps[:, OA:OA + 512], scr[:, 0:512], ALU.mult)
                        v.tensor_tensor(OcB[:, q0:q0 + 512], ps[:, OB:OB + 512], scr[:, 512:1024], ALU.mult).then_inc(sV.h, 1); sV.n += 1
                    else:
                        v.tensor_tensor(scr[:, 1024:1536], ps[:, OA:OA + 512], scr[:, 0:512], ALU.mult)
                        v.tensor_tensor(scr[:, 1536:2048], ps[:, OB:OB + 512], scr[:, 512:1024], ALU.mult)
                        v.drain()
                        v.scalar_tensor_tensor(OcA[:, q0:q0 + 512], scr[:, 1024:1536], z.consts[:, 5:6],
                                               OcA[:, q0:q0 + 512], ALU.mult, ALU.add)
                        v.scalar_tensor_tensor(OcB[:, q0:q0 + 512], scr[:, 1536:2048], z.consts[:, 5:6],
                                               OcB[:, q0:q0 + 512], ALU.mult, ALU.add).then_inc(sV.h, 1); sV.n += 1
                tok["ozfree"] = (sV, sV.n)

        for (Oc, h) in ((OcA, hA), (OcB, hB)):
            @blk.scalar
            def _(s):
                wt(s, sV)
                s.activation(scr[:], Oc[:], AF.Square).then_inc(sA.h, 1); sA.n += 1
            tok_gsq = sA.n

            @blk.tensor
            def _(t):
                wt(t, sA, tok_gsq)
                s_, v_ = tok["stats_evd_b"] if "stats_evd_b" in tok else (sV, 0)
                wt(t, s_, v_)
                for j in range(4):
                    t.matmul(ps[:, ST6:ST6 + 512], z.ones128f[:], Oc[:, j * 512:(j + 1) * 512],
                             start=(j == 0), stop=(j == 3))
                t.drain().then_inc(sT.h, 1); sT.n += 1
            tok_gsum = sT.n

            @blk.vector
            def _(v):
                wt(v, sT, tok_gsum)
                v.tensor_reduce(gnmu[:, 0:1], ps[:, ST6:ST6 + 512], mybir.AxisListType.X, ALU.add)
                v.drain()
                v.tensor_scalar(gnmu[:, 0:1], gnmu[:, 0:1], 1.0 / 262144.0, None, ALU.mult)
                v.drain().then_inc(sV.h, 1); sV.n += 1
            tok_gsr = sV.n

            @blk.tensor
            def _(t):
                wt(t, sV, tok_gsr)
                for j in range(4):
                    t.matmul(ps[:, ST6:ST6 + 512], z.ones128f[:], scr[:, j * 512:(j + 1) * 512],
                             start=(j == 0), stop=(j == 3))
                t.drain().then_inc(sT.h, 1); sT.n += 1
            tok_gss = sT.n

            @blk.vector
            def _(v):
                wt(v, sT, tok_gss)
                v.tensor_reduce(gnmu[:, 1:2], ps[:, ST6:ST6 + 512], mybir.AxisListType.X, ALU.add)
                v.drain()
                v.tensor_scalar(gnmu[:, 1:2], gnmu[:, 1:2], 1.0 / 262144.0, None, ALU.mult)
                v.drain()
                v.tensor_tensor(scr[:, 0:1], gnmu[:, 0:1], gnmu[:, 0:1], ALU.mult)
                v.drain()
                v.tensor_tensor(gnmu[:, 1:2], gnmu[:, 1:2], scr[:, 0:1], ALU.subtract)
                v.drain().then_inc(sV.h, 1); sV.n += 1
            tok["stats_evd_b"] = (sV, sV.n)
            tok_gvar = sV.n

            @blk.scalar
            def _(s):
                wt(s, sV, tok_gvar)
                s.activation(gnmu[:, 1:2], gnmu[:, 1:2], AF.Sqrt,
                             bias=z.consts[:, 2:3]).then_inc(sA.h, 1); sA.n += 1
            tok_gstd = sA.n

            @blk.vector
            def _(v):
                wt(v, sA, tok_gstd)
                v.reciprocal(gnmu[:, 1:2], gnmu[:, 1:2])
                v.drain()
                v.tensor_scalar(gnmu[:, 1:2], gnmu[:, 1:2], 0.8, None, ALU.mult)
                v.drain()
                wt(v, sT)
                v.tensor_scalar(wvp[:], Oc[:], gnmu[:, 0:1], gnmu[:, 1:2], ALU.subtract, ALU.mult)
                v.drain().then_inc(sV.h, 1); sV.n += 1
            tok_gn = sV.n

            for b in range(2):
                @blk.tensor
                def _(t):
                    wt(t, sV, tok_gn)
                    s_, v_ = tok["bank7_ev"]
                    wt(t, s_, v_)
                    for q_lo in range(8):
                        ql = b * 8 + q_lo
                        t.transpose(psbf[:, q_lo * 128:(q_lo + 1) * 128],
                                    bass.AP(wvp, ql, [[2048, 128], [16, 128]]), z.ident[:])
                    t.drain().then_inc(sT.h, 1); sT.n += 1
                tok_tr = sT.n

                @blk.vector
                def _(v):
                    wt(v, sT, tok_tr)
                    v.tensor_copy(z.TT[:, h * 2048 + b * 1024: h * 2048 + (b + 1) * 1024], psbf[:])
                    v.drain().then_inc(sV.h, 1); sV.n += 1
                tok["bank7_ev"] = (sV, sV.n)
            tok["wvp_free"] = [(sT, sT.n), (sT, sT.n)]

    z.tok["phaseB_done"] = (sV, z.sV.n)
    return pes


def _phase_c(z, seq):
    nc, blk = z.nc, z.blk
    dL, dS, sT, sV, sA, sG = z.dL, z.dS, z.sT, z.sV, z.sA, z.sG
    W = z.weights
    ps = z.ps

    pes = contextlib.ExitStack()
    sb = lambda name, shape, dt=F32: pes.enter_context(nc.sbuf_tensor(name + z.sfx, shape, dt))
    wpj = sb("wpj", [128, 4096], BF16)
    av = sb("av", [128, 4096], F32)
    scr1 = sb("c1s", [128, 2048], F32)
    scr2 = sb("c2s", [128, 2048], F32)
    scr3 = sb("c3s", [128, 2048], F32)
    scr4 = sb("c4s", [128, 2048], F32)

    tok = z.tok
    tok["wpj_free"] = (sV, sV.n)
    tok["av_free"] = (sV, sV.n)
    tok["chain_free"] = (sV, sV.n)

    for j in range(8):
        par = j % 2
        dX = z.dX[par]

        @blk.sync
        def _(q):
            s_, v_ = tok["wpj_free"]
            wt(q, s_, v_)
            q.dma_start(out=bass.AP(wpj, 0, [[4096, 128], [256, 16], [1, 128]]),
                        in_=bass.AP(W["wproj_c"].tensor, j * 128, [[2048, 128], [128 * 2048, 16], [1, 128]])
                        ).then_inc(dX.h, 16); dX.n += 16
            q.dma_start(out=bass.AP(wpj, 128, [[4096, 128], [256, 16], [1, 128]]),
                        in_=bass.AP(W["wproj_c"].tensor, (j + 8) * 128, [[2048, 128], [128 * 2048, 16], [1, 128]])
                        ).then_inc(dX.h, 16); dX.n += 16
            s_, v_ = tok["chain_free"]
            wt(q, s_, v_)
            q.dma_start(out=scr3[:],
                        in_=bass.AP(z.xt.tensor, seq * 2097152 + j * 128,
                                    [[1024, 128], [131072, 16], [1, 128]])
                        ).then_inc(dX.h, 16); dX.n += 16
        tok_w = dX.n

        for half in range(2):
            @blk.tensor
            def _(t):
                wt(t, dX, tok_w)
                if half == 0:
                    s_, v_ = tok["av_free"]
                    wt(t, s_, v_)
                else:
                    wt(t, sV)
                for q4 in range(4):
                    for c in range(16):
                        t.matmul(ps[:, q4 * 512:(q4 + 1) * 512],
                                 wpj[:, c * 256 + half * 128:c * 256 + half * 128 + 128],
                                 z.TT[:, c * 2048 + q4 * 512:c * 2048 + (q4 + 1) * 512],
                                 start=(c == 0), stop=(c == 15))
                t.drain().then_inc(sT.h, 1); sT.n += 1
            tok_mm = sT.n

            @blk.vector
            def _(v):
                wt(v, sT, tok_mm)
                v.tensor_copy(av[:, half * 2048:(half + 1) * 2048], ps[:, 0:2048])
                v.drain().then_inc(sV.h, 1); sV.n += 1
        tok["av_evict"] = (sV, sV.n)

        @blk.vector
        def _(v):
            wt(v, dX, tok_w)
            v.tensor_scalar(scr2[:], av[:, 0:2048], -1.0, None, ALU.mult)
            v.drain()
            v.tensor_tensor(scr1[:], av[:, 0:2048], scr2[:], ALU.max)
            v.drain()
            v.tensor_scalar(scr1[:], scr1[:], 1e-30, None, ALU.max)
            v.drain()
            v.reciprocal(scr2[:], scr1[:])
            v.tensor_scalar(scr4[:], scr1[:], 1.0, None, ALU.is_gt)
            v.drain()
            v.tensor_tensor(scr1[:], scr1[:], scr2[:], ALU.min)
            v.drain().then_inc(sV.h, 1); sV.n += 1
        tok_pre = sV.n

        @blk.scalar
        def _(s):
            wt(s, sV, tok_pre)
            s.activation(scr2[:], scr1[:], AF.Arctan)
            s.drain()
            s.activation(scr1[:], av[:, 0:2048], AF.Sign)
            s.drain().then_inc(sA.h, 1); sA.n += 1
        tok_act = sA.n

        @blk.vector
        def _(v):
            wt(v, sA, tok_act)
            v.scalar_tensor_tensor(av[:, 0:2048], scr4[:], math.pi / 2, scr2[:], ALU.mult, ALU.add)
            v.drain()
            v.tensor_tensor(scr4[:], scr4[:], scr2[:], ALU.mult)
            v.drain()
            v.scalar_tensor_tensor(scr4[:], scr4[:], -2.0, av[:, 0:2048], ALU.mult, ALU.add)
            v.drain()
            v.tensor_tensor(scr2[:], scr4[:], scr1[:], ALU.mult)
            v.drain()
            v.tensor_scalar(scr2[:], scr2[:], z.consts[:, 6:7], z.consts[:, 7:8], ALU.mult, ALU.add)
            v.drain()
            v.tensor_tensor(scr1[:], scr2[:], av[:, 2048:4096], ALU.mult)
            v.drain()
            v.tensor_tensor(z.xs[:, j * 2048:(j + 1) * 2048], scr1[:], scr3[:], ALU.add)
            v.drain().then_inc(sV.h, 1); sV.n += 1
        tok["chain_free"] = (sV, sV.n)
        tok["av_free"] = (sV, sV.n)
        tok["wpj_free"] = (sV, sV.n)

    z.tok["phaseC_done"] = (sV, z.sV.n)
    return pes


def _phase_ffn(z, seq, tag):
    nc, blk = z.nc, z.blk
    dL, dS, sT, sV, sA, sG = z.dL, z.dS, z.sT, z.sV, z.sA, z.sG
    W = z.weights
    ps = z.ps
    QW = 512

    pes = contextlib.ExitStack()
    sb = lambda name, shape, dt=F32: pes.enter_context(nc.sbuf_tensor(name + z.sfx, shape, dt))
    hbuf = sb("hbuf" + tag, [128, 32 * QW], F32)
    k2b = sb("k2b" + tag, [128, 32 * QW], BF16)
    k1 = sb("k1" + tag, [128, 8 * QW], BF16)
    reps = sb("reps" + tag, [128, 3 * QW], F32)
    scr = sb("scrD" + tag, [128, 2 * QW], F32)
    scr2 = sb("scr2D" + tag, [128, 2 * QW], F32)
    sqb8 = sb("sqbD" + tag, [128, 4096], BF16)
    tch = sb("tchD" + tag, [128, 4096], BF16)

    MU, QR, A1 = 0, QW, 2 * QW
    rp = lambda a: reps[:, a:a + QW]
    tok = z.tok

    for tq in _BUILD_OPTS.get("ffn_tq", range(4)):
        t0c = tq * QW
        xsl = lambda fc: z.xs[:, fc * 2048 + t0c:fc * 2048 + t0c + QW]

        xs_v = bass.AP(z.xs, t0c, [[16384, 128], [2048, 8], [1, 512]])
        mu_v = bass.AP(reps, MU, [[1536, 128], [0, 8], [1, 512]])
        qr_v = bass.AP(reps, QR, [[1536, 128], [0, 8], [1, 512]])

        @blk.vector
        def _(v):
            wt(v, sT)
            wt(v, sA)
            v.tensor_copy(sqb8[:], xs_v).then_inc(sV.h, 1); sV.n += 1
        tok_cast = sV.n

        @blk.tensor
        def _(t):
            wt(t, sV, tok_cast)
            wt(t, sV)
            for fc in range(8):
                last = t.matmul(ps[:, 0:QW], z.ones128[:], sqb8[:, fc * QW:(fc + 1) * QW],
                                start=(fc == 0), stop=(fc == 7))
            last.then_inc(sT.h, 1); sT.n += 1
        tok_mm0 = sT.n

        @blk.vector
        def _(v):
            wt(v, sT, tok_mm0)
            v.tensor_scalar(rp(MU), ps[:, 0:QW], 1.0 / 1024.0, None, ALU.mult).then_inc(sV.h, 1); sV.n += 1
        tok_mu = sV.n

        @blk.scalar
        def _(s):
            wt(s, sT, tok_mm0)
            s.activation(sqb8[:], sqb8[:], AF.Square).then_inc(sA.h, 1); sA.n += 1
        tok_sq8 = sA.n

        @blk.tensor
        def _(t):
            wt(t, sA, tok_sq8)
            wt(t, sV, tok_mu)
            for fc in range(8):
                last = t.matmul(ps[:, 0:QW], z.ones128[:], sqb8[:, fc * QW:(fc + 1) * QW],
                                start=(fc == 0), stop=(fc == 7))
            last.then_inc(sT.h, 1); sT.n += 1
        tok_mm1 = sT.n

        @blk.vector
        def _(v):
            wt(v, sT, tok_mm1)
            v.tensor_scalar(scr2[:, 0:QW], ps[:, 0:QW], 1.0 / 1024.0, None, ALU.mult)
            v.drain()
            v.tensor_tensor(scr[:, 0:QW], rp(MU), rp(MU), ALU.mult)
            v.drain()
            v.tensor_tensor(scr[:, 0:QW], scr2[:, 0:QW], scr[:, 0:QW], ALU.subtract).then_inc(sV.h, 1); sV.n += 1
        tok_var = sV.n

        @blk.scalar
        def _(s):
            wt(s, sV, tok_var)
            s.activation(scr2[:, 0:QW], scr[:, 0:QW], AF.Sqrt, bias=z.consts[:, 2:3]).then_inc(sA.h, 1); sA.n += 1
        tok_std = sA.n

        @blk.vector
        def _(v):
            wt(v, sA, tok_std)
            v.reciprocal(rp(A1), scr2[:, 0:QW])
            v.drain()
            v.tensor_scalar(scr2[:, 0:QW], scr[:, 0:QW], 1e-5, None, ALU.add)
            v.drain()
            v.reciprocal(scr2[:, 0:QW], scr2[:, 0:QW])
            v.drain()
            v.tensor_tensor(scr2[:, 0:QW], scr[:, 0:QW], scr2[:, 0:QW], ALU.mult).then_inc(sV.h, 1); sV.n += 1
        tok_mu2 = sV.n

        @blk.scalar
        def _(s):
            wt(s, sV, tok_mu2)
            s.activation(scr2[:, 0:QW], scr2[:, 0:QW], AF.Sqrt, bias=z.consts[:, 2:3]).then_inc(sA.h, 1); sA.n += 1
        tok_rr = sA.n

        @blk.vector
        def _(v):
            wt(v, sA, tok_rr)
            v.reciprocal(scr2[:, 0:QW], scr2[:, 0:QW])
            v.tensor_reduce(rp(QR), bass.AP(z.xs, t0c, [[16384, 128], [1, 512], [2048, 8]]),
                            mybir.AxisListType.X, ALU.max)
            v.tensor_reduce(scr[:, 0:QW], bass.AP(z.xs, t0c, [[16384, 128], [1, 512], [2048, 8]]),
                            mybir.AxisListType.X, ALU.min)
            v.drain()
            v.tensor_tensor(rp(QR), rp(QR), rp(MU), ALU.subtract)
            v.tensor_tensor(scr[:, 0:QW], rp(MU), scr[:, 0:QW], ALU.subtract)
            v.drain()
            v.tensor_tensor(rp(QR), rp(QR), scr[:, 0:QW], ALU.max).then_inc(sV.h, 1); sV.n += 1
        tok_fold = sV.n

        @blk.tensor
        def _(t):
            wt(t, sV, tok_fold)
            wt(t, sV)
            for i in range(4):
                last = t.transpose(ps[:, 1024 + i * 128:1024 + (i + 1) * 128],
                                   reps[:, QW + i * 128:QW + (i + 1) * 128], z.identf[:])
            last.then_inc(sT.h, 1); sT.n += 1
        tok_amt = sT.n

        @blk.vector
        def _(v):
            wt(v, sT, tok_amt)
            for i in range(4):
                last = v.tensor_reduce(scr[:, QW + i:QW + i + 1], ps[:, 1024 + i * 128:1024 + (i + 1) * 128],
                                       mybir.AxisListType.X, ALU.max)
            last.then_inc(sV.h, 1); sV.n += 1
        tok_amr = sV.n

        @blk.gpsimd
        def _(g):
            wt(g, sV, tok_amr)
            wt(g, dL)
            for i in range(4):
                g.dma_start(out=z.bounce[i * 128:(i + 1) * 128],
                            in_=scr[:, QW + i:QW + i + 1]).then_inc(dS.h, 16); dS.n += 16
        @blk.sync
        def _(q):
            wt(q, dS)
            wt(q, dL)
            q.dma_start(out=scr[:, 0:QW],
                        in_=bass.AP(z.bounce.tensor, 0, [[0, 128], [1, 512]])).then_inc(dL.h, 16); dL.n += 16
        tok_am = dL.n

        @blk.vector
        def _(v):
            wt(v, dL, tok_am)
            v.tensor_tensor(scr[:, 0:QW], scr[:, 0:QW], rp(A1), ALU.mult)
            v.drain()
            v.tensor_tensor(scr[:, 0:QW], scr[:, 0:QW], scr2[:, 0:QW], ALU.mult)
            v.drain()
            v.tensor_scalar(scr[:, 0:QW], scr[:, 0:QW], 1e-5, None, ALU.max)
            v.drain()
            v.reciprocal(rp(QR), scr[:, 0:QW])
            v.drain()
            v.tensor_scalar(rp(QR), rp(QR), 127.0, None, ALU.mult)
            v.drain()
            v.tensor_tensor(rp(QR), rp(QR), scr2[:, 0:QW], ALU.mult)
            v.drain()
            v.tensor_tensor(rp(QR), rp(QR), rp(A1), ALU.mult)
            v.drain()
            v.tensor_scalar(rp(A1), scr[:, 0:QW], z.consts[:, 9:10], 1.0 / 127.0, ALU.mult, ALU.mult)
            v.drain().then_inc(sV.h, 1); sV.n += 1

        @blk.vector
        def _(v):
            wt(v, sT)
            v.tensor_tensor(hbuf[:, 0:4096], xs_v, mu_v, ALU.subtract)
            v.drain()
            v.tensor_tensor(hbuf[:, 0:4096], hbuf[:, 0:4096], qr_v, ALU.mult)
            v.drain()
            v.tensor_scalar(k1[:], hbuf[:, 0:4096], MAGIC, MAGIC, ALU.add, ALU.subtract)
            v.drain().then_inc(sV.h, 1); sV.n += 1
        tok_k1 = sV.n

        tch1tok = tok.setdefault("tch1_free" + tag, [(sV, 0), (sV, 0)])
        tch1tok[0] = (sT, sT.n)
        tch1tok[1] = (sT, sT.n)
        for hf in range(32):
            par1 = hf % 2
            dX = z.dX[par1]
            @blk.sync
            def _(q):
                s_, v_ = tch1tok[par1]
                wt(q, s_, v_)
                q.dma_start(out=tch[:, par1 * 1024:par1 * 1024 + 1024],
                            in_=bass.AP(W["t1_c"].tensor, hf * 128, [[4096, 128], [128 * 4096, 8], [1, 128]])
                            ).then_inc(dX.h, 16); dX.n += 16
            tok_t = dX.n

            @blk.tensor
            def _(t):
                wt(t, dX, tok_t)
                if hf == 0:
                    wt(t, sV, tok_k1)
                if hf >= 2:
                    s_, v_ = tok["ffn1_ev" + tag]
                    wt(t, s_, v_)
                pb = par1 * 512
                for c in range(8):
                    last = t.matmul(ps[:, pb:pb + 512],
                                    tch[:, par1 * 1024 + c * 128:par1 * 1024 + (c + 1) * 128],
                                    k1[:, c * QW:(c + 1) * QW], start=(c == 0), stop=(c == 7))
                last.then_inc(sT.h, 1); sT.n += 1
            tch1tok[par1] = (sT, sT.n)
            if hf % 2 == 1:
                tok_pairmm = sT.n
                @blk.vector
                def _(v):
                    wt(v, sT, tok_pairmm)
                    a1_v = bass.AP(reps, A1, [[1536, 128], [0, 2], [1, 512]])
                    b1_v = bass.AP(z.b1, hf - 1, [[32, 128], [1, 2], [0, 512]])
                    v.tensor_tensor(scr[:, 0:1024], ps[:, 0:1024], a1_v, ALU.mult)
                    v.drain()
                    v.tensor_tensor(hbuf[:, (hf - 1) * QW:(hf + 1) * QW], scr[:, 0:1024],
                                    b1_v, ALU.add).then_inc(sV.h, 1); sV.n += 1
                tok["ffn1_ev" + tag] = (sV, sV.n)

        @blk.scalar
        def _(s):
            s_, v_ = tok["ffn1_ev" + tag]
            wt(s, s_, v_)
            s.activation(hbuf[:], hbuf[:], AF.Gelu).then_inc(sA.h, 1); sA.n += 1
        tok_gelu = sA.n

        for g8 in range(8):
            sqoff = (g8 % 2) * 2048
            @blk.scalar
            def _(s):
                if g8 == 0:
                    wt(s, sA, tok_gelu)
                if g8 >= 2:
                    wt(s, sT)
                s.activation(sqb8[:, sqoff:sqoff + 2048], hbuf[:, g8 * 2048:(g8 + 1) * 2048],
                             AF.Square).then_inc(sA.h, 1); sA.n += 1
            tok_sqg = sA.n
            @blk.tensor
            def _(t):
                wt(t, sA, tok_sqg)
                if g8 == 0:
                    wt(t, sV)
                for c in range(4):
                    hf = g8 * 4 + c
                    last = t.matmul(ps[:, 0:QW], z.ones128[:], sqb8[:, sqoff + c * 512:sqoff + (c + 1) * 512],
                                    start=(hf == 0), stop=(hf == 31))
                last.then_inc(sT.h, 1); sT.n += 1
        tok_hsq = sT.n

        @blk.vector
        def _(v):
            wt(v, sT, tok_hsq)
            wt(v, sA)
            v.tensor_scalar(scr[:, 0:QW], ps[:, 0:QW], 1.0 / 4096.0, None, ALU.mult)
            v.tensor_reduce(scr2[:, 0:QW], bass.AP(hbuf, 0, [[16384, 128], [1, 512], [512, 32]]),
                            mybir.AxisListType.X, ALU.max)
            v.tensor_reduce(scr2[:, QW:2 * QW], bass.AP(hbuf, 0, [[16384, 128], [1, 512], [512, 32]]),
                            mybir.AxisListType.X, ALU.min)
            v.drain()
            v.tensor_scalar(scr2[:, QW:2 * QW], scr2[:, QW:2 * QW], -1.0, None, ALU.mult)
            v.drain()
            v.tensor_tensor(scr2[:, 0:QW], scr2[:, 0:QW], scr2[:, QW:2 * QW], ALU.max).then_inc(sV.h, 1); sV.n += 1
        tok_hf = sV.n

        @blk.tensor
        def _(t):
            wt(t, sV, tok_hf)
            for i in range(4):
                last = t.transpose(ps[:, 1024 + i * 128:1024 + (i + 1) * 128],
                                   scr2[:, i * 128:(i + 1) * 128], z.identf[:])
            last.then_inc(sT.h, 1); sT.n += 1
        tok_hamt = sT.n

        @blk.vector
        def _(v):
            wt(v, sT, tok_hamt)
            for i in range(4):
                last = v.tensor_reduce(reps[:, i:i + 1], ps[:, 1024 + i * 128:1024 + (i + 1) * 128],
                                       mybir.AxisListType.X, ALU.max)
            last.then_inc(sV.h, 1); sV.n += 1
        tok_hamr = sV.n

        @blk.gpsimd
        def _(g):
            wt(g, sV, tok_hamr)
            wt(g, dL)
            for i in range(4):
                g.dma_start(out=z.bounce[i * 128:(i + 1) * 128],
                            in_=reps[:, i:i + 1]).then_inc(dS.h, 16); dS.n += 16
        @blk.sync
        def _(q):
            wt(q, dS)
            wt(q, dL)
            q.dma_start(out=scr2[:, QW:2 * QW],
                        in_=bass.AP(z.bounce.tensor, 0, [[0, 128], [1, 512]])).then_inc(dL.h, 16); dL.n += 16
        tok_ham = dL.n

        @blk.scalar
        def _(s):
            wt(s, sV, tok_hf)
            s.activation(scr[:, QW:2 * QW], scr[:, 0:QW], AF.Sqrt, bias=z.consts[:, 2:3]).then_inc(sA.h, 1); sA.n += 1
        tok_hstd = sA.n

        @blk.vector
        def _(v):
            wt(v, sA, tok_hstd)
            wt(v, dL, tok_ham)
            v.reciprocal(scr[:, QW:2 * QW], scr[:, QW:2 * QW])
            v.drain()
            v.tensor_tensor(scr2[:, 0:QW], scr[:, QW:2 * QW], scr2[:, QW:2 * QW], ALU.mult)
            v.drain()
            v.tensor_scalar(scr2[:, 0:QW], scr2[:, 0:QW], 1e-5, None, ALU.max)
            v.drain()
            v.tensor_scalar(scr2[:, QW:2 * QW], scr2[:, 0:QW], z.consts[:, 10:11], 1.0 / 127.0,
                            ALU.mult, ALU.mult)
            v.drain()
            v.reciprocal(scr2[:, 0:QW], scr2[:, 0:QW])
            v.drain()
            v.tensor_scalar(scr2[:, 0:QW], scr2[:, 0:QW], 127.0, None, ALU.mult)
            v.drain()
            v.tensor_tensor(scr[:, QW:2 * QW], scr[:, QW:2 * QW], scr2[:, 0:QW], ALU.mult)
            v.drain().then_inc(sV.h, 1); sV.n += 1

        @blk.vector
        def _(v):
            wt(v, sT)
            q2r_v = bass.AP(scr, QW, [[1024, 128], [0, 32], [1, 512]])
            v.tensor_tensor(hbuf[:], hbuf[:], q2r_v, ALU.mult)
            v.drain()
            v.tensor_scalar(k2b[:], hbuf[:], MAGIC, MAGIC, ALU.add, ALU.subtract)
            v.drain().then_inc(sV.h, 1); sV.n += 1
        tok_k2 = sV.n

        tch2tok = tok.setdefault("tch2_free" + tag, [(sV, 0), (sV, 0)])
        tch2tok[0] = (sT, sT.n)
        tch2tok[1] = (sT, sT.n)
        ev2tok = tok.setdefault("ffn2_ev" + tag, [(sV, 0), (sV, 0)])
        for of in range(8):
            for sub in range(2):
                hidx = (of * 2 + sub) % 2
                dX = z.dX[hidx]
                @blk.sync
                def _(q):
                    s_, v_ = tch2tok[hidx]
                    wt(q, s_, v_)
                    q.dma_start(out=tch[:, hidx * 2048:hidx * 2048 + 2048],
                                in_=bass.AP(W["t2_c"].tensor, of * 128 + sub * 16 * 128 * 1024,
                                            [[1024, 128], [128 * 1024, 16], [1, 128]])
                                ).then_inc(dX.h, 16); dX.n += 16
                tok_t = dX.n
                @blk.tensor
                def _(t):
                    wt(t, dX, tok_t)
                    if of == 0 and sub == 0:
                        wt(t, sV, tok_k2)
                        wt(t, sV)
                    if sub == 0 and of >= 2:
                        s_, v_ = ev2tok[of % 2]
                        wt(t, s_, v_)
                    pb = (of % 2) * 512
                    for c in range(16):
                        hc = sub * 16 + c
                        last = t.matmul(ps[:, 1024 + pb:1024 + pb + 512],
                                        tch[:, hidx * 2048 + c * 128:hidx * 2048 + (c + 1) * 128],
                                        k2b[:, hc * QW:(hc + 1) * QW], start=(hc == 0), stop=(hc == 31))
                    last.then_inc(sT.h, 1); sT.n += 1
                tch2tok[hidx] = (sT, sT.n)
            tok_m2 = sT.n

            @blk.vector
            def _(v):
                wt(v, sT, tok_m2)
                pb = (of % 2) * 512
                v.tensor_tensor(scr2[:, 0:QW], ps[:, 1024 + pb:1024 + pb + QW], scr2[:, QW:2 * QW], ALU.mult)
                v.drain()
                v.scalar_tensor_tensor(xsl(of), scr2[:, 0:QW], z.b2[:, of:of + 1],
                                       xsl(of), ALU.add, ALU.add)
                v.drain().then_inc(sV.h, 1); sV.n += 1
            ev2tok[of % 2] = (sV, sV.n)

    z.tok["ffn_done" + tag] = (sV, z.sV.n)
    return pes


def _ln_stats_rep(z, srcfn, sqb, dstmu, dstms, nfeat_chunks, width, inv_n):
    nc, blk = z.nc, z.blk
    sT, sV, sA = z.sT, z.sV, z.sA
    ps = z.ps
    nq = width // 512
    for pss in range(2):
        for fc in range(nfeat_chunks):
            @blk.vector
            def _(v):
                wt(v, sT)
                v.tensor_copy(sqb[:, 0:width], srcfn(fc))
                v.drain().then_inc(sV.h, 1); sV.n += 1
            tok_cast = sV.n
            if pss == 1:
                @blk.scalar
                def _(s):
                    wt(s, sV, tok_cast)
                    s.activation(sqb[:, 0:width], sqb[:, 0:width], AF.Square).then_inc(sA.h, 1); sA.n += 1
            @blk.tensor
            def _(t):
                wt(t, sV, tok_cast)
                if pss == 1:
                    wt(t, sA)
                if fc == 0:
                    wt(t, sV)
                for q4 in range(nq):
                    t.matmul(ps[:, q4 * 512:(q4 + 1) * 512], z.ones128[:],
                             sqb[:, q4 * 512:(q4 + 1) * 512], start=(fc == 0), stop=(fc == nfeat_chunks - 1))
                t.drain().then_inc(sT.h, 1); sT.n += 1
        tok_mm = z.sT.n
        @blk.vector
        def _(v):
            wt(v, sT, tok_mm)
            dst = dstmu if pss == 0 else dstms
            v.tensor_scalar(dst, ps[:, 0:width], inv_n, None, ALU.mult)
            v.drain().then_inc(sV.h, 1); sV.n += 1


def _rstd_from_stats(z, mu, ms, var_scr, out_rstd):
    nc, blk = z.nc, z.blk
    sV, sA = z.sV, z.sA
    @blk.vector
    def _(v):
        v.tensor_tensor(var_scr, mu, mu, ALU.mult)
        v.drain()
        v.tensor_tensor(var_scr, ms, var_scr, ALU.subtract)
        v.drain().then_inc(sV.h, 1); sV.n += 1
    tok_v = sV.n
    @blk.scalar
    def _(s):
        wt(s, sV, tok_v)
        s.activation(out_rstd, var_scr, AF.Sqrt, bias=z.consts[:, 2:3]).then_inc(sA.h, 1); sA.n += 1
    tok_s = sA.n
    @blk.vector
    def _(v):
        wt(v, sA, tok_s)
        v.reciprocal(out_rstd, out_rstd)
        v.drain().then_inc(sV.h, 1); sV.n += 1


def _phase_e(z, seq):
    nc, blk = z.nc, z.blk
    dL, dS, sT, sV, sA, sG = z.dL, z.dS, z.sT, z.sV, z.sA, z.sG
    W = z.weights
    ps, psbf = z.ps, z.psbf

    pesE = contextlib.ExitStack()
    sb = lambda name, shape, dt=F32: pesE.enter_context(nc.sbuf_tensor(name + z.sfx, shape, dt))
    oT = pesE.enter_context(nc.sbuf_tensor("oT" + z.sfx, [128, 8 * 2048], BF16, side="right"))
    repsE = sb("repsE", [128, 4096], F32)
    scrE = sb("scrE", [128, 2048], F32)
    scrE2 = sb("scrE2", [128, 2048], F32)
    sqbE = sb("sqbE", [128, 2048], BF16)

    tok = z.tok

    _ln_stats_rep(z, lambda fc: z.xs[:, fc * 2048:(fc + 1) * 2048], sqbE,
                  repsE[:, 0:2048], scrE2[:], 8, 2048, 1.0 / 1024.0)
    _rstd_from_stats(z, repsE[:, 0:2048], scrE2[:], scrE[:], repsE[:, 2048:4096])

    pes1 = contextlib.ExitStack()
    sb1 = lambda name, shape, dt=F32: pes1.enter_context(nc.sbuf_tensor(name + z.sfx, shape, dt))
    u3 = pes1.enter_context(nc.sbuf_tensor("u3" + z.sfx, [128, 8 * 2048], BF16, side="right"))
    wbv_b = sb1("wbv_b", [128, 1024], F32)

    @blk.sync
    def _(q):
        wt(q, dL)
        q.dma_start(out=wbv_b[:], in_=bass.AP(z.wbv_row.tensor, 0, [[0, 128], [1, E]])).then_inc(dL.h, 16); dL.n += 16
    tok_wbv = dL.n
    qkT = sb1("qkT", [128, 4096], BF16)
    vp = sb1("vp", [128, 16 * 128], BF16)
    expAB = sb1("expAB", [128, 2 * 2560], BF16)
    EbufE = sb1("EbufE", [128, 2048], BF16)
    wwch = sb1("wwch", [128, 1024], BF16)
    wwf = sb1("wwf", [128, 1024], F32)

    @blk.vector
    def _(v):
        for fc in range(8):
            v.tensor_tensor(scrE[:], z.xs[:, fc * 2048:(fc + 1) * 2048], repsE[:, 0:2048], ALU.subtract)
            v.drain()
            v.tensor_tensor(u3[:, fc * 2048:(fc + 1) * 2048], scrE[:], repsE[:, 2048:4096], ALU.mult)
            v.drain().then_inc(sV.h, 1); sV.n += 1
    tok_u3 = sV.n

    SA, SB = 0, 512
    OO = 1024
    ZA, ZB_ = 1536, 2048
    VB = 2560
    tok["expdoneE"] = (sA, sA.n)
    tok["ozfreeE"] = (sV, sV.n)
    tok["EbufE_free"] = (sT, sT.n)

    for pr in range(8):
        par = pr % 2
        dX = z.dX[par]

        @blk.sync
        def _(q):
            s_, v_ = tok.get("wwch_free", (sV, tok_u3))
            wt(q, s_, v_)
            q.dma_start(out=wwf[:],
                        in_=bass.AP(z.wdirect["wwqT"].tensor, pr * 128, [[1024, 128], [131072, 8], [1, 128]])
                        ).then_inc(dX.h, 16); dX.n += 16
            q.dma_start(out=expAB[:, 0:2560], in_=W["expb_c"][2 * pr]).then_inc(dX.h, 16); dX.n += 16
            q.dma_start(out=expAB[:, 2560:5120], in_=W["expb_c"][2 * pr + 1]).then_inc(dX.h, 16); dX.n += 16
        tok_wwf = dX.n
        @blk.vector
        def _(v):
            wt(v, dX, tok_wwf)
            wt(v, sT)
            v.tensor_copy(wwch[:], wwf[:]).then_inc(sV.h, 1); sV.n += 1
        tok_ww = sV.n

        for which in range(2):
            if which == 1:
                @blk.sync
                def _(q):
                    wt(q, sV)
                    q.dma_start(out=wwf[:],
                                in_=bass.AP(z.wdirect["wwkT"].tensor, pr * 128, [[1024, 128], [131072, 8], [1, 128]])
                                ).then_inc(dX.h, 16); dX.n += 16
                tok_wwkf = dX.n
                @blk.vector
                def _(v):
                    wt(v, dX, tok_wwkf)
                    wt(v, sT)
                    v.tensor_copy(wwch[:], wwf[:]).then_inc(sV.h, 1); sV.n += 1
                tok_ww = sV.n
            @blk.tensor
            def _(t):
                wt(t, sV, tok_ww)
                wt(t, sV)
                for q4 in range(4):
                    for c in range(8):
                        t.matmul(ps[:, q4 * 512:(q4 + 1) * 512],
                                 wwch[:, c * 128:(c + 1) * 128],
                                 u3[:, c * 2048 + q4 * 512:c * 2048 + (q4 + 1) * 512],
                                 start=(c == 0), stop=(c == 7))
                t.drain().then_inc(sT.h, 1); sT.n += 1
            tok_qk = sT.n

            @blk.vector
            def _(v):
                wt(v, sT, tok_qk)
                bias = z.wbq if which == 0 else z.wbk
                if which == 0:
                    v.tensor_scalar(qkT[:, 0:2048], ps[:, 0:2048], bias[:, pr:pr + 1], 1.0 / 32.0, ALU.add, ALU.mult)
                else:
                    v.tensor_scalar(qkT[:, 2048:4096], ps[:, 0:2048], bias[:, pr:pr + 1], None, ALU.add)
                v.drain().then_inc(sV.h, 1); sV.n += 1
        tok_qkev = sV.n

        @blk.sync
        def _(q):
            wt(q, sV)
            q.dma_start(out=wwf[:],
                        in_=bass.AP(z.wdirect["wwvT"].tensor, pr * 128, [[1024, 128], [131072, 8], [1, 128]])
                        ).then_inc(dX.h, 16); dX.n += 16
        tok_wwvf = dX.n
        @blk.vector
        def _(v):
            wt(v, dX, tok_wwvf)
            wt(v, sT)
            v.tensor_copy(wwch[:], wwf[:]).then_inc(sV.h, 1); sV.n += 1
        tok_wwv = sV.n
        for tt in range(NT):
            vb = VB + (tt % 2) * 512
            @blk.tensor
            def _(t):
                if tt == 0:
                    wt(t, sV, tok_wwv)
                if tt >= 2:
                    s_, v_ = tok["vevE" + str(tt % 2)]
                    wt(t, s_, v_)
                for c in range(8):
                    t.matmul(ps[:, vb:vb + 128], u3[:, c * 2048 + tt * 128:c * 2048 + (tt + 1) * 128],
                             wwch[:, c * 128:(c + 1) * 128], start=(c == 0), stop=(c == 7))
                t.drain().then_inc(sT.h, 1); sT.n += 1
            tok_vm = sT.n
            @blk.vector
            def _(v):
                wt(v, sT, tok_vm)
                if tt == 0 and pr == 0:
                    wt(v, dL, tok_wbv)
                v.tensor_tensor(vp[:, tt * 128:(tt + 1) * 128], ps[:, vb:vb + 128],
                                wbv_b[:, pr * 128:(pr + 1) * 128], ALU.add)
                v.drain().then_inc(sV.h, 1); sV.n += 1
            tok["vevE" + str(tt % 2)] = (sV, sV.n)
        tok_vdone = sV.n

        for which in range(2):
            @blk.vector
            def _(v):
                if which == 0:
                    wt(v, sV, tok_vdone)
                v.tensor_tensor(sqbE[:], qkT[:, which * 2048:(which + 1) * 2048],
                                qkT[:, which * 2048:(which + 1) * 2048], ALU.mult)
                v.drain().then_inc(sV.h, 1); sV.n += 1
            tok_sqq = sV.n
            @blk.tensor
            def _(t):
                wt(t, sV, tok_sqq)
                if which == 0:
                    wt(t, sV)
                for q4 in range(4):
                    t.matmul(ps[:, q4 * 512:(q4 + 1) * 512], z.ones128[:],
                             sqbE[:, q4 * 512:(q4 + 1) * 512], start=True, stop=True)
                t.drain().then_inc(sT.h, 1); sT.n += 1
            tok_nmm = sT.n
            @blk.vector
            def _(v):
                wt(v, sT, tok_nmm)
                v.tensor_reduce(z.consts2[:, 4 + which:5 + which], ps[:, 0:2048],
                                mybir.AxisListType.X, ALU.max)
                v.drain().then_inc(sV.h, 1); sV.n += 1
        tok_bam = sV.n

        @blk.vector
        def _(v):
            v.tensor_tensor(z.consts2[:, 6:7], z.consts2[:, 4:5], z.consts2[:, 5:6], ALU.mult)
            v.drain().then_inc(sV.h, 1); sV.n += 1
        tok_bm2 = sV.n

        @blk.scalar
        def _(s):
            wt(s, sV, tok_bm2)
            s.activation(z.consts2[:, 6:7], z.consts2[:, 6:7], AF.Sqrt).then_inc(sA.h, 1); sA.n += 1
        tok_bsq = sA.n

        @blk.vector
        def _(v):
            wt(v, sA, tok_bsq)
            v.tensor_scalar(z.consts2[:, 6:7], z.consts2[:, 6:7], -1.0, None, ALU.mult)
            v.drain().then_inc(sV.h, 1); sV.n += 1
        tok_bb = sV.n

        for qb in range(4):
            q0 = qb * 512
            nkt = 4 * (qb + 1)
            sT0, sA0, sV0 = sT.n, sA.n, sV.n
            oz_s, oz_v = tok["ozfreeE"]

            def zov(t, kt, nkt):
                eb = (kt % 2) * 1024
                fl = dict(start=(kt == 0), stop=(kt == nkt - 1))
                t.matmul(ps[:, ZA:ZA + 512], z.ones128[:], EbufE[:, eb:eb + 512], **fl)
                t.matmul(ps[:, ZB_:ZB_ + 512], z.ones128[:], EbufE[:, eb + 512:eb + 1024], **fl)
                t.matmul(ps[0:64, OO:OO + 512], vp[:, kt * 128:kt * 128 + 64], EbufE[:, eb:eb + 512],
                         tile_position=(0, 0), **fl)
                return t.matmul(ps[64:128, OO:OO + 512], vp[:, kt * 128 + 64:kt * 128 + 128],
                                EbufE[:, eb + 512:eb + 1024],
                                tile_position=(0, 64), skip_group_check=True, **fl)

            @blk.tensor
            def _(t):
                for kt in range(nkt):
                    wt(t, sA, sA0 + kt)
                    if kt == 0:
                        wt(t, oz_s, oz_v)
                        wt(t, sV, tok_bb)
                    t.matmul(ps[:, SA:SA + 512], qkT[0:64, 2048 + kt * 128:2048 + (kt + 1) * 128],
                             qkT[0:64, q0:q0 + 512], start=True, stop=True, tile_position=(0, 0))
                    t.matmul(ps[:, SB:SB + 512], qkT[64:128, 2048 + kt * 128:2048 + (kt + 1) * 128],
                             qkT[64:128, q0:q0 + 512], start=True, stop=True,
                             tile_position=(64, 0)).then_inc(sT.h, 1); sT.n += 1
                    if kt > 0:
                        wt(t, sV, sV0 + kt)
                        zov(t, kt - 1, nkt)
                wt(t, sV, sV0 + nkt)
                zov(t, nkt - 1, nkt).then_inc(sT.h, 1); sT.n += 1

            @blk.scalar
            def _(s):
                for kt in range(nkt):
                    eb = (kt % 2) * 1024
                    wt(s, sT, sT0 + kt + 1)
                    s.activation(EbufE[:, eb:eb + 1024], ps[:, 0:1024], AF.Exp,
                                 bias=z.consts2[:, 6:7]).then_inc(sA.h, 1); sA.n += 1

            @blk.vector
            def _(v):
                for kt in range(nkt):
                    eb = (kt % 2) * 1024
                    co = q0 - kt * 128 + 512
                    wt(v, sA, sA0 + kt + 1)
                    v.tensor_tensor(EbufE[:, eb:eb + 512], EbufE[:, eb:eb + 512],
                                    expAB[:, co:co + 512], ALU.mult)
                    v.tensor_tensor(EbufE[:, eb + 512:eb + 1024], EbufE[:, eb + 512:eb + 1024],
                                    expAB[:, 2560 + co:2560 + co + 512], ALU.mult)
                    v.drain().then_inc(sV.h, 1); sV.n += 1
            tok_oz = sT.n

            @blk.vector
            def _(v):
                wt(v, sT, tok_oz)
                v.reciprocal(scrE[:, 0:512], ps[:, ZA:ZA + 512])
                v.reciprocal(scrE[:, 512:1024], ps[:, ZB_:ZB_ + 512])
                v.drain()
                v.tensor_tensor(oT[0:64, pr * 2048 + q0:pr * 2048 + q0 + 512],
                                ps[0:64, OO:OO + 512], scrE[0:64, 0:512], ALU.mult)
                v.tensor_tensor(oT[64:128, pr * 2048 + q0:pr * 2048 + q0 + 512],
                                ps[64:128, OO:OO + 512], scrE[64:128, 512:1024], ALU.mult)
                v.drain().then_inc(sV.h, 1); sV.n += 1
            tok["ozfreeE"] = (sV, sV.n)
            tok["expdoneE"] = (sA, sA.n)
        tok["wwch_free"] = (sT, sT.n)

    pes1.close()
    _quiesce(z)

    pes2 = contextlib.ExitStack()
    sb2 = lambda name, shape, dt=F32: pes2.enter_context(nc.sbuf_tensor(name + z.sfx, shape, dt))
    ybuf = sb2("ybuf", [128, 8 * 1024], F32)
    pwch = sb2("pwch", [128, 1024], BF16)
    pwf = sb2("pwf", [128, 1024], F32)

    for th in range(2):
        t0c = th * 1024

        @blk.vector
        def _(v):
            wt(v, sT)
            xs_h = bass.AP(z.xs, t0c, [[16384, 128], [2048, 8], [1, 1024]])
            mu_h = bass.AP(repsE, t0c, [[4096, 128], [0, 8], [1, 1024]])
            rs_h = bass.AP(repsE, 2048 + t0c, [[4096, 128], [0, 8], [1, 1024]])
            v.tensor_tensor(ybuf[:], xs_h, mu_h, ALU.subtract)
            v.drain()
            v.tensor_tensor(ybuf[:], ybuf[:], rs_h, ALU.mult)
            v.drain().then_inc(sV.h, 1); sV.n += 1
        tok_uh = sV.n

        for of in range(8):
            dX = z.dX[of % 2]
            @blk.sync
            def _(q):
                s_, v_ = tok.get("pwf_free", (sV, sV.n))
                wt(q, s_, v_)
                q.dma_start(out=pwf[:],
                            in_=bass.AP(z.wdirect["projwT"].tensor, of * 128, [[1024, 128], [131072, 8], [1, 128]])
                            ).then_inc(dX.h, 16); dX.n += 16
            tok_pwf = dX.n
            @blk.vector
            def _(v):
                wt(v, dX, tok_pwf)
                s_, v_ = tok.get("pwch_free", (sV, 0))
                wt(v, s_, v_)
                v.tensor_copy(pwch[:], pwf[:]).then_inc(sV.h, 1); sV.n += 1
            tok_pw = sV.n
            tok["pwf_free"] = (sV, tok_pw)

            for q4 in range(2):
                qq0 = t0c + q4 * 512
                @blk.tensor
                def _(t):
                    if q4 == 0:
                        wt(t, sV, tok_pw)
                    wt(t, sV)
                    for oc in range(8):
                        t.matmul(ps[:, (q4 % 2) * 512:(q4 % 2) * 512 + 512],
                                 pwch[:, oc * 128:(oc + 1) * 128],
                                 oT[:, oc * 2048 + qq0:oc * 2048 + qq0 + 512],
                                 start=(oc == 0), stop=(oc == 7))
                    t.drain().then_inc(sT.h, 1); sT.n += 1
                tok_pm = sT.n
                tok["pwch_free"] = (sT, sT.n)

                @blk.vector
                def _(v):
                    wt(v, sT, tok_pm)
                    if of == 0 and q4 == 0:
                        wt(v, sV, tok_uh)
                    yslc = ybuf[:, of * 1024 + q4 * 512:of * 1024 + (q4 + 1) * 512]
                    v.scalar_tensor_tensor(yslc, ps[:, (q4 % 2) * 512:(q4 % 2) * 512 + 512],
                                           z.pbias[:, of:of + 1], yslc, ALU.add, ALU.add)
                    v.drain().then_inc(sV.h, 1); sV.n += 1

        _ln_stats_rep(z, lambda fc: ybuf[:, fc * 1024:(fc + 1) * 1024], sqbE,
                      scrE[:, 0:1024], scrE2[:, 0:1024], 8, 1024, 1.0 / 1024.0)
        _rstd_from_stats(z, scrE[:, 0:1024], scrE2[:, 0:1024], scrE2[:, 1024:2048], scrE[:, 1024:2048])

        @blk.vector
        def _(v):
            mu2_h = bass.AP(scrE, 0, [[2048, 128], [0, 8], [1, 1024]])
            rs2_h = bass.AP(scrE, 1024, [[2048, 128], [0, 8], [1, 1024]])
            lng_h = bass.AP(z.lng, 0, [[8, 128], [1, 8], [0, 1024]])
            lnb_h = bass.AP(z.lnb, 0, [[8, 128], [1, 8], [0, 1024]])
            xs_h = bass.AP(z.xs, t0c, [[16384, 128], [2048, 8], [1, 1024]])
            v.tensor_tensor(ybuf[:], ybuf[:], mu2_h, ALU.subtract)
            v.drain()
            v.tensor_tensor(ybuf[:], ybuf[:], rs2_h, ALU.mult)
            v.drain()
            v.tensor_tensor(ybuf[:], ybuf[:], lng_h, ALU.mult)
            v.drain()
            v.tensor_tensor(ybuf[:], ybuf[:], lnb_h, ALU.add)
            v.drain()
            v.tensor_tensor(xs_h, xs_h, ybuf[:], ALU.add)
            v.drain().then_inc(sV.h, 1); sV.n += 1

    pes2.close()
    z.tok["phaseE_done"] = (sV, z.sV.n)
    return pesE


# version 08
# speedup vs baseline: 1.1829x; 1.1829x over previous
import math
import contextlib
import numpy as np

import concourse.bass as bass
import concourse.mybir as mybir
from concourse.bass_utils import run_bass_kernel_spmd

F32 = mybir.dt.float32
BF16 = mybir.dt.bfloat16
AF = mybir.ActivationFunctionType
ALU = mybir.AluOpType

MAGIC = 12582912.0
S = 2048
E = 1024
NT = 16
NC_ = 8
NH = 16

_DEBUG_TAPS = []
_BUILD_OPTS = {"n_seq": 2, "stop_after": None}


def _host_prepare(inp):
    x = np.ascontiguousarray(inp["L0_attn_a_x"])
    perm = np.empty(E, dtype=np.int64)
    idx = 0
    for g in range(8):
        half = 0 if g < 4 else 1
        hh = (g % 4) * 4
        for h2 in range(4):
            h = hh + h2
            for d in range(32):
                perm[idx] = half * 512 + h * 32 + d
                idx += 1
    wq = np.ascontiguousarray(inp["L0_attn_a_Wq"])
    wk = np.ascontiguousarray(inp["L0_attn_a_Wk"])
    wqT = np.ascontiguousarray(wq[perm].T)
    wkT = np.ascontiguousarray(wk[perm].T)
    wvT = np.ascontiguousarray(inp["L0_attn_a_Wv"].T)
    wprojT = np.ascontiguousarray(inp["L0_attn_a_Wproj"].T)
    ww = inp["L0_attn_b_W_w"]
    wwqT = np.ascontiguousarray(ww[0:1024].T)
    wwkT = np.ascontiguousarray(ww[1024:2048].T)
    wwvT = np.ascontiguousarray(ww[2048:3072].T)
    projwT = np.ascontiguousarray(inp["L0_attn_b_proj_w"].T)
    w1T = np.ascontiguousarray(inp["L0_ffn_w1"].T)
    w2T = np.ascontiguousarray(inp["L0_ffn_w2"].T)

    wb = inp["L0_attn_b_W_b"]
    def pcol(v):
        v = np.asarray(v).reshape(-1)
        return np.ascontiguousarray(v.reshape(-1, 128).T)
    wbq = pcol(wb[0:1024] * 1.0) if False else pcol(wb[0:1024])
    wbk = pcol(wb[1024:2048])
    wbv_row = np.ascontiguousarray(wb[2048:3072])
    b1 = pcol(inp["L0_ffn_b1"])
    b2 = pcol(inp["L0_ffn_b2"])
    pb = pcol(inp["L0_attn_b_proj_b"])
    lng = pcol(inp["L0_attn_b_ln_g"])
    lnb = pcol(inp["L0_attn_b_ln_b"])

    rpb = np.asarray(inp["L0_attn_b_rpb"])
    pi = np.arange(128)[:, None] - (np.arange(2560)[None, :] - 512) + 2047
    pi = np.clip(pi, 0, 4094)
    ZB = np.ascontiguousarray(rpb[pi, :].transpose(2, 0, 1))

    maskf = (np.arange(128)[:, None] <= (np.arange(2560)[None, :] - 512)).astype(np.float32)
    ident = np.eye(128, dtype=np.float32)
    import ml_dtypes
    i16 = np.arange(16, dtype=np.float64)
    inv_freq = 1.0 / (10000.0 ** (i16 / 256.0))
    t = np.arange(S, dtype=np.float64)
    fr = np.outer(t, inv_freq)
    cosb = np.cos(fr).astype(np.float32).astype(ml_dtypes.bfloat16).astype(np.float32)
    sinb = np.sin(fr).astype(np.float32).astype(ml_dtypes.bfloat16).astype(np.float32)
    def tmtab(tab):
        return np.ascontiguousarray(tab.reshape(NT, 128, 16).transpose(1, 0, 2).reshape(128, NT * 16))
    cosq = tmtab(cosb * 0.125)
    sinq = tmtab(sinb * 0.125)
    cosk = tmtab(cosb)
    sink = tmtab(sinb)

    xv = x.reshape(16, NT, 128, NC_, 128).transpose(0, 1, 4, 3, 2)
    xt_all = np.ascontiguousarray(xv.reshape(16, NT, 128, E))

    per_core = []
    for core in range(8):
        d = {
            "xt": np.ascontiguousarray(xt_all[2 * core:2 * core + 2]),
            "wqT": wqT, "wkT": wkT, "wvT": wvT, "wprojT": wprojT,
            "wwqT": wwqT, "wwkT": wwkT, "wwvT": wwvT, "projwT": projwT,
            "w1T": w1T, "w2T": w2T,
            "wbq": wbq, "wbk": wbk, "wbv_row": wbv_row,
            "b1": b1, "b2": b2, "pbias": pb, "lng": lng, "lnb": lnb,
            "ZB": ZB, "maskf": maskf, "ident": ident,
            "cosq": cosq, "sinq": sinq, "cosk": cosk, "sink": sink,
            "lq1": np.asarray(inp["L0_attn_a_lambda_q1"]).reshape(1, 64),
            "lq2": np.asarray(inp["L0_attn_a_lambda_q2"]).reshape(1, 64),
            "lk1": np.asarray(inp["L0_attn_a_lambda_k1"]).reshape(1, 64),
            "lk2": np.asarray(inp["L0_attn_a_lambda_k2"]).reshape(1, 64),
            "alpha": np.asarray(inp["L0_attn_a_alpha"]).reshape(1, 1),
        }
        per_core.append({k: np.ascontiguousarray(v.astype(np.float32, copy=False)) for k, v in d.items()})
    return per_core


def _host_gather(results):
    out = np.empty((16, S, E), dtype=np.float32)
    for core, r in enumerate(results):
        y = r["y"]
        for s in range(2):
            out[2 * core + s] = y[s].reshape(E, S).T
    return out


class Sem:
    __slots__ = ("h", "n", "name")


class Ctx:
    def __init__(self, nc, es):
        self.nc = nc
        self.es = es
        self._sems = []

    def sem(self, name):
        s = Sem()
        s.h = self.es.enter_context(self.nc.semaphore(name))
        s.n = 0
        s.name = name
        self._sems.append(s)
        return s


def inc(instr, sem, k=1):
    instr.then_inc(sem.h, k)
    sem.n += k
    return sem.n


def wt(eng, sem, v=None):
    eng.wait_ge(sem.h, sem.n if v is None else v)


def build_nc():
    nc = bass.Bass("TRN2", target_bir_lowering=False)
    es = contextlib.ExitStack()
    with es:
        _build_body(nc, es)
    return nc


class NS:
    pass


def _build_body(nc, es):
    dt_in = {}
    def din(name, shape):
        dt_in[name] = nc.dram_tensor(name, shape, F32, kind="ExternalInput").ap()
        return dt_in[name]

    xt = din("xt", [2, NT, 128, E])
    wqT = din("wqT", [E, E]); wkT = din("wkT", [E, E]); wvT = din("wvT", [E, 2 * E])
    wprojT = din("wprojT", [2 * E, 2 * E])
    wwqT = din("wwqT", [E, E]); wwkT = din("wwkT", [E, E]); wwvT = din("wwvT", [E, E])
    projwT = din("projwT", [E, E])
    w1T = din("w1T", [E, 4 * E]); w2T = din("w2T", [4 * E, E])
    wbq = din("wbq", [128, 8]); wbk = din("wbk", [128, 8]); wbv_row = din("wbv_row", [E])
    b1 = din("b1", [128, 32]); b2 = din("b2", [128, 8])
    pbias = din("pbias", [128, 8]); lng = din("lng", [128, 8]); lnb = din("lnb", [128, 8])
    ZBin = din("ZB", [16, 128, 2560])
    maskf = din("maskf", [128, 2560]); identin = din("ident", [128, 128])
    cosq_in = din("cosq", [128, 256]); sinq_in = din("sinq", [128, 256])
    cosk_in = din("cosk", [128, 256]); sink_in = din("sink", [128, 256])
    lq1 = din("lq1", [1, 64]); lq2 = din("lq2", [1, 64])
    lk1 = din("lk1", [1, 64]); lk2 = din("lk2", [1, 64])
    alpha_in = din("alpha", [1, 1])

    yout = nc.dram_tensor("y", [2, NC_, 128, S], F32, kind="ExternalOutput").ap()
    taps = {name: nc.dram_tensor("tap_" + name, shape, dt, kind="ExternalOutput").ap()
            for (name, shape, dt) in _DEBUG_TAPS}

    wq_c = nc.dram_tensor("wq_c", [E, E], BF16).ap()
    wk_c = nc.dram_tensor("wk_c", [E, E], BF16).ap()
    wv_c = nc.dram_tensor("wv_c", [E, 2 * E], BF16).ap()
    wproj_c = nc.dram_tensor("wproj_c", [2 * E, 2 * E], BF16).ap()
    wwq_c = nc.dram_tensor("wwq_c", [E, E], BF16).ap()
    wwk_c = nc.dram_tensor("wwk_c", [E, E], BF16).ap()
    wwv_c = nc.dram_tensor("wwv_c", [E, E], BF16).ap()
    projw_c = nc.dram_tensor("projw_c", [E, E], BF16).ap()
    t1_c = nc.dram_tensor("t1_c", [E, 4 * E], BF16).ap()
    t2_c = nc.dram_tensor("t2_c", [4 * E, E], BF16).ap()
    expb_c = nc.dram_tensor("expb_c", [16, 128, 2560], BF16).ap()
    bounce_c = nc.dram_tensor("bounce_c", [4096], F32).ap()

    ctx = Ctx(nc, es)
    dL = ctx.sem("dL")
    dX0 = ctx.sem("dX0")
    dX1 = ctx.sem("dX1")
    dS = ctx.sem("dS")
    sT = ctx.sem("sT")
    sV = ctx.sem("sV")
    sA = ctx.sem("sA")
    sG = ctx.sem("sG")

    sb = lambda name, shape, dt=F32: es.enter_context(nc.sbuf_tensor(name, shape, dt))

    z = NS()
    z.ps = es.enter_context(nc.psum_tensor("ps", [128, 3584], F32))
    z.psbf = es.enter_context(nc.psum_tensor("psbf", [128, 1024], BF16))
    z.ident = sb("identb", [128, 128], BF16)
    z.ones_f = sb("ones_f", [128, 1], F32)
    z.ones_b = sb("ones_b", [128, 1], BF16)
    z.ones128 = sb("ones128", [128, 128], BF16)
    z.ones128f = sb("ones128f", [128, 128], F32)
    z.identf = sb("identf", [128, 128], F32)
    z.consts = sb("consts", [128, 16], F32)
    z.rows = sb("rows", [1, 1024], F32)
    z.b1 = sb("b1_sb", [128, 32], F32); z.b2 = sb("b2_sb", [128, 8], F32)
    z.wbq = sb("wbq_sb", [128, 8], F32); z.wbk = sb("wbk_sb", [128, 8], F32)
    z.pbias = sb("pbias_sb", [128, 8], F32)
    z.lng = sb("lng_sb", [128, 8], F32); z.lnb = sb("lnb_sb", [128, 8], F32)
    z.consts2 = sb("consts2", [128, 8], F32)
    prep_es = contextlib.ExitStack()
    z.lrows = prep_es.enter_context(nc.sbuf_tensor("lrows", [1, 512], F32))
    z.maskf = prep_es.enter_context(nc.sbuf_tensor("maskf_sb", [128, 2560], F32))
    z.wchunk = prep_es.enter_context(nc.sbuf_tensor("wchunk", [128, 4096], F32))
    z.wchunk2 = prep_es.enter_context(nc.sbuf_tensor("wchunk2", [128, 4096], F32))
    z.bchunk = prep_es.enter_context(nc.sbuf_tensor("bchunk", [128, 4096], BF16))
    z.tpart = prep_es.enter_context(nc.sbuf_tensor("tpart", [1, 4096], F32))

    blk = es.enter_context(nc.Block())
    z.nc = nc; z.blk = blk; z.ctx = ctx
    z.dL, z.dS, z.sT, z.sV, z.sA, z.sG = dL, dS, sT, sV, sA, sG
    z.dX = [dX0, dX1]
    z.taps = taps
    z.yout = yout
    z.xt = xt
    z.dram_tabs = (cosq_in, sinq_in, cosk_in, sink_in)
    z.wbv_row = wbv_row
    z.bounce = bounce_c

    @blk.sync
    def _(q):
        q.dma_start(out=z.maskf[:], in_=maskf).then_inc(dL.h, 16); dL.n += 16
        q.dma_start(out=z.b1[:], in_=b1).then_inc(dL.h, 16); dL.n += 16
        q.dma_start(out=z.b2[:], in_=b2).then_inc(dL.h, 16); dL.n += 16
        q.dma_start(out=z.wbq[:], in_=wbq).then_inc(dL.h, 16); dL.n += 16
        q.dma_start(out=z.wbk[:], in_=wbk).then_inc(dL.h, 16); dL.n += 16
        q.dma_start(out=z.pbias[:], in_=pbias).then_inc(dL.h, 16); dL.n += 16
        q.dma_start(out=z.lng[:], in_=lng).then_inc(dL.h, 16); dL.n += 16
        q.dma_start(out=z.lnb[:], in_=lnb).then_inc(dL.h, 16); dL.n += 16
        q.dma_start(out=z.lrows[0:1, 0:64], in_=lq1).then_inc(dL.h, 16); dL.n += 16
        q.dma_start(out=z.lrows[0:1, 64:128], in_=lk1).then_inc(dL.h, 16); dL.n += 16
        q.dma_start(out=z.lrows[0:1, 128:192], in_=lq2).then_inc(dL.h, 16); dL.n += 16
        q.dma_start(out=z.lrows[0:1, 192:256], in_=lk2).then_inc(dL.h, 16); dL.n += 16
        q.dma_start(out=z.lrows[0:1, 256:257] if False else z.rows[0:1, 0:1], in_=alpha_in).then_inc(dL.h, 16); dL.n += 16
        q.dma_start(out=z.wchunk[:, 0:128], in_=identin).then_inc(dL.h, 16); dL.n += 16
        q.dma_start(out=z.identf[:], in_=identin).then_inc(dL.h, 16); dL.n += 16
    ld_consts = dL.n

    @blk.vector
    def _(v):
        wt(v, dL, ld_consts)
        v.memset(z.consts[:], 0.0)
        v.memset(z.consts2[:], 0.0)
        v.drain()
        v.memset(z.ones_f[:], 1.0)
        v.memset(z.ones_b[:], 1.0)
        v.memset(z.ones128[:], 1.0)
        v.memset(z.ones128f[:], 1.0)
        v.memset(z.consts[:, 0:1], -0.5)
        v.memset(z.consts[:, 1:2], 0.5)
        v.memset(z.consts[:, 2:3], 1e-5)
        v.memset(z.consts[:, 3:4], 1.0 / 1024.0)
        v.memset(z.consts[:, 8:9], 1e-30)
        v.tensor_copy(z.ident[:], z.wchunk[:, 0:128])
        v.drain()
        v.tensor_tensor(z.lrows[0:1, 256:320], z.lrows[0:1, 0:64], z.lrows[0:1, 64:128], ALU.mult)
        v.tensor_tensor(z.lrows[0:1, 320:384], z.lrows[0:1, 128:192], z.lrows[0:1, 192:256], ALU.mult)
        v.drain()
        v.tensor_reduce(z.lrows[0:1, 384:385], z.lrows[0:1, 256:320], mybir.AxisListType.X, ALU.add)
        v.tensor_reduce(z.lrows[0:1, 385:386], z.lrows[0:1, 320:384], mybir.AxisListType.X, ALU.add)
        v.drain().then_inc(sV.h, 1); sV.n += 1
    v_dots = sV.n

    @blk.scalar
    def _(s):
        wt(s, sV, v_dots)
        s.activation(z.lrows[0:1, 386:388], z.lrows[0:1, 384:386], AF.Exp).then_inc(sA.h, 1); sA.n += 1
    a_exps = sA.n

    @blk.vector
    def _(v):
        wt(v, sA, a_exps)
        v.tensor_tensor(z.lrows[0:1, 388:389], z.lrows[0:1, 386:387], z.lrows[0:1, 387:388], ALU.subtract)
        v.drain()
        v.tensor_scalar(z.lrows[0:1, 389:390], z.lrows[0:1, 388:389], 0.2, None, ALU.add)
        v.drain()
        v.tensor_scalar(z.lrows[0:1, 390:391], z.lrows[0:1, 389:390], -1.0, None, ALU.mult)
        v.tensor_scalar(z.rows[0:1, 1:2], z.rows[0:1, 0:1], 2.0, 1.0, ALU.mult, ALU.add)
        v.drain()
        v.tensor_scalar(z.rows[0:1, 2:3], z.rows[0:1, 1:2], 1.0 / math.pi, None, ALU.mult)
        v.scalar_tensor_tensor(z.rows[0:1, 3:4], z.rows[0:1, 1:2], 0.5, z.rows[0:1, 0:1], ALU.mult, ALU.subtract)
        v.drain().then_inc(sV.h, 1); sV.n += 1
    v_consts = sV.n

    @blk.gpsimd
    def _(g):
        wt(g, sV, v_consts)
        g.dma_start(out=z.bounce[0:1], in_=z.lrows[0:1, 389:390]).then_inc(dS.h, 16); dS.n += 16
        g.dma_start(out=z.bounce[1:2], in_=z.lrows[0:1, 390:391]).then_inc(dS.h, 16); dS.n += 16
        g.dma_start(out=z.bounce[2:3], in_=z.rows[0:1, 2:3]).then_inc(dS.h, 16); dS.n += 16
        g.dma_start(out=z.bounce[3:4], in_=z.rows[0:1, 3:4]).then_inc(dS.h, 16); dS.n += 16
    @blk.sync
    def _(q):
        wt(q, dS)
        wt(q, dL)
        q.dma_start(out=z.consts[:, 4:8], in_=bass.AP(z.bounce.tensor, 0, [[0, 128], [1, 4]])).then_inc(dL.h, 16); dL.n += 16
    z.gp_consts = dL.n

    def cache_weight(src, dst, rows, cols):
        for r0 in range(0, rows, 128):
            @blk.sync
            def _(q):
                wt(q, sV)
                wt(q, dL)
                q.dma_start(out=z.wchunk[:, 0:cols], in_=src[r0:r0 + 128, :]).then_inc(dL.h, 16); dL.n += 16
            @blk.vector
            def _(v):
                wt(v, dL)
                wt(v, dS)
                v.tensor_copy(z.bchunk[:, 0:cols], z.wchunk[:, 0:cols]).then_inc(sV.h, 1); sV.n += 1
            @blk.gpsimd
            def _(g):
                wt(g, sV)
                g.dma_start(out=dst[r0:r0 + 128, :], in_=z.bchunk[:, 0:cols]).then_inc(dS.h, 16); dS.n += 16

    cache_weight(wprojT, wproj_c, 2 * E, 2 * E)
    z.wdirect = dict(wqT=wqT, wkT=wkT, wvT=wvT, wwqT=wwqT, wwkT=wwkT, wwvT=wwvT,
                     projwT=projwT)

    def tern_weight(srcT, dstT, rows, cols, mean_count, s_col, inv_col):
        nch = rows // 128
        for r0i in range(nch):
            r0 = r0i * 128
            @blk.sync
            def _(q):
                wt(q, sA)
                wt(q, sV)
                wt(q, dL)
                q.dma_start(out=z.wchunk[:, 0:cols], in_=srcT[r0:r0 + 128, :]).then_inc(dL.h, 16); dL.n += 16
            @blk.scalar
            def _(s):
                wt(s, dL)
                wt(s, sT)
                s.activation(z.wchunk2[:, 0:cols], z.wchunk[:, 0:cols], AF.Abs).then_inc(sA.h, 1); sA.n += 1
            @blk.tensor
            def _(t):
                wt(t, sA)
                for ji, j in enumerate(range(0, cols, 512)):
                    if rows == 1024:
                        k = ji % 4
                        fl = dict(start=(r0i % 4 == 0 and ji < 4), stop=(r0i % 4 == 3 and ji >= 4))
                        if r0i % 4 == 0 and ji < 4:
                            wt(t, sV)
                    else:
                        k = ji * 2 + (r0i // 8) % 2
                        fl = dict(start=(r0i % 8 == 0), stop=(r0i % 8 == 7))
                        if r0i % 16 == 0 and ji == 0:
                            wt(t, sV)
                    t.matmul(z.ps[0:1, k * 512:(k + 1) * 512], z.ones_f[:], z.wchunk2[:, j:j + 512], **fl)
                t.drain().then_inc(sT.h, 1); sT.n += 1
            if r0i == nch // 2 - 1 or r0i == nch - 1:
                half = 0 if r0i == nch // 2 - 1 else 1
                @blk.vector
                def _(v):
                    wt(v, sT)
                    v.tensor_copy(z.tpart[0:1, half * 2048:(half + 1) * 2048], z.ps[0:1, 0:2048])
                    v.drain().then_inc(sV.h, 1); sV.n += 1
        @blk.vector
        def _(v):
            wt(v, sV)
            v.tensor_scalar(z.wchunk[0:1, 0:4096], z.tpart[0:1, :], 128.0, MAGIC, ALU.mult, ALU.add)
            v.drain()
            v.tensor_scalar(z.wchunk[0:1, 0:4096], z.wchunk[0:1, 0:4096], MAGIC, 1.0 / 128.0,
                            ALU.subtract, ALU.mult)
            v.drain()
            v.tensor_tensor(z.wchunk2[0:1, 0:4096], z.tpart[0:1, :], z.wchunk[0:1, 0:4096], ALU.subtract)
            v.drain()
            v.tensor_reduce(z.rows[0:1, 8:9], z.wchunk[0:1, 0:4096], mybir.AxisListType.X, ALU.add)
            v.tensor_reduce(z.rows[0:1, 10:11], z.wchunk2[0:1, 0:4096], mybir.AxisListType.X, ALU.add)
            v.drain()
            v.tensor_tensor(z.rows[0:1, 8:9], z.rows[0:1, 8:9], z.rows[0:1, 10:11], ALU.add)
            v.drain()
            v.tensor_scalar(z.rows[0:1, 8:9], z.rows[0:1, 8:9], 1.0 / mean_count, None, ALU.mult)
            v.drain()
            v.tensor_scalar(z.rows[0:1, 8:9], z.rows[0:1, 8:9], 1e-5, None, ALU.max)
            v.drain()
            v.reciprocal(z.rows[0:1, 9:10], z.rows[0:1, 8:9])
            v.drain().then_inc(sV.h, 1); sV.n += 1
        @blk.gpsimd
        def _(g):
            wt(g, sV)
            g.dma_start(out=z.bounce[0:2], in_=z.rows[0:1, 8:10]).then_inc(dS.h, 16); dS.n += 16
        @blk.sync
        def _(q):
            wt(q, dS)
            wt(q, dL)
            q.dma_start(out=z.consts[:, s_col:s_col + 1], in_=bass.AP(z.bounce.tensor, 0, [[0, 128], [1, 1]])).then_inc(dL.h, 16); dL.n += 16
            q.dma_start(out=z.consts[:, inv_col:inv_col + 1], in_=bass.AP(z.bounce.tensor, 1, [[0, 128], [1, 1]])).then_inc(dL.h, 16); dL.n += 16
        @blk.gpsimd
        def _(g):
            wt(g, dL)
            g.drain().then_inc(sG.h, 1); sG.n += 1
        for r0 in range(0, rows, 128):
            @blk.sync
            def _(q):
                wt(q, sA)
                wt(q, sV)
                wt(q, dL)
                q.dma_start(out=z.wchunk[:, 0:cols], in_=srcT[r0:r0 + 128, :]).then_inc(dL.h, 16); dL.n += 16
            @blk.scalar
            def _(s):
                wt(s, dL)
                wt(s, sG)
                wt(s, sV)
                s.activation(z.wchunk2[:, 0:cols], z.wchunk[:, 0:cols], AF.Sign,
                             bias=z.consts[:, 0:1], scale=z.consts[:, inv_col:inv_col + 1])
                s.drain()
                s.activation(z.wchunk[:, 0:cols], z.wchunk[:, 0:cols], AF.Sign,
                             bias=z.consts[:, 1:2], scale=z.consts[:, inv_col:inv_col + 1])
                s.drain().then_inc(sA.h, 1); sA.n += 1
            @blk.vector
            def _(v):
                wt(v, sA)
                wt(v, dS)
                v.tensor_tensor(z.wchunk2[:, 0:cols], z.wchunk2[:, 0:cols], z.wchunk[:, 0:cols], ALU.add)
                v.drain()
                v.tensor_scalar(z.bchunk[:, 0:cols], z.wchunk2[:, 0:cols], 0.5, None, ALU.mult).then_inc(sV.h, 1); sV.n += 1
            @blk.gpsimd
            def _(g):
                wt(g, sV)
                g.dma_start(out=dstT[r0:r0 + 128, :], in_=z.bchunk[:, 0:cols]).then_inc(dS.h, 16); dS.n += 16

    tern_weight(w1T, t1_c, E, 4 * E, E * 4 * E, 9, 11)
    tern_weight(w2T, t2_c, 4 * E, E, E * 4 * E, 10, 12)

    for h in range(16):
        @blk.sync
        def _(q):
            wt(q, sA)
            wt(q, sV)
            wt(q, dL)
            q.dma_start(out=z.wchunk[:, 0:2560], in_=ZBin[h]).then_inc(dL.h, 16); dL.n += 16
        @blk.scalar
        def _(s):
            wt(s, dL)
            wt(s, sV)
            s.activation(z.wchunk2[:, 0:2560], z.wchunk[:, 0:2560], AF.Exp).then_inc(sA.h, 1); sA.n += 1
        @blk.vector
        def _(v):
            wt(v, sA)
            wt(v, dS)
            v.tensor_tensor(z.bchunk[:, 0:2560], z.wchunk2[:, 0:2560], z.maskf[:], ALU.mult).then_inc(sV.h, 1); sV.n += 1
        @blk.gpsimd
        def _(g):
            wt(g, sV)
            g.dma_start(out=expb_c[h], in_=z.bchunk[:, 0:2560]).then_inc(dS.h, 16); dS.n += 16

    prep_es.close()
    z.weights = dict(wq_c=wq_c, wk_c=wk_c, wv_c=wv_c, wproj_c=wproj_c, wwq_c=wwq_c,
                     wwk_c=wwk_c, wwv_c=wwv_c, projw_c=projw_c, t1_c=t1_c, t2_c=t2_c,
                     expb_c=expb_c)

    for seq in range(_BUILD_OPTS["n_seq"]):
        _build_seq(z, seq)

    @blk.gpsimd
    def _(g):
        wt(g, dS)
        g.nop()

    nc.all_engine_barrier()


def _quiesce(z):
    blk = z.blk
    sems = [z.dL, z.dS, z.sT, z.sV, z.sA, z.sG]
    for opener in (blk.sync, blk.tensor, blk.vector, blk.scalar, blk.gpsimd):
        def body(eng):
            for s in sems:
                eng.wait_ge(s.h, s.n)
        opener(body)


def _phase_a(z, seq):
    nc, blk = z.nc, z.blk
    dL, dS, sT, sV, sA, sG = z.dL, z.dS, z.sT, z.sV, z.sA, z.sG
    W = z.weights
    ps, psbf = z.ps, z.psbf

    pes = contextlib.ExitStack()
    sb = lambda name, shape, dt=F32: pes.enter_context(nc.sbuf_tensor(name + z.sfx, shape, dt))
    xtile = sb("xtile", [128, 2048], F32)
    xsq = sb("xsq", [128, 1024], F32)
    rtmp = sb("rtmp", [128, 4096], F32)
    ropeQ = sb("ropeQ", [128, 1024], BF16)
    ropeK = sb("ropeK", [128, 1024], BF16)
    mu_b = sb("mu_b", [128, 128], F32)
    r_b = sb("r_b", [128, 128], F32)
    wq_sb = sb("wq_sb", [128, 8 * 1024], BF16)
    wk_sb = sb("wk_sb", [128, 8 * 1024], BF16)
    wstg = sb("wstg", [128, 8 * 1024], F32)
    qknorm = sb("qknorm", [128, 64], F32)
    qkmax = sb("qkmax", [128, 64], F32)
    tabs = sb("tabs", [128, 4 * 256], F32)

    @blk.sync
    def _(q):
        wt(q, dL)
        q.dma_start(out=wstg[:], in_=bass.AP(z.wdirect["wqT"].tensor, 0, [[1024, 128], [131072, 8], [1, 1024]])).then_inc(dL.h, 16); dL.n += 16
        for i in range(4):
            q.dma_start(out=tabs[:, i * 256:(i + 1) * 256], in_=z.dram_tabs[i]).then_inc(dL.h, 16); dL.n += 16
    tok_wq = dL.n
    @blk.vector
    def _(v):
        wt(v, dL, tok_wq)
        v.tensor_copy(wq_sb[:], wstg[:]).then_inc(sV.h, 1); sV.n += 1
    tok_wqc = sV.n
    @blk.sync
    def _(q):
        wt(q, sV, tok_wqc)
        wt(q, dL)
        q.dma_start(out=wstg[:], in_=bass.AP(z.wdirect["wkT"].tensor, 0, [[1024, 128], [131072, 8], [1, 1024]])).then_inc(dL.h, 16); dL.n += 16
    tok_wk = dL.n
    @blk.vector
    def _(v):
        wt(v, dL, tok_wk)
        v.tensor_copy(wk_sb[:], wstg[:]).then_inc(sV.h, 1); sV.n += 1
    z.tok["qk_w_ready"] = (sV, sV.n)
    tok_w = dL.n

    tok = z.tok
    tok["xtile_free"] = [(sV, 0), (sV, 0)]
    tok["stats_evd"] = (sV, 0)
    tok["xsq_free"] = (sT, 0)
    tok["rope_done"] = (sV, 0)
    tok["bank7_ev"] = (sV, 0)
    tok["mub_free"] = (sV, 0)

    for tt in range(NT):
        par = tt % 2
        xsl = lambda a, b: xtile[:, par * 1024 + a: par * 1024 + b]
        u1sl = lambda a, b: z.u1all[:, tt * 1024 + a: tt * 1024 + b]

        dX = z.dX[par]
        @blk.sync
        def _(q):
            s_, v_ = tok["xtile_free"][par]
            wt(q, s_, v_)
            q.dma_start(out=xsl(0, 1024), in_=z.xt[seq, tt]).then_inc(dX.h, 16); dX.n += 16
        tok_x = dX.n

        @blk.scalar
        def _(s):
            wt(s, dX, tok_x)
            s_, v_ = tok["xsq_free"]
            wt(s, s_, v_)
            s.activation(xsq[:], xsl(0, 1024), AF.Square).then_inc(sA.h, 1); sA.n += 1
        tok_sq = sA.n

        @blk.tensor
        def _(t):
            wt(t, dX, tok_x)
            wt(t, sA, tok_sq)
            s_, v_ = tok["stats_evd"]
            wt(t, s_, v_)
            t.matmul(ps[:, 2048:2560], z.ones128f[:], xsl(0, 512), start=True, stop=True)
            t.matmul(ps[:, 2560:3072], z.ones128f[:], xsl(512, 1024), start=True, stop=True).then_inc(sT.h, 1); sT.n += 1
        tok_summ = sT.n

        @blk.vector
        def _(v):
            wt(v, sT, tok_summ)
            v.tensor_reduce(mu_b[:], bass.AP(ps, 2048, [[3584, 128], [1, 128], [128, 8]]),
                            mybir.AxisListType.X, ALU.add)
            v.drain()
            v.tensor_scalar(mu_b[:], mu_b[:], 1.0 / 1024.0, None, ALU.mult).then_inc(sV.h, 1); sV.n += 1
        tok_sumred = sV.n

        @blk.tensor
        def _(t):
            wt(t, sV, tok_sumred)
            t.matmul(ps[:, 2048:2560], z.ones128f[:], xsq[:, 0:512], start=True, stop=True)
            t.matmul(ps[:, 2560:3072], z.ones128f[:], xsq[:, 512:1024], start=True, stop=True).then_inc(sT.h, 1); sT.n += 1
        tok_ssqm = sT.n

        @blk.vector
        def _(v):
            wt(v, sT, tok_ssqm)
            v.tensor_reduce(r_b[:], bass.AP(ps, 2048, [[3584, 128], [1, 128], [128, 8]]),
                            mybir.AxisListType.X, ALU.add)
            v.drain()
            v.tensor_scalar(r_b[:], r_b[:], 1.0 / 1024.0, None, ALU.mult)
            v.drain()
            v.tensor_tensor(xsq[:, 0:128], mu_b[:], mu_b[:], ALU.mult)
            v.drain()
            v.tensor_tensor(r_b[:], r_b[:], xsq[:, 0:128], ALU.subtract).then_inc(sV.h, 1); sV.n += 1
        tok_var = sV.n
        tok["stats_evd"] = (sV, tok_var)

        @blk.scalar
        def _(s):
            wt(s, sV, tok_var)
            s.activation(r_b[:], r_b[:], AF.Sqrt, bias=z.consts[:, 2:3]).then_inc(sA.h, 1); sA.n += 1
        tok_std = sA.n

        @blk.vector
        def _(v):
            wt(v, sA, tok_std)
            v.reciprocal(r_b[:], r_b[:])
            v.drain()
            mu_v = bass.AP(mu_b, 0, [[128, 128], [0, 8], [1, 128]])
            r_v = bass.AP(r_b, 0, [[128, 128], [0, 8], [1, 128]])
            v.tensor_tensor(xsq[:], xsl(0, 1024), mu_v, ALU.subtract)
            v.drain()
            v.tensor_tensor(u1sl(0, 1024), xsq[:], r_v, ALU.mult).then_inc(sV.h, 1); sV.n += 1
        tok_u1 = sV.n
        tok["xtile_free"][par] = (sV, tok_u1)
        tok["mub_free"] = (sV, tok_u1)
        tok["xsq_free"] = (sV, tok_u1)

        @blk.tensor
        def _(t):
            wt(t, sV, tok_u1)
            if tt == 0:
                wt(t, dL, tok_w)
                s_, v_ = z.tok["qk_w_ready"]
                wt(t, s_, v_)
            s_, v_ = tok["rope_done"]
            wt(t, s_, v_)
            for c in range(8):
                fl = dict(start=(c == 0), stop=(c == 7))
                u1c = u1sl(c * 128, (c + 1) * 128)
                t.matmul(ps[:, 0:512], u1c, wq_sb[:, c * 1024 + 0:c * 1024 + 512], **fl)
                t.matmul(ps[:, 512:1024], u1c, wq_sb[:, c * 1024 + 512:c * 1024 + 1024], **fl)
                t.matmul(ps[:, 1024:1536], u1c, wk_sb[:, c * 1024 + 0:c * 1024 + 512], **fl)
                last = t.matmul(ps[:, 1536:2048], u1c, wk_sb[:, c * 1024 + 512:c * 1024 + 1024], **fl)
            last.then_inc(sT.h, 1); sT.n += 1
        tok_qk = sT.n

        @blk.vector
        def _(v):
            wt(v, sT, tok_qk)
            vws = []
            for (src_off, dst, cbase, sbase, toff) in ((0, ropeQ, 0, 256, 0),
                                                       (1024, ropeK, 512, 768, 2048)):
                x1v = bass.AP(ps, src_off, [[3584, 128], [128, 8], [32, 4], [1, 16]])
                x2v = bass.AP(ps, src_off + 16, [[3584, 128], [128, 8], [32, 4], [1, 16]])
                cv = bass.AP(tabs, cbase + tt * 16, [[1024, 128], [0, 8], [0, 4], [1, 16]])
                sv_ = bass.AP(tabs, sbase + tt * 16, [[1024, 128], [0, 8], [0, 4], [1, 16]])
                t1v = bass.AP(rtmp, toff, [[4096, 128], [128, 8], [32, 4], [1, 16]])
                t2v = bass.AP(rtmp, toff + 16, [[4096, 128], [128, 8], [32, 4], [1, 16]])
                t3v = bass.AP(rtmp, toff + 1024, [[4096, 128], [128, 8], [32, 4], [1, 16]])
                t4v = bass.AP(rtmp, toff + 1040, [[4096, 128], [128, 8], [32, 4], [1, 16]])
                y1v = bass.AP(dst, 0, [[1024, 128], [128, 8], [32, 4], [1, 16]])
                y2v = bass.AP(dst, 16, [[1024, 128], [128, 8], [32, 4], [1, 16]])
                vws.append((x1v, x2v, cv, sv_, t1v, t2v, t3v, t4v, y1v, y2v))
            for (x1v, x2v, cv, sv_, t1v, t2v, t3v, t4v, y1v, y2v) in vws:
                v.tensor_tensor(t1v, x1v, cv, ALU.mult)
                v.tensor_tensor(t2v, x2v, sv_, ALU.mult)
                v.tensor_tensor(t3v, x2v, cv, ALU.mult)
                v.tensor_tensor(t4v, x1v, sv_, ALU.mult)
            v.drain()
            for (x1v, x2v, cv, sv_, t1v, t2v, t3v, t4v, y1v, y2v) in vws:
                v.tensor_tensor(y1v, t1v, t2v, ALU.subtract)
                last = v.tensor_tensor(y2v, t3v, t4v, ALU.add)
            last.then_inc(sV.h, 1); sV.n += 1
        tok_rope = sV.n
        tok["rope_done"] = (sV, tok_rope)

        @blk.tensor
        def _(t):
            wt(t, sV, tok_rope)
            s_, v_ = tok["bank7_ev"]
            wt(t, s_, v_)
            for g in range(8):
                last = t.transpose(psbf[:, g * 128:(g + 1) * 128], ropeQ[:, g * 128:(g + 1) * 128], z.ident[:])
            last.then_inc(sT.h, 1); sT.n += 1
        tok_trQ = sT.n

        @blk.vector
        def _(v):
            wt(v, sT, tok_trQ)
            v.tensor_copy(bass.AP(z.QT, tt * 128, [[16384, 128], [2048, 8], [1, 128]]), psbf[:, 0:1024]).then_inc(sV.h, 1); sV.n += 1
        tok_qtev = sV.n

        @blk.tensor
        def _(t):
            wt(t, sV, tok_qtev)
            for g in range(8):
                last = t.transpose(psbf[:, g * 128:(g + 1) * 128], ropeK[:, g * 128:(g + 1) * 128], z.ident[:])
            last.then_inc(sT.h, 1); sT.n += 1
        tok_trK = sT.n

        @blk.vector
        def _(v):
            wt(v, sT, tok_trK)
            v.tensor_copy(bass.AP(z.KT, tt * 128, [[16384, 128], [2048, 8], [1, 128]]), psbf[:, 0:1024]).then_inc(sV.h, 1); sV.n += 1
        tok_ktev = sV.n
        tok["bank7_ev"] = (sV, tok_ktev)

        @blk.vector
        def _(v):
            v.tensor_tensor(rtmp[:, 0:1024], ropeQ[:], ropeQ[:], ALU.mult)
            v.tensor_tensor(rtmp[:, 1024:2048], ropeK[:], ropeK[:], ALU.mult)
            v.drain()
            if tt == 0:
                v.tensor_reduce(qknorm[:, 0:32], bass.AP(rtmp, 0, [[4096, 128], [32, 32], [1, 32]]),
                                mybir.AxisListType.X, ALU.add)
                last = v.tensor_reduce(qknorm[:, 32:64], bass.AP(rtmp, 1024, [[4096, 128], [32, 32], [1, 32]]),
                                       mybir.AxisListType.X, ALU.add)
            else:
                v.tensor_reduce(qkmax[:, 0:32], bass.AP(rtmp, 0, [[4096, 128], [32, 32], [1, 32]]),
                                mybir.AxisListType.X, ALU.add)
                v.tensor_reduce(qkmax[:, 32:64], bass.AP(rtmp, 1024, [[4096, 128], [32, 32], [1, 32]]),
                                mybir.AxisListType.X, ALU.add)
                v.drain()
                last = v.tensor_tensor(qknorm[:], qknorm[:], qkmax[:], ALU.max)
            last.then_inc(sV.h, 1); sV.n += 1

    @blk.tensor
    def _(t):
        wt(t, sV)
        t.transpose(ps[0:64, 3072:3200], qknorm[:], z.identf[:])
        t.drain().then_inc(sT.h, 1); sT.n += 1
    tok_btr = z.sT.n

    @blk.vector
    def _(v):
        wt(v, sT, tok_btr)
        v.tensor_reduce(qkmax[0:64, 0:1], ps[0:64, 3072:3200], mybir.AxisListType.X, ALU.max)
        v.drain().then_inc(sV.h, 1); sV.n += 1
    tok_bred = z.sV.n

    @blk.tensor
    def _(t):
        wt(t, sV, tok_bred)
        t.transpose(ps[0:1, 3200:3264], qkmax[0:64, 0:1], z.identf[0:64, 0:64])
        t.drain().then_inc(sT.h, 1); sT.n += 1
    tok_btr2 = z.sT.n

    @blk.vector
    def _(v):
        wt(v, sT, tok_btr2)
        v.tensor_reduce(z.rows[0:1, 640:644], bass.AP(ps, 3200, [[3584, 1], [16, 4], [1, 16]]),
                        mybir.AxisListType.X, ALU.max)
        v.drain()
        v.tensor_tensor(z.rows[0:1, 644:646], z.rows[0:1, 640:642], z.rows[0:1, 642:644], ALU.mult)
        v.drain().then_inc(sV.h, 1); sV.n += 1
    tok_bmul = z.sV.n

    @blk.scalar
    def _(s):
        wt(s, sV, tok_bmul)
        s.activation(z.rows[0:1, 644:646], z.rows[0:1, 644:646], AF.Sqrt).then_inc(sA.h, 1); sA.n += 1
    tok_bsq = z.sA.n

    @blk.vector
    def _(v):
        wt(v, sA, tok_bsq)
        v.tensor_scalar(z.rows[0:1, 646:648], z.rows[0:1, 644:646], -1.0, None, ALU.mult)
        v.drain().then_inc(sV.h, 1); sV.n += 1
    tok_bneg = z.sV.n

    @blk.gpsimd
    def _(g):
        wt(g, sV, tok_bneg)
        wt(g, dL)
        g.dma_start(out=z.bounce[0:2], in_=z.rows[0:1, 646:648]).then_inc(dS.h, 16); dS.n += 16
    @blk.sync
    def _(q):
        wt(q, dS)
        wt(q, dL)
        q.dma_start(out=z.consts2[:, 0:2],
                    in_=bass.AP(z.bounce.tensor, 0, [[0, 128], [1, 2]])).then_inc(dL.h, 16); dL.n += 16
    z.tok["phaseA_done"] = (dL, z.dL.n)
    return pes


def _build_seq(z, seq):
    z.tok = {}
    z.sfx = "_s%d" % seq
    nc = z.nc
    ab_es = contextlib.ExitStack()
    z.QT = ab_es.enter_context(nc.sbuf_tensor("QT" + z.sfx, [128, 8 * 2048], BF16, side="right"))
    z.KT = ab_es.enter_context(nc.sbuf_tensor("KT" + z.sfx, [128, 8 * 2048], BF16, side="right"))
    z.u1all = ab_es.enter_context(nc.sbuf_tensor("u1all" + z.sfx, [128, NT * 1024], BF16, side="right"))
    _quiesce(z)
    pes = _phase_a(z, seq)
    stop = _BUILD_OPTS.get("stop_after")
    if stop == "A":
        if seq == 0:
            _dump_taps(z, dict(QT=z.QT, KT=z.KT, u1all=z.u1all, consts2=z.consts2, consts=z.consts))
        pes.close(); ab_es.close()
        return
    pes.close()
    tt_es = contextlib.ExitStack()
    z.TT = tt_es.enter_context(nc.sbuf_tensor("TT" + z.sfx, [128, 16 * 2048], BF16))
    _quiesce(z)
    pesb = _phase_b(z, seq)
    pesb.close()
    if stop == "B":
        if seq == 0:
            _dump_taps(z, dict(TT=z.TT, consts2=z.consts2))
        ab_es.close(); tt_es.close()
        return
    ab_es.close()
    stream_es = contextlib.ExitStack()
    z.xs = stream_es.enter_context(nc.sbuf_tensor("xs" + z.sfx, [128, 8 * 2048], F32, side="right"))
    _quiesce(z)
    pesc = _phase_c(z, seq)
    pesc.close()
    tt_es.close()
    if stop == "C":
        if seq == 0:
            _dump_taps(z, dict(xs=z.xs))
        stream_es.close()
        return
    _quiesce(z)
    pesd = _phase_ffn(z, seq, "D" + z.sfx)
    pesd.close()
    if stop == "D":
        if seq == 0:
            _dump_taps(z, dict(xs=z.xs))
        stream_es.close()
        return
    _quiesce(z)
    pese = _phase_e(z, seq)
    pese.close()
    if stop == "E":
        if seq == 0:
            _dump_taps(z, dict(xs=z.xs))
        stream_es.close()
        return
    _quiesce(z)
    pesf = _phase_ffn(z, seq, "F" + z.sfx)
    pesf.close()
    blk = z.blk
    @blk.gpsimd
    def _(g):
        wt(g, z.sV)
        for fc in range(8):
            g.dma_start(out=z.yout[seq, fc], in_=z.xs[:, fc * 2048:(fc + 1) * 2048]).then_inc(z.dS.h, 16); z.dS.n += 16
        wt(g, z.dS)
    stream_es.close()


def _dump_taps(z, table):
    if not z.taps:
        return
    blk = z.blk
    _quiesce(z)
    @blk.gpsimd
    def _(g):
        for name, ap in z.taps.items():
            src = table[name]
            g.dma_start(out=ap, in_=src[:]).then_inc(z.dS.h, 16); z.dS.n += 16


def kernel(**inputs):
    per_core = _host_prepare(inputs)
    nc = build_nc()
    res = run_bass_kernel_spmd(nc, per_core, core_ids=list(range(8)))
    return _host_gather(res.results)


def _phase_b(z, seq):
    nc, blk = z.nc, z.blk
    dL, dS, sT, sV, sA, sG = z.dL, z.dS, z.sT, z.sV, z.sA, z.sG
    W = z.weights
    ps, psbf = z.ps, z.psbf

    pes = contextlib.ExitStack()
    sb = lambda name, shape, dt=F32: pes.enter_context(nc.sbuf_tensor(name + z.sfx, shape, dt))
    wvp = sb("wvp", [128, 2048], BF16)
    Vp = sb("Vp", [128, 4096], BF16)
    Ebuf = sb("Ebuf", [128, 2048], BF16)
    scr = sb("scr", [128, 2048], F32)
    OcA = sb("OcA", [128, 2048], F32)
    OcB = sb("OcB", [128, 2048], F32)
    gnmu = sb("gnmu", [128, 2], F32)

    tok = z.tok
    tok["wvp_free"] = [(sV, sV.n), (sV, sV.n)]
    tok["expdone"] = (sA, sA.n)
    tok["ozfree"] = (sV, sV.n)
    tok["bank7_ev"] = (sV, sV.n)
    tok["gnmu_free"] = (sV, sV.n)
    tok["vev"] = [(sV, sV.n), (sV, sV.n)]

    SA, SB = 0, 512
    OA, OB = 1024, 1536
    ZA, ZB = 2048, 2560
    ST6 = 3072

    for pr in range(8):
        hA, hB = 2 * pr, 2 * pr + 1
        g = pr // 2
        r0 = (pr % 2) * 64
        par = pr % 2
        dX = z.dX[par]

        @blk.sync
        def _(q):
            s_, v_ = tok["wvp_free"][0]
            wt(q, s_, v_)
            q.dma_start(out=scr[:],
                        in_=bass.AP(z.wdirect["wvT"].tensor, hA * 128, [[2048, 128], [128 * 2048, 8], [1, 256]])
                        ).then_inc(dX.h, 16); dX.n += 16
        tok_wvf = dX.n
        @blk.vector
        def _(v):
            wt(v, dX, tok_wvf)
            s_, v_ = tok["wvp_free"][1]
            wt(v, s_, v_)
            v.tensor_copy(wvp[:], scr[:]).then_inc(sV.h, 1); sV.n += 1
        tok_wv = sV.n

        for tt in range(NT):
            vb = (tt % 2) * 512
            @blk.tensor
            def _(t):
                if tt == 0:
                    wt(t, sV, tok_wv)
                    s_, v_ = tok["expdone"]
                    wt(t, s_, v_)
                    wt(t, sV)
                else:
                    s_, v_ = tok["vev"][tt % 2]
                    wt(t, s_, v_)
                for c in range(8):
                    last = t.matmul(ps[:, vb:vb + 256], z.u1all[:, tt * 1024 + c * 128:tt * 1024 + (c + 1) * 128],
                             wvp[:, c * 256:(c + 1) * 256],
                             start=(c == 0), stop=(c == 7))
                last.then_inc(sT.h, 1); sT.n += 1
            tok_vmm = sT.n

            @blk.vector
            def _(v):
                wt(v, sT, tok_vmm)
                v.tensor_copy(Vp[:, tt * 256:tt * 256 + 256], ps[:, vb:vb + 256]).then_inc(sV.h, 1); sV.n += 1
            tok["vev"][tt % 2] = (sV, sV.n)
        tok["wvp_free"][0] = (sV, tok_wv)
        tok["wvp_free"][1] = (sT, sT.n)
        tok["ozfree"] = (sV, sV.n)

        for qb in range(4):
            q0 = qb * 512
            for s_i in (1, 2):
                gs = g + 4 * (s_i - 1)
                qcol = gs * 2048 + q0
                sT0 = sT.n
                sA0 = sA.n
                oz_s, oz_v = tok["ozfree"]

                @blk.tensor
                def _(t):
                    for kt in range(NT):
                        kcol = gs * 2048 + kt * 128
                        eb = (kt % 2) * 1024
                        wt(t, sA, sA0 + kt)
                        if kt == 0:
                            wt(t, oz_s, oz_v)
                        t.matmul(ps[:, SA:SA + 512], z.KT[r0:r0 + 32, kcol:kcol + 128],
                                 z.QT[r0:r0 + 32, qcol:qcol + 512], start=True, stop=True,
                                 tile_position=(r0, 0))
                        t.matmul(ps[:, SB:SB + 512], z.KT[r0 + 32:r0 + 64, kcol:kcol + 128],
                                 z.QT[r0 + 32:r0 + 64, qcol:qcol + 512], start=True, stop=True,
                                 tile_position=(r0 + 32, 0)).then_inc(sT.h, 1); sT.n += 1
                        if kt > 0:
                            ep = ((kt - 1) % 2) * 1024
                            fl = dict(start=(kt - 1 == 0), stop=False)
                            t.matmul(ps[:, ZA:ZA + 512], z.ones128[:], Ebuf[:, ep:ep + 512], **fl)
                            t.matmul(ps[:, ZB:ZB + 512], z.ones128[:], Ebuf[:, ep + 512:ep + 1024], **fl)
                            t.matmul(ps[:, OA:OA + 512], Vp[:, (kt - 1) * 256:(kt - 1) * 256 + 128],
                                     Ebuf[:, ep:ep + 512], **fl)
                            t.matmul(ps[:, OB:OB + 512], Vp[:, (kt - 1) * 256 + 128:(kt - 1) * 256 + 256],
                                     Ebuf[:, ep + 512:ep + 1024], **fl)
                    wt(t, sA, sA0 + NT)
                    ep = ((NT - 1) % 2) * 1024
                    fl = dict(start=False, stop=True)
                    t.matmul(ps[:, ZA:ZA + 512], z.ones128[:], Ebuf[:, ep:ep + 512], **fl)
                    t.matmul(ps[:, ZB:ZB + 512], z.ones128[:], Ebuf[:, ep + 512:ep + 1024], **fl)
                    t.matmul(ps[:, OA:OA + 512], Vp[:, (NT - 1) * 256:(NT - 1) * 256 + 128],
                             Ebuf[:, ep:ep + 512], **fl)
                    t.matmul(ps[:, OB:OB + 512], Vp[:, (NT - 1) * 256 + 128:(NT - 1) * 256 + 256],
                             Ebuf[:, ep + 512:ep + 1024], **fl).then_inc(sT.h, 1); sT.n += 1

                @blk.scalar
                def _(s):
                    for kt in range(NT):
                        eb = (kt % 2) * 1024
                        wt(s, sT, sT0 + kt + 1)
                        s.activation(Ebuf[:, eb:eb + 1024], ps[:, 0:1024], AF.Exp,
                                     bias=z.consts2[:, s_i - 1:s_i]).then_inc(sA.h, 1); sA.n += 1
                assert sA.n == sA0 + NT
                tok["expdone"] = (sA, sA.n)
                tok_oz = sT.n

                @blk.vector
                def _(v):
                    wt(v, sT, tok_oz)
                    v.reciprocal(scr[:, 0:512], ps[:, ZA:ZA + 512])
                    v.reciprocal(scr[:, 512:1024], ps[:, ZB:ZB + 512])
                    v.drain()
                    if s_i == 1:
                        v.tensor_tensor(OcA[:, q0:q0 + 512], ps[:, OA:OA + 512], scr[:, 0:512], ALU.mult)
                        v.tensor_tensor(OcB[:, q0:q0 + 512], ps[:, OB:OB + 512], scr[:, 512:1024], ALU.mult).then_inc(sV.h, 1); sV.n += 1
                    else:
                        v.tensor_tensor(scr[:, 1024:1536], ps[:, OA:OA + 512], scr[:, 0:512], ALU.mult)
                        v.tensor_tensor(scr[:, 1536:2048], ps[:, OB:OB + 512], scr[:, 512:1024], ALU.mult)
                        v.drain()
                        v.scalar_tensor_tensor(OcA[:, q0:q0 + 512], scr[:, 1024:1536], z.consts[:, 5:6],
                                               OcA[:, q0:q0 + 512], ALU.mult, ALU.add)
                        v.scalar_tensor_tensor(OcB[:, q0:q0 + 512], scr[:, 1536:2048], z.consts[:, 5:6],
                                               OcB[:, q0:q0 + 512], ALU.mult, ALU.add).then_inc(sV.h, 1); sV.n += 1
                tok["ozfree"] = (sV, sV.n)

        for (Oc, h) in ((OcA, hA), (OcB, hB)):
            @blk.scalar
            def _(s):
                wt(s, sV)
                s.activation(scr[:], Oc[:], AF.Square).then_inc(sA.h, 1); sA.n += 1
            tok_gsq = sA.n

            @blk.tensor
            def _(t):
                wt(t, sA, tok_gsq)
                s_, v_ = tok["stats_evd_b"] if "stats_evd_b" in tok else (sV, 0)
                wt(t, s_, v_)
                for j in range(4):
                    t.matmul(ps[:, ST6:ST6 + 512], z.ones128f[:], Oc[:, j * 512:(j + 1) * 512],
                             start=(j == 0), stop=(j == 3))
                t.drain().then_inc(sT.h, 1); sT.n += 1
            tok_gsum = sT.n

            @blk.vector
            def _(v):
                wt(v, sT, tok_gsum)
                v.tensor_reduce(gnmu[:, 0:1], ps[:, ST6:ST6 + 512], mybir.AxisListType.X, ALU.add)
                v.drain()
                v.tensor_scalar(gnmu[:, 0:1], gnmu[:, 0:1], 1.0 / 262144.0, None, ALU.mult)
                v.drain().then_inc(sV.h, 1); sV.n += 1
            tok_gsr = sV.n

            @blk.tensor
            def _(t):
                wt(t, sV, tok_gsr)
                for j in range(4):
                    t.matmul(ps[:, ST6:ST6 + 512], z.ones128f[:], scr[:, j * 512:(j + 1) * 512],
                             start=(j == 0), stop=(j == 3))
                t.drain().then_inc(sT.h, 1); sT.n += 1
            tok_gss = sT.n

            @blk.vector
            def _(v):
                wt(v, sT, tok_gss)
                v.tensor_reduce(gnmu[:, 1:2], ps[:, ST6:ST6 + 512], mybir.AxisListType.X, ALU.add)
                v.drain()
                v.tensor_scalar(gnmu[:, 1:2], gnmu[:, 1:2], 1.0 / 262144.0, None, ALU.mult)
                v.drain()
                v.tensor_tensor(scr[:, 0:1], gnmu[:, 0:1], gnmu[:, 0:1], ALU.mult)
                v.drain()
                v.tensor_tensor(gnmu[:, 1:2], gnmu[:, 1:2], scr[:, 0:1], ALU.subtract)
                v.drain().then_inc(sV.h, 1); sV.n += 1
            tok["stats_evd_b"] = (sV, sV.n)
            tok_gvar = sV.n

            @blk.scalar
            def _(s):
                wt(s, sV, tok_gvar)
                s.activation(gnmu[:, 1:2], gnmu[:, 1:2], AF.Sqrt,
                             bias=z.consts[:, 2:3]).then_inc(sA.h, 1); sA.n += 1
            tok_gstd = sA.n

            @blk.vector
            def _(v):
                wt(v, sA, tok_gstd)
                v.reciprocal(gnmu[:, 1:2], gnmu[:, 1:2])
                v.drain()
                v.tensor_scalar(gnmu[:, 1:2], gnmu[:, 1:2], 0.8, None, ALU.mult)
                v.drain()
                wt(v, sT)
                v.tensor_scalar(wvp[:], Oc[:], gnmu[:, 0:1], gnmu[:, 1:2], ALU.subtract, ALU.mult)
                v.drain().then_inc(sV.h, 1); sV.n += 1
            tok_gn = sV.n

            for b in range(2):
                @blk.tensor
                def _(t):
                    wt(t, sV, tok_gn)
                    s_, v_ = tok["bank7_ev"]
                    wt(t, s_, v_)
                    for q_lo in range(8):
                        ql = b * 8 + q_lo
                        t.transpose(psbf[:, q_lo * 128:(q_lo + 1) * 128],
                                    bass.AP(wvp, ql, [[2048, 128], [16, 128]]), z.ident[:])
                    t.drain().then_inc(sT.h, 1); sT.n += 1
                tok_tr = sT.n

                @blk.vector
                def _(v):
                    wt(v, sT, tok_tr)
                    v.tensor_copy(z.TT[:, h * 2048 + b * 1024: h * 2048 + (b + 1) * 1024], psbf[:])
                    v.drain().then_inc(sV.h, 1); sV.n += 1
                tok["bank7_ev"] = (sV, sV.n)
            tok["wvp_free"] = [(sT, sT.n), (sT, sT.n)]

    z.tok["phaseB_done"] = (sV, z.sV.n)
    return pes


def _phase_c(z, seq):
    nc, blk = z.nc, z.blk
    dL, dS, sT, sV, sA, sG = z.dL, z.dS, z.sT, z.sV, z.sA, z.sG
    W = z.weights
    ps = z.ps

    pes = contextlib.ExitStack()
    sb = lambda name, shape, dt=F32: pes.enter_context(nc.sbuf_tensor(name + z.sfx, shape, dt))
    wpj = sb("wpj", [128, 4096], BF16)
    av = sb("av", [128, 4096], F32)
    scr1 = sb("c1s", [128, 2048], F32)
    scr2 = sb("c2s", [128, 2048], F32)
    scr3 = sb("c3s", [128, 2048], F32)
    scr4 = sb("c4s", [128, 2048], F32)

    tok = z.tok
    tok["wpj_free"] = (sV, sV.n)
    tok["av_free"] = (sV, sV.n)
    tok["chain_free"] = (sV, sV.n)

    for j in range(8):
        par = j % 2
        dX = z.dX[par]

        @blk.sync
        def _(q):
            s_, v_ = tok["wpj_free"]
            wt(q, s_, v_)
            q.dma_start(out=bass.AP(wpj, 0, [[4096, 128], [256, 16], [1, 128]]),
                        in_=bass.AP(W["wproj_c"].tensor, j * 128, [[2048, 128], [128 * 2048, 16], [1, 128]])
                        ).then_inc(dX.h, 16); dX.n += 16
            q.dma_start(out=bass.AP(wpj, 128, [[4096, 128], [256, 16], [1, 128]]),
                        in_=bass.AP(W["wproj_c"].tensor, (j + 8) * 128, [[2048, 128], [128 * 2048, 16], [1, 128]])
                        ).then_inc(dX.h, 16); dX.n += 16
            s_, v_ = tok["chain_free"]
            wt(q, s_, v_)
            q.dma_start(out=scr3[:],
                        in_=bass.AP(z.xt.tensor, seq * 2097152 + j * 128,
                                    [[1024, 128], [131072, 16], [1, 128]])
                        ).then_inc(dX.h, 16); dX.n += 16
        tok_w = dX.n

        for half in range(2):
            @blk.tensor
            def _(t):
                wt(t, dX, tok_w)
                if half == 0:
                    s_, v_ = tok["av_free"]
                    wt(t, s_, v_)
                else:
                    wt(t, sV)
                for q4 in range(4):
                    for c in range(16):
                        t.matmul(ps[:, q4 * 512:(q4 + 1) * 512],
                                 wpj[:, c * 256 + half * 128:c * 256 + half * 128 + 128],
                                 z.TT[:, c * 2048 + q4 * 512:c * 2048 + (q4 + 1) * 512],
                                 start=(c == 0), stop=(c == 15))
                t.drain().then_inc(sT.h, 1); sT.n += 1
            tok_mm = sT.n

            @blk.vector
            def _(v):
                wt(v, sT, tok_mm)
                v.tensor_copy(av[:, half * 2048:(half + 1) * 2048], ps[:, 0:2048])
                v.drain().then_inc(sV.h, 1); sV.n += 1
        tok["av_evict"] = (sV, sV.n)

        @blk.vector
        def _(v):
            wt(v, dX, tok_w)
            v.tensor_scalar(scr2[:], av[:, 0:2048], -1.0, None, ALU.mult)
            v.drain()
            v.tensor_tensor(scr1[:], av[:, 0:2048], scr2[:], ALU.max)
            v.drain()
            v.tensor_scalar(scr1[:], scr1[:], 1e-30, None, ALU.max)
            v.drain()
            v.reciprocal(scr2[:], scr1[:])
            v.tensor_scalar(scr4[:], scr1[:], 1.0, None, ALU.is_gt)
            v.drain()
            v.tensor_tensor(scr1[:], scr1[:], scr2[:], ALU.min)
            v.drain().then_inc(sV.h, 1); sV.n += 1
        tok_pre = sV.n

        @blk.scalar
        def _(s):
            wt(s, sV, tok_pre)
            s.activation(scr2[:], scr1[:], AF.Arctan)
            s.drain()
            s.activation(scr1[:], av[:, 0:2048], AF.Sign)
            s.drain().then_inc(sA.h, 1); sA.n += 1
        tok_act = sA.n

        @blk.vector
        def _(v):
            wt(v, sA, tok_act)
            v.scalar_tensor_tensor(av[:, 0:2048], scr4[:], math.pi / 2, scr2[:], ALU.mult, ALU.add)
            v.drain()
            v.tensor_tensor(scr4[:], scr4[:], scr2[:], ALU.mult)
            v.drain()
            v.scalar_tensor_tensor(scr4[:], scr4[:], -2.0, av[:, 0:2048], ALU.mult, ALU.add)
            v.drain()
            v.tensor_tensor(scr2[:], scr4[:], scr1[:], ALU.mult)
            v.drain()
            v.tensor_scalar(scr2[:], scr2[:], z.consts[:, 6:7], z.consts[:, 7:8], ALU.mult, ALU.add)
            v.drain()
            v.tensor_tensor(scr1[:], scr2[:], av[:, 2048:4096], ALU.mult)
            v.drain()
            v.tensor_tensor(z.xs[:, j * 2048:(j + 1) * 2048], scr1[:], scr3[:], ALU.add)
            v.drain().then_inc(sV.h, 1); sV.n += 1
        tok["chain_free"] = (sV, sV.n)
        tok["av_free"] = (sV, sV.n)
        tok["wpj_free"] = (sV, sV.n)

    z.tok["phaseC_done"] = (sV, z.sV.n)
    return pes


def _phase_ffn(z, seq, tag):
    nc, blk = z.nc, z.blk
    dL, dS, sT, sV, sA, sG = z.dL, z.dS, z.sT, z.sV, z.sA, z.sG
    W = z.weights
    ps = z.ps
    QW = 512

    pes = contextlib.ExitStack()
    sb = lambda name, shape, dt=F32: pes.enter_context(nc.sbuf_tensor(name + z.sfx, shape, dt))
    hbuf = sb("hbuf" + tag, [128, 32 * QW], F32)
    k2b = sb("k2b" + tag, [128, 32 * QW], BF16)
    k1 = sb("k1" + tag, [128, 8 * QW], BF16)
    reps = sb("reps" + tag, [128, 3 * QW], F32)
    scr = sb("scrD" + tag, [128, 2 * QW], F32)
    scr2 = sb("scr2D" + tag, [128, 2 * QW], F32)
    sqb8 = sb("sqbD" + tag, [128, 4096], BF16)
    tch = sb("tchD" + tag, [128, 4096], BF16)

    MU, QR, A1 = 0, QW, 2 * QW
    rp = lambda a: reps[:, a:a + QW]
    tok = z.tok

    for tq in _BUILD_OPTS.get("ffn_tq", range(4)):
        t0c = tq * QW
        xsl = lambda fc: z.xs[:, fc * 2048 + t0c:fc * 2048 + t0c + QW]

        xs_v = bass.AP(z.xs, t0c, [[16384, 128], [2048, 8], [1, 512]])
        mu_v = bass.AP(reps, MU, [[1536, 128], [0, 8], [1, 512]])
        qr_v = bass.AP(reps, QR, [[1536, 128], [0, 8], [1, 512]])

        @blk.vector
        def _(v):
            wt(v, sT)
            wt(v, sA)
            v.tensor_copy(sqb8[:], xs_v).then_inc(sV.h, 1); sV.n += 1
        tok_cast = sV.n

        @blk.tensor
        def _(t):
            wt(t, sV, tok_cast)
            wt(t, sV)
            for fc in range(8):
                last = t.matmul(ps[:, 0:QW], z.ones128[:], sqb8[:, fc * QW:(fc + 1) * QW],
                                start=(fc == 0), stop=(fc == 7))
            last.then_inc(sT.h, 1); sT.n += 1
        tok_mm0 = sT.n

        @blk.vector
        def _(v):
            wt(v, sT, tok_mm0)
            v.tensor_scalar(rp(MU), ps[:, 0:QW], 1.0 / 1024.0, None, ALU.mult).then_inc(sV.h, 1); sV.n += 1
        tok_mu = sV.n

        @blk.scalar
        def _(s):
            wt(s, sT, tok_mm0)
            s.activation(sqb8[:], sqb8[:], AF.Square).then_inc(sA.h, 1); sA.n += 1
        tok_sq8 = sA.n

        @blk.tensor
        def _(t):
            wt(t, sA, tok_sq8)
            wt(t, sV, tok_mu)
            for fc in range(8):
                last = t.matmul(ps[:, 0:QW], z.ones128[:], sqb8[:, fc * QW:(fc + 1) * QW],
                                start=(fc == 0), stop=(fc == 7))
            last.then_inc(sT.h, 1); sT.n += 1
        tok_mm1 = sT.n

        @blk.vector
        def _(v):
            wt(v, sT, tok_mm1)
            v.tensor_scalar(scr2[:, 0:QW], ps[:, 0:QW], 1.0 / 1024.0, None, ALU.mult)
            v.drain()
            v.tensor_tensor(scr[:, 0:QW], rp(MU), rp(MU), ALU.mult)
            v.drain()
            v.tensor_tensor(scr[:, 0:QW], scr2[:, 0:QW], scr[:, 0:QW], ALU.subtract).then_inc(sV.h, 1); sV.n += 1
        tok_var = sV.n

        @blk.scalar
        def _(s):
            wt(s, sV, tok_var)
            s.activation(scr2[:, 0:QW], scr[:, 0:QW], AF.Sqrt, bias=z.consts[:, 2:3]).then_inc(sA.h, 1); sA.n += 1
        tok_std = sA.n

        @blk.vector
        def _(v):
            wt(v, sA, tok_std)
            v.reciprocal(rp(A1), scr2[:, 0:QW])
            v.drain()
            v.tensor_scalar(scr2[:, 0:QW], scr[:, 0:QW], 1e-5, None, ALU.add)
            v.drain()
            v.reciprocal(scr2[:, 0:QW], scr2[:, 0:QW])
            v.drain()
            v.tensor_tensor(scr2[:, 0:QW], scr[:, 0:QW], scr2[:, 0:QW], ALU.mult).then_inc(sV.h, 1); sV.n += 1
        tok_mu2 = sV.n

        @blk.scalar
        def _(s):
            wt(s, sV, tok_mu2)
            s.activation(scr2[:, 0:QW], scr2[:, 0:QW], AF.Sqrt, bias=z.consts[:, 2:3]).then_inc(sA.h, 1); sA.n += 1
        tok_rr = sA.n

        @blk.vector
        def _(v):
            wt(v, sA, tok_rr)
            v.reciprocal(scr2[:, 0:QW], scr2[:, 0:QW])
            v.tensor_reduce(rp(QR), bass.AP(z.xs, t0c, [[16384, 128], [1, 512], [2048, 8]]),
                            mybir.AxisListType.X, ALU.max)
            v.tensor_reduce(scr[:, 0:QW], bass.AP(z.xs, t0c, [[16384, 128], [1, 512], [2048, 8]]),
                            mybir.AxisListType.X, ALU.min)
            v.drain()
            v.tensor_tensor(rp(QR), rp(QR), rp(MU), ALU.subtract)
            v.tensor_tensor(scr[:, 0:QW], rp(MU), scr[:, 0:QW], ALU.subtract)
            v.drain()
            v.tensor_tensor(rp(QR), rp(QR), scr[:, 0:QW], ALU.max).then_inc(sV.h, 1); sV.n += 1
        tok_fold = sV.n

        @blk.tensor
        def _(t):
            wt(t, sV, tok_fold)
            wt(t, sV)
            for i in range(4):
                last = t.transpose(ps[:, 1024 + i * 128:1024 + (i + 1) * 128],
                                   reps[:, QW + i * 128:QW + (i + 1) * 128], z.identf[:])
            last.then_inc(sT.h, 1); sT.n += 1
        tok_amt = sT.n

        @blk.vector
        def _(v):
            wt(v, sT, tok_amt)
            for i in range(4):
                last = v.tensor_reduce(scr[:, QW + i:QW + i + 1], ps[:, 1024 + i * 128:1024 + (i + 1) * 128],
                                       mybir.AxisListType.X, ALU.max)
            last.then_inc(sV.h, 1); sV.n += 1
        tok_amr = sV.n

        @blk.gpsimd
        def _(g):
            wt(g, sV, tok_amr)
            wt(g, dL)
            for i in range(4):
                g.dma_start(out=z.bounce[i * 128:(i + 1) * 128],
                            in_=scr[:, QW + i:QW + i + 1]).then_inc(dS.h, 16); dS.n += 16
        @blk.sync
        def _(q):
            wt(q, dS)
            wt(q, dL)
            q.dma_start(out=scr[:, 0:QW],
                        in_=bass.AP(z.bounce.tensor, 0, [[0, 128], [1, 512]])).then_inc(dL.h, 16); dL.n += 16
        tok_am = dL.n

        @blk.vector
        def _(v):
            wt(v, dL, tok_am)
            v.tensor_tensor(scr[:, 0:QW], scr[:, 0:QW], rp(A1), ALU.mult)
            v.drain()
            v.tensor_tensor(scr[:, 0:QW], scr[:, 0:QW], scr2[:, 0:QW], ALU.mult)
            v.drain()
            v.tensor_scalar(scr[:, 0:QW], scr[:, 0:QW], 1e-5, None, ALU.max)
            v.drain()
            v.reciprocal(rp(QR), scr[:, 0:QW])
            v.drain()
            v.tensor_scalar(rp(QR), rp(QR), 127.0, None, ALU.mult)
            v.drain()
            v.tensor_tensor(rp(QR), rp(QR), scr2[:, 0:QW], ALU.mult)
            v.drain()
            v.tensor_tensor(rp(QR), rp(QR), rp(A1), ALU.mult)
            v.drain()
            v.tensor_scalar(rp(A1), scr[:, 0:QW], z.consts[:, 9:10], 1.0 / 127.0, ALU.mult, ALU.mult)
            v.drain().then_inc(sV.h, 1); sV.n += 1

        @blk.vector
        def _(v):
            wt(v, sT)
            v.tensor_tensor(hbuf[:, 0:4096], xs_v, mu_v, ALU.subtract)
            v.drain()
            v.tensor_tensor(hbuf[:, 0:4096], hbuf[:, 0:4096], qr_v, ALU.mult)
            v.drain()
            v.tensor_scalar(k1[:], hbuf[:, 0:4096], MAGIC, MAGIC, ALU.add, ALU.subtract)
            v.drain().then_inc(sV.h, 1); sV.n += 1
        tok_k1 = sV.n

        tch1tok = tok.setdefault("tch1_free" + tag, [(sV, 0), (sV, 0)])
        tch1tok[0] = (sT, sT.n)
        tch1tok[1] = (sT, sT.n)
        for hf in range(32):
            par1 = hf % 2
            dX = z.dX[par1]
            @blk.sync
            def _(q):
                s_, v_ = tch1tok[par1]
                wt(q, s_, v_)
                q.dma_start(out=tch[:, par1 * 1024:par1 * 1024 + 1024],
                            in_=bass.AP(W["t1_c"].tensor, hf * 128, [[4096, 128], [128 * 4096, 8], [1, 128]])
                            ).then_inc(dX.h, 16); dX.n += 16
            tok_t = dX.n

            @blk.tensor
            def _(t):
                wt(t, dX, tok_t)
                if hf == 0:
                    wt(t, sV, tok_k1)
                if hf >= 2:
                    s_, v_ = tok["ffn1_ev" + tag]
                    wt(t, s_, v_)
                pb = par1 * 512
                for c in range(8):
                    last = t.matmul(ps[:, pb:pb + 512],
                                    tch[:, par1 * 1024 + c * 128:par1 * 1024 + (c + 1) * 128],
                                    k1[:, c * QW:(c + 1) * QW], start=(c == 0), stop=(c == 7))
                last.then_inc(sT.h, 1); sT.n += 1
            tch1tok[par1] = (sT, sT.n)
            if hf % 2 == 1:
                tok_pairmm = sT.n
                @blk.vector
                def _(v):
                    wt(v, sT, tok_pairmm)
                    a1_v = bass.AP(reps, A1, [[1536, 128], [0, 2], [1, 512]])
                    b1_v = bass.AP(z.b1, hf - 1, [[32, 128], [1, 2], [0, 512]])
                    v.tensor_tensor(scr[:, 0:1024], ps[:, 0:1024], a1_v, ALU.mult)
                    v.drain()
                    v.tensor_tensor(hbuf[:, (hf - 1) * QW:(hf + 1) * QW], scr[:, 0:1024],
                                    b1_v, ALU.add).then_inc(sV.h, 1); sV.n += 1
                tok["ffn1_ev" + tag] = (sV, sV.n)

        @blk.scalar
        def _(s):
            s_, v_ = tok["ffn1_ev" + tag]
            wt(s, s_, v_)
            s.activation(hbuf[:], hbuf[:], AF.Gelu).then_inc(sA.h, 1); sA.n += 1
        tok_gelu = sA.n

        for g8 in range(8):
            sqoff = (g8 % 2) * 2048
            @blk.scalar
            def _(s):
                if g8 == 0:
                    wt(s, sA, tok_gelu)
                if g8 >= 2:
                    wt(s, sT)
                s.activation(sqb8[:, sqoff:sqoff + 2048], hbuf[:, g8 * 2048:(g8 + 1) * 2048],
                             AF.Square).then_inc(sA.h, 1); sA.n += 1
            tok_sqg = sA.n
            @blk.tensor
            def _(t):
                wt(t, sA, tok_sqg)
                if g8 == 0:
                    wt(t, sV)
                for c in range(4):
                    hf = g8 * 4 + c
                    last = t.matmul(ps[:, 0:QW], z.ones128[:], sqb8[:, sqoff + c * 512:sqoff + (c + 1) * 512],
                                    start=(hf == 0), stop=(hf == 31))
                last.then_inc(sT.h, 1); sT.n += 1
        tok_hsq = sT.n

        @blk.vector
        def _(v):
            wt(v, sT, tok_hsq)
            wt(v, sA)
            v.tensor_scalar(scr[:, 0:QW], ps[:, 0:QW], 1.0 / 4096.0, None, ALU.mult)
            v.tensor_reduce(scr2[:, 0:QW], bass.AP(hbuf, 0, [[16384, 128], [1, 512], [512, 32]]),
                            mybir.AxisListType.X, ALU.max)
            v.tensor_reduce(scr2[:, QW:2 * QW], bass.AP(hbuf, 0, [[16384, 128], [1, 512], [512, 32]]),
                            mybir.AxisListType.X, ALU.min)
            v.drain()
            v.tensor_scalar(scr2[:, QW:2 * QW], scr2[:, QW:2 * QW], -1.0, None, ALU.mult)
            v.drain()
            v.tensor_tensor(scr2[:, 0:QW], scr2[:, 0:QW], scr2[:, QW:2 * QW], ALU.max).then_inc(sV.h, 1); sV.n += 1
        tok_hf = sV.n

        @blk.tensor
        def _(t):
            wt(t, sV, tok_hf)
            for i in range(4):
                last = t.transpose(ps[:, 1024 + i * 128:1024 + (i + 1) * 128],
                                   scr2[:, i * 128:(i + 1) * 128], z.identf[:])
            last.then_inc(sT.h, 1); sT.n += 1
        tok_hamt = sT.n

        @blk.vector
        def _(v):
            wt(v, sT, tok_hamt)
            for i in range(4):
                last = v.tensor_reduce(reps[:, i:i + 1], ps[:, 1024 + i * 128:1024 + (i + 1) * 128],
                                       mybir.AxisListType.X, ALU.max)
            last.then_inc(sV.h, 1); sV.n += 1
        tok_hamr = sV.n

        @blk.gpsimd
        def _(g):
            wt(g, sV, tok_hamr)
            wt(g, dL)
            for i in range(4):
                g.dma_start(out=z.bounce[i * 128:(i + 1) * 128],
                            in_=reps[:, i:i + 1]).then_inc(dS.h, 16); dS.n += 16
        @blk.sync
        def _(q):
            wt(q, dS)
            wt(q, dL)
            q.dma_start(out=scr2[:, QW:2 * QW],
                        in_=bass.AP(z.bounce.tensor, 0, [[0, 128], [1, 512]])).then_inc(dL.h, 16); dL.n += 16
        tok_ham = dL.n

        @blk.scalar
        def _(s):
            wt(s, sV, tok_hf)
            s.activation(scr[:, QW:2 * QW], scr[:, 0:QW], AF.Sqrt, bias=z.consts[:, 2:3]).then_inc(sA.h, 1); sA.n += 1
        tok_hstd = sA.n

        @blk.vector
        def _(v):
            wt(v, sA, tok_hstd)
            wt(v, dL, tok_ham)
            v.reciprocal(scr[:, QW:2 * QW], scr[:, QW:2 * QW])
            v.drain()
            v.tensor_tensor(scr2[:, 0:QW], scr[:, QW:2 * QW], scr2[:, QW:2 * QW], ALU.mult)
            v.drain()
            v.tensor_scalar(scr2[:, 0:QW], scr2[:, 0:QW], 1e-5, None, ALU.max)
            v.drain()
            v.tensor_scalar(scr2[:, QW:2 * QW], scr2[:, 0:QW], z.consts[:, 10:11], 1.0 / 127.0,
                            ALU.mult, ALU.mult)
            v.drain()
            v.reciprocal(scr2[:, 0:QW], scr2[:, 0:QW])
            v.drain()
            v.tensor_scalar(scr2[:, 0:QW], scr2[:, 0:QW], 127.0, None, ALU.mult)
            v.drain()
            v.tensor_tensor(scr[:, QW:2 * QW], scr[:, QW:2 * QW], scr2[:, 0:QW], ALU.mult)
            v.drain().then_inc(sV.h, 1); sV.n += 1

        @blk.vector
        def _(v):
            wt(v, sT)
            q2r_v = bass.AP(scr, QW, [[1024, 128], [0, 32], [1, 512]])
            v.tensor_tensor(hbuf[:], hbuf[:], q2r_v, ALU.mult)
            v.drain()
            v.tensor_scalar(k2b[:], hbuf[:], MAGIC, MAGIC, ALU.add, ALU.subtract)
            v.drain().then_inc(sV.h, 1); sV.n += 1
        tok_k2 = sV.n

        tch2tok = tok.setdefault("tch2_free" + tag, [(sV, 0), (sV, 0)])
        tch2tok[0] = (sT, sT.n)
        tch2tok[1] = (sT, sT.n)
        ev2tok = tok.setdefault("ffn2_ev" + tag, [(sV, 0), (sV, 0)])
        for of in range(8):
            for sub in range(2):
                hidx = (of * 2 + sub) % 2
                dX = z.dX[hidx]
                @blk.sync
                def _(q):
                    s_, v_ = tch2tok[hidx]
                    wt(q, s_, v_)
                    q.dma_start(out=tch[:, hidx * 2048:hidx * 2048 + 2048],
                                in_=bass.AP(W["t2_c"].tensor, of * 128 + sub * 16 * 128 * 1024,
                                            [[1024, 128], [128 * 1024, 16], [1, 128]])
                                ).then_inc(dX.h, 16); dX.n += 16
                tok_t = dX.n
                @blk.tensor
                def _(t):
                    wt(t, dX, tok_t)
                    if of == 0 and sub == 0:
                        wt(t, sV, tok_k2)
                        wt(t, sV)
                    if sub == 0 and of >= 2:
                        s_, v_ = ev2tok[of % 2]
                        wt(t, s_, v_)
                    pb = (of % 2) * 512
                    for c in range(16):
                        hc = sub * 16 + c
                        last = t.matmul(ps[:, 1024 + pb:1024 + pb + 512],
                                        tch[:, hidx * 2048 + c * 128:hidx * 2048 + (c + 1) * 128],
                                        k2b[:, hc * QW:(hc + 1) * QW], start=(hc == 0), stop=(hc == 31))
                    last.then_inc(sT.h, 1); sT.n += 1
                tch2tok[hidx] = (sT, sT.n)
            tok_m2 = sT.n

            @blk.vector
            def _(v):
                wt(v, sT, tok_m2)
                pb = (of % 2) * 512
                v.tensor_tensor(scr2[:, 0:QW], ps[:, 1024 + pb:1024 + pb + QW], scr2[:, QW:2 * QW], ALU.mult)
                v.drain()
                v.scalar_tensor_tensor(xsl(of), scr2[:, 0:QW], z.b2[:, of:of + 1],
                                       xsl(of), ALU.add, ALU.add)
                v.drain().then_inc(sV.h, 1); sV.n += 1
            ev2tok[of % 2] = (sV, sV.n)

    z.tok["ffn_done" + tag] = (sV, z.sV.n)
    return pes


def _ln_stats_rep(z, srcfn, sqb, dstmu, dstms, nfeat_chunks, width, inv_n):
    nc, blk = z.nc, z.blk
    sT, sV, sA = z.sT, z.sV, z.sA
    ps = z.ps
    nq = width // 512
    for pss in range(2):
        for fc in range(nfeat_chunks):
            @blk.vector
            def _(v):
                wt(v, sT)
                v.tensor_copy(sqb[:, 0:width], srcfn(fc))
                v.drain().then_inc(sV.h, 1); sV.n += 1
            tok_cast = sV.n
            if pss == 1:
                @blk.scalar
                def _(s):
                    wt(s, sV, tok_cast)
                    s.activation(sqb[:, 0:width], sqb[:, 0:width], AF.Square).then_inc(sA.h, 1); sA.n += 1
            @blk.tensor
            def _(t):
                wt(t, sV, tok_cast)
                if pss == 1:
                    wt(t, sA)
                if fc == 0:
                    wt(t, sV)
                for q4 in range(nq):
                    t.matmul(ps[:, q4 * 512:(q4 + 1) * 512], z.ones128[:],
                             sqb[:, q4 * 512:(q4 + 1) * 512], start=(fc == 0), stop=(fc == nfeat_chunks - 1))
                t.drain().then_inc(sT.h, 1); sT.n += 1
        tok_mm = z.sT.n
        @blk.vector
        def _(v):
            wt(v, sT, tok_mm)
            dst = dstmu if pss == 0 else dstms
            v.tensor_scalar(dst, ps[:, 0:width], inv_n, None, ALU.mult)
            v.drain().then_inc(sV.h, 1); sV.n += 1


def _rstd_from_stats(z, mu, ms, var_scr, out_rstd):
    nc, blk = z.nc, z.blk
    sV, sA = z.sV, z.sA
    @blk.vector
    def _(v):
        v.tensor_tensor(var_scr, mu, mu, ALU.mult)
        v.drain()
        v.tensor_tensor(var_scr, ms, var_scr, ALU.subtract)
        v.drain().then_inc(sV.h, 1); sV.n += 1
    tok_v = sV.n
    @blk.scalar
    def _(s):
        wt(s, sV, tok_v)
        s.activation(out_rstd, var_scr, AF.Sqrt, bias=z.consts[:, 2:3]).then_inc(sA.h, 1); sA.n += 1
    tok_s = sA.n
    @blk.vector
    def _(v):
        wt(v, sA, tok_s)
        v.reciprocal(out_rstd, out_rstd)
        v.drain().then_inc(sV.h, 1); sV.n += 1


def _phase_e(z, seq):
    nc, blk = z.nc, z.blk
    dL, dS, sT, sV, sA, sG = z.dL, z.dS, z.sT, z.sV, z.sA, z.sG
    W = z.weights
    ps, psbf = z.ps, z.psbf

    pesE = contextlib.ExitStack()
    sb = lambda name, shape, dt=F32: pesE.enter_context(nc.sbuf_tensor(name + z.sfx, shape, dt))
    oT = pesE.enter_context(nc.sbuf_tensor("oT" + z.sfx, [128, 8 * 2048], BF16, side="right"))
    repsE = sb("repsE", [128, 4096], F32)
    scrE = sb("scrE", [128, 2048], F32)
    scrE2 = sb("scrE2", [128, 2048], F32)
    sqbE = sb("sqbE", [128, 2048], BF16)

    tok = z.tok

    _ln_stats_rep(z, lambda fc: z.xs[:, fc * 2048:(fc + 1) * 2048], sqbE,
                  repsE[:, 0:2048], scrE2[:], 8, 2048, 1.0 / 1024.0)
    _rstd_from_stats(z, repsE[:, 0:2048], scrE2[:], scrE[:], repsE[:, 2048:4096])

    pes1 = contextlib.ExitStack()
    sb1 = lambda name, shape, dt=F32: pes1.enter_context(nc.sbuf_tensor(name + z.sfx, shape, dt))
    u3 = pes1.enter_context(nc.sbuf_tensor("u3" + z.sfx, [128, 8 * 2048], BF16, side="right"))
    wbv_b = sb1("wbv_b", [128, 1024], F32)

    @blk.sync
    def _(q):
        wt(q, dL)
        q.dma_start(out=wbv_b[:], in_=bass.AP(z.wbv_row.tensor, 0, [[0, 128], [1, E]])).then_inc(dL.h, 16); dL.n += 16
    tok_wbv = dL.n
    qkT = sb1("qkT", [128, 4096], BF16)
    vp = sb1("vp", [128, 16 * 128], BF16)
    expAB = sb1("expAB", [128, 2 * 2560], BF16)
    EbufE = sb1("EbufE", [128, 2048], BF16)
    wwch = sb1("wwch", [128, 1024], BF16)
    wwf = sb1("wwf", [128, 1024], F32)

    @blk.vector
    def _(v):
        for fc in range(8):
            v.tensor_tensor(scrE[:], z.xs[:, fc * 2048:(fc + 1) * 2048], repsE[:, 0:2048], ALU.subtract)
            v.drain()
            v.tensor_tensor(u3[:, fc * 2048:(fc + 1) * 2048], scrE[:], repsE[:, 2048:4096], ALU.mult)
            v.drain().then_inc(sV.h, 1); sV.n += 1
    tok_u3 = sV.n

    SA, SB = 0, 512
    OO = 1024
    ZA, ZB_ = 1536, 2048
    VB = 2560
    tok["expdoneE"] = (sA, sA.n)
    tok["ozfreeE"] = (sV, sV.n)
    tok["EbufE_free"] = (sT, sT.n)

    for pr in range(8):
        par = pr % 2
        dX = z.dX[par]

        @blk.sync
        def _(q):
            s_, v_ = tok.get("wwch_free", (sV, tok_u3))
            wt(q, s_, v_)
            q.dma_start(out=wwf[:],
                        in_=bass.AP(z.wdirect["wwqT"].tensor, pr * 128, [[1024, 128], [131072, 8], [1, 128]])
                        ).then_inc(dX.h, 16); dX.n += 16
            wt(q, dL)
            q.dma_start(out=expAB[:, 0:2560], in_=W["expb_c"][2 * pr]).then_inc(dL.h, 16); dL.n += 16
            q.dma_start(out=expAB[:, 2560:5120], in_=W["expb_c"][2 * pr + 1]).then_inc(dL.h, 16); dL.n += 16
        tok_wwf = dX.n
        tok_expab = dL.n
        @blk.vector
        def _(v):
            wt(v, dX, tok_wwf)
            wt(v, sT)
            v.tensor_copy(wwch[:], wwf[:]).then_inc(sV.h, 1); sV.n += 1
        tok_ww = sV.n

        for which in range(2):
            if which == 1:
                @blk.sync
                def _(q):
                    wt(q, sV)
                    wt(q, dX)
                    q.dma_start(out=wwf[:],
                                in_=bass.AP(z.wdirect["wwkT"].tensor, pr * 128, [[1024, 128], [131072, 8], [1, 128]])
                                ).then_inc(dX.h, 16); dX.n += 16
                tok_wwkf = dX.n
                @blk.vector
                def _(v):
                    wt(v, dX, tok_wwkf)
                    wt(v, sT)
                    v.tensor_copy(wwch[:], wwf[:]).then_inc(sV.h, 1); sV.n += 1
                tok_ww = sV.n
            @blk.tensor
            def _(t):
                wt(t, sV, tok_ww)
                wt(t, sV)
                for q4 in range(4):
                    for c in range(8):
                        t.matmul(ps[:, q4 * 512:(q4 + 1) * 512],
                                 wwch[:, c * 128:(c + 1) * 128],
                                 u3[:, c * 2048 + q4 * 512:c * 2048 + (q4 + 1) * 512],
                                 start=(c == 0), stop=(c == 7))
                t.drain().then_inc(sT.h, 1); sT.n += 1
            tok_qk = sT.n

            @blk.vector
            def _(v):
                wt(v, sT, tok_qk)
                bias = z.wbq if which == 0 else z.wbk
                if which == 0:
                    v.tensor_scalar(qkT[:, 0:2048], ps[:, 0:2048], bias[:, pr:pr + 1], 1.0 / 32.0, ALU.add, ALU.mult)
                else:
                    v.tensor_scalar(qkT[:, 2048:4096], ps[:, 0:2048], bias[:, pr:pr + 1], None, ALU.add)
                v.drain().then_inc(sV.h, 1); sV.n += 1
        tok_qkev = sV.n

        @blk.sync
        def _(q):
            wt(q, sV)
            wt(q, dX)
            q.dma_start(out=wwf[:],
                        in_=bass.AP(z.wdirect["wwvT"].tensor, pr * 128, [[1024, 128], [131072, 8], [1, 128]])
                        ).then_inc(dX.h, 16); dX.n += 16
        tok_wwvf = dX.n
        @blk.vector
        def _(v):
            wt(v, dX, tok_wwvf)
            wt(v, sT)
            v.tensor_copy(wwch[:], wwf[:]).then_inc(sV.h, 1); sV.n += 1
        tok_wwv = sV.n
        for tt in range(NT):
            vb = VB + (tt % 2) * 512
            @blk.tensor
            def _(t):
                if tt == 0:
                    wt(t, sV, tok_wwv)
                if tt >= 2:
                    s_, v_ = tok["vevE" + str(tt % 2)]
                    wt(t, s_, v_)
                for c in range(8):
                    t.matmul(ps[:, vb:vb + 128], u3[:, c * 2048 + tt * 128:c * 2048 + (tt + 1) * 128],
                             wwch[:, c * 128:(c + 1) * 128], start=(c == 0), stop=(c == 7))
                t.drain().then_inc(sT.h, 1); sT.n += 1
            tok_vm = sT.n
            @blk.vector
            def _(v):
                wt(v, sT, tok_vm)
                if tt == 0 and pr == 0:
                    wt(v, dL, tok_wbv)
                v.tensor_tensor(vp[:, tt * 128:(tt + 1) * 128], ps[:, vb:vb + 128],
                                wbv_b[:, pr * 128:(pr + 1) * 128], ALU.add)
                v.drain().then_inc(sV.h, 1); sV.n += 1
            tok["vevE" + str(tt % 2)] = (sV, sV.n)
        tok_vdone = sV.n

        for which in range(2):
            @blk.vector
            def _(v):
                if which == 0:
                    wt(v, sV, tok_vdone)
                v.tensor_tensor(sqbE[:], qkT[:, which * 2048:(which + 1) * 2048],
                                qkT[:, which * 2048:(which + 1) * 2048], ALU.mult)
                v.drain().then_inc(sV.h, 1); sV.n += 1
            tok_sqq = sV.n
            @blk.tensor
            def _(t):
                wt(t, sV, tok_sqq)
                if which == 0:
                    wt(t, sV)
                for q4 in range(4):
                    t.matmul(ps[:, q4 * 512:(q4 + 1) * 512], z.ones128[:],
                             sqbE[:, q4 * 512:(q4 + 1) * 512], start=True, stop=True)
                t.drain().then_inc(sT.h, 1); sT.n += 1
            tok_nmm = sT.n
            @blk.vector
            def _(v):
                wt(v, sT, tok_nmm)
                v.tensor_reduce(z.consts2[:, 4 + which:5 + which], ps[:, 0:2048],
                                mybir.AxisListType.X, ALU.max)
                v.drain().then_inc(sV.h, 1); sV.n += 1
        tok_bam = sV.n

        @blk.vector
        def _(v):
            v.tensor_tensor(z.consts2[:, 6:7], z.consts2[:, 4:5], z.consts2[:, 5:6], ALU.mult)
            v.drain().then_inc(sV.h, 1); sV.n += 1
        tok_bm2 = sV.n

        @blk.scalar
        def _(s):
            wt(s, sV, tok_bm2)
            s.activation(z.consts2[:, 6:7], z.consts2[:, 6:7], AF.Sqrt).then_inc(sA.h, 1); sA.n += 1
        tok_bsq = sA.n

        @blk.vector
        def _(v):
            wt(v, sA, tok_bsq)
            v.tensor_scalar(z.consts2[:, 6:7], z.consts2[:, 6:7], -1.0, None, ALU.mult)
            v.drain().then_inc(sV.h, 1); sV.n += 1
        tok_bb = sV.n

        for qb in range(4):
            q0 = qb * 512
            nkt = 4 * (qb + 1)
            sT0, sA0, sV0 = sT.n, sA.n, sV.n
            oz_s, oz_v = tok["ozfreeE"]

            def zov(t, kt, nkt):
                eb = (kt % 2) * 1024
                fl = dict(start=(kt == 0), stop=(kt == nkt - 1))
                t.matmul(ps[:, ZA:ZA + 512], z.ones128[:], EbufE[:, eb:eb + 512], **fl)
                t.matmul(ps[:, ZB_:ZB_ + 512], z.ones128[:], EbufE[:, eb + 512:eb + 1024], **fl)
                t.matmul(ps[0:64, OO:OO + 512], vp[:, kt * 128:kt * 128 + 64], EbufE[:, eb:eb + 512],
                         tile_position=(0, 0), **fl)
                return t.matmul(ps[64:128, OO:OO + 512], vp[:, kt * 128 + 64:kt * 128 + 128],
                                EbufE[:, eb + 512:eb + 1024],
                                tile_position=(0, 64), skip_group_check=True, **fl)

            @blk.tensor
            def _(t):
                for kt in range(nkt):
                    wt(t, sA, sA0 + kt)
                    if kt == 0:
                        wt(t, oz_s, oz_v)
                        wt(t, sV, tok_bb)
                    t.matmul(ps[:, SA:SA + 512], qkT[0:64, 2048 + kt * 128:2048 + (kt + 1) * 128],
                             qkT[0:64, q0:q0 + 512], start=True, stop=True, tile_position=(0, 0))
                    t.matmul(ps[:, SB:SB + 512], qkT[64:128, 2048 + kt * 128:2048 + (kt + 1) * 128],
                             qkT[64:128, q0:q0 + 512], start=True, stop=True,
                             tile_position=(64, 0)).then_inc(sT.h, 1); sT.n += 1
                    if kt > 0:
                        wt(t, sV, sV0 + kt)
                        zov(t, kt - 1, nkt)
                wt(t, sV, sV0 + nkt)
                zov(t, nkt - 1, nkt).then_inc(sT.h, 1); sT.n += 1

            @blk.scalar
            def _(s):
                for kt in range(nkt):
                    eb = (kt % 2) * 1024
                    wt(s, sT, sT0 + kt + 1)
                    s.activation(EbufE[:, eb:eb + 1024], ps[:, 0:1024], AF.Exp,
                                 bias=z.consts2[:, 6:7]).then_inc(sA.h, 1); sA.n += 1

            @blk.vector
            def _(v):
                wt(v, dL, tok_expab)
                for kt in range(nkt):
                    eb = (kt % 2) * 1024
                    co = q0 - kt * 128 + 512
                    wt(v, sA, sA0 + kt + 1)
                    v.tensor_tensor(EbufE[:, eb:eb + 512], EbufE[:, eb:eb + 512],
                                    expAB[:, co:co + 512], ALU.mult)
                    v.tensor_tensor(EbufE[:, eb + 512:eb + 1024], EbufE[:, eb + 512:eb + 1024],
                                    expAB[:, 2560 + co:2560 + co + 512], ALU.mult)
                    v.drain().then_inc(sV.h, 1); sV.n += 1
            tok_oz = sT.n

            @blk.vector
            def _(v):
                wt(v, sT, tok_oz)
                v.reciprocal(scrE[:, 0:512], ps[:, ZA:ZA + 512])
                v.reciprocal(scrE[:, 512:1024], ps[:, ZB_:ZB_ + 512])
                v.drain()
                v.tensor_tensor(oT[0:64, pr * 2048 + q0:pr * 2048 + q0 + 512],
                                ps[0:64, OO:OO + 512], scrE[0:64, 0:512], ALU.mult)
                v.tensor_tensor(oT[64:128, pr * 2048 + q0:pr * 2048 + q0 + 512],
                                ps[64:128, OO:OO + 512], scrE[64:128, 512:1024], ALU.mult)
                v.drain().then_inc(sV.h, 1); sV.n += 1
            tok["ozfreeE"] = (sV, sV.n)
            tok["expdoneE"] = (sA, sA.n)
        tok["wwch_free"] = (sT, sT.n)

    pes1.close()
    _quiesce(z)

    pes2 = contextlib.ExitStack()
    sb2 = lambda name, shape, dt=F32: pes2.enter_context(nc.sbuf_tensor(name + z.sfx, shape, dt))
    ybuf = sb2("ybuf", [128, 8 * 1024], F32)
    pwch = sb2("pwch", [128, 1024], BF16)
    pwf = sb2("pwf", [128, 1024], F32)

    for th in range(2):
        t0c = th * 1024

        @blk.vector
        def _(v):
            wt(v, sT)
            xs_h = bass.AP(z.xs, t0c, [[16384, 128], [2048, 8], [1, 1024]])
            mu_h = bass.AP(repsE, t0c, [[4096, 128], [0, 8], [1, 1024]])
            rs_h = bass.AP(repsE, 2048 + t0c, [[4096, 128], [0, 8], [1, 1024]])
            v.tensor_tensor(ybuf[:], xs_h, mu_h, ALU.subtract)
            v.drain()
            v.tensor_tensor(ybuf[:], ybuf[:], rs_h, ALU.mult)
            v.drain().then_inc(sV.h, 1); sV.n += 1
        tok_uh = sV.n

        for of in range(8):
            dX = z.dX[of % 2]
            @blk.sync
            def _(q):
                s_, v_ = tok.get("pwf_free", (sV, sV.n))
                wt(q, s_, v_)
                q.dma_start(out=pwf[:],
                            in_=bass.AP(z.wdirect["projwT"].tensor, of * 128, [[1024, 128], [131072, 8], [1, 128]])
                            ).then_inc(dX.h, 16); dX.n += 16
            tok_pwf = dX.n
            @blk.vector
            def _(v):
                wt(v, dX, tok_pwf)
                s_, v_ = tok.get("pwch_free", (sV, 0))
                wt(v, s_, v_)
                v.tensor_copy(pwch[:], pwf[:]).then_inc(sV.h, 1); sV.n += 1
            tok_pw = sV.n
            tok["pwf_free"] = (sV, tok_pw)

            for q4 in range(2):
                qq0 = t0c + q4 * 512
                @blk.tensor
                def _(t):
                    if q4 == 0:
                        wt(t, sV, tok_pw)
                    wt(t, sV)
                    for oc in range(8):
                        t.matmul(ps[:, (q4 % 2) * 512:(q4 % 2) * 512 + 512],
                                 pwch[:, oc * 128:(oc + 1) * 128],
                                 oT[:, oc * 2048 + qq0:oc * 2048 + qq0 + 512],
                                 start=(oc == 0), stop=(oc == 7))
                    t.drain().then_inc(sT.h, 1); sT.n += 1
                tok_pm = sT.n
                tok["pwch_free"] = (sT, sT.n)

                @blk.vector
                def _(v):
                    wt(v, sT, tok_pm)
                    if of == 0 and q4 == 0:
                        wt(v, sV, tok_uh)
                    yslc = ybuf[:, of * 1024 + q4 * 512:of * 1024 + (q4 + 1) * 512]
                    v.scalar_tensor_tensor(yslc, ps[:, (q4 % 2) * 512:(q4 % 2) * 512 + 512],
                                           z.pbias[:, of:of + 1], yslc, ALU.add, ALU.add)
                    v.drain().then_inc(sV.h, 1); sV.n += 1

        _ln_stats_rep(z, lambda fc: ybuf[:, fc * 1024:(fc + 1) * 1024], sqbE,
                      scrE[:, 0:1024], scrE2[:, 0:1024], 8, 1024, 1.0 / 1024.0)
        _rstd_from_stats(z, scrE[:, 0:1024], scrE2[:, 0:1024], scrE2[:, 1024:2048], scrE[:, 1024:2048])

        @blk.vector
        def _(v):
            mu2_h = bass.AP(scrE, 0, [[2048, 128], [0, 8], [1, 1024]])
            rs2_h = bass.AP(scrE, 1024, [[2048, 128], [0, 8], [1, 1024]])
            lng_h = bass.AP(z.lng, 0, [[8, 128], [1, 8], [0, 1024]])
            lnb_h = bass.AP(z.lnb, 0, [[8, 128], [1, 8], [0, 1024]])
            xs_h = bass.AP(z.xs, t0c, [[16384, 128], [2048, 8], [1, 1024]])
            v.tensor_tensor(ybuf[:], ybuf[:], mu2_h, ALU.subtract)
            v.drain()
            v.tensor_tensor(ybuf[:], ybuf[:], rs2_h, ALU.mult)
            v.drain()
            v.tensor_tensor(ybuf[:], ybuf[:], lng_h, ALU.mult)
            v.drain()
            v.tensor_tensor(ybuf[:], ybuf[:], lnb_h, ALU.add)
            v.drain()
            v.tensor_tensor(xs_h, xs_h, ybuf[:], ALU.add)
            v.drain().then_inc(sV.h, 1); sV.n += 1

    pes2.close()
    z.tok["phaseE_done"] = (sV, z.sV.n)
    return pesE


# version 09
# speedup vs baseline: 1.1858x; 1.0024x over previous
import math
import contextlib
import numpy as np

import concourse.bass as bass
import concourse.mybir as mybir
from concourse.bass_utils import run_bass_kernel_spmd

F32 = mybir.dt.float32
BF16 = mybir.dt.bfloat16
AF = mybir.ActivationFunctionType
ALU = mybir.AluOpType

MAGIC = 12582912.0
S = 2048
E = 1024
NT = 16
NC_ = 8
NH = 16

_DEBUG_TAPS = []
_BUILD_OPTS = {"n_seq": 2, "stop_after": None}


def _host_prepare(inp):
    x = np.ascontiguousarray(inp["L0_attn_a_x"])
    perm = np.empty(E, dtype=np.int64)
    idx = 0
    for g in range(8):
        half = 0 if g < 4 else 1
        hh = (g % 4) * 4
        for h2 in range(4):
            h = hh + h2
            for d in range(32):
                perm[idx] = half * 512 + h * 32 + d
                idx += 1
    wq = np.ascontiguousarray(inp["L0_attn_a_Wq"])
    wk = np.ascontiguousarray(inp["L0_attn_a_Wk"])
    wqT = np.ascontiguousarray(wq[perm].T)
    wkT = np.ascontiguousarray(wk[perm].T)
    wvT = np.ascontiguousarray(inp["L0_attn_a_Wv"].T)
    wprojT = np.ascontiguousarray(inp["L0_attn_a_Wproj"].T)
    ww = inp["L0_attn_b_W_w"]
    wwqT = np.ascontiguousarray(ww[0:1024].T)
    wwkT = np.ascontiguousarray(ww[1024:2048].T)
    wwvT = np.ascontiguousarray(ww[2048:3072].T)
    projwT = np.ascontiguousarray(inp["L0_attn_b_proj_w"].T)
    w1T = np.ascontiguousarray(inp["L0_ffn_w1"].T)
    w2T = np.ascontiguousarray(inp["L0_ffn_w2"].T)

    wb = inp["L0_attn_b_W_b"]
    def pcol(v):
        v = np.asarray(v).reshape(-1)
        return np.ascontiguousarray(v.reshape(-1, 128).T)
    wbq = pcol(wb[0:1024] * 1.0) if False else pcol(wb[0:1024])
    wbk = pcol(wb[1024:2048])
    wbv_row = np.ascontiguousarray(wb[2048:3072])
    b1 = pcol(inp["L0_ffn_b1"])
    b2 = pcol(inp["L0_ffn_b2"])
    pb = pcol(inp["L0_attn_b_proj_b"])
    lng = pcol(inp["L0_attn_b_ln_g"])
    lnb = pcol(inp["L0_attn_b_ln_b"])

    rpb = np.asarray(inp["L0_attn_b_rpb"])
    pi = np.arange(128)[:, None] - (np.arange(2560)[None, :] - 512) + 2047
    pi = np.clip(pi, 0, 4094)
    ZB = np.ascontiguousarray(rpb[pi, :].transpose(2, 0, 1))

    maskf = (np.arange(128)[:, None] <= (np.arange(2560)[None, :] - 512)).astype(np.float32)
    ident = np.eye(128, dtype=np.float32)
    import ml_dtypes
    i16 = np.arange(16, dtype=np.float64)
    inv_freq = 1.0 / (10000.0 ** (i16 / 256.0))
    t = np.arange(S, dtype=np.float64)
    fr = np.outer(t, inv_freq)
    cosb = np.cos(fr).astype(np.float32).astype(ml_dtypes.bfloat16).astype(np.float32)
    sinb = np.sin(fr).astype(np.float32).astype(ml_dtypes.bfloat16).astype(np.float32)
    def tmtab(tab):
        return np.ascontiguousarray(tab.reshape(NT, 128, 16).transpose(1, 0, 2).reshape(128, NT * 16))
    cosq = tmtab(cosb * 0.125)
    sinq = tmtab(sinb * 0.125)
    cosk = tmtab(cosb)
    sink = tmtab(sinb)

    xv = x.reshape(16, NT, 128, NC_, 128).transpose(0, 1, 4, 3, 2)
    xt_all = np.ascontiguousarray(xv.reshape(16, NT, 128, E))

    per_core = []
    for core in range(8):
        d = {
            "xt": np.ascontiguousarray(xt_all[2 * core:2 * core + 2]),
            "wqT": wqT, "wkT": wkT, "wvT": wvT, "wprojT": wprojT,
            "wwqT": wwqT, "wwkT": wwkT, "wwvT": wwvT, "projwT": projwT,
            "w1T": w1T, "w2T": w2T,
            "wbq": wbq, "wbk": wbk, "wbv_row": wbv_row,
            "b1": b1, "b2": b2, "pbias": pb, "lng": lng, "lnb": lnb,
            "ZB": ZB, "maskf": maskf, "ident": ident,
            "cosq": cosq, "sinq": sinq, "cosk": cosk, "sink": sink,
            "lq1": np.asarray(inp["L0_attn_a_lambda_q1"]).reshape(1, 64),
            "lq2": np.asarray(inp["L0_attn_a_lambda_q2"]).reshape(1, 64),
            "lk1": np.asarray(inp["L0_attn_a_lambda_k1"]).reshape(1, 64),
            "lk2": np.asarray(inp["L0_attn_a_lambda_k2"]).reshape(1, 64),
            "alpha": np.asarray(inp["L0_attn_a_alpha"]).reshape(1, 1),
        }
        per_core.append({k: np.ascontiguousarray(v.astype(np.float32, copy=False)) for k, v in d.items()})
    return per_core


def _host_gather(results):
    out = np.empty((16, S, E), dtype=np.float32)
    for core, r in enumerate(results):
        y = r["y"]
        for s in range(2):
            out[2 * core + s] = y[s].reshape(E, S).T
    return out


class Sem:
    __slots__ = ("h", "n", "name")


class Ctx:
    def __init__(self, nc, es):
        self.nc = nc
        self.es = es
        self._sems = []

    def sem(self, name):
        s = Sem()
        s.h = self.es.enter_context(self.nc.semaphore(name))
        s.n = 0
        s.name = name
        self._sems.append(s)
        return s


def inc(instr, sem, k=1):
    instr.then_inc(sem.h, k)
    sem.n += k
    return sem.n


def wt(eng, sem, v=None):
    eng.wait_ge(sem.h, sem.n if v is None else v)


def build_nc():
    nc = bass.Bass("TRN2", target_bir_lowering=False)
    es = contextlib.ExitStack()
    with es:
        _build_body(nc, es)
    return nc


class NS:
    pass


def _build_body(nc, es):
    dt_in = {}
    def din(name, shape):
        dt_in[name] = nc.dram_tensor(name, shape, F32, kind="ExternalInput").ap()
        return dt_in[name]

    xt = din("xt", [2, NT, 128, E])
    wqT = din("wqT", [E, E]); wkT = din("wkT", [E, E]); wvT = din("wvT", [E, 2 * E])
    wprojT = din("wprojT", [2 * E, 2 * E])
    wwqT = din("wwqT", [E, E]); wwkT = din("wwkT", [E, E]); wwvT = din("wwvT", [E, E])
    projwT = din("projwT", [E, E])
    w1T = din("w1T", [E, 4 * E]); w2T = din("w2T", [4 * E, E])
    wbq = din("wbq", [128, 8]); wbk = din("wbk", [128, 8]); wbv_row = din("wbv_row", [E])
    b1 = din("b1", [128, 32]); b2 = din("b2", [128, 8])
    pbias = din("pbias", [128, 8]); lng = din("lng", [128, 8]); lnb = din("lnb", [128, 8])
    ZBin = din("ZB", [16, 128, 2560])
    maskf = din("maskf", [128, 2560]); identin = din("ident", [128, 128])
    cosq_in = din("cosq", [128, 256]); sinq_in = din("sinq", [128, 256])
    cosk_in = din("cosk", [128, 256]); sink_in = din("sink", [128, 256])
    lq1 = din("lq1", [1, 64]); lq2 = din("lq2", [1, 64])
    lk1 = din("lk1", [1, 64]); lk2 = din("lk2", [1, 64])
    alpha_in = din("alpha", [1, 1])

    yout = nc.dram_tensor("y", [2, NC_, 128, S], F32, kind="ExternalOutput").ap()
    taps = {name: nc.dram_tensor("tap_" + name, shape, dt, kind="ExternalOutput").ap()
            for (name, shape, dt) in _DEBUG_TAPS}

    wq_c = nc.dram_tensor("wq_c", [E, E], BF16).ap()
    wk_c = nc.dram_tensor("wk_c", [E, E], BF16).ap()
    wv_c = nc.dram_tensor("wv_c", [E, 2 * E], BF16).ap()
    wproj_c = nc.dram_tensor("wproj_c", [2 * E, 2 * E], BF16).ap()
    wwq_c = nc.dram_tensor("wwq_c", [E, E], BF16).ap()
    wwk_c = nc.dram_tensor("wwk_c", [E, E], BF16).ap()
    wwv_c = nc.dram_tensor("wwv_c", [E, E], BF16).ap()
    projw_c = nc.dram_tensor("projw_c", [E, E], BF16).ap()
    t1_c = nc.dram_tensor("t1_c", [E, 4 * E], BF16).ap()
    t2_c = nc.dram_tensor("t2_c", [4 * E, E], BF16).ap()
    expb_c = nc.dram_tensor("expb_c", [16, 128, 2560], BF16).ap()
    bounce_c = nc.dram_tensor("bounce_c", [4096], F32).ap()

    ctx = Ctx(nc, es)
    dL = ctx.sem("dL")
    dX0 = ctx.sem("dX0")
    dX1 = ctx.sem("dX1")
    dS = ctx.sem("dS")
    sT = ctx.sem("sT")
    sV = ctx.sem("sV")
    sA = ctx.sem("sA")
    sG = ctx.sem("sG")

    sb = lambda name, shape, dt=F32: es.enter_context(nc.sbuf_tensor(name, shape, dt))

    z = NS()
    z.ps = es.enter_context(nc.psum_tensor("ps", [128, 3584], F32))
    z.psbf = es.enter_context(nc.psum_tensor("psbf", [128, 1024], BF16))
    z.ident = sb("identb", [128, 128], BF16)
    z.ones_f = sb("ones_f", [128, 1], F32)
    z.ones_b = sb("ones_b", [128, 1], BF16)
    z.ones128 = sb("ones128", [128, 128], BF16)
    z.ones128f = sb("ones128f", [128, 128], F32)
    z.identf = sb("identf", [128, 128], F32)
    z.consts = sb("consts", [128, 16], F32)
    z.rows = sb("rows", [1, 1024], F32)
    z.b1 = sb("b1_sb", [128, 32], F32); z.b2 = sb("b2_sb", [128, 8], F32)
    z.wbq = sb("wbq_sb", [128, 8], F32); z.wbk = sb("wbk_sb", [128, 8], F32)
    z.pbias = sb("pbias_sb", [128, 8], F32)
    z.lng = sb("lng_sb", [128, 8], F32); z.lnb = sb("lnb_sb", [128, 8], F32)
    z.consts2 = sb("consts2", [128, 8], F32)
    prep_es = contextlib.ExitStack()
    z.lrows = prep_es.enter_context(nc.sbuf_tensor("lrows", [1, 512], F32))
    z.maskf = prep_es.enter_context(nc.sbuf_tensor("maskf_sb", [128, 2560], F32))
    z.wchunk = prep_es.enter_context(nc.sbuf_tensor("wchunk", [128, 4096], F32))
    z.wchunk2 = prep_es.enter_context(nc.sbuf_tensor("wchunk2", [128, 4096], F32))
    z.bchunk = prep_es.enter_context(nc.sbuf_tensor("bchunk", [128, 4096], BF16))
    z.tpart = prep_es.enter_context(nc.sbuf_tensor("tpart", [1, 4096], F32))

    blk = es.enter_context(nc.Block())
    z.nc = nc; z.blk = blk; z.ctx = ctx
    z.dL, z.dS, z.sT, z.sV, z.sA, z.sG = dL, dS, sT, sV, sA, sG
    z.dX = [dX0, dX1]
    z.taps = taps
    z.yout = yout
    z.xt = xt
    z.dram_tabs = (cosq_in, sinq_in, cosk_in, sink_in)
    z.wbv_row = wbv_row
    z.bounce = bounce_c

    @blk.sync
    def _(q):
        q.dma_start(out=z.maskf[:], in_=maskf).then_inc(dL.h, 16); dL.n += 16
        q.dma_start(out=z.b1[:], in_=b1).then_inc(dL.h, 16); dL.n += 16
        q.dma_start(out=z.b2[:], in_=b2).then_inc(dL.h, 16); dL.n += 16
        q.dma_start(out=z.wbq[:], in_=wbq).then_inc(dL.h, 16); dL.n += 16
        q.dma_start(out=z.wbk[:], in_=wbk).then_inc(dL.h, 16); dL.n += 16
        q.dma_start(out=z.pbias[:], in_=pbias).then_inc(dL.h, 16); dL.n += 16
        q.dma_start(out=z.lng[:], in_=lng).then_inc(dL.h, 16); dL.n += 16
        q.dma_start(out=z.lnb[:], in_=lnb).then_inc(dL.h, 16); dL.n += 16
        q.dma_start(out=z.lrows[0:1, 0:64], in_=lq1).then_inc(dL.h, 16); dL.n += 16
        q.dma_start(out=z.lrows[0:1, 64:128], in_=lk1).then_inc(dL.h, 16); dL.n += 16
        q.dma_start(out=z.lrows[0:1, 128:192], in_=lq2).then_inc(dL.h, 16); dL.n += 16
        q.dma_start(out=z.lrows[0:1, 192:256], in_=lk2).then_inc(dL.h, 16); dL.n += 16
        q.dma_start(out=z.lrows[0:1, 256:257] if False else z.rows[0:1, 0:1], in_=alpha_in).then_inc(dL.h, 16); dL.n += 16
        q.dma_start(out=z.wchunk[:, 0:128], in_=identin).then_inc(dL.h, 16); dL.n += 16
        q.dma_start(out=z.identf[:], in_=identin).then_inc(dL.h, 16); dL.n += 16
    ld_consts = dL.n

    @blk.vector
    def _(v):
        wt(v, dL, ld_consts)
        v.memset(z.consts[:], 0.0)
        v.memset(z.consts2[:], 0.0)
        v.drain()
        v.memset(z.ones_f[:], 1.0)
        v.memset(z.ones_b[:], 1.0)
        v.memset(z.ones128[:], 1.0)
        v.memset(z.ones128f[:], 1.0)
        v.memset(z.consts[:, 0:1], -0.5)
        v.memset(z.consts[:, 1:2], 0.5)
        v.memset(z.consts[:, 2:3], 1e-5)
        v.memset(z.consts[:, 3:4], 1.0 / 1024.0)
        v.memset(z.consts[:, 8:9], 1e-30)
        v.tensor_copy(z.ident[:], z.wchunk[:, 0:128])
        v.drain()
        v.tensor_tensor(z.lrows[0:1, 256:320], z.lrows[0:1, 0:64], z.lrows[0:1, 64:128], ALU.mult)
        v.tensor_tensor(z.lrows[0:1, 320:384], z.lrows[0:1, 128:192], z.lrows[0:1, 192:256], ALU.mult)
        v.drain()
        v.tensor_reduce(z.lrows[0:1, 384:385], z.lrows[0:1, 256:320], mybir.AxisListType.X, ALU.add)
        v.tensor_reduce(z.lrows[0:1, 385:386], z.lrows[0:1, 320:384], mybir.AxisListType.X, ALU.add)
        v.drain().then_inc(sV.h, 1); sV.n += 1
    v_dots = sV.n

    @blk.scalar
    def _(s):
        wt(s, sV, v_dots)
        s.activation(z.lrows[0:1, 386:388], z.lrows[0:1, 384:386], AF.Exp).then_inc(sA.h, 1); sA.n += 1
    a_exps = sA.n

    @blk.vector
    def _(v):
        wt(v, sA, a_exps)
        v.tensor_tensor(z.lrows[0:1, 388:389], z.lrows[0:1, 386:387], z.lrows[0:1, 387:388], ALU.subtract)
        v.drain()
        v.tensor_scalar(z.lrows[0:1, 389:390], z.lrows[0:1, 388:389], 0.2, None, ALU.add)
        v.drain()
        v.tensor_scalar(z.lrows[0:1, 390:391], z.lrows[0:1, 389:390], -1.0, None, ALU.mult)
        v.tensor_scalar(z.rows[0:1, 1:2], z.rows[0:1, 0:1], 2.0, 1.0, ALU.mult, ALU.add)
        v.drain()
        v.tensor_scalar(z.rows[0:1, 2:3], z.rows[0:1, 1:2], 1.0 / math.pi, None, ALU.mult)
        v.scalar_tensor_tensor(z.rows[0:1, 3:4], z.rows[0:1, 1:2], 0.5, z.rows[0:1, 0:1], ALU.mult, ALU.subtract)
        v.drain().then_inc(sV.h, 1); sV.n += 1
    v_consts = sV.n

    @blk.gpsimd
    def _(g):
        wt(g, sV, v_consts)
        g.dma_start(out=z.bounce[0:1], in_=z.lrows[0:1, 389:390]).then_inc(dS.h, 16); dS.n += 16
        g.dma_start(out=z.bounce[1:2], in_=z.lrows[0:1, 390:391]).then_inc(dS.h, 16); dS.n += 16
        g.dma_start(out=z.bounce[2:3], in_=z.rows[0:1, 2:3]).then_inc(dS.h, 16); dS.n += 16
        g.dma_start(out=z.bounce[3:4], in_=z.rows[0:1, 3:4]).then_inc(dS.h, 16); dS.n += 16
    @blk.sync
    def _(q):
        wt(q, dS)
        wt(q, dL)
        q.dma_start(out=z.consts[:, 4:8], in_=bass.AP(z.bounce.tensor, 0, [[0, 128], [1, 4]])).then_inc(dL.h, 16); dL.n += 16
    z.gp_consts = dL.n

    def cache_weight(src, dst, rows, cols):
        for r0 in range(0, rows, 128):
            @blk.sync
            def _(q):
                wt(q, sV)
                wt(q, dL)
                q.dma_start(out=z.wchunk[:, 0:cols], in_=src[r0:r0 + 128, :]).then_inc(dL.h, 16); dL.n += 16
            @blk.vector
            def _(v):
                wt(v, dL)
                wt(v, dS)
                v.tensor_copy(z.bchunk[:, 0:cols], z.wchunk[:, 0:cols]).then_inc(sV.h, 1); sV.n += 1
            @blk.gpsimd
            def _(g):
                wt(g, sV)
                g.dma_start(out=dst[r0:r0 + 128, :], in_=z.bchunk[:, 0:cols]).then_inc(dS.h, 16); dS.n += 16

    cache_weight(wprojT, wproj_c, 2 * E, 2 * E)
    z.wdirect = dict(wqT=wqT, wkT=wkT, wvT=wvT, wwqT=wwqT, wwkT=wwkT, wwvT=wwvT,
                     projwT=projwT)

    def tern_weight(srcT, dstT, rows, cols, mean_count, s_col, inv_col):
        nch = rows // 128
        for r0i in range(nch):
            r0 = r0i * 128
            @blk.sync
            def _(q):
                wt(q, sA)
                wt(q, sV)
                wt(q, dL)
                q.dma_start(out=z.wchunk[:, 0:cols], in_=srcT[r0:r0 + 128, :]).then_inc(dL.h, 16); dL.n += 16
            @blk.scalar
            def _(s):
                wt(s, dL)
                wt(s, sT)
                s.activation(z.wchunk2[:, 0:cols], z.wchunk[:, 0:cols], AF.Abs).then_inc(sA.h, 1); sA.n += 1
            @blk.tensor
            def _(t):
                wt(t, sA)
                for ji, j in enumerate(range(0, cols, 512)):
                    if rows == 1024:
                        k = ji % 4
                        fl = dict(start=(r0i % 4 == 0 and ji < 4), stop=(r0i % 4 == 3 and ji >= 4))
                        if r0i % 4 == 0 and ji < 4:
                            wt(t, sV)
                    else:
                        k = ji * 2 + (r0i // 8) % 2
                        fl = dict(start=(r0i % 8 == 0), stop=(r0i % 8 == 7))
                        if r0i % 16 == 0 and ji == 0:
                            wt(t, sV)
                    t.matmul(z.ps[0:1, k * 512:(k + 1) * 512], z.ones_f[:], z.wchunk2[:, j:j + 512], **fl)
                t.drain().then_inc(sT.h, 1); sT.n += 1
            if r0i == nch // 2 - 1 or r0i == nch - 1:
                half = 0 if r0i == nch // 2 - 1 else 1
                @blk.vector
                def _(v):
                    wt(v, sT)
                    v.tensor_copy(z.tpart[0:1, half * 2048:(half + 1) * 2048], z.ps[0:1, 0:2048])
                    v.drain().then_inc(sV.h, 1); sV.n += 1
        @blk.vector
        def _(v):
            wt(v, sV)
            v.tensor_scalar(z.wchunk[0:1, 0:4096], z.tpart[0:1, :], 128.0, MAGIC, ALU.mult, ALU.add)
            v.drain()
            v.tensor_scalar(z.wchunk[0:1, 0:4096], z.wchunk[0:1, 0:4096], MAGIC, 1.0 / 128.0,
                            ALU.subtract, ALU.mult)
            v.drain()
            v.tensor_tensor(z.wchunk2[0:1, 0:4096], z.tpart[0:1, :], z.wchunk[0:1, 0:4096], ALU.subtract)
            v.drain()
            v.tensor_reduce(z.rows[0:1, 8:9], z.wchunk[0:1, 0:4096], mybir.AxisListType.X, ALU.add)
            v.tensor_reduce(z.rows[0:1, 10:11], z.wchunk2[0:1, 0:4096], mybir.AxisListType.X, ALU.add)
            v.drain()
            v.tensor_tensor(z.rows[0:1, 8:9], z.rows[0:1, 8:9], z.rows[0:1, 10:11], ALU.add)
            v.drain()
            v.tensor_scalar(z.rows[0:1, 8:9], z.rows[0:1, 8:9], 1.0 / mean_count, None, ALU.mult)
            v.drain()
            v.tensor_scalar(z.rows[0:1, 8:9], z.rows[0:1, 8:9], 1e-5, None, ALU.max)
            v.drain()
            v.reciprocal(z.rows[0:1, 9:10], z.rows[0:1, 8:9])
            v.drain().then_inc(sV.h, 1); sV.n += 1
        @blk.gpsimd
        def _(g):
            wt(g, sV)
            g.dma_start(out=z.bounce[0:2], in_=z.rows[0:1, 8:10]).then_inc(dS.h, 16); dS.n += 16
        @blk.sync
        def _(q):
            wt(q, dS)
            wt(q, dL)
            q.dma_start(out=z.consts[:, s_col:s_col + 1], in_=bass.AP(z.bounce.tensor, 0, [[0, 128], [1, 1]])).then_inc(dL.h, 16); dL.n += 16
            q.dma_start(out=z.consts[:, inv_col:inv_col + 1], in_=bass.AP(z.bounce.tensor, 1, [[0, 128], [1, 1]])).then_inc(dL.h, 16); dL.n += 16
        @blk.gpsimd
        def _(g):
            wt(g, dL)
            g.drain().then_inc(sG.h, 1); sG.n += 1
        for r0 in range(0, rows, 128):
            @blk.sync
            def _(q):
                wt(q, sA)
                wt(q, sV)
                wt(q, dL)
                q.dma_start(out=z.wchunk[:, 0:cols], in_=srcT[r0:r0 + 128, :]).then_inc(dL.h, 16); dL.n += 16
            @blk.scalar
            def _(s):
                wt(s, dL)
                wt(s, sG)
                wt(s, sV)
                s.activation(z.wchunk2[:, 0:cols], z.wchunk[:, 0:cols], AF.Sign,
                             bias=z.consts[:, 0:1], scale=z.consts[:, inv_col:inv_col + 1])
                s.drain()
                s.activation(z.wchunk[:, 0:cols], z.wchunk[:, 0:cols], AF.Sign,
                             bias=z.consts[:, 1:2], scale=z.consts[:, inv_col:inv_col + 1])
                s.drain().then_inc(sA.h, 1); sA.n += 1
            @blk.vector
            def _(v):
                wt(v, sA)
                wt(v, dS)
                v.tensor_tensor(z.wchunk2[:, 0:cols], z.wchunk2[:, 0:cols], z.wchunk[:, 0:cols], ALU.add)
                v.drain()
                v.tensor_scalar(z.bchunk[:, 0:cols], z.wchunk2[:, 0:cols], 0.5, None, ALU.mult).then_inc(sV.h, 1); sV.n += 1
            @blk.gpsimd
            def _(g):
                wt(g, sV)
                g.dma_start(out=dstT[r0:r0 + 128, :], in_=z.bchunk[:, 0:cols]).then_inc(dS.h, 16); dS.n += 16

    tern_weight(w1T, t1_c, E, 4 * E, E * 4 * E, 9, 11)
    tern_weight(w2T, t2_c, 4 * E, E, E * 4 * E, 10, 12)

    for h in range(16):
        @blk.sync
        def _(q):
            wt(q, sA)
            wt(q, sV)
            wt(q, dL)
            q.dma_start(out=z.wchunk[:, 0:2560], in_=ZBin[h]).then_inc(dL.h, 16); dL.n += 16
        @blk.scalar
        def _(s):
            wt(s, dL)
            wt(s, sV)
            s.activation(z.wchunk2[:, 0:2560], z.wchunk[:, 0:2560], AF.Exp).then_inc(sA.h, 1); sA.n += 1
        @blk.vector
        def _(v):
            wt(v, sA)
            wt(v, dS)
            v.tensor_tensor(z.bchunk[:, 0:2560], z.wchunk2[:, 0:2560], z.maskf[:], ALU.mult).then_inc(sV.h, 1); sV.n += 1
        @blk.gpsimd
        def _(g):
            wt(g, sV)
            g.dma_start(out=expb_c[h], in_=z.bchunk[:, 0:2560]).then_inc(dS.h, 16); dS.n += 16

    prep_es.close()
    z.weights = dict(wq_c=wq_c, wk_c=wk_c, wv_c=wv_c, wproj_c=wproj_c, wwq_c=wwq_c,
                     wwk_c=wwk_c, wwv_c=wwv_c, projw_c=projw_c, t1_c=t1_c, t2_c=t2_c,
                     expb_c=expb_c)

    for seq in range(_BUILD_OPTS["n_seq"]):
        _build_seq(z, seq)

    @blk.gpsimd
    def _(g):
        wt(g, dS)
        g.nop()

    nc.all_engine_barrier()


def _quiesce(z):
    blk = z.blk
    sems = [z.dL, z.dS, z.sT, z.sV, z.sA, z.sG]
    for opener in (blk.sync, blk.tensor, blk.vector, blk.scalar, blk.gpsimd):
        def body(eng):
            for s in sems:
                eng.wait_ge(s.h, s.n)
        opener(body)


def _phase_a(z, seq):
    nc, blk = z.nc, z.blk
    dL, dS, sT, sV, sA, sG = z.dL, z.dS, z.sT, z.sV, z.sA, z.sG
    W = z.weights
    ps, psbf = z.ps, z.psbf

    pes = contextlib.ExitStack()
    sb = lambda name, shape, dt=F32: pes.enter_context(nc.sbuf_tensor(name + z.sfx, shape, dt))
    xtile = sb("xtile", [128, 2048], F32)
    xsq = sb("xsq", [128, 1024], F32)
    rtmp = sb("rtmp", [128, 4096], F32)
    ropeQ = sb("ropeQ", [128, 1024], BF16)
    ropeK = sb("ropeK", [128, 1024], BF16)
    mu_b = sb("mu_b", [128, 128], F32)
    r_b = sb("r_b", [128, 128], F32)
    wq_sb = sb("wq_sb", [128, 8 * 1024], BF16)
    wk_sb = sb("wk_sb", [128, 8 * 1024], BF16)
    wstg = sb("wstg", [128, 8 * 1024], F32)
    qknorm = sb("qknorm", [128, 64], F32)
    qkmax = sb("qkmax", [128, 64], F32)
    tabs = sb("tabs", [128, 4 * 256], F32)

    @blk.sync
    def _(q):
        wt(q, dL)
        q.dma_start(out=wstg[:], in_=bass.AP(z.wdirect["wqT"].tensor, 0, [[1024, 128], [131072, 8], [1, 1024]])).then_inc(dL.h, 16); dL.n += 16
        for i in range(4):
            q.dma_start(out=tabs[:, i * 256:(i + 1) * 256], in_=z.dram_tabs[i]).then_inc(dL.h, 16); dL.n += 16
    tok_wq = dL.n
    @blk.vector
    def _(v):
        wt(v, dL, tok_wq)
        v.tensor_copy(wq_sb[:], wstg[:]).then_inc(sV.h, 1); sV.n += 1
    tok_wqc = sV.n
    @blk.sync
    def _(q):
        wt(q, sV, tok_wqc)
        wt(q, dL)
        q.dma_start(out=wstg[:], in_=bass.AP(z.wdirect["wkT"].tensor, 0, [[1024, 128], [131072, 8], [1, 1024]])).then_inc(dL.h, 16); dL.n += 16
    tok_wk = dL.n
    @blk.vector
    def _(v):
        wt(v, dL, tok_wk)
        v.tensor_copy(wk_sb[:], wstg[:]).then_inc(sV.h, 1); sV.n += 1
    z.tok["qk_w_ready"] = (sV, sV.n)
    tok_w = dL.n

    tok = z.tok
    tok["xtile_free"] = [(sV, 0), (sV, 0)]
    tok["stats_evd"] = (sV, 0)
    tok["xsq_free"] = (sT, 0)
    tok["rope_done"] = (sV, 0)
    tok["bank7_ev"] = (sV, 0)
    tok["mub_free"] = (sV, 0)

    for tt in range(NT):
        par = tt % 2
        xsl = lambda a, b: xtile[:, par * 1024 + a: par * 1024 + b]
        u1sl = lambda a, b: z.u1all[:, tt * 1024 + a: tt * 1024 + b]

        dX = z.dX[par]
        @blk.sync
        def _(q):
            s_, v_ = tok["xtile_free"][par]
            wt(q, s_, v_)
            q.dma_start(out=xsl(0, 1024), in_=z.xt[seq, tt]).then_inc(dX.h, 16); dX.n += 16
        tok_x = dX.n

        @blk.scalar
        def _(s):
            wt(s, dX, tok_x)
            s_, v_ = tok["xsq_free"]
            wt(s, s_, v_)
            s.activation(xsq[:], xsl(0, 1024), AF.Square).then_inc(sA.h, 1); sA.n += 1
        tok_sq = sA.n

        @blk.tensor
        def _(t):
            wt(t, dX, tok_x)
            wt(t, sA, tok_sq)
            s_, v_ = tok["stats_evd"]
            wt(t, s_, v_)
            t.matmul(ps[:, 2048:2560], z.ones128f[:], xsl(0, 512), start=True, stop=True)
            t.matmul(ps[:, 2560:3072], z.ones128f[:], xsl(512, 1024), start=True, stop=True).then_inc(sT.h, 1); sT.n += 1
        tok_summ = sT.n

        @blk.vector
        def _(v):
            wt(v, sT, tok_summ)
            v.tensor_reduce(mu_b[:], bass.AP(ps, 2048, [[3584, 128], [1, 128], [128, 8]]),
                            mybir.AxisListType.X, ALU.add)
            v.drain()
            v.tensor_scalar(mu_b[:], mu_b[:], 1.0 / 1024.0, None, ALU.mult).then_inc(sV.h, 1); sV.n += 1
        tok_sumred = sV.n

        @blk.tensor
        def _(t):
            wt(t, sV, tok_sumred)
            t.matmul(ps[:, 2048:2560], z.ones128f[:], xsq[:, 0:512], start=True, stop=True)
            t.matmul(ps[:, 2560:3072], z.ones128f[:], xsq[:, 512:1024], start=True, stop=True).then_inc(sT.h, 1); sT.n += 1
        tok_ssqm = sT.n

        @blk.vector
        def _(v):
            wt(v, sT, tok_ssqm)
            v.tensor_reduce(r_b[:], bass.AP(ps, 2048, [[3584, 128], [1, 128], [128, 8]]),
                            mybir.AxisListType.X, ALU.add)
            v.drain()
            v.tensor_scalar(r_b[:], r_b[:], 1.0 / 1024.0, None, ALU.mult)
            v.drain()
            v.tensor_tensor(xsq[:, 0:128], mu_b[:], mu_b[:], ALU.mult)
            v.drain()
            v.tensor_tensor(r_b[:], r_b[:], xsq[:, 0:128], ALU.subtract).then_inc(sV.h, 1); sV.n += 1
        tok_var = sV.n
        tok["stats_evd"] = (sV, tok_var)

        @blk.scalar
        def _(s):
            wt(s, sV, tok_var)
            s.activation(r_b[:], r_b[:], AF.Sqrt, bias=z.consts[:, 2:3]).then_inc(sA.h, 1); sA.n += 1
        tok_std = sA.n

        @blk.vector
        def _(v):
            wt(v, sA, tok_std)
            v.reciprocal(r_b[:], r_b[:])
            v.drain()
            mu_v = bass.AP(mu_b, 0, [[128, 128], [0, 8], [1, 128]])
            r_v = bass.AP(r_b, 0, [[128, 128], [0, 8], [1, 128]])
            v.tensor_tensor(xsq[:], xsl(0, 1024), mu_v, ALU.subtract)
            v.drain()
            v.tensor_tensor(u1sl(0, 1024), xsq[:], r_v, ALU.mult).then_inc(sV.h, 1); sV.n += 1
        tok_u1 = sV.n
        tok["xtile_free"][par] = (sV, tok_u1)
        tok["mub_free"] = (sV, tok_u1)
        tok["xsq_free"] = (sV, tok_u1)

        @blk.tensor
        def _(t):
            wt(t, sV, tok_u1)
            if tt == 0:
                wt(t, dL, tok_w)
                s_, v_ = z.tok["qk_w_ready"]
                wt(t, s_, v_)
            s_, v_ = tok["rope_done"]
            wt(t, s_, v_)
            for c in range(8):
                fl = dict(start=(c == 0), stop=(c == 7))
                u1c = u1sl(c * 128, (c + 1) * 128)
                t.matmul(ps[:, 0:512], u1c, wq_sb[:, c * 1024 + 0:c * 1024 + 512], **fl)
                t.matmul(ps[:, 512:1024], u1c, wq_sb[:, c * 1024 + 512:c * 1024 + 1024], **fl)
                t.matmul(ps[:, 1024:1536], u1c, wk_sb[:, c * 1024 + 0:c * 1024 + 512], **fl)
                last = t.matmul(ps[:, 1536:2048], u1c, wk_sb[:, c * 1024 + 512:c * 1024 + 1024], **fl)
            last.then_inc(sT.h, 1); sT.n += 1
        tok_qk = sT.n

        @blk.vector
        def _(v):
            wt(v, sT, tok_qk)
            vws = []
            for (src_off, dst, cbase, sbase, toff) in ((0, ropeQ, 0, 256, 0),
                                                       (1024, ropeK, 512, 768, 2048)):
                x1v = bass.AP(ps, src_off, [[3584, 128], [128, 8], [32, 4], [1, 16]])
                x2v = bass.AP(ps, src_off + 16, [[3584, 128], [128, 8], [32, 4], [1, 16]])
                cv = bass.AP(tabs, cbase + tt * 16, [[1024, 128], [0, 8], [0, 4], [1, 16]])
                sv_ = bass.AP(tabs, sbase + tt * 16, [[1024, 128], [0, 8], [0, 4], [1, 16]])
                t1v = bass.AP(rtmp, toff, [[4096, 128], [128, 8], [32, 4], [1, 16]])
                t2v = bass.AP(rtmp, toff + 16, [[4096, 128], [128, 8], [32, 4], [1, 16]])
                t3v = bass.AP(rtmp, toff + 1024, [[4096, 128], [128, 8], [32, 4], [1, 16]])
                t4v = bass.AP(rtmp, toff + 1040, [[4096, 128], [128, 8], [32, 4], [1, 16]])
                y1v = bass.AP(dst, 0, [[1024, 128], [128, 8], [32, 4], [1, 16]])
                y2v = bass.AP(dst, 16, [[1024, 128], [128, 8], [32, 4], [1, 16]])
                vws.append((x1v, x2v, cv, sv_, t1v, t2v, t3v, t4v, y1v, y2v))
            for (x1v, x2v, cv, sv_, t1v, t2v, t3v, t4v, y1v, y2v) in vws:
                v.tensor_tensor(t1v, x1v, cv, ALU.mult)
                v.tensor_tensor(t2v, x2v, sv_, ALU.mult)
                v.tensor_tensor(t3v, x2v, cv, ALU.mult)
                v.tensor_tensor(t4v, x1v, sv_, ALU.mult)
            v.drain()
            for (x1v, x2v, cv, sv_, t1v, t2v, t3v, t4v, y1v, y2v) in vws:
                v.tensor_tensor(y1v, t1v, t2v, ALU.subtract)
                last = v.tensor_tensor(y2v, t3v, t4v, ALU.add)
            last.then_inc(sV.h, 1); sV.n += 1
        tok_rope = sV.n
        tok["rope_done"] = (sV, tok_rope)

        @blk.tensor
        def _(t):
            wt(t, sV, tok_rope)
            s_, v_ = tok["bank7_ev"]
            wt(t, s_, v_)
            for g in range(8):
                last = t.transpose(psbf[:, g * 128:(g + 1) * 128], ropeQ[:, g * 128:(g + 1) * 128], z.ident[:])
            last.then_inc(sT.h, 1); sT.n += 1
        tok_trQ = sT.n

        @blk.vector
        def _(v):
            wt(v, sT, tok_trQ)
            v.tensor_copy(bass.AP(z.QT, tt * 128, [[16384, 128], [2048, 8], [1, 128]]), psbf[:, 0:1024]).then_inc(sV.h, 1); sV.n += 1
        tok_qtev = sV.n

        @blk.tensor
        def _(t):
            wt(t, sV, tok_qtev)
            for g in range(8):
                last = t.transpose(psbf[:, g * 128:(g + 1) * 128], ropeK[:, g * 128:(g + 1) * 128], z.ident[:])
            last.then_inc(sT.h, 1); sT.n += 1
        tok_trK = sT.n

        @blk.vector
        def _(v):
            wt(v, sT, tok_trK)
            v.tensor_copy(bass.AP(z.KT, tt * 128, [[16384, 128], [2048, 8], [1, 128]]), psbf[:, 0:1024]).then_inc(sV.h, 1); sV.n += 1
        tok_ktev = sV.n
        tok["bank7_ev"] = (sV, tok_ktev)

        @blk.vector
        def _(v):
            v.tensor_tensor(rtmp[:, 0:1024], ropeQ[:], ropeQ[:], ALU.mult)
            v.tensor_tensor(rtmp[:, 1024:2048], ropeK[:], ropeK[:], ALU.mult)
            v.drain()
            if tt == 0:
                v.tensor_reduce(qknorm[:, 0:32], bass.AP(rtmp, 0, [[4096, 128], [32, 32], [1, 32]]),
                                mybir.AxisListType.X, ALU.add)
                last = v.tensor_reduce(qknorm[:, 32:64], bass.AP(rtmp, 1024, [[4096, 128], [32, 32], [1, 32]]),
                                       mybir.AxisListType.X, ALU.add)
            else:
                v.tensor_reduce(qkmax[:, 0:32], bass.AP(rtmp, 0, [[4096, 128], [32, 32], [1, 32]]),
                                mybir.AxisListType.X, ALU.add)
                v.tensor_reduce(qkmax[:, 32:64], bass.AP(rtmp, 1024, [[4096, 128], [32, 32], [1, 32]]),
                                mybir.AxisListType.X, ALU.add)
                v.drain()
                last = v.tensor_tensor(qknorm[:], qknorm[:], qkmax[:], ALU.max)
            last.then_inc(sV.h, 1); sV.n += 1

    @blk.tensor
    def _(t):
        wt(t, sV)
        t.transpose(ps[0:64, 3072:3200], qknorm[:], z.identf[:])
        t.drain().then_inc(sT.h, 1); sT.n += 1
    tok_btr = z.sT.n

    @blk.vector
    def _(v):
        wt(v, sT, tok_btr)
        v.tensor_reduce(qkmax[0:64, 0:1], ps[0:64, 3072:3200], mybir.AxisListType.X, ALU.max)
        v.drain().then_inc(sV.h, 1); sV.n += 1
    tok_bred = z.sV.n

    @blk.tensor
    def _(t):
        wt(t, sV, tok_bred)
        t.transpose(ps[0:1, 3200:3264], qkmax[0:64, 0:1], z.identf[0:64, 0:64])
        t.drain().then_inc(sT.h, 1); sT.n += 1
    tok_btr2 = z.sT.n

    @blk.vector
    def _(v):
        wt(v, sT, tok_btr2)
        v.tensor_reduce(z.rows[0:1, 640:644], bass.AP(ps, 3200, [[3584, 1], [16, 4], [1, 16]]),
                        mybir.AxisListType.X, ALU.max)
        v.drain()
        v.tensor_tensor(z.rows[0:1, 644:646], z.rows[0:1, 640:642], z.rows[0:1, 642:644], ALU.mult)
        v.drain().then_inc(sV.h, 1); sV.n += 1
    tok_bmul = z.sV.n

    @blk.scalar
    def _(s):
        wt(s, sV, tok_bmul)
        s.activation(z.rows[0:1, 644:646], z.rows[0:1, 644:646], AF.Sqrt).then_inc(sA.h, 1); sA.n += 1
    tok_bsq = z.sA.n

    @blk.vector
    def _(v):
        wt(v, sA, tok_bsq)
        v.tensor_scalar(z.rows[0:1, 646:648], z.rows[0:1, 644:646], -1.0, None, ALU.mult)
        v.drain().then_inc(sV.h, 1); sV.n += 1
    tok_bneg = z.sV.n

    @blk.gpsimd
    def _(g):
        wt(g, sV, tok_bneg)
        wt(g, dL)
        g.dma_start(out=z.bounce[0:2], in_=z.rows[0:1, 646:648]).then_inc(dS.h, 16); dS.n += 16
    @blk.sync
    def _(q):
        wt(q, dS)
        wt(q, dL)
        q.dma_start(out=z.consts2[:, 0:2],
                    in_=bass.AP(z.bounce.tensor, 0, [[0, 128], [1, 2]])).then_inc(dL.h, 16); dL.n += 16
    z.tok["phaseA_done"] = (dL, z.dL.n)
    return pes


def _build_seq(z, seq):
    z.tok = {}
    z.sfx = "_s%d" % seq
    nc = z.nc
    ab_es = contextlib.ExitStack()
    z.QT = ab_es.enter_context(nc.sbuf_tensor("QT" + z.sfx, [128, 8 * 2048], BF16, side="right"))
    z.KT = ab_es.enter_context(nc.sbuf_tensor("KT" + z.sfx, [128, 8 * 2048], BF16, side="right"))
    z.u1all = ab_es.enter_context(nc.sbuf_tensor("u1all" + z.sfx, [128, NT * 1024], BF16, side="right"))
    _quiesce(z)
    pes = _phase_a(z, seq)
    stop = _BUILD_OPTS.get("stop_after")
    if stop == "A":
        if seq == 0:
            _dump_taps(z, dict(QT=z.QT, KT=z.KT, u1all=z.u1all, consts2=z.consts2, consts=z.consts))
        pes.close(); ab_es.close()
        return
    pes.close()
    tt_es = contextlib.ExitStack()
    z.TT = tt_es.enter_context(nc.sbuf_tensor("TT" + z.sfx, [128, 16 * 2048], BF16))
    _quiesce(z)
    pesb = _phase_b(z, seq)
    pesb.close()
    if stop == "B":
        if seq == 0:
            _dump_taps(z, dict(TT=z.TT, consts2=z.consts2))
        ab_es.close(); tt_es.close()
        return
    ab_es.close()
    stream_es = contextlib.ExitStack()
    z.xs = stream_es.enter_context(nc.sbuf_tensor("xs" + z.sfx, [128, 8 * 2048], F32, side="right"))
    _quiesce(z)
    pesc = _phase_c(z, seq)
    pesc.close()
    tt_es.close()
    if stop == "C":
        if seq == 0:
            _dump_taps(z, dict(xs=z.xs))
        stream_es.close()
        return
    _quiesce(z)
    pesd = _phase_ffn(z, seq, "D" + z.sfx)
    pesd.close()
    if stop == "D":
        if seq == 0:
            _dump_taps(z, dict(xs=z.xs))
        stream_es.close()
        return
    _quiesce(z)
    pese = _phase_e(z, seq)
    pese.close()
    if stop == "E":
        if seq == 0:
            _dump_taps(z, dict(xs=z.xs))
        stream_es.close()
        return
    _quiesce(z)
    pesf = _phase_ffn(z, seq, "F" + z.sfx)
    pesf.close()
    blk = z.blk
    @blk.gpsimd
    def _(g):
        wt(g, z.sV)
        for fc in range(8):
            g.dma_start(out=z.yout[seq, fc], in_=z.xs[:, fc * 2048:(fc + 1) * 2048]).then_inc(z.dS.h, 16); z.dS.n += 16
        wt(g, z.dS)
    stream_es.close()


def _dump_taps(z, table):
    if not z.taps:
        return
    blk = z.blk
    _quiesce(z)
    @blk.gpsimd
    def _(g):
        for name, ap in z.taps.items():
            src = table[name]
            g.dma_start(out=ap, in_=src[:]).then_inc(z.dS.h, 16); z.dS.n += 16


def kernel(**inputs):
    per_core = _host_prepare(inputs)
    nc = build_nc()
    res = run_bass_kernel_spmd(nc, per_core, core_ids=list(range(8)))
    return _host_gather(res.results)


def _phase_b(z, seq):
    nc, blk = z.nc, z.blk
    dL, dS, sT, sV, sA, sG = z.dL, z.dS, z.sT, z.sV, z.sA, z.sG
    W = z.weights
    ps, psbf = z.ps, z.psbf

    pes = contextlib.ExitStack()
    sb = lambda name, shape, dt=F32: pes.enter_context(nc.sbuf_tensor(name + z.sfx, shape, dt))
    wvp = sb("wvp", [128, 2048], BF16)
    Vp = sb("Vp", [128, 4096], BF16)
    Ebuf = sb("Ebuf", [128, 2048], BF16)
    scr = sb("scr", [128, 2048], F32)
    OcA = sb("OcA", [128, 2048], F32)
    OcB = sb("OcB", [128, 2048], F32)
    gnmu = sb("gnmu", [128, 2], F32)

    tok = z.tok
    tok["wvp_free"] = [(sV, sV.n), (sV, sV.n)]
    tok["expdone"] = (sA, sA.n)
    tok["ozfree"] = (sV, sV.n)
    tok["bank7_ev"] = (sV, sV.n)
    tok["gnmu_free"] = (sV, sV.n)
    tok["vev"] = [(sV, sV.n), (sV, sV.n)]

    SA, SB = 0, 512
    OA, OB = 1024, 1536
    ZA, ZB = 2048, 2560
    ST6 = 3072

    for pr in range(8):
        hA, hB = 2 * pr, 2 * pr + 1
        g = pr // 2
        r0 = (pr % 2) * 64
        par = pr % 2
        dX = z.dX[par]

        @blk.sync
        def _(q):
            s_, v_ = tok["wvp_free"][0]
            wt(q, s_, v_)
            q.dma_start(out=scr[:],
                        in_=bass.AP(z.wdirect["wvT"].tensor, hA * 128, [[2048, 128], [128 * 2048, 8], [1, 256]])
                        ).then_inc(dX.h, 16); dX.n += 16
        tok_wvf = dX.n
        @blk.vector
        def _(v):
            wt(v, dX, tok_wvf)
            s_, v_ = tok["wvp_free"][1]
            wt(v, s_, v_)
            v.tensor_copy(wvp[:], scr[:]).then_inc(sV.h, 1); sV.n += 1
        tok_wv = sV.n

        for tt in range(NT):
            vb = (tt % 2) * 512
            @blk.tensor
            def _(t):
                if tt == 0:
                    wt(t, sV, tok_wv)
                    s_, v_ = tok["expdone"]
                    wt(t, s_, v_)
                    wt(t, sV)
                else:
                    s_, v_ = tok["vev"][tt % 2]
                    wt(t, s_, v_)
                for c in range(8):
                    last = t.matmul(ps[:, vb:vb + 256], z.u1all[:, tt * 1024 + c * 128:tt * 1024 + (c + 1) * 128],
                             wvp[:, c * 256:(c + 1) * 256],
                             start=(c == 0), stop=(c == 7))
                last.then_inc(sT.h, 1); sT.n += 1
            tok_vmm = sT.n

            @blk.vector
            def _(v):
                wt(v, sT, tok_vmm)
                v.tensor_copy(Vp[:, tt * 256:tt * 256 + 256], ps[:, vb:vb + 256]).then_inc(sV.h, 1); sV.n += 1
            tok["vev"][tt % 2] = (sV, sV.n)
        tok["wvp_free"][0] = (sV, tok_wv)
        tok["wvp_free"][1] = (sT, sT.n)
        tok["ozfree"] = (sV, sV.n)

        for qb in range(4):
            q0 = qb * 512
            for s_i in (1, 2):
                gs = g + 4 * (s_i - 1)
                qcol = gs * 2048 + q0
                sT0 = sT.n
                sA0 = sA.n
                oz_s, oz_v = tok["ozfree"]

                @blk.tensor
                def _(t):
                    for kt in range(NT):
                        kcol = gs * 2048 + kt * 128
                        eb = (kt % 2) * 1024
                        wt(t, sA, sA0 + kt)
                        if kt == 0:
                            wt(t, oz_s, oz_v)
                        t.matmul(ps[:, SA:SA + 512], z.KT[r0:r0 + 32, kcol:kcol + 128],
                                 z.QT[r0:r0 + 32, qcol:qcol + 512], start=True, stop=True,
                                 tile_position=(r0, 0))
                        t.matmul(ps[:, SB:SB + 512], z.KT[r0 + 32:r0 + 64, kcol:kcol + 128],
                                 z.QT[r0 + 32:r0 + 64, qcol:qcol + 512], start=True, stop=True,
                                 tile_position=(r0 + 32, 0)).then_inc(sT.h, 1); sT.n += 1
                        if kt > 0:
                            ep = ((kt - 1) % 2) * 1024
                            fl = dict(start=(kt - 1 == 0), stop=False)
                            t.matmul(ps[:, ZA:ZA + 512], z.ones128[:], Ebuf[:, ep:ep + 512], **fl)
                            t.matmul(ps[:, ZB:ZB + 512], z.ones128[:], Ebuf[:, ep + 512:ep + 1024], **fl)
                            t.matmul(ps[:, OA:OA + 512], Vp[:, (kt - 1) * 256:(kt - 1) * 256 + 128],
                                     Ebuf[:, ep:ep + 512], **fl)
                            t.matmul(ps[:, OB:OB + 512], Vp[:, (kt - 1) * 256 + 128:(kt - 1) * 256 + 256],
                                     Ebuf[:, ep + 512:ep + 1024], **fl)
                    wt(t, sA, sA0 + NT)
                    ep = ((NT - 1) % 2) * 1024
                    fl = dict(start=False, stop=True)
                    t.matmul(ps[:, ZA:ZA + 512], z.ones128[:], Ebuf[:, ep:ep + 512], **fl)
                    t.matmul(ps[:, ZB:ZB + 512], z.ones128[:], Ebuf[:, ep + 512:ep + 1024], **fl)
                    t.matmul(ps[:, OA:OA + 512], Vp[:, (NT - 1) * 256:(NT - 1) * 256 + 128],
                             Ebuf[:, ep:ep + 512], **fl)
                    t.matmul(ps[:, OB:OB + 512], Vp[:, (NT - 1) * 256 + 128:(NT - 1) * 256 + 256],
                             Ebuf[:, ep + 512:ep + 1024], **fl).then_inc(sT.h, 1); sT.n += 1

                @blk.scalar
                def _(s):
                    for kt in range(NT):
                        eb = (kt % 2) * 1024
                        wt(s, sT, sT0 + kt + 1)
                        s.activation(Ebuf[:, eb:eb + 1024], ps[:, 0:1024], AF.Exp,
                                     bias=z.consts2[:, s_i - 1:s_i]).then_inc(sA.h, 1); sA.n += 1
                assert sA.n == sA0 + NT
                tok["expdone"] = (sA, sA.n)
                tok_oz = sT.n

                @blk.vector
                def _(v):
                    wt(v, sT, tok_oz)
                    v.reciprocal(scr[:, 0:512], ps[:, ZA:ZA + 512])
                    v.reciprocal(scr[:, 512:1024], ps[:, ZB:ZB + 512])
                    v.drain()
                    if s_i == 1:
                        v.tensor_tensor(OcA[:, q0:q0 + 512], ps[:, OA:OA + 512], scr[:, 0:512], ALU.mult)
                        v.tensor_tensor(OcB[:, q0:q0 + 512], ps[:, OB:OB + 512], scr[:, 512:1024], ALU.mult).then_inc(sV.h, 1); sV.n += 1
                    else:
                        v.tensor_tensor(scr[:, 1024:1536], ps[:, OA:OA + 512], scr[:, 0:512], ALU.mult)
                        v.tensor_tensor(scr[:, 1536:2048], ps[:, OB:OB + 512], scr[:, 512:1024], ALU.mult)
                        v.drain()
                        v.scalar_tensor_tensor(OcA[:, q0:q0 + 512], scr[:, 1024:1536], z.consts[:, 5:6],
                                               OcA[:, q0:q0 + 512], ALU.mult, ALU.add)
                        v.scalar_tensor_tensor(OcB[:, q0:q0 + 512], scr[:, 1536:2048], z.consts[:, 5:6],
                                               OcB[:, q0:q0 + 512], ALU.mult, ALU.add).then_inc(sV.h, 1); sV.n += 1
                tok["ozfree"] = (sV, sV.n)

        for (Oc, h) in ((OcA, hA), (OcB, hB)):
            @blk.scalar
            def _(s):
                wt(s, sV)
                s.activation(scr[:], Oc[:], AF.Square).then_inc(sA.h, 1); sA.n += 1
            tok_gsq = sA.n

            @blk.tensor
            def _(t):
                wt(t, sA, tok_gsq)
                s_, v_ = tok["stats_evd_b"] if "stats_evd_b" in tok else (sV, 0)
                wt(t, s_, v_)
                for j in range(4):
                    t.matmul(ps[:, ST6:ST6 + 512], z.ones128f[:], Oc[:, j * 512:(j + 1) * 512],
                             start=(j == 0), stop=(j == 3))
                t.drain().then_inc(sT.h, 1); sT.n += 1
            tok_gsum = sT.n

            @blk.vector
            def _(v):
                wt(v, sT, tok_gsum)
                v.tensor_reduce(gnmu[:, 0:1], ps[:, ST6:ST6 + 512], mybir.AxisListType.X, ALU.add)
                v.drain()
                v.tensor_scalar(gnmu[:, 0:1], gnmu[:, 0:1], 1.0 / 262144.0, None, ALU.mult)
                v.drain().then_inc(sV.h, 1); sV.n += 1
            tok_gsr = sV.n

            @blk.tensor
            def _(t):
                wt(t, sV, tok_gsr)
                for j in range(4):
                    t.matmul(ps[:, ST6:ST6 + 512], z.ones128f[:], scr[:, j * 512:(j + 1) * 512],
                             start=(j == 0), stop=(j == 3))
                t.drain().then_inc(sT.h, 1); sT.n += 1
            tok_gss = sT.n

            @blk.vector
            def _(v):
                wt(v, sT, tok_gss)
                v.tensor_reduce(gnmu[:, 1:2], ps[:, ST6:ST6 + 512], mybir.AxisListType.X, ALU.add)
                v.drain()
                v.tensor_scalar(gnmu[:, 1:2], gnmu[:, 1:2], 1.0 / 262144.0, None, ALU.mult)
                v.drain()
                v.tensor_tensor(scr[:, 0:1], gnmu[:, 0:1], gnmu[:, 0:1], ALU.mult)
                v.drain()
                v.tensor_tensor(gnmu[:, 1:2], gnmu[:, 1:2], scr[:, 0:1], ALU.subtract)
                v.drain().then_inc(sV.h, 1); sV.n += 1
            tok["stats_evd_b"] = (sV, sV.n)
            tok_gvar = sV.n

            @blk.scalar
            def _(s):
                wt(s, sV, tok_gvar)
                s.activation(gnmu[:, 1:2], gnmu[:, 1:2], AF.Sqrt,
                             bias=z.consts[:, 2:3]).then_inc(sA.h, 1); sA.n += 1
            tok_gstd = sA.n

            @blk.vector
            def _(v):
                wt(v, sA, tok_gstd)
                v.reciprocal(gnmu[:, 1:2], gnmu[:, 1:2])
                v.drain()
                v.tensor_scalar(gnmu[:, 1:2], gnmu[:, 1:2], 0.8, None, ALU.mult)
                v.drain()
                wt(v, sT)
                v.tensor_scalar(wvp[:], Oc[:], gnmu[:, 0:1], gnmu[:, 1:2], ALU.subtract, ALU.mult)
                v.drain().then_inc(sV.h, 1); sV.n += 1
            tok_gn = sV.n

            for b in range(2):
                @blk.tensor
                def _(t):
                    wt(t, sV, tok_gn)
                    s_, v_ = tok["bank7_ev"]
                    wt(t, s_, v_)
                    for q_lo in range(8):
                        ql = b * 8 + q_lo
                        t.transpose(psbf[:, q_lo * 128:(q_lo + 1) * 128],
                                    bass.AP(wvp, ql, [[2048, 128], [16, 128]]), z.ident[:])
                    t.drain().then_inc(sT.h, 1); sT.n += 1
                tok_tr = sT.n

                @blk.vector
                def _(v):
                    wt(v, sT, tok_tr)
                    v.tensor_copy(z.TT[:, h * 2048 + b * 1024: h * 2048 + (b + 1) * 1024], psbf[:])
                    v.drain().then_inc(sV.h, 1); sV.n += 1
                tok["bank7_ev"] = (sV, sV.n)
            tok["wvp_free"] = [(sT, sT.n), (sT, sT.n)]

    z.tok["phaseB_done"] = (sV, z.sV.n)
    return pes


def _phase_c(z, seq):
    nc, blk = z.nc, z.blk
    dL, dS, sT, sV, sA, sG = z.dL, z.dS, z.sT, z.sV, z.sA, z.sG
    W = z.weights
    ps = z.ps

    pes = contextlib.ExitStack()
    sb = lambda name, shape, dt=F32: pes.enter_context(nc.sbuf_tensor(name + z.sfx, shape, dt))
    wpj = sb("wpj", [128, 4096], BF16)
    av = sb("av", [128, 4096], F32)
    scr1 = sb("c1s", [128, 2048], F32)
    scr2 = sb("c2s", [128, 2048], F32)
    scr3 = sb("c3s", [128, 2048], F32)
    scr4 = sb("c4s", [128, 2048], F32)

    tok = z.tok
    tok["wpj_free"] = (sV, sV.n)
    tok["av_free"] = (sV, sV.n)
    tok["chain_free"] = (sV, sV.n)

    for j in range(8):
        par = j % 2
        dX = z.dX[par]

        @blk.sync
        def _(q):
            s_, v_ = tok["wpj_free"]
            wt(q, s_, v_)
            q.dma_start(out=bass.AP(wpj, 0, [[4096, 128], [256, 16], [1, 128]]),
                        in_=bass.AP(W["wproj_c"].tensor, j * 128, [[2048, 128], [128 * 2048, 16], [1, 128]])
                        ).then_inc(dX.h, 16); dX.n += 16
            q.dma_start(out=bass.AP(wpj, 128, [[4096, 128], [256, 16], [1, 128]]),
                        in_=bass.AP(W["wproj_c"].tensor, (j + 8) * 128, [[2048, 128], [128 * 2048, 16], [1, 128]])
                        ).then_inc(dX.h, 16); dX.n += 16
            s_, v_ = tok["chain_free"]
            wt(q, s_, v_)
            q.dma_start(out=scr3[:],
                        in_=bass.AP(z.xt.tensor, seq * 2097152 + j * 128,
                                    [[1024, 128], [131072, 16], [1, 128]])
                        ).then_inc(dX.h, 16); dX.n += 16
        tok_w = dX.n

        for half in range(2):
            @blk.tensor
            def _(t):
                wt(t, dX, tok_w)
                if half == 0:
                    s_, v_ = tok["av_free"]
                    wt(t, s_, v_)
                else:
                    wt(t, sV)
                for q4 in range(4):
                    for c in range(16):
                        t.matmul(ps[:, q4 * 512:(q4 + 1) * 512],
                                 wpj[:, c * 256 + half * 128:c * 256 + half * 128 + 128],
                                 z.TT[:, c * 2048 + q4 * 512:c * 2048 + (q4 + 1) * 512],
                                 start=(c == 0), stop=(c == 15))
                t.drain().then_inc(sT.h, 1); sT.n += 1
            tok_mm = sT.n

            @blk.vector
            def _(v):
                wt(v, sT, tok_mm)
                v.tensor_copy(av[:, half * 2048:(half + 1) * 2048], ps[:, 0:2048])
                v.drain().then_inc(sV.h, 1); sV.n += 1
        tok["av_evict"] = (sV, sV.n)

        @blk.vector
        def _(v):
            wt(v, dX, tok_w)
            v.tensor_scalar(scr2[:], av[:, 0:2048], -1.0, None, ALU.mult)
            v.drain()
            v.tensor_tensor(scr1[:], av[:, 0:2048], scr2[:], ALU.max)
            v.drain()
            v.tensor_scalar(scr1[:], scr1[:], 1e-30, None, ALU.max)
            v.drain()
            v.reciprocal(scr2[:], scr1[:])
            v.tensor_scalar(scr4[:], scr1[:], 1.0, None, ALU.is_gt)
            v.drain()
            v.tensor_tensor(scr1[:], scr1[:], scr2[:], ALU.min)
            v.drain().then_inc(sV.h, 1); sV.n += 1
        tok_pre = sV.n

        @blk.scalar
        def _(s):
            wt(s, sV, tok_pre)
            s.activation(scr2[:], scr1[:], AF.Arctan)
            s.drain()
            s.activation(scr1[:], av[:, 0:2048], AF.Sign)
            s.drain().then_inc(sA.h, 1); sA.n += 1
        tok_act = sA.n

        @blk.vector
        def _(v):
            wt(v, sA, tok_act)
            v.scalar_tensor_tensor(av[:, 0:2048], scr4[:], math.pi / 2, scr2[:], ALU.mult, ALU.add)
            v.drain()
            v.tensor_tensor(scr4[:], scr4[:], scr2[:], ALU.mult)
            v.drain()
            v.scalar_tensor_tensor(scr4[:], scr4[:], -2.0, av[:, 0:2048], ALU.mult, ALU.add)
            v.drain()
            v.tensor_tensor(scr2[:], scr4[:], scr1[:], ALU.mult)
            v.drain()
            v.tensor_scalar(scr2[:], scr2[:], z.consts[:, 6:7], z.consts[:, 7:8], ALU.mult, ALU.add)
            v.drain()
            v.tensor_tensor(scr1[:], scr2[:], av[:, 2048:4096], ALU.mult)
            v.drain()
            v.tensor_tensor(z.xs[:, j * 2048:(j + 1) * 2048], scr1[:], scr3[:], ALU.add)
            v.drain().then_inc(sV.h, 1); sV.n += 1
        tok["chain_free"] = (sV, sV.n)
        tok["av_free"] = (sV, sV.n)
        tok["wpj_free"] = (sV, sV.n)

    z.tok["phaseC_done"] = (sV, z.sV.n)
    return pes


def _phase_ffn(z, seq, tag):
    nc, blk = z.nc, z.blk
    dL, dS, sT, sV, sA, sG = z.dL, z.dS, z.sT, z.sV, z.sA, z.sG
    W = z.weights
    ps = z.ps
    QW = 512

    pes = contextlib.ExitStack()
    sb = lambda name, shape, dt=F32: pes.enter_context(nc.sbuf_tensor(name + z.sfx, shape, dt))
    hbuf = sb("hbuf" + tag, [128, 32 * QW], F32)
    k2b = sb("k2b" + tag, [128, 32 * QW], BF16)
    k1 = sb("k1" + tag, [128, 8 * QW], BF16)
    reps = sb("reps" + tag, [128, 3 * QW], F32)
    scr = sb("scrD" + tag, [128, 2 * QW], F32)
    scr2 = sb("scr2D" + tag, [128, 2 * QW], F32)
    sqb8 = sb("sqbD" + tag, [128, 4096], BF16)
    tch = sb("tchD" + tag, [128, 4096], BF16)

    MU, QR, A1 = 0, QW, 2 * QW
    rp = lambda a: reps[:, a:a + QW]
    tok = z.tok

    for tq in _BUILD_OPTS.get("ffn_tq", range(4)):
        t0c = tq * QW
        xsl = lambda fc: z.xs[:, fc * 2048 + t0c:fc * 2048 + t0c + QW]

        xs_v = bass.AP(z.xs, t0c, [[16384, 128], [2048, 8], [1, 512]])
        mu_v = bass.AP(reps, MU, [[1536, 128], [0, 8], [1, 512]])
        qr_v = bass.AP(reps, QR, [[1536, 128], [0, 8], [1, 512]])

        @blk.vector
        def _(v):
            wt(v, sT)
            wt(v, sA)
            v.tensor_copy(sqb8[:], xs_v).then_inc(sV.h, 1); sV.n += 1
        tok_cast = sV.n

        @blk.tensor
        def _(t):
            wt(t, sV, tok_cast)
            wt(t, sV)
            for fc in range(8):
                last = t.matmul(ps[:, 0:QW], z.ones128[:], sqb8[:, fc * QW:(fc + 1) * QW],
                                start=(fc == 0), stop=(fc == 7))
            last.then_inc(sT.h, 1); sT.n += 1
        tok_mm0 = sT.n

        @blk.vector
        def _(v):
            wt(v, sT, tok_mm0)
            v.tensor_scalar(rp(MU), ps[:, 0:QW], 1.0 / 1024.0, None, ALU.mult).then_inc(sV.h, 1); sV.n += 1
        tok_mu = sV.n

        @blk.scalar
        def _(s):
            wt(s, sT, tok_mm0)
            s.activation(sqb8[:], sqb8[:], AF.Square).then_inc(sA.h, 1); sA.n += 1
        tok_sq8 = sA.n

        @blk.tensor
        def _(t):
            wt(t, sA, tok_sq8)
            wt(t, sV, tok_mu)
            for fc in range(8):
                last = t.matmul(ps[:, 0:QW], z.ones128[:], sqb8[:, fc * QW:(fc + 1) * QW],
                                start=(fc == 0), stop=(fc == 7))
            last.then_inc(sT.h, 1); sT.n += 1
        tok_mm1 = sT.n

        @blk.vector
        def _(v):
            wt(v, sT, tok_mm1)
            v.tensor_scalar(scr2[:, 0:QW], ps[:, 0:QW], 1.0 / 1024.0, None, ALU.mult)
            v.drain()
            v.tensor_tensor(scr[:, 0:QW], rp(MU), rp(MU), ALU.mult)
            v.drain()
            v.tensor_tensor(scr[:, 0:QW], scr2[:, 0:QW], scr[:, 0:QW], ALU.subtract).then_inc(sV.h, 1); sV.n += 1
        tok_var = sV.n

        @blk.scalar
        def _(s):
            wt(s, sV, tok_var)
            s.activation(scr2[:, 0:QW], scr[:, 0:QW], AF.Sqrt, bias=z.consts[:, 2:3]).then_inc(sA.h, 1); sA.n += 1
        tok_std = sA.n

        @blk.vector
        def _(v):
            wt(v, sA, tok_std)
            v.reciprocal(rp(A1), scr2[:, 0:QW])
            v.drain()
            v.tensor_scalar(scr2[:, 0:QW], scr[:, 0:QW], 1e-5, None, ALU.add)
            v.drain()
            v.reciprocal(scr2[:, 0:QW], scr2[:, 0:QW])
            v.drain()
            v.tensor_tensor(scr2[:, 0:QW], scr[:, 0:QW], scr2[:, 0:QW], ALU.mult).then_inc(sV.h, 1); sV.n += 1
        tok_mu2 = sV.n

        @blk.scalar
        def _(s):
            wt(s, sV, tok_mu2)
            s.activation(scr2[:, 0:QW], scr2[:, 0:QW], AF.Sqrt, bias=z.consts[:, 2:3]).then_inc(sA.h, 1); sA.n += 1
        tok_rr = sA.n

        @blk.vector
        def _(v):
            wt(v, sA, tok_rr)
            v.reciprocal(scr2[:, 0:QW], scr2[:, 0:QW])
            v.tensor_reduce(rp(QR), bass.AP(z.xs, t0c, [[16384, 128], [1, 512], [2048, 8]]),
                            mybir.AxisListType.X, ALU.max)
            v.tensor_reduce(scr[:, 0:QW], bass.AP(z.xs, t0c, [[16384, 128], [1, 512], [2048, 8]]),
                            mybir.AxisListType.X, ALU.min)
            v.drain()
            v.tensor_tensor(rp(QR), rp(QR), rp(MU), ALU.subtract)
            v.tensor_tensor(scr[:, 0:QW], rp(MU), scr[:, 0:QW], ALU.subtract)
            v.drain()
            v.tensor_tensor(rp(QR), rp(QR), scr[:, 0:QW], ALU.max).then_inc(sV.h, 1); sV.n += 1
        tok_fold = sV.n

        @blk.tensor
        def _(t):
            wt(t, sV, tok_fold)
            wt(t, sV)
            for i in range(4):
                last = t.transpose(ps[:, 1024 + i * 128:1024 + (i + 1) * 128],
                                   reps[:, QW + i * 128:QW + (i + 1) * 128], z.identf[:])
            last.then_inc(sT.h, 1); sT.n += 1
        tok_amt = sT.n

        @blk.vector
        def _(v):
            wt(v, sT, tok_amt)
            for i in range(4):
                last = v.tensor_reduce(scr[:, QW + i:QW + i + 1], ps[:, 1024 + i * 128:1024 + (i + 1) * 128],
                                       mybir.AxisListType.X, ALU.max)
            last.then_inc(sV.h, 1); sV.n += 1
        tok_amr = sV.n

        @blk.gpsimd
        def _(g):
            wt(g, sV, tok_amr)
            wt(g, dL)
            for i in range(4):
                g.dma_start(out=z.bounce[i * 128:(i + 1) * 128],
                            in_=scr[:, QW + i:QW + i + 1]).then_inc(dS.h, 16); dS.n += 16
        @blk.sync
        def _(q):
            wt(q, dS)
            wt(q, dL)
            q.dma_start(out=scr[:, 0:QW],
                        in_=bass.AP(z.bounce.tensor, 0, [[0, 128], [1, 512]])).then_inc(dL.h, 16); dL.n += 16
        tok_am = dL.n

        @blk.vector
        def _(v):
            wt(v, dL, tok_am)
            v.tensor_tensor(scr[:, 0:QW], scr[:, 0:QW], rp(A1), ALU.mult)
            v.drain()
            v.tensor_tensor(scr[:, 0:QW], scr[:, 0:QW], scr2[:, 0:QW], ALU.mult)
            v.drain()
            v.tensor_scalar(scr[:, 0:QW], scr[:, 0:QW], 1e-5, None, ALU.max)
            v.drain()
            v.reciprocal(rp(QR), scr[:, 0:QW])
            v.drain()
            v.tensor_scalar(rp(QR), rp(QR), 127.0, None, ALU.mult)
            v.drain()
            v.tensor_tensor(rp(QR), rp(QR), scr2[:, 0:QW], ALU.mult)
            v.drain()
            v.tensor_tensor(rp(QR), rp(QR), rp(A1), ALU.mult)
            v.drain()
            v.tensor_scalar(rp(A1), scr[:, 0:QW], z.consts[:, 9:10], 1.0 / 127.0, ALU.mult, ALU.mult)
            v.drain().then_inc(sV.h, 1); sV.n += 1

        @blk.vector
        def _(v):
            wt(v, sT)
            v.tensor_tensor(hbuf[:, 0:4096], xs_v, mu_v, ALU.subtract)
            v.drain()
            v.tensor_tensor(hbuf[:, 0:4096], hbuf[:, 0:4096], qr_v, ALU.mult)
            v.drain()
            v.tensor_scalar(k1[:], hbuf[:, 0:4096], MAGIC, MAGIC, ALU.add, ALU.subtract)
            v.drain().then_inc(sV.h, 1); sV.n += 1
        tok_k1 = sV.n

        tch1tok = tok.setdefault("tch1_free" + tag, [(sV, 0), (sV, 0)])
        tch1tok[0] = (sT, sT.n)
        tch1tok[1] = (sT, sT.n)
        for hp in range(16):
            hf = hp * 2
            par1 = hp % 2
            dX = z.dX[par1]
            @blk.sync
            def _(q):
                s_, v_ = tch1tok[par1]
                wt(q, s_, v_)
                q.dma_start(out=tch[:, par1 * 2048:par1 * 2048 + 1024],
                            in_=bass.AP(W["t1_c"].tensor, hf * 128, [[4096, 128], [128 * 4096, 8], [1, 128]])
                            ).then_inc(dX.h, 16); dX.n += 16
                q.dma_start(out=tch[:, par1 * 2048 + 1024:par1 * 2048 + 2048],
                            in_=bass.AP(W["t1_c"].tensor, (hf + 1) * 128, [[4096, 128], [128 * 4096, 8], [1, 128]])
                            ).then_inc(dX.h, 16); dX.n += 16
            tok_t = dX.n

            pb1 = (hp % 2) * 1024
            ev1tok = tok.setdefault("ffn1_evp" + tag, [(sV, 0), (sV, 0)])
            @blk.tensor
            def _(t):
                wt(t, dX, tok_t)
                if hp == 0:
                    wt(t, sV, tok_k1)
                if hp >= 2:
                    s_, v_ = ev1tok[hp % 2]
                    wt(t, s_, v_)
                for c in range(8):
                    t.matmul(ps[:, pb1:pb1 + 512],
                             tch[:, par1 * 2048 + c * 128:par1 * 2048 + (c + 1) * 128],
                             k1[:, c * QW:(c + 1) * QW], start=(c == 0), stop=(c == 7))
                for c in range(8):
                    last = t.matmul(ps[:, pb1 + 512:pb1 + 1024],
                                    tch[:, par1 * 2048 + 1024 + c * 128:par1 * 2048 + 1024 + (c + 1) * 128],
                                    k1[:, c * QW:(c + 1) * QW], start=(c == 0), stop=(c == 7))
                last.then_inc(sT.h, 1); sT.n += 1
            tch1tok[par1] = (sT, sT.n)
            tok_pairmm = sT.n
            @blk.vector
            def _(v):
                wt(v, sT, tok_pairmm)
                etmp = scr[:, 0:1024] if hp % 2 == 0 else scr2[:, 0:1024]
                a1_v = bass.AP(reps, A1, [[1536, 128], [0, 2], [1, 512]])
                b1_v = bass.AP(z.b1, hf, [[32, 128], [1, 2], [0, 512]])
                v.tensor_tensor(etmp, ps[:, pb1:pb1 + 1024], a1_v, ALU.mult)
                v.drain()
                v.tensor_tensor(hbuf[:, hf * QW:(hf + 2) * QW], etmp,
                                b1_v, ALU.add).then_inc(sV.h, 1); sV.n += 1
            ev1tok[hp % 2] = (sV, sV.n)
            tok["ffn1_ev" + tag] = (sV, sV.n)

        @blk.scalar
        def _(s):
            s_, v_ = tok["ffn1_ev" + tag]
            wt(s, s_, v_)
            s.activation(hbuf[:], hbuf[:], AF.Gelu).then_inc(sA.h, 1); sA.n += 1
        tok_gelu = sA.n

        for g8 in range(8):
            sqoff = (g8 % 2) * 2048
            @blk.scalar
            def _(s):
                if g8 == 0:
                    wt(s, sA, tok_gelu)
                if g8 >= 2:
                    wt(s, sT)
                s.activation(sqb8[:, sqoff:sqoff + 2048], hbuf[:, g8 * 2048:(g8 + 1) * 2048],
                             AF.Square).then_inc(sA.h, 1); sA.n += 1
            tok_sqg = sA.n
            @blk.tensor
            def _(t):
                wt(t, sA, tok_sqg)
                if g8 == 0:
                    wt(t, sV)
                for c in range(4):
                    hf = g8 * 4 + c
                    last = t.matmul(ps[:, 0:QW], z.ones128[:], sqb8[:, sqoff + c * 512:sqoff + (c + 1) * 512],
                                    start=(hf == 0), stop=(hf == 31))
                last.then_inc(sT.h, 1); sT.n += 1
        tok_hsq = sT.n

        @blk.vector
        def _(v):
            wt(v, sT, tok_hsq)
            wt(v, sA)
            v.tensor_scalar(scr[:, 0:QW], ps[:, 0:QW], 1.0 / 4096.0, None, ALU.mult)
            v.tensor_reduce(scr2[:, 0:QW], bass.AP(hbuf, 0, [[16384, 128], [1, 512], [512, 32]]),
                            mybir.AxisListType.X, ALU.max)
            v.tensor_reduce(scr2[:, QW:2 * QW], bass.AP(hbuf, 0, [[16384, 128], [1, 512], [512, 32]]),
                            mybir.AxisListType.X, ALU.min)
            v.drain()
            v.tensor_scalar(scr2[:, QW:2 * QW], scr2[:, QW:2 * QW], -1.0, None, ALU.mult)
            v.drain()
            v.tensor_tensor(scr2[:, 0:QW], scr2[:, 0:QW], scr2[:, QW:2 * QW], ALU.max).then_inc(sV.h, 1); sV.n += 1
        tok_hf = sV.n

        @blk.tensor
        def _(t):
            wt(t, sV, tok_hf)
            for i in range(4):
                last = t.transpose(ps[:, 1024 + i * 128:1024 + (i + 1) * 128],
                                   scr2[:, i * 128:(i + 1) * 128], z.identf[:])
            last.then_inc(sT.h, 1); sT.n += 1
        tok_hamt = sT.n

        @blk.vector
        def _(v):
            wt(v, sT, tok_hamt)
            for i in range(4):
                last = v.tensor_reduce(reps[:, i:i + 1], ps[:, 1024 + i * 128:1024 + (i + 1) * 128],
                                       mybir.AxisListType.X, ALU.max)
            last.then_inc(sV.h, 1); sV.n += 1
        tok_hamr = sV.n

        @blk.gpsimd
        def _(g):
            wt(g, sV, tok_hamr)
            wt(g, dL)
            for i in range(4):
                g.dma_start(out=z.bounce[i * 128:(i + 1) * 128],
                            in_=reps[:, i:i + 1]).then_inc(dS.h, 16); dS.n += 16
        @blk.sync
        def _(q):
            wt(q, dS)
            wt(q, dL)
            q.dma_start(out=scr2[:, QW:2 * QW],
                        in_=bass.AP(z.bounce.tensor, 0, [[0, 128], [1, 512]])).then_inc(dL.h, 16); dL.n += 16
        tok_ham = dL.n

        @blk.scalar
        def _(s):
            wt(s, sV, tok_hf)
            s.activation(scr[:, QW:2 * QW], scr[:, 0:QW], AF.Sqrt, bias=z.consts[:, 2:3]).then_inc(sA.h, 1); sA.n += 1
        tok_hstd = sA.n

        @blk.vector
        def _(v):
            wt(v, sA, tok_hstd)
            wt(v, dL, tok_ham)
            v.reciprocal(scr[:, QW:2 * QW], scr[:, QW:2 * QW])
            v.drain()
            v.tensor_tensor(scr2[:, 0:QW], scr[:, QW:2 * QW], scr2[:, QW:2 * QW], ALU.mult)
            v.drain()
            v.tensor_scalar(scr2[:, 0:QW], scr2[:, 0:QW], 1e-5, None, ALU.max)
            v.drain()
            v.tensor_scalar(scr2[:, QW:2 * QW], scr2[:, 0:QW], z.consts[:, 10:11], 1.0 / 127.0,
                            ALU.mult, ALU.mult)
            v.drain()
            v.reciprocal(scr2[:, 0:QW], scr2[:, 0:QW])
            v.drain()
            v.tensor_scalar(scr2[:, 0:QW], scr2[:, 0:QW], 127.0, None, ALU.mult)
            v.drain()
            v.tensor_tensor(scr[:, QW:2 * QW], scr[:, QW:2 * QW], scr2[:, 0:QW], ALU.mult)
            v.drain().then_inc(sV.h, 1); sV.n += 1

        @blk.vector
        def _(v):
            wt(v, sT)
            q2r_v = bass.AP(scr, QW, [[1024, 128], [0, 32], [1, 512]])
            v.tensor_tensor(hbuf[:], hbuf[:], q2r_v, ALU.mult)
            v.drain()
            v.tensor_scalar(k2b[:], hbuf[:], MAGIC, MAGIC, ALU.add, ALU.subtract)
            v.drain().then_inc(sV.h, 1); sV.n += 1
        tok_k2 = sV.n

        tch2tok = tok.setdefault("tch2_free" + tag, [(sV, 0), (sV, 0)])
        tch2tok[0] = (sT, sT.n)
        tch2tok[1] = (sT, sT.n)
        ev2tok = tok.setdefault("ffn2_ev" + tag, [(sV, 0), (sV, 0)])
        for of in range(8):
            toks = []
            for sub in range(2):
                hidx = (of * 2 + sub) % 2
                dXs = z.dX[hidx]
                @blk.sync
                def _(q):
                    s_, v_ = tch2tok[hidx]
                    wt(q, s_, v_)
                    q.dma_start(out=tch[:, hidx * 2048:hidx * 2048 + 2048],
                                in_=bass.AP(W["t2_c"].tensor, of * 128 + sub * 16 * 128 * 1024,
                                            [[1024, 128], [128 * 1024, 16], [1, 128]])
                                ).then_inc(dXs.h, 16); dXs.n += 16
                toks.append((hidx, dXs, dXs.n))
            @blk.tensor
            def _(t):
                if of == 0:
                    wt(t, sV, tok_k2)
                    wt(t, sV)
                if of >= 2:
                    s_, v_ = ev2tok[of % 2]
                    wt(t, s_, v_)
                pb = (of % 2) * 512
                for sub in range(2):
                    hidx, dXs, dv = toks[sub]
                    wt(t, dXs, dv)
                    for c in range(16):
                        hc = sub * 16 + c
                        last = t.matmul(ps[:, 1024 + pb:1024 + pb + 512],
                                        tch[:, hidx * 2048 + c * 128:hidx * 2048 + (c + 1) * 128],
                                        k2b[:, hc * QW:(hc + 1) * QW], start=(hc == 0), stop=(hc == 31))
                last.then_inc(sT.h, 1); sT.n += 1
            for sub in range(2):
                hidx = (of * 2 + sub) % 2
                tch2tok[hidx] = (sT, sT.n)
            tok_m2 = sT.n

            @blk.vector
            def _(v):
                wt(v, sT, tok_m2)
                pb = (of % 2) * 512
                v.tensor_tensor(scr2[:, 0:QW], ps[:, 1024 + pb:1024 + pb + QW], scr2[:, QW:2 * QW], ALU.mult)
                v.drain()
                v.scalar_tensor_tensor(xsl(of), scr2[:, 0:QW], z.b2[:, of:of + 1],
                                       xsl(of), ALU.add, ALU.add)
                v.drain().then_inc(sV.h, 1); sV.n += 1
            ev2tok[of % 2] = (sV, sV.n)

    z.tok["ffn_done" + tag] = (sV, z.sV.n)
    return pes


def _ln_stats_rep(z, srcfn, sqb, dstmu, dstms, nfeat_chunks, width, inv_n):
    nc, blk = z.nc, z.blk
    sT, sV, sA = z.sT, z.sV, z.sA
    ps = z.ps
    nq = width // 512
    for pss in range(2):
        for fc in range(nfeat_chunks):
            @blk.vector
            def _(v):
                wt(v, sT)
                v.tensor_copy(sqb[:, 0:width], srcfn(fc))
                v.drain().then_inc(sV.h, 1); sV.n += 1
            tok_cast = sV.n
            if pss == 1:
                @blk.scalar
                def _(s):
                    wt(s, sV, tok_cast)
                    s.activation(sqb[:, 0:width], sqb[:, 0:width], AF.Square).then_inc(sA.h, 1); sA.n += 1
            @blk.tensor
            def _(t):
                wt(t, sV, tok_cast)
                if pss == 1:
                    wt(t, sA)
                if fc == 0:
                    wt(t, sV)
                for q4 in range(nq):
                    t.matmul(ps[:, q4 * 512:(q4 + 1) * 512], z.ones128[:],
                             sqb[:, q4 * 512:(q4 + 1) * 512], start=(fc == 0), stop=(fc == nfeat_chunks - 1))
                t.drain().then_inc(sT.h, 1); sT.n += 1
        tok_mm = z.sT.n
        @blk.vector
        def _(v):
            wt(v, sT, tok_mm)
            dst = dstmu if pss == 0 else dstms
            v.tensor_scalar(dst, ps[:, 0:width], inv_n, None, ALU.mult)
            v.drain().then_inc(sV.h, 1); sV.n += 1


def _rstd_from_stats(z, mu, ms, var_scr, out_rstd):
    nc, blk = z.nc, z.blk
    sV, sA = z.sV, z.sA
    @blk.vector
    def _(v):
        v.tensor_tensor(var_scr, mu, mu, ALU.mult)
        v.drain()
        v.tensor_tensor(var_scr, ms, var_scr, ALU.subtract)
        v.drain().then_inc(sV.h, 1); sV.n += 1
    tok_v = sV.n
    @blk.scalar
    def _(s):
        wt(s, sV, tok_v)
        s.activation(out_rstd, var_scr, AF.Sqrt, bias=z.consts[:, 2:3]).then_inc(sA.h, 1); sA.n += 1
    tok_s = sA.n
    @blk.vector
    def _(v):
        wt(v, sA, tok_s)
        v.reciprocal(out_rstd, out_rstd)
        v.drain().then_inc(sV.h, 1); sV.n += 1


def _phase_e(z, seq):
    nc, blk = z.nc, z.blk
    dL, dS, sT, sV, sA, sG = z.dL, z.dS, z.sT, z.sV, z.sA, z.sG
    W = z.weights
    ps, psbf = z.ps, z.psbf

    pesE = contextlib.ExitStack()
    sb = lambda name, shape, dt=F32: pesE.enter_context(nc.sbuf_tensor(name + z.sfx, shape, dt))
    oT = pesE.enter_context(nc.sbuf_tensor("oT" + z.sfx, [128, 8 * 2048], BF16, side="right"))
    repsE = sb("repsE", [128, 4096], F32)
    scrE = sb("scrE", [128, 2048], F32)
    scrE2 = sb("scrE2", [128, 2048], F32)
    sqbE = sb("sqbE", [128, 2048], BF16)

    tok = z.tok

    _ln_stats_rep(z, lambda fc: z.xs[:, fc * 2048:(fc + 1) * 2048], sqbE,
                  repsE[:, 0:2048], scrE2[:], 8, 2048, 1.0 / 1024.0)
    _rstd_from_stats(z, repsE[:, 0:2048], scrE2[:], scrE[:], repsE[:, 2048:4096])

    pes1 = contextlib.ExitStack()
    sb1 = lambda name, shape, dt=F32: pes1.enter_context(nc.sbuf_tensor(name + z.sfx, shape, dt))
    u3 = pes1.enter_context(nc.sbuf_tensor("u3" + z.sfx, [128, 8 * 2048], BF16, side="right"))
    wbv_b = sb1("wbv_b", [128, 1024], F32)

    @blk.sync
    def _(q):
        wt(q, dL)
        q.dma_start(out=wbv_b[:], in_=bass.AP(z.wbv_row.tensor, 0, [[0, 128], [1, E]])).then_inc(dL.h, 16); dL.n += 16
    tok_wbv = dL.n
    qkT = sb1("qkT", [128, 4096], BF16)
    vp = sb1("vp", [128, 16 * 128], BF16)
    expAB = sb1("expAB", [128, 2 * 2560], BF16)
    EbufE = sb1("EbufE", [128, 2048], BF16)
    wwch = sb1("wwch", [128, 1024], BF16)
    wwf = sb1("wwf", [128, 1024], F32)

    @blk.vector
    def _(v):
        for fc in range(8):
            v.tensor_tensor(scrE[:], z.xs[:, fc * 2048:(fc + 1) * 2048], repsE[:, 0:2048], ALU.subtract)
            v.drain()
            v.tensor_tensor(u3[:, fc * 2048:(fc + 1) * 2048], scrE[:], repsE[:, 2048:4096], ALU.mult)
            v.drain().then_inc(sV.h, 1); sV.n += 1
    tok_u3 = sV.n

    SA, SB = 0, 512
    OO = 1024
    ZA, ZB_ = 1536, 2048
    VB = 2560
    tok["expdoneE"] = (sA, sA.n)
    tok["ozfreeE"] = (sV, sV.n)
    tok["EbufE_free"] = (sT, sT.n)

    for pr in range(8):
        par = pr % 2
        dX = z.dX[par]

        @blk.sync
        def _(q):
            s_, v_ = tok.get("wwch_free", (sV, tok_u3))
            wt(q, s_, v_)
            q.dma_start(out=wwf[:],
                        in_=bass.AP(z.wdirect["wwqT"].tensor, pr * 128, [[1024, 128], [131072, 8], [1, 128]])
                        ).then_inc(dX.h, 16); dX.n += 16
            wt(q, dL)
            q.dma_start(out=expAB[:, 0:2560], in_=W["expb_c"][2 * pr]).then_inc(dL.h, 16); dL.n += 16
            q.dma_start(out=expAB[:, 2560:5120], in_=W["expb_c"][2 * pr + 1]).then_inc(dL.h, 16); dL.n += 16
        tok_wwf = dX.n
        tok_expab = dL.n
        @blk.vector
        def _(v):
            wt(v, dX, tok_wwf)
            wt(v, sT)
            v.tensor_copy(wwch[:], wwf[:]).then_inc(sV.h, 1); sV.n += 1
        tok_ww = sV.n

        for which in range(2):
            if which == 1:
                @blk.sync
                def _(q):
                    wt(q, sV)
                    wt(q, dX)
                    q.dma_start(out=wwf[:],
                                in_=bass.AP(z.wdirect["wwkT"].tensor, pr * 128, [[1024, 128], [131072, 8], [1, 128]])
                                ).then_inc(dX.h, 16); dX.n += 16
                tok_wwkf = dX.n
                @blk.vector
                def _(v):
                    wt(v, dX, tok_wwkf)
                    wt(v, sT)
                    v.tensor_copy(wwch[:], wwf[:]).then_inc(sV.h, 1); sV.n += 1
                tok_ww = sV.n
            @blk.tensor
            def _(t):
                wt(t, sV, tok_ww)
                wt(t, sV)
                for q4 in range(4):
                    for c in range(8):
                        t.matmul(ps[:, q4 * 512:(q4 + 1) * 512],
                                 wwch[:, c * 128:(c + 1) * 128],
                                 u3[:, c * 2048 + q4 * 512:c * 2048 + (q4 + 1) * 512],
                                 start=(c == 0), stop=(c == 7))
                t.drain().then_inc(sT.h, 1); sT.n += 1
            tok_qk = sT.n

            @blk.vector
            def _(v):
                wt(v, sT, tok_qk)
                bias = z.wbq if which == 0 else z.wbk
                if which == 0:
                    v.tensor_scalar(qkT[:, 0:2048], ps[:, 0:2048], bias[:, pr:pr + 1], 1.0 / 32.0, ALU.add, ALU.mult)
                else:
                    v.tensor_scalar(qkT[:, 2048:4096], ps[:, 0:2048], bias[:, pr:pr + 1], None, ALU.add)
                v.drain().then_inc(sV.h, 1); sV.n += 1
        tok_qkev = sV.n

        @blk.sync
        def _(q):
            wt(q, sV)
            wt(q, dX)
            q.dma_start(out=wwf[:],
                        in_=bass.AP(z.wdirect["wwvT"].tensor, pr * 128, [[1024, 128], [131072, 8], [1, 128]])
                        ).then_inc(dX.h, 16); dX.n += 16
        tok_wwvf = dX.n
        @blk.vector
        def _(v):
            wt(v, dX, tok_wwvf)
            wt(v, sT)
            v.tensor_copy(wwch[:], wwf[:]).then_inc(sV.h, 1); sV.n += 1
        tok_wwv = sV.n
        for tt in range(NT):
            vb = VB + (tt % 2) * 512
            @blk.tensor
            def _(t):
                if tt == 0:
                    wt(t, sV, tok_wwv)
                if tt >= 2:
                    s_, v_ = tok["vevE" + str(tt % 2)]
                    wt(t, s_, v_)
                for c in range(8):
                    t.matmul(ps[:, vb:vb + 128], u3[:, c * 2048 + tt * 128:c * 2048 + (tt + 1) * 128],
                             wwch[:, c * 128:(c + 1) * 128], start=(c == 0), stop=(c == 7))
                t.drain().then_inc(sT.h, 1); sT.n += 1
            tok_vm = sT.n
            @blk.vector
            def _(v):
                wt(v, sT, tok_vm)
                if tt == 0 and pr == 0:
                    wt(v, dL, tok_wbv)
                v.tensor_tensor(vp[:, tt * 128:(tt + 1) * 128], ps[:, vb:vb + 128],
                                wbv_b[:, pr * 128:(pr + 1) * 128], ALU.add)
                v.drain().then_inc(sV.h, 1); sV.n += 1
            tok["vevE" + str(tt % 2)] = (sV, sV.n)
        tok_vdone = sV.n

        for which in range(2):
            @blk.vector
            def _(v):
                if which == 0:
                    wt(v, sV, tok_vdone)
                v.tensor_tensor(sqbE[:], qkT[:, which * 2048:(which + 1) * 2048],
                                qkT[:, which * 2048:(which + 1) * 2048], ALU.mult)
                v.drain().then_inc(sV.h, 1); sV.n += 1
            tok_sqq = sV.n
            @blk.tensor
            def _(t):
                wt(t, sV, tok_sqq)
                if which == 0:
                    wt(t, sV)
                for q4 in range(4):
                    t.matmul(ps[:, q4 * 512:(q4 + 1) * 512], z.ones128[:],
                             sqbE[:, q4 * 512:(q4 + 1) * 512], start=True, stop=True)
                t.drain().then_inc(sT.h, 1); sT.n += 1
            tok_nmm = sT.n
            @blk.vector
            def _(v):
                wt(v, sT, tok_nmm)
                v.tensor_reduce(z.consts2[:, 4 + which:5 + which], ps[:, 0:2048],
                                mybir.AxisListType.X, ALU.max)
                v.drain().then_inc(sV.h, 1); sV.n += 1
        tok_bam = sV.n

        @blk.vector
        def _(v):
            v.tensor_tensor(z.consts2[:, 6:7], z.consts2[:, 4:5], z.consts2[:, 5:6], ALU.mult)
            v.drain().then_inc(sV.h, 1); sV.n += 1
        tok_bm2 = sV.n

        @blk.scalar
        def _(s):
            wt(s, sV, tok_bm2)
            s.activation(z.consts2[:, 6:7], z.consts2[:, 6:7], AF.Sqrt).then_inc(sA.h, 1); sA.n += 1
        tok_bsq = sA.n

        @blk.vector
        def _(v):
            wt(v, sA, tok_bsq)
            v.tensor_scalar(z.consts2[:, 6:7], z.consts2[:, 6:7], -1.0, None, ALU.mult)
            v.drain().then_inc(sV.h, 1); sV.n += 1
        tok_bb = sV.n

        for qb in range(4):
            q0 = qb * 512
            nkt = 4 * (qb + 1)
            sT0, sA0, sV0 = sT.n, sA.n, sV.n
            oz_s, oz_v = tok["ozfreeE"]

            def zov(t, kt, nkt):
                eb = (kt % 2) * 1024
                fl = dict(start=(kt == 0), stop=(kt == nkt - 1))
                t.matmul(ps[:, ZA:ZA + 512], z.ones128[:], EbufE[:, eb:eb + 512], **fl)
                t.matmul(ps[:, ZB_:ZB_ + 512], z.ones128[:], EbufE[:, eb + 512:eb + 1024], **fl)
                t.matmul(ps[0:64, OO:OO + 512], vp[:, kt * 128:kt * 128 + 64], EbufE[:, eb:eb + 512],
                         tile_position=(0, 0), **fl)
                return t.matmul(ps[64:128, OO:OO + 512], vp[:, kt * 128 + 64:kt * 128 + 128],
                                EbufE[:, eb + 512:eb + 1024],
                                tile_position=(0, 64), skip_group_check=True, **fl)

            @blk.tensor
            def _(t):
                for kt in range(nkt):
                    wt(t, sA, sA0 + kt)
                    if kt == 0:
                        wt(t, oz_s, oz_v)
                        wt(t, sV, tok_bb)
                    t.matmul(ps[:, SA:SA + 512], qkT[0:64, 2048 + kt * 128:2048 + (kt + 1) * 128],
                             qkT[0:64, q0:q0 + 512], start=True, stop=True, tile_position=(0, 0))
                    t.matmul(ps[:, SB:SB + 512], qkT[64:128, 2048 + kt * 128:2048 + (kt + 1) * 128],
                             qkT[64:128, q0:q0 + 512], start=True, stop=True,
                             tile_position=(64, 0)).then_inc(sT.h, 1); sT.n += 1
                    if kt > 0:
                        wt(t, sV, sV0 + kt)
                        zov(t, kt - 1, nkt)
                wt(t, sV, sV0 + nkt)
                zov(t, nkt - 1, nkt).then_inc(sT.h, 1); sT.n += 1

            @blk.scalar
            def _(s):
                for kt in range(nkt):
                    eb = (kt % 2) * 1024
                    wt(s, sT, sT0 + kt + 1)
                    s.activation(EbufE[:, eb:eb + 1024], ps[:, 0:1024], AF.Exp,
                                 bias=z.consts2[:, 6:7]).then_inc(sA.h, 1); sA.n += 1

            @blk.vector
            def _(v):
                wt(v, dL, tok_expab)
                for kt in range(nkt):
                    eb = (kt % 2) * 1024
                    co = q0 - kt * 128 + 512
                    wt(v, sA, sA0 + kt + 1)
                    v.tensor_tensor(EbufE[:, eb:eb + 512], EbufE[:, eb:eb + 512],
                                    expAB[:, co:co + 512], ALU.mult)
                    v.tensor_tensor(EbufE[:, eb + 512:eb + 1024], EbufE[:, eb + 512:eb + 1024],
                                    expAB[:, 2560 + co:2560 + co + 512], ALU.mult)
                    v.drain().then_inc(sV.h, 1); sV.n += 1
            tok_oz = sT.n

            @blk.vector
            def _(v):
                wt(v, sT, tok_oz)
                v.reciprocal(scrE[:, 0:512], ps[:, ZA:ZA + 512])
                v.reciprocal(scrE[:, 512:1024], ps[:, ZB_:ZB_ + 512])
                v.drain()
                v.tensor_tensor(oT[0:64, pr * 2048 + q0:pr * 2048 + q0 + 512],
                                ps[0:64, OO:OO + 512], scrE[0:64, 0:512], ALU.mult)
                v.tensor_tensor(oT[64:128, pr * 2048 + q0:pr * 2048 + q0 + 512],
                                ps[64:128, OO:OO + 512], scrE[64:128, 512:1024], ALU.mult)
                v.drain().then_inc(sV.h, 1); sV.n += 1
            tok["ozfreeE"] = (sV, sV.n)
            tok["expdoneE"] = (sA, sA.n)
        tok["wwch_free"] = (sT, sT.n)

    pes1.close()
    _quiesce(z)

    pes2 = contextlib.ExitStack()
    sb2 = lambda name, shape, dt=F32: pes2.enter_context(nc.sbuf_tensor(name + z.sfx, shape, dt))
    ybuf = sb2("ybuf", [128, 8 * 1024], F32)
    pwch = sb2("pwch", [128, 1024], BF16)
    pwf = sb2("pwf", [128, 1024], F32)

    for th in range(2):
        t0c = th * 1024

        @blk.vector
        def _(v):
            wt(v, sT)
            xs_h = bass.AP(z.xs, t0c, [[16384, 128], [2048, 8], [1, 1024]])
            mu_h = bass.AP(repsE, t0c, [[4096, 128], [0, 8], [1, 1024]])
            rs_h = bass.AP(repsE, 2048 + t0c, [[4096, 128], [0, 8], [1, 1024]])
            v.tensor_tensor(ybuf[:], xs_h, mu_h, ALU.subtract)
            v.drain()
            v.tensor_tensor(ybuf[:], ybuf[:], rs_h, ALU.mult)
            v.drain().then_inc(sV.h, 1); sV.n += 1
        tok_uh = sV.n

        for of in range(8):
            dX = z.dX[of % 2]
            @blk.sync
            def _(q):
                s_, v_ = tok.get("pwf_free", (sV, sV.n))
                wt(q, s_, v_)
                q.dma_start(out=pwf[:],
                            in_=bass.AP(z.wdirect["projwT"].tensor, of * 128, [[1024, 128], [131072, 8], [1, 128]])
                            ).then_inc(dX.h, 16); dX.n += 16
            tok_pwf = dX.n
            @blk.vector
            def _(v):
                wt(v, dX, tok_pwf)
                s_, v_ = tok.get("pwch_free", (sV, 0))
                wt(v, s_, v_)
                v.tensor_copy(pwch[:], pwf[:]).then_inc(sV.h, 1); sV.n += 1
            tok_pw = sV.n
            tok["pwf_free"] = (sV, tok_pw)

            for q4 in range(2):
                qq0 = t0c + q4 * 512
                @blk.tensor
                def _(t):
                    if q4 == 0:
                        wt(t, sV, tok_pw)
                    wt(t, sV)
                    for oc in range(8):
                        t.matmul(ps[:, (q4 % 2) * 512:(q4 % 2) * 512 + 512],
                                 pwch[:, oc * 128:(oc + 1) * 128],
                                 oT[:, oc * 2048 + qq0:oc * 2048 + qq0 + 512],
                                 start=(oc == 0), stop=(oc == 7))
                    t.drain().then_inc(sT.h, 1); sT.n += 1
                tok_pm = sT.n
                tok["pwch_free"] = (sT, sT.n)

                @blk.vector
                def _(v):
                    wt(v, sT, tok_pm)
                    if of == 0 and q4 == 0:
                        wt(v, sV, tok_uh)
                    yslc = ybuf[:, of * 1024 + q4 * 512:of * 1024 + (q4 + 1) * 512]
                    v.scalar_tensor_tensor(yslc, ps[:, (q4 % 2) * 512:(q4 % 2) * 512 + 512],
                                           z.pbias[:, of:of + 1], yslc, ALU.add, ALU.add)
                    v.drain().then_inc(sV.h, 1); sV.n += 1

        _ln_stats_rep(z, lambda fc: ybuf[:, fc * 1024:(fc + 1) * 1024], sqbE,
                      scrE[:, 0:1024], scrE2[:, 0:1024], 8, 1024, 1.0 / 1024.0)
        _rstd_from_stats(z, scrE[:, 0:1024], scrE2[:, 0:1024], scrE2[:, 1024:2048], scrE[:, 1024:2048])

        @blk.vector
        def _(v):
            mu2_h = bass.AP(scrE, 0, [[2048, 128], [0, 8], [1, 1024]])
            rs2_h = bass.AP(scrE, 1024, [[2048, 128], [0, 8], [1, 1024]])
            lng_h = bass.AP(z.lng, 0, [[8, 128], [1, 8], [0, 1024]])
            lnb_h = bass.AP(z.lnb, 0, [[8, 128], [1, 8], [0, 1024]])
            xs_h = bass.AP(z.xs, t0c, [[16384, 128], [2048, 8], [1, 1024]])
            v.tensor_tensor(ybuf[:], ybuf[:], mu2_h, ALU.subtract)
            v.drain()
            v.tensor_tensor(ybuf[:], ybuf[:], rs2_h, ALU.mult)
            v.drain()
            v.tensor_tensor(ybuf[:], ybuf[:], lng_h, ALU.mult)
            v.drain()
            v.tensor_tensor(ybuf[:], ybuf[:], lnb_h, ALU.add)
            v.drain()
            v.tensor_tensor(xs_h, xs_h, ybuf[:], ALU.add)
            v.drain().then_inc(sV.h, 1); sV.n += 1

    pes2.close()
    z.tok["phaseE_done"] = (sV, z.sV.n)
    return pesE


# version 10 (final)
# speedup vs baseline: 1.2184x; 1.0275x over previous
import math
import contextlib
import numpy as np

import concourse.bass as bass
import concourse.mybir as mybir
from concourse.bass_utils import run_bass_kernel_spmd

F32 = mybir.dt.float32
BF16 = mybir.dt.bfloat16
AF = mybir.ActivationFunctionType
ALU = mybir.AluOpType

MAGIC = 12582912.0
S = 2048
E = 1024
NT = 16
NC_ = 8
NH = 16

_DEBUG_TAPS = []
_BUILD_OPTS = {"n_seq": 2, "stop_after": None}


def _host_prepare(inp):
    x = np.ascontiguousarray(inp["L0_attn_a_x"])
    perm = np.empty(E, dtype=np.int64)
    idx = 0
    for g in range(8):
        half = 0 if g < 4 else 1
        hh = (g % 4) * 4
        for h2 in range(4):
            h = hh + h2
            for d in range(32):
                perm[idx] = half * 512 + h * 32 + d
                idx += 1
    wq = np.ascontiguousarray(inp["L0_attn_a_Wq"])
    wk = np.ascontiguousarray(inp["L0_attn_a_Wk"])
    wqT = np.ascontiguousarray(wq[perm].T)
    wkT = np.ascontiguousarray(wk[perm].T)
    wvT = np.ascontiguousarray(inp["L0_attn_a_Wv"].T)
    wprojT = np.ascontiguousarray(inp["L0_attn_a_Wproj"].T)
    ww = inp["L0_attn_b_W_w"]
    wwqT = np.ascontiguousarray(ww[0:1024].T)
    wwkT = np.ascontiguousarray(ww[1024:2048].T)
    wwvT = np.ascontiguousarray(ww[2048:3072].T)
    projwT = np.ascontiguousarray(inp["L0_attn_b_proj_w"].T)
    w1T = np.ascontiguousarray(inp["L0_ffn_w1"].T)
    w2T = np.ascontiguousarray(inp["L0_ffn_w2"].T)

    wb = inp["L0_attn_b_W_b"]
    def pcol(v):
        v = np.asarray(v).reshape(-1)
        return np.ascontiguousarray(v.reshape(-1, 128).T)
    wbq = pcol(wb[0:1024] * 1.0) if False else pcol(wb[0:1024])
    wbk = pcol(wb[1024:2048])
    wbv_row = np.ascontiguousarray(wb[2048:3072])
    b1 = pcol(inp["L0_ffn_b1"])
    b2 = pcol(inp["L0_ffn_b2"])
    pb = pcol(inp["L0_attn_b_proj_b"])
    lng = pcol(inp["L0_attn_b_ln_g"])
    lnb = pcol(inp["L0_attn_b_ln_b"])

    rpb = np.asarray(inp["L0_attn_b_rpb"])
    pi = np.arange(128)[:, None] - (np.arange(2560)[None, :] - 512) + 2047
    pi = np.clip(pi, 0, 4094)
    ZB = np.ascontiguousarray(rpb[pi, :].transpose(2, 0, 1))

    maskf = (np.arange(128)[:, None] <= (np.arange(2560)[None, :] - 512)).astype(np.float32)
    ident = np.eye(128, dtype=np.float32)
    import ml_dtypes
    i16 = np.arange(16, dtype=np.float64)
    inv_freq = 1.0 / (10000.0 ** (i16 / 256.0))
    t = np.arange(S, dtype=np.float64)
    fr = np.outer(t, inv_freq)
    cosb = np.cos(fr).astype(np.float32).astype(ml_dtypes.bfloat16).astype(np.float32)
    sinb = np.sin(fr).astype(np.float32).astype(ml_dtypes.bfloat16).astype(np.float32)
    def tmtab(tab):
        return np.ascontiguousarray(tab.reshape(NT, 128, 16).transpose(1, 0, 2).reshape(128, NT * 16))
    cosq = tmtab(cosb * 0.125)
    sinq = tmtab(sinb * 0.125)
    cosk = tmtab(cosb)
    sink = tmtab(sinb)

    xv = x.reshape(16, NT, 128, NC_, 128).transpose(0, 1, 4, 3, 2)
    xt_all = np.ascontiguousarray(xv.reshape(16, NT, 128, E))

    per_core = []
    for core in range(8):
        d = {
            "xt": np.ascontiguousarray(xt_all[2 * core:2 * core + 2]),
            "wqT": wqT, "wkT": wkT, "wvT": wvT, "wprojT": wprojT,
            "wwqT": wwqT, "wwkT": wwkT, "wwvT": wwvT, "projwT": projwT,
            "w1T": w1T, "w2T": w2T,
            "wbq": wbq, "wbk": wbk, "wbv_row": wbv_row,
            "b1": b1, "b2": b2, "pbias": pb, "lng": lng, "lnb": lnb,
            "ZB": ZB, "maskf": maskf, "ident": ident,
            "cosq": cosq, "sinq": sinq, "cosk": cosk, "sink": sink,
            "lq1": np.asarray(inp["L0_attn_a_lambda_q1"]).reshape(1, 64),
            "lq2": np.asarray(inp["L0_attn_a_lambda_q2"]).reshape(1, 64),
            "lk1": np.asarray(inp["L0_attn_a_lambda_k1"]).reshape(1, 64),
            "lk2": np.asarray(inp["L0_attn_a_lambda_k2"]).reshape(1, 64),
            "alpha": np.asarray(inp["L0_attn_a_alpha"]).reshape(1, 1),
        }
        per_core.append({k: np.ascontiguousarray(v.astype(np.float32, copy=False)) for k, v in d.items()})
    return per_core


def _host_gather(results):
    out = np.empty((16, S, E), dtype=np.float32)
    for core, r in enumerate(results):
        y = r["y"]
        for s in range(2):
            out[2 * core + s] = y[s].reshape(E, S).T
    return out


class Sem:
    __slots__ = ("h", "n", "name")


class Ctx:
    def __init__(self, nc, es):
        self.nc = nc
        self.es = es
        self._sems = []

    def sem(self, name):
        s = Sem()
        s.h = self.es.enter_context(self.nc.semaphore(name))
        s.n = 0
        s.name = name
        self._sems.append(s)
        return s


def inc(instr, sem, k=1):
    instr.then_inc(sem.h, k)
    sem.n += k
    return sem.n


def wt(eng, sem, v=None):
    eng.wait_ge(sem.h, sem.n if v is None else v)


def build_nc():
    nc = bass.Bass("TRN2", target_bir_lowering=False)
    es = contextlib.ExitStack()
    with es:
        _build_body(nc, es)
    return nc


class NS:
    pass


def _build_body(nc, es):
    dt_in = {}
    def din(name, shape):
        dt_in[name] = nc.dram_tensor(name, shape, F32, kind="ExternalInput").ap()
        return dt_in[name]

    xt = din("xt", [2, NT, 128, E])
    wqT = din("wqT", [E, E]); wkT = din("wkT", [E, E]); wvT = din("wvT", [E, 2 * E])
    wprojT = din("wprojT", [2 * E, 2 * E])
    wwqT = din("wwqT", [E, E]); wwkT = din("wwkT", [E, E]); wwvT = din("wwvT", [E, E])
    projwT = din("projwT", [E, E])
    w1T = din("w1T", [E, 4 * E]); w2T = din("w2T", [4 * E, E])
    wbq = din("wbq", [128, 8]); wbk = din("wbk", [128, 8]); wbv_row = din("wbv_row", [E])
    b1 = din("b1", [128, 32]); b2 = din("b2", [128, 8])
    pbias = din("pbias", [128, 8]); lng = din("lng", [128, 8]); lnb = din("lnb", [128, 8])
    ZBin = din("ZB", [16, 128, 2560])
    maskf = din("maskf", [128, 2560]); identin = din("ident", [128, 128])
    cosq_in = din("cosq", [128, 256]); sinq_in = din("sinq", [128, 256])
    cosk_in = din("cosk", [128, 256]); sink_in = din("sink", [128, 256])
    lq1 = din("lq1", [1, 64]); lq2 = din("lq2", [1, 64])
    lk1 = din("lk1", [1, 64]); lk2 = din("lk2", [1, 64])
    alpha_in = din("alpha", [1, 1])

    yout = nc.dram_tensor("y", [2, NC_, 128, S], F32, kind="ExternalOutput").ap()
    taps = {name: nc.dram_tensor("tap_" + name, shape, dt, kind="ExternalOutput").ap()
            for (name, shape, dt) in _DEBUG_TAPS}

    wq_c = nc.dram_tensor("wq_c", [E, E], BF16).ap()
    wk_c = nc.dram_tensor("wk_c", [E, E], BF16).ap()
    wv_c = nc.dram_tensor("wv_c", [E, 2 * E], BF16).ap()
    wproj_c = nc.dram_tensor("wproj_c", [2 * E, 2 * E], BF16).ap()
    wwq_c = nc.dram_tensor("wwq_c", [E, E], BF16).ap()
    wwk_c = nc.dram_tensor("wwk_c", [E, E], BF16).ap()
    wwv_c = nc.dram_tensor("wwv_c", [E, E], BF16).ap()
    projw_c = nc.dram_tensor("projw_c", [E, E], BF16).ap()
    t1_c = nc.dram_tensor("t1_c", [E, 4 * E], BF16).ap()
    t2_c = nc.dram_tensor("t2_c", [4 * E, E], BF16).ap()
    expb_c = nc.dram_tensor("expb_c", [16, 128, 2560], BF16).ap()
    bounce_c = nc.dram_tensor("bounce_c", [4096], F32).ap()

    ctx = Ctx(nc, es)
    dL = ctx.sem("dL")
    dX0 = ctx.sem("dX0")
    dY0 = ctx.sem("dY0")
    dY1 = ctx.sem("dY1")
    dX1 = ctx.sem("dX1")
    dS = ctx.sem("dS")
    sT = ctx.sem("sT")
    sV = ctx.sem("sV")
    sA = ctx.sem("sA")
    sG = ctx.sem("sG")

    sb = lambda name, shape, dt=F32: es.enter_context(nc.sbuf_tensor(name, shape, dt))

    z = NS()
    z.ps = es.enter_context(nc.psum_tensor("ps", [128, 3584], F32))
    z.psbf = es.enter_context(nc.psum_tensor("psbf", [128, 1024], BF16))
    z.ident = sb("identb", [128, 128], BF16)
    z.ones_f = sb("ones_f", [128, 1], F32)
    z.ones_b = sb("ones_b", [128, 1], BF16)
    z.ones128 = sb("ones128", [128, 128], BF16)
    z.ones128f = sb("ones128f", [128, 128], F32)
    z.identf = sb("identf", [128, 128], F32)
    z.consts = sb("consts", [128, 16], F32)
    z.rows = sb("rows", [1, 1024], F32)
    z.b1 = sb("b1_sb", [128, 32], F32); z.b2 = sb("b2_sb", [128, 8], F32)
    z.wbq = sb("wbq_sb", [128, 8], F32); z.wbk = sb("wbk_sb", [128, 8], F32)
    z.pbias = sb("pbias_sb", [128, 8], F32)
    z.lng = sb("lng_sb", [128, 8], F32); z.lnb = sb("lnb_sb", [128, 8], F32)
    z.consts2 = sb("consts2", [128, 8], F32)
    prep_es = contextlib.ExitStack()
    z.lrows = prep_es.enter_context(nc.sbuf_tensor("lrows", [1, 512], F32))
    z.maskf = prep_es.enter_context(nc.sbuf_tensor("maskf_sb", [128, 2560], F32))
    z.wchunk = prep_es.enter_context(nc.sbuf_tensor("wchunk", [128, 4096], F32))
    z.wchunk2 = prep_es.enter_context(nc.sbuf_tensor("wchunk2", [128, 4096], F32))
    z.bchunk = prep_es.enter_context(nc.sbuf_tensor("bchunk", [128, 4096], BF16))
    z.tpart = prep_es.enter_context(nc.sbuf_tensor("tpart", [1, 4096], F32))
    z.sg1 = prep_es.enter_context(nc.sbuf_tensor("sg1", [128, 4096], BF16))
    z.sg2 = prep_es.enter_context(nc.sbuf_tensor("sg2", [128, 4096], BF16))
    z.bch3 = prep_es.enter_context(nc.sbuf_tensor("bch3", [128, 2560], BF16))
    z.bch4 = prep_es.enter_context(nc.sbuf_tensor("bch4", [128, 4096], BF16))

    blk = es.enter_context(nc.Block())
    z.nc = nc; z.blk = blk; z.ctx = ctx
    z.dL, z.dS, z.sT, z.sV, z.sA, z.sG = dL, dS, sT, sV, sA, sG
    z.dX = [dX0, dX1]
    z.dY = [dY0, dY1]
    z.taps = taps
    z.yout = yout
    z.xt = xt
    z.dram_tabs = (cosq_in, sinq_in, cosk_in, sink_in)
    z.wbv_row = wbv_row
    z.bounce = bounce_c

    @blk.sync
    def _(q):
        q.dma_start(out=z.maskf[:], in_=maskf).then_inc(dL.h, 16); dL.n += 16
        q.dma_start(out=z.b1[:], in_=b1).then_inc(dL.h, 16); dL.n += 16
        q.dma_start(out=z.b2[:], in_=b2).then_inc(dL.h, 16); dL.n += 16
        q.dma_start(out=z.wbq[:], in_=wbq).then_inc(dL.h, 16); dL.n += 16
        q.dma_start(out=z.wbk[:], in_=wbk).then_inc(dL.h, 16); dL.n += 16
        q.dma_start(out=z.pbias[:], in_=pbias).then_inc(dL.h, 16); dL.n += 16
        q.dma_start(out=z.lng[:], in_=lng).then_inc(dL.h, 16); dL.n += 16
        q.dma_start(out=z.lnb[:], in_=lnb).then_inc(dL.h, 16); dL.n += 16
        q.dma_start(out=z.lrows[0:1, 0:64], in_=lq1).then_inc(dL.h, 16); dL.n += 16
        q.dma_start(out=z.lrows[0:1, 64:128], in_=lk1).then_inc(dL.h, 16); dL.n += 16
        q.dma_start(out=z.lrows[0:1, 128:192], in_=lq2).then_inc(dL.h, 16); dL.n += 16
        q.dma_start(out=z.lrows[0:1, 192:256], in_=lk2).then_inc(dL.h, 16); dL.n += 16
        q.dma_start(out=z.lrows[0:1, 256:257] if False else z.rows[0:1, 0:1], in_=alpha_in).then_inc(dL.h, 16); dL.n += 16
        q.dma_start(out=z.wchunk[:, 0:128], in_=identin).then_inc(dL.h, 16); dL.n += 16
        q.dma_start(out=z.identf[:], in_=identin).then_inc(dL.h, 16); dL.n += 16
    ld_consts = dL.n

    @blk.vector
    def _(v):
        wt(v, dL, ld_consts)
        v.memset(z.consts[:], 0.0)
        v.memset(z.consts2[:], 0.0)
        v.drain()
        v.memset(z.ones_f[:], 1.0)
        v.memset(z.ones_b[:], 1.0)
        v.memset(z.ones128[:], 1.0)
        v.memset(z.ones128f[:], 1.0)
        v.memset(z.consts[:, 0:1], -0.5)
        v.memset(z.consts[:, 1:2], 0.5)
        v.memset(z.consts[:, 2:3], 1e-5)
        v.memset(z.consts[:, 3:4], 1.0 / 1024.0)
        v.memset(z.consts[:, 8:9], 1e-30)
        v.tensor_copy(z.ident[:], z.wchunk[:, 0:128])
        v.drain()
        v.tensor_tensor(z.lrows[0:1, 256:320], z.lrows[0:1, 0:64], z.lrows[0:1, 64:128], ALU.mult)
        v.tensor_tensor(z.lrows[0:1, 320:384], z.lrows[0:1, 128:192], z.lrows[0:1, 192:256], ALU.mult)
        v.drain()
        v.tensor_reduce(z.lrows[0:1, 384:385], z.lrows[0:1, 256:320], mybir.AxisListType.X, ALU.add)
        v.tensor_reduce(z.lrows[0:1, 385:386], z.lrows[0:1, 320:384], mybir.AxisListType.X, ALU.add)
        v.drain().then_inc(sV.h, 1); sV.n += 1
    v_dots = sV.n

    @blk.scalar
    def _(s):
        wt(s, sV, v_dots)
        s.activation(z.lrows[0:1, 386:388], z.lrows[0:1, 384:386], AF.Exp).then_inc(sA.h, 1); sA.n += 1
    a_exps = sA.n

    @blk.vector
    def _(v):
        wt(v, sA, a_exps)
        v.tensor_tensor(z.lrows[0:1, 388:389], z.lrows[0:1, 386:387], z.lrows[0:1, 387:388], ALU.subtract)
        v.drain()
        v.tensor_scalar(z.lrows[0:1, 389:390], z.lrows[0:1, 388:389], 0.2, None, ALU.add)
        v.drain()
        v.tensor_scalar(z.lrows[0:1, 390:391], z.lrows[0:1, 389:390], -1.0, None, ALU.mult)
        v.tensor_scalar(z.rows[0:1, 1:2], z.rows[0:1, 0:1], 2.0, 1.0, ALU.mult, ALU.add)
        v.drain()
        v.tensor_scalar(z.rows[0:1, 2:3], z.rows[0:1, 1:2], 1.0 / math.pi, None, ALU.mult)
        v.scalar_tensor_tensor(z.rows[0:1, 3:4], z.rows[0:1, 1:2], 0.5, z.rows[0:1, 0:1], ALU.mult, ALU.subtract)
        v.drain().then_inc(sV.h, 1); sV.n += 1
    v_consts = sV.n

    @blk.gpsimd
    def _(g):
        wt(g, sV, v_consts)
        g.dma_start(out=z.bounce[0:1], in_=z.lrows[0:1, 389:390]).then_inc(dS.h, 16); dS.n += 16
        g.dma_start(out=z.bounce[1:2], in_=z.lrows[0:1, 390:391]).then_inc(dS.h, 16); dS.n += 16
        g.dma_start(out=z.bounce[2:3], in_=z.rows[0:1, 2:3]).then_inc(dS.h, 16); dS.n += 16
        g.dma_start(out=z.bounce[3:4], in_=z.rows[0:1, 3:4]).then_inc(dS.h, 16); dS.n += 16
    @blk.sync
    def _(q):
        wt(q, dS)
        wt(q, dL)
        q.dma_start(out=z.consts[:, 4:8], in_=bass.AP(z.bounce.tensor, 0, [[0, 128], [1, 4]])).then_inc(dL.h, 16); dL.n += 16
    z.gp_consts = dL.n

    dY = z.dY
    z.ldtok = [(sV, v_consts), (sV, v_consts)]

    def cache_weight(src_t, dst, rows, cols):
        half = 4096 // 2
        store_cnt = [0, 0]
        for k in range(rows // 128):
            r0 = k * 128
            p = k % 2
            dXk, dYk = z.dX[p], dY[p]
            buf = (z.wchunk, z.wchunk2)[p]
            outb = (z.bchunk, z.bch4)[p]
            @blk.sync
            def _(q):
                s_, v_ = z.ldtok[p]
                wt(q, s_, v_)
                q.dma_start(out=buf[:, 0:cols], in_=src_t[r0:r0 + 128, :]).then_inc(dXk.h, 16); dXk.n += 16
            tok_l = dXk.n
            @blk.vector
            def _(v):
                wt(v, dXk, tok_l)
                wt(v, dYk, 16 * store_cnt[p])
                v.tensor_copy(outb[:, 0:cols], buf[:, 0:cols]).then_inc(sV.h, 1); sV.n += 1
            z.ldtok[p] = (sV, sV.n)
            tok_c = sV.n
            @blk.gpsimd
            def _(g):
                wt(g, sV, tok_c)
                g.dma_start(out=dst[r0:r0 + 128, :], in_=outb[:, 0:cols]).then_inc(dYk.h, 16); dYk.n += 16
            store_cnt[p] += 1

    cache_weight(wprojT, wproj_c, 2 * E, 2 * E)
    z.wdirect = dict(wqT=wqT, wkT=wkT, wvT=wvT, wwqT=wwqT, wwkT=wwkT, wwvT=wwvT,
                     projwT=projwT)

    def tern_weight(srcT, dstT, rows, cols, mean_count, s_col, inv_col):
        nch = rows // 128
        bufs = (z.wchunk, z.wchunk2)
        ev_tok = [0]
        for r0i in range(nch):
            r0 = r0i * 128
            p = r0i % 2
            buf = bufs[p]
            dXk = z.dX[p]
            @blk.sync
            def _(q):
                s_, v_ = z.ldtok[p]
                wt(q, s_, v_)
                q.dma_start(out=buf[:, 0:cols], in_=srcT[r0:r0 + 128, :]).then_inc(dXk.h, 16); dXk.n += 16
            tok_l = dXk.n
            @blk.scalar
            def _(s):
                wt(s, dXk, tok_l)
                s.activation(buf[:, 0:cols], buf[:, 0:cols], AF.Abs).then_inc(sA.h, 1); sA.n += 1
            tok_a = sA.n
            @blk.tensor
            def _(t):
                wt(t, sA, tok_a)
                for ji, j in enumerate(range(0, cols, 512)):
                    if rows == 1024:
                        k = ji % 4
                        fl = dict(start=(r0i % 4 == 0 and ji < 4), stop=(r0i % 4 == 3 and ji >= 4))
                        if r0i % 4 == 0 and ji == 0:
                            wt(t, sV, ev_tok[0])
                    else:
                        k = ji * 2 + (r0i // 8) % 2
                        fl = dict(start=(r0i % 8 == 0), stop=(r0i % 8 == 7))
                        if r0i % 16 == 0 and ji == 0:
                            wt(t, sV, ev_tok[0])
                    t.matmul(z.ps[0:1, k * 512:(k + 1) * 512], z.ones_f[:], buf[:, j:j + 512], **fl)
                t.drain().then_inc(sT.h, 1); sT.n += 1
            z.ldtok[p] = (sT, sT.n)
            if r0i == nch // 2 - 1 or r0i == nch - 1:
                half = 0 if r0i == nch // 2 - 1 else 1
                @blk.vector
                def _(v):
                    wt(v, sT)
                    v.tensor_copy(z.tpart[0:1, half * 2048:(half + 1) * 2048], z.ps[0:1, 0:2048]).then_inc(sV.h, 1); sV.n += 1
                ev_tok[0] = sV.n
        @blk.vector
        def _(v):
            wt(v, sV)
            wt(v, sA)
            v.tensor_scalar(z.wchunk[0:1, 0:4096], z.tpart[0:1, :], 128.0, MAGIC, ALU.mult, ALU.add)
            v.drain()
            v.tensor_scalar(z.wchunk[0:1, 0:4096], z.wchunk[0:1, 0:4096], MAGIC, 1.0 / 128.0,
                            ALU.subtract, ALU.mult)
            v.drain()
            v.tensor_tensor(z.wchunk2[0:1, 0:4096], z.tpart[0:1, :], z.wchunk[0:1, 0:4096], ALU.subtract)
            v.drain()
            v.tensor_reduce(z.rows[0:1, 8:9], z.wchunk[0:1, 0:4096], mybir.AxisListType.X, ALU.add)
            v.tensor_reduce(z.rows[0:1, 10:11], z.wchunk2[0:1, 0:4096], mybir.AxisListType.X, ALU.add)
            v.drain()
            v.tensor_tensor(z.rows[0:1, 8:9], z.rows[0:1, 8:9], z.rows[0:1, 10:11], ALU.add)
            v.drain()
            v.tensor_scalar(z.rows[0:1, 8:9], z.rows[0:1, 8:9], 1.0 / mean_count, None, ALU.mult)
            v.drain()
            v.tensor_scalar(z.rows[0:1, 8:9], z.rows[0:1, 8:9], 1e-5, None, ALU.max)
            v.drain()
            v.reciprocal(z.rows[0:1, 9:10], z.rows[0:1, 8:9])
            v.drain().then_inc(sV.h, 1); sV.n += 1
        tok_red = sV.n
        @blk.gpsimd
        def _(g):
            wt(g, sV, tok_red)
            wt(g, dL)
            g.dma_start(out=z.bounce[0:2], in_=z.rows[0:1, 8:10]).then_inc(dS.h, 16); dS.n += 16
        @blk.sync
        def _(q):
            wt(q, dS)
            wt(q, dL)
            q.dma_start(out=z.consts[:, s_col:s_col + 1], in_=bass.AP(z.bounce.tensor, 0, [[0, 128], [1, 1]])).then_inc(dL.h, 16); dL.n += 16
            q.dma_start(out=z.consts[:, inv_col:inv_col + 1], in_=bass.AP(z.bounce.tensor, 1, [[0, 128], [1, 1]])).then_inc(dL.h, 16); dL.n += 16
        @blk.gpsimd
        def _(g):
            wt(g, dL)
            g.drain().then_inc(sG.h, 1); sG.n += 1
        tok_sg = sG.n

        add_toks = [(sV, tok_red)]
        store_cnt = [dY[0].n // 16, dY[1].n // 16]
        for r0i in range(nch):
            r0 = r0i * 128
            p = r0i % 2
            buf = bufs[p]
            dXk, dYk = z.dX[p], dY[p]
            @blk.sync
            def _(q):
                if r0i < 2:
                    wt(q, sV, tok_red)
                s_, v_ = z.ldtok[p]
                wt(q, s_, v_)
                q.dma_start(out=buf[:, 0:cols], in_=srcT[r0:r0 + 128, :]).then_inc(dXk.h, 16); dXk.n += 16
            tok_l = dXk.n
            @blk.scalar
            def _(s):
                wt(s, dXk, tok_l)
                if r0i == 0:
                    wt(s, sG, tok_sg)
                if r0i >= 1:
                    s_, v_ = add_toks[-1]
                    wt(s, s_, v_)
                s.activation(z.sg1[:, 0:cols], buf[:, 0:cols], AF.Sign,
                             bias=z.consts[:, 0:1], scale=z.consts[:, inv_col:inv_col + 1])
                s.drain()
                s.activation(z.sg2[:, 0:cols], buf[:, 0:cols], AF.Sign,
                             bias=z.consts[:, 1:2], scale=z.consts[:, inv_col:inv_col + 1])
                s.drain().then_inc(sA.h, 1); sA.n += 1
            z.ldtok[p] = (sA, sA.n)
            tok_s2 = sA.n
            outb = (z.bchunk, z.bch4)[p]
            @blk.vector
            def _(v):
                wt(v, sA, tok_s2)
                wt(v, dYk, 16 * store_cnt[p])
                v.tensor_tensor(outb[:, 0:cols], z.sg1[:, 0:cols], z.sg2[:, 0:cols], ALU.add)
                v.drain()
                v.tensor_scalar(outb[:, 0:cols], outb[:, 0:cols], 0.5, None, ALU.mult)
                v.drain().then_inc(sV.h, 1); sV.n += 1
            add_toks.append((sV, sV.n))
            tok_add = sV.n
            @blk.gpsimd
            def _(g):
                wt(g, sV, tok_add)
                g.dma_start(out=dstT[r0:r0 + 128, :], in_=outb[:, 0:cols]).then_inc(dYk.h, 16); dYk.n += 16
            store_cnt[p] += 1

    tern_weight(w1T, t1_c, E, 4 * E, E * 4 * E, 9, 11)
    tern_weight(w2T, t2_c, 4 * E, E, E * 4 * E, 10, 12)

    mul_toks = [(sV, sV.n), (sV, sV.n)]
    store_cnt = [dY[0].n // 16, dY[1].n // 16]
    for h in range(16):
        p = h % 2
        buf = (z.wchunk, z.wchunk2)[p]
        sgb = (z.sg1, z.sg2)[p]
        outb = z.bchunk[:, 0:2560] if p == 0 else z.bch3[:]
        dXk, dYk = z.dX[p], dY[p]
        @blk.sync
        def _(q):
            s_, v_ = z.ldtok[p]
            wt(q, s_, v_)
            q.dma_start(out=buf[:, 0:2560], in_=ZBin[h]).then_inc(dXk.h, 16); dXk.n += 16
        tok_l = dXk.n
        @blk.scalar
        def _(s):
            wt(s, dXk, tok_l)
            s_, v_ = mul_toks[p]
            wt(s, s_, v_)
            s.activation(sgb[:, 0:2560], buf[:, 0:2560], AF.Exp).then_inc(sA.h, 1); sA.n += 1
        z.ldtok[p] = (sA, sA.n)
        tok_e = sA.n
        @blk.vector
        def _(v):
            wt(v, sA, tok_e)
            wt(v, dYk, 16 * store_cnt[p])
            v.tensor_tensor(outb, sgb[:, 0:2560], z.maskf[:], ALU.mult).then_inc(sV.h, 1); sV.n += 1
        mul_toks[p] = (sV, sV.n)
        tok_m = sV.n
        @blk.gpsimd
        def _(g):
            wt(g, sV, tok_m)
            g.dma_start(out=expb_c[h], in_=outb).then_inc(dYk.h, 16); dYk.n += 16
        store_cnt[p] += 1

    prep_es.close()
    z.weights = dict(wq_c=wq_c, wk_c=wk_c, wv_c=wv_c, wproj_c=wproj_c, wwq_c=wwq_c,
                     wwk_c=wwk_c, wwv_c=wwv_c, projw_c=projw_c, t1_c=t1_c, t2_c=t2_c,
                     expb_c=expb_c)

    for seq in range(_BUILD_OPTS["n_seq"]):
        _build_seq(z, seq)

    @blk.gpsimd
    def _(g):
        wt(g, dS)
        g.nop()

    nc.all_engine_barrier()


def _quiesce(z):
    blk = z.blk
    sems = [z.dL, z.dS, z.sT, z.sV, z.sA, z.sG, z.dY[0], z.dY[1]]
    for opener in (blk.sync, blk.tensor, blk.vector, blk.scalar, blk.gpsimd):
        def body(eng):
            for s in sems:
                eng.wait_ge(s.h, s.n)
        opener(body)


def _phase_a(z, seq):
    nc, blk = z.nc, z.blk
    dL, dS, sT, sV, sA, sG = z.dL, z.dS, z.sT, z.sV, z.sA, z.sG
    W = z.weights
    ps, psbf = z.ps, z.psbf

    pes = contextlib.ExitStack()
    sb = lambda name, shape, dt=F32: pes.enter_context(nc.sbuf_tensor(name + z.sfx, shape, dt))
    xtile = sb("xtile", [128, 2048], F32)
    xsq = sb("xsq", [128, 1024], F32)
    rtmp = sb("rtmp", [128, 4096], F32)
    ropeQ = sb("ropeQ", [128, 1024], BF16)
    ropeK = sb("ropeK", [128, 1024], BF16)
    mu_b = sb("mu_b", [128, 128], F32)
    r_b = sb("r_b", [128, 128], F32)
    wq_sb = sb("wq_sb", [128, 8 * 1024], BF16)
    wk_sb = sb("wk_sb", [128, 8 * 1024], BF16)
    wstg = sb("wstg", [128, 8 * 1024], F32)
    qknorm = sb("qknorm", [128, 64], F32)
    qkmax = sb("qkmax", [128, 64], F32)
    tabs = sb("tabs", [128, 4 * 256], F32)

    @blk.sync
    def _(q):
        wt(q, dL)
        q.dma_start(out=wstg[:], in_=bass.AP(z.wdirect["wqT"].tensor, 0, [[1024, 128], [131072, 8], [1, 1024]])).then_inc(dL.h, 16); dL.n += 16
        for i in range(4):
            q.dma_start(out=tabs[:, i * 256:(i + 1) * 256], in_=z.dram_tabs[i]).then_inc(dL.h, 16); dL.n += 16
    tok_wq = dL.n
    @blk.vector
    def _(v):
        wt(v, dL, tok_wq)
        v.tensor_copy(wq_sb[:], wstg[:]).then_inc(sV.h, 1); sV.n += 1
    tok_wqc = sV.n
    @blk.sync
    def _(q):
        wt(q, sV, tok_wqc)
        wt(q, dL)
        q.dma_start(out=wstg[:], in_=bass.AP(z.wdirect["wkT"].tensor, 0, [[1024, 128], [131072, 8], [1, 1024]])).then_inc(dL.h, 16); dL.n += 16
    tok_wk = dL.n
    @blk.vector
    def _(v):
        wt(v, dL, tok_wk)
        v.tensor_copy(wk_sb[:], wstg[:]).then_inc(sV.h, 1); sV.n += 1
    z.tok["qk_w_ready"] = (sV, sV.n)
    tok_w = dL.n

    tok = z.tok
    tok["xtile_free"] = [(sV, 0), (sV, 0)]
    tok["stats_evd"] = (sV, 0)
    tok["xsq_free"] = (sT, 0)
    tok["rope_done"] = (sV, 0)
    tok["bank7_ev"] = (sV, 0)
    tok["mub_free"] = (sV, 0)

    for tt in range(NT):
        par = tt % 2
        xsl = lambda a, b: xtile[:, par * 1024 + a: par * 1024 + b]
        u1sl = lambda a, b: z.u1all[:, tt * 1024 + a: tt * 1024 + b]

        dX = z.dX[par]
        @blk.sync
        def _(q):
            s_, v_ = tok["xtile_free"][par]
            wt(q, s_, v_)
            q.dma_start(out=xsl(0, 1024), in_=z.xt[seq, tt]).then_inc(dX.h, 16); dX.n += 16
        tok_x = dX.n

        @blk.scalar
        def _(s):
            wt(s, dX, tok_x)
            s_, v_ = tok["xsq_free"]
            wt(s, s_, v_)
            s.activation(xsq[:], xsl(0, 1024), AF.Square).then_inc(sA.h, 1); sA.n += 1
        tok_sq = sA.n

        @blk.tensor
        def _(t):
            wt(t, dX, tok_x)
            wt(t, sA, tok_sq)
            s_, v_ = tok["stats_evd"]
            wt(t, s_, v_)
            t.matmul(ps[:, 2048:2560], z.ones128f[:], xsl(0, 512), start=True, stop=True)
            t.matmul(ps[:, 2560:3072], z.ones128f[:], xsl(512, 1024), start=True, stop=True).then_inc(sT.h, 1); sT.n += 1
        tok_summ = sT.n

        @blk.vector
        def _(v):
            wt(v, sT, tok_summ)
            v.tensor_reduce(mu_b[:], bass.AP(ps, 2048, [[3584, 128], [1, 128], [128, 8]]),
                            mybir.AxisListType.X, ALU.add)
            v.drain()
            v.tensor_scalar(mu_b[:], mu_b[:], 1.0 / 1024.0, None, ALU.mult).then_inc(sV.h, 1); sV.n += 1
        tok_sumred = sV.n

        @blk.tensor
        def _(t):
            wt(t, sV, tok_sumred)
            t.matmul(ps[:, 2048:2560], z.ones128f[:], xsq[:, 0:512], start=True, stop=True)
            t.matmul(ps[:, 2560:3072], z.ones128f[:], xsq[:, 512:1024], start=True, stop=True).then_inc(sT.h, 1); sT.n += 1
        tok_ssqm = sT.n

        @blk.vector
        def _(v):
            wt(v, sT, tok_ssqm)
            v.tensor_reduce(r_b[:], bass.AP(ps, 2048, [[3584, 128], [1, 128], [128, 8]]),
                            mybir.AxisListType.X, ALU.add)
            v.drain()
            v.tensor_scalar(r_b[:], r_b[:], 1.0 / 1024.0, None, ALU.mult)
            v.drain()
            v.tensor_tensor(xsq[:, 0:128], mu_b[:], mu_b[:], ALU.mult)
            v.drain()
            v.tensor_tensor(r_b[:], r_b[:], xsq[:, 0:128], ALU.subtract).then_inc(sV.h, 1); sV.n += 1
        tok_var = sV.n
        tok["stats_evd"] = (sV, tok_var)

        @blk.scalar
        def _(s):
            wt(s, sV, tok_var)
            s.activation(r_b[:], r_b[:], AF.Sqrt, bias=z.consts[:, 2:3]).then_inc(sA.h, 1); sA.n += 1
        tok_std = sA.n

        @blk.vector
        def _(v):
            wt(v, sA, tok_std)
            v.reciprocal(r_b[:], r_b[:])
            v.drain()
            mu_v = bass.AP(mu_b, 0, [[128, 128], [0, 8], [1, 128]])
            r_v = bass.AP(r_b, 0, [[128, 128], [0, 8], [1, 128]])
            v.tensor_tensor(xsq[:], xsl(0, 1024), mu_v, ALU.subtract)
            v.drain()
            v.tensor_tensor(u1sl(0, 1024), xsq[:], r_v, ALU.mult).then_inc(sV.h, 1); sV.n += 1
        tok_u1 = sV.n
        tok["xtile_free"][par] = (sV, tok_u1)
        tok["mub_free"] = (sV, tok_u1)
        tok["xsq_free"] = (sV, tok_u1)

        @blk.tensor
        def _(t):
            wt(t, sV, tok_u1)
            if tt == 0:
                wt(t, dL, tok_w)
                s_, v_ = z.tok["qk_w_ready"]
                wt(t, s_, v_)
            s_, v_ = tok["rope_done"]
            wt(t, s_, v_)
            for c in range(8):
                fl = dict(start=(c == 0), stop=(c == 7))
                u1c = u1sl(c * 128, (c + 1) * 128)
                t.matmul(ps[:, 0:512], u1c, wq_sb[:, c * 1024 + 0:c * 1024 + 512], **fl)
                t.matmul(ps[:, 512:1024], u1c, wq_sb[:, c * 1024 + 512:c * 1024 + 1024], **fl)
                t.matmul(ps[:, 1024:1536], u1c, wk_sb[:, c * 1024 + 0:c * 1024 + 512], **fl)
                last = t.matmul(ps[:, 1536:2048], u1c, wk_sb[:, c * 1024 + 512:c * 1024 + 1024], **fl)
            last.then_inc(sT.h, 1); sT.n += 1
        tok_qk = sT.n

        @blk.vector
        def _(v):
            wt(v, sT, tok_qk)
            vws = []
            for (src_off, dst, cbase, sbase, toff) in ((0, ropeQ, 0, 256, 0),
                                                       (1024, ropeK, 512, 768, 2048)):
                x1v = bass.AP(ps, src_off, [[3584, 128], [128, 8], [32, 4], [1, 16]])
                x2v = bass.AP(ps, src_off + 16, [[3584, 128], [128, 8], [32, 4], [1, 16]])
                cv = bass.AP(tabs, cbase + tt * 16, [[1024, 128], [0, 8], [0, 4], [1, 16]])
                sv_ = bass.AP(tabs, sbase + tt * 16, [[1024, 128], [0, 8], [0, 4], [1, 16]])
                t1v = bass.AP(rtmp, toff, [[4096, 128], [128, 8], [32, 4], [1, 16]])
                t2v = bass.AP(rtmp, toff + 16, [[4096, 128], [128, 8], [32, 4], [1, 16]])
                t3v = bass.AP(rtmp, toff + 1024, [[4096, 128], [128, 8], [32, 4], [1, 16]])
                t4v = bass.AP(rtmp, toff + 1040, [[4096, 128], [128, 8], [32, 4], [1, 16]])
                y1v = bass.AP(dst, 0, [[1024, 128], [128, 8], [32, 4], [1, 16]])
                y2v = bass.AP(dst, 16, [[1024, 128], [128, 8], [32, 4], [1, 16]])
                vws.append((x1v, x2v, cv, sv_, t1v, t2v, t3v, t4v, y1v, y2v))
            for (x1v, x2v, cv, sv_, t1v, t2v, t3v, t4v, y1v, y2v) in vws:
                v.tensor_tensor(t1v, x1v, cv, ALU.mult)
                v.tensor_tensor(t2v, x2v, sv_, ALU.mult)
                v.tensor_tensor(t3v, x2v, cv, ALU.mult)
                v.tensor_tensor(t4v, x1v, sv_, ALU.mult)
            v.drain()
            for (x1v, x2v, cv, sv_, t1v, t2v, t3v, t4v, y1v, y2v) in vws:
                v.tensor_tensor(y1v, t1v, t2v, ALU.subtract)
                last = v.tensor_tensor(y2v, t3v, t4v, ALU.add)
            last.then_inc(sV.h, 1); sV.n += 1
        tok_rope = sV.n
        tok["rope_done"] = (sV, tok_rope)

        @blk.tensor
        def _(t):
            wt(t, sV, tok_rope)
            s_, v_ = tok["bank7_ev"]
            wt(t, s_, v_)
            for g in range(8):
                last = t.transpose(psbf[:, g * 128:(g + 1) * 128], ropeQ[:, g * 128:(g + 1) * 128], z.ident[:])
            last.then_inc(sT.h, 1); sT.n += 1
        tok_trQ = sT.n

        @blk.vector
        def _(v):
            wt(v, sT, tok_trQ)
            v.tensor_copy(bass.AP(z.QT, tt * 128, [[16384, 128], [2048, 8], [1, 128]]), psbf[:, 0:1024]).then_inc(sV.h, 1); sV.n += 1
        tok_qtev = sV.n

        @blk.tensor
        def _(t):
            wt(t, sV, tok_qtev)
            for g in range(8):
                last = t.transpose(psbf[:, g * 128:(g + 1) * 128], ropeK[:, g * 128:(g + 1) * 128], z.ident[:])
            last.then_inc(sT.h, 1); sT.n += 1
        tok_trK = sT.n

        @blk.vector
        def _(v):
            wt(v, sT, tok_trK)
            v.tensor_copy(bass.AP(z.KT, tt * 128, [[16384, 128], [2048, 8], [1, 128]]), psbf[:, 0:1024]).then_inc(sV.h, 1); sV.n += 1
        tok_ktev = sV.n
        tok["bank7_ev"] = (sV, tok_ktev)

        @blk.vector
        def _(v):
            v.tensor_tensor(rtmp[:, 0:1024], ropeQ[:], ropeQ[:], ALU.mult)
            v.tensor_tensor(rtmp[:, 1024:2048], ropeK[:], ropeK[:], ALU.mult)
            v.drain()
            if tt == 0:
                v.tensor_reduce(qknorm[:, 0:32], bass.AP(rtmp, 0, [[4096, 128], [32, 32], [1, 32]]),
                                mybir.AxisListType.X, ALU.add)
                last = v.tensor_reduce(qknorm[:, 32:64], bass.AP(rtmp, 1024, [[4096, 128], [32, 32], [1, 32]]),
                                       mybir.AxisListType.X, ALU.add)
            else:
                v.tensor_reduce(qkmax[:, 0:32], bass.AP(rtmp, 0, [[4096, 128], [32, 32], [1, 32]]),
                                mybir.AxisListType.X, ALU.add)
                v.tensor_reduce(qkmax[:, 32:64], bass.AP(rtmp, 1024, [[4096, 128], [32, 32], [1, 32]]),
                                mybir.AxisListType.X, ALU.add)
                v.drain()
                last = v.tensor_tensor(qknorm[:], qknorm[:], qkmax[:], ALU.max)
            last.then_inc(sV.h, 1); sV.n += 1

    @blk.tensor
    def _(t):
        wt(t, sV)
        t.transpose(ps[0:64, 3072:3200], qknorm[:], z.identf[:])
        t.drain().then_inc(sT.h, 1); sT.n += 1
    tok_btr = z.sT.n

    @blk.vector
    def _(v):
        wt(v, sT, tok_btr)
        v.tensor_reduce(qkmax[0:64, 0:1], ps[0:64, 3072:3200], mybir.AxisListType.X, ALU.max)
        v.drain().then_inc(sV.h, 1); sV.n += 1
    tok_bred = z.sV.n

    @blk.tensor
    def _(t):
        wt(t, sV, tok_bred)
        t.transpose(ps[0:1, 3200:3264], qkmax[0:64, 0:1], z.identf[0:64, 0:64])
        t.drain().then_inc(sT.h, 1); sT.n += 1
    tok_btr2 = z.sT.n

    @blk.vector
    def _(v):
        wt(v, sT, tok_btr2)
        v.tensor_reduce(z.rows[0:1, 640:644], bass.AP(ps, 3200, [[3584, 1], [16, 4], [1, 16]]),
                        mybir.AxisListType.X, ALU.max)
        v.drain()
        v.tensor_tensor(z.rows[0:1, 644:646], z.rows[0:1, 640:642], z.rows[0:1, 642:644], ALU.mult)
        v.drain().then_inc(sV.h, 1); sV.n += 1
    tok_bmul = z.sV.n

    @blk.scalar
    def _(s):
        wt(s, sV, tok_bmul)
        s.activation(z.rows[0:1, 644:646], z.rows[0:1, 644:646], AF.Sqrt).then_inc(sA.h, 1); sA.n += 1
    tok_bsq = z.sA.n

    @blk.vector
    def _(v):
        wt(v, sA, tok_bsq)
        v.tensor_scalar(z.rows[0:1, 646:648], z.rows[0:1, 644:646], -1.0, None, ALU.mult)
        v.drain().then_inc(sV.h, 1); sV.n += 1
    tok_bneg = z.sV.n

    @blk.gpsimd
    def _(g):
        wt(g, sV, tok_bneg)
        wt(g, dL)
        g.dma_start(out=z.bounce[0:2], in_=z.rows[0:1, 646:648]).then_inc(dS.h, 16); dS.n += 16
    @blk.sync
    def _(q):
        wt(q, dS)
        wt(q, dL)
        q.dma_start(out=z.consts2[:, 0:2],
                    in_=bass.AP(z.bounce.tensor, 0, [[0, 128], [1, 2]])).then_inc(dL.h, 16); dL.n += 16
    z.tok["phaseA_done"] = (dL, z.dL.n)
    return pes


def _build_seq(z, seq):
    z.tok = {}
    z.sfx = "_s%d" % seq
    nc = z.nc
    ab_es = contextlib.ExitStack()
    z.QT = ab_es.enter_context(nc.sbuf_tensor("QT" + z.sfx, [128, 8 * 2048], BF16, side="right"))
    z.KT = ab_es.enter_context(nc.sbuf_tensor("KT" + z.sfx, [128, 8 * 2048], BF16, side="right"))
    z.u1all = ab_es.enter_context(nc.sbuf_tensor("u1all" + z.sfx, [128, NT * 1024], BF16, side="right"))
    _quiesce(z)
    pes = _phase_a(z, seq)
    stop = _BUILD_OPTS.get("stop_after")
    if stop == "A":
        if seq == 0:
            _dump_taps(z, dict(QT=z.QT, KT=z.KT, u1all=z.u1all, consts2=z.consts2, consts=z.consts))
        pes.close(); ab_es.close()
        return
    pes.close()
    tt_es = contextlib.ExitStack()
    z.TT = tt_es.enter_context(nc.sbuf_tensor("TT" + z.sfx, [128, 16 * 2048], BF16))
    _quiesce(z)
    pesb = _phase_b(z, seq)
    pesb.close()
    if stop == "B":
        if seq == 0:
            _dump_taps(z, dict(TT=z.TT, consts2=z.consts2))
        ab_es.close(); tt_es.close()
        return
    ab_es.close()
    stream_es = contextlib.ExitStack()
    z.xs = stream_es.enter_context(nc.sbuf_tensor("xs" + z.sfx, [128, 8 * 2048], F32, side="right"))
    _quiesce(z)
    pesc = _phase_c(z, seq)
    pesc.close()
    tt_es.close()
    if stop == "C":
        if seq == 0:
            _dump_taps(z, dict(xs=z.xs))
        stream_es.close()
        return
    _quiesce(z)
    pesd = _phase_ffn(z, seq, "D" + z.sfx)
    pesd.close()
    if stop == "D":
        if seq == 0:
            _dump_taps(z, dict(xs=z.xs))
        stream_es.close()
        return
    _quiesce(z)
    pese = _phase_e(z, seq)
    pese.close()
    if stop == "E":
        if seq == 0:
            _dump_taps(z, dict(xs=z.xs))
        stream_es.close()
        return
    _quiesce(z)
    pesf = _phase_ffn(z, seq, "F" + z.sfx)
    pesf.close()
    blk = z.blk
    @blk.gpsimd
    def _(g):
        wt(g, z.sV)
        for fc in range(8):
            g.dma_start(out=z.yout[seq, fc], in_=z.xs[:, fc * 2048:(fc + 1) * 2048]).then_inc(z.dS.h, 16); z.dS.n += 16
        wt(g, z.dS)
    stream_es.close()


def _dump_taps(z, table):
    if not z.taps:
        return
    blk = z.blk
    _quiesce(z)
    @blk.gpsimd
    def _(g):
        for name, ap in z.taps.items():
            src = table[name]
            g.dma_start(out=ap, in_=src[:]).then_inc(z.dS.h, 16); z.dS.n += 16


def kernel(**inputs):
    per_core = _host_prepare(inputs)
    nc = build_nc()
    res = run_bass_kernel_spmd(nc, per_core, core_ids=list(range(8)))
    return _host_gather(res.results)


def _phase_b(z, seq):
    nc, blk = z.nc, z.blk
    dL, dS, sT, sV, sA, sG = z.dL, z.dS, z.sT, z.sV, z.sA, z.sG
    W = z.weights
    ps, psbf = z.ps, z.psbf

    pes = contextlib.ExitStack()
    sb = lambda name, shape, dt=F32: pes.enter_context(nc.sbuf_tensor(name + z.sfx, shape, dt))
    wvp = sb("wvp", [128, 2048], BF16)
    Vp = sb("Vp", [128, 4096], BF16)
    Ebuf = sb("Ebuf", [128, 2048], BF16)
    scr = sb("scr", [128, 2048], F32)
    OcA = sb("OcA", [128, 2048], F32)
    OcB = sb("OcB", [128, 2048], F32)
    gnmu = sb("gnmu", [128, 2], F32)

    tok = z.tok
    tok["wvp_free"] = [(sV, sV.n), (sV, sV.n)]
    tok["expdone"] = (sA, sA.n)
    tok["ozfree"] = (sV, sV.n)
    tok["bank7_ev"] = (sV, sV.n)
    tok["gnmu_free"] = (sV, sV.n)
    tok["vev"] = [(sV, sV.n), (sV, sV.n)]

    SA, SB = 0, 512
    OA, OB = 1024, 1536
    ZA, ZB = 2048, 2560
    ST6 = 3072

    for pr in range(8):
        hA, hB = 2 * pr, 2 * pr + 1
        g = pr // 2
        r0 = (pr % 2) * 64
        par = pr % 2
        dX = z.dX[par]

        @blk.sync
        def _(q):
            s_, v_ = tok["wvp_free"][0]
            wt(q, s_, v_)
            q.dma_start(out=scr[:],
                        in_=bass.AP(z.wdirect["wvT"].tensor, hA * 128, [[2048, 128], [128 * 2048, 8], [1, 256]])
                        ).then_inc(dX.h, 16); dX.n += 16
        tok_wvf = dX.n
        @blk.vector
        def _(v):
            wt(v, dX, tok_wvf)
            s_, v_ = tok["wvp_free"][1]
            wt(v, s_, v_)
            v.tensor_copy(wvp[:], scr[:]).then_inc(sV.h, 1); sV.n += 1
        tok_wv = sV.n

        for tt in range(NT):
            vb = (tt % 2) * 512
            @blk.tensor
            def _(t):
                if tt == 0:
                    wt(t, sV, tok_wv)
                    s_, v_ = tok["expdone"]
                    wt(t, s_, v_)
                    wt(t, sV)
                else:
                    s_, v_ = tok["vev"][tt % 2]
                    wt(t, s_, v_)
                for c in range(8):
                    last = t.matmul(ps[:, vb:vb + 256], z.u1all[:, tt * 1024 + c * 128:tt * 1024 + (c + 1) * 128],
                             wvp[:, c * 256:(c + 1) * 256],
                             start=(c == 0), stop=(c == 7))
                last.then_inc(sT.h, 1); sT.n += 1
            tok_vmm = sT.n

            @blk.vector
            def _(v):
                wt(v, sT, tok_vmm)
                v.tensor_copy(Vp[:, tt * 256:tt * 256 + 256], ps[:, vb:vb + 256]).then_inc(sV.h, 1); sV.n += 1
            tok["vev"][tt % 2] = (sV, sV.n)
        tok["wvp_free"][0] = (sV, tok_wv)
        tok["wvp_free"][1] = (sT, sT.n)
        tok["ozfree"] = (sV, sV.n)

        for qb in range(4):
            q0 = qb * 512
            for s_i in (1, 2):
                gs = g + 4 * (s_i - 1)
                qcol = gs * 2048 + q0
                sT0 = sT.n
                sA0 = sA.n
                oz_s, oz_v = tok["ozfree"]

                @blk.tensor
                def _(t):
                    for kt in range(NT):
                        kcol = gs * 2048 + kt * 128
                        eb = (kt % 2) * 1024
                        wt(t, sA, sA0 + kt)
                        if kt == 0:
                            wt(t, oz_s, oz_v)
                        t.matmul(ps[:, SA:SA + 512], z.KT[r0:r0 + 32, kcol:kcol + 128],
                                 z.QT[r0:r0 + 32, qcol:qcol + 512], start=True, stop=True,
                                 tile_position=(r0, 0))
                        t.matmul(ps[:, SB:SB + 512], z.KT[r0 + 32:r0 + 64, kcol:kcol + 128],
                                 z.QT[r0 + 32:r0 + 64, qcol:qcol + 512], start=True, stop=True,
                                 tile_position=(r0 + 32, 0)).then_inc(sT.h, 1); sT.n += 1
                        if kt > 0:
                            ep = ((kt - 1) % 2) * 1024
                            fl = dict(start=(kt - 1 == 0), stop=False)
                            t.matmul(ps[:, ZA:ZA + 512], z.ones128[:], Ebuf[:, ep:ep + 512], **fl)
                            t.matmul(ps[:, ZB:ZB + 512], z.ones128[:], Ebuf[:, ep + 512:ep + 1024], **fl)
                            t.matmul(ps[:, OA:OA + 512], Vp[:, (kt - 1) * 256:(kt - 1) * 256 + 128],
                                     Ebuf[:, ep:ep + 512], **fl)
                            t.matmul(ps[:, OB:OB + 512], Vp[:, (kt - 1) * 256 + 128:(kt - 1) * 256 + 256],
                                     Ebuf[:, ep + 512:ep + 1024], **fl)
                    wt(t, sA, sA0 + NT)
                    ep = ((NT - 1) % 2) * 1024
                    fl = dict(start=False, stop=True)
                    t.matmul(ps[:, ZA:ZA + 512], z.ones128[:], Ebuf[:, ep:ep + 512], **fl)
                    t.matmul(ps[:, ZB:ZB + 512], z.ones128[:], Ebuf[:, ep + 512:ep + 1024], **fl)
                    t.matmul(ps[:, OA:OA + 512], Vp[:, (NT - 1) * 256:(NT - 1) * 256 + 128],
                             Ebuf[:, ep:ep + 512], **fl)
                    t.matmul(ps[:, OB:OB + 512], Vp[:, (NT - 1) * 256 + 128:(NT - 1) * 256 + 256],
                             Ebuf[:, ep + 512:ep + 1024], **fl).then_inc(sT.h, 1); sT.n += 1

                @blk.scalar
                def _(s):
                    for kt in range(NT):
                        eb = (kt % 2) * 1024
                        wt(s, sT, sT0 + kt + 1)
                        s.activation(Ebuf[:, eb:eb + 1024], ps[:, 0:1024], AF.Exp,
                                     bias=z.consts2[:, s_i - 1:s_i]).then_inc(sA.h, 1); sA.n += 1
                assert sA.n == sA0 + NT
                tok["expdone"] = (sA, sA.n)
                tok_oz = sT.n

                @blk.vector
                def _(v):
                    wt(v, sT, tok_oz)
                    v.reciprocal(scr[:, 0:512], ps[:, ZA:ZA + 512])
                    v.reciprocal(scr[:, 512:1024], ps[:, ZB:ZB + 512])
                    v.drain()
                    if s_i == 1:
                        v.tensor_tensor(OcA[:, q0:q0 + 512], ps[:, OA:OA + 512], scr[:, 0:512], ALU.mult)
                        v.tensor_tensor(OcB[:, q0:q0 + 512], ps[:, OB:OB + 512], scr[:, 512:1024], ALU.mult).then_inc(sV.h, 1); sV.n += 1
                    else:
                        v.tensor_tensor(scr[:, 1024:1536], ps[:, OA:OA + 512], scr[:, 0:512], ALU.mult)
                        v.tensor_tensor(scr[:, 1536:2048], ps[:, OB:OB + 512], scr[:, 512:1024], ALU.mult)
                        v.drain()
                        v.scalar_tensor_tensor(OcA[:, q0:q0 + 512], scr[:, 1024:1536], z.consts[:, 5:6],
                                               OcA[:, q0:q0 + 512], ALU.mult, ALU.add)
                        v.scalar_tensor_tensor(OcB[:, q0:q0 + 512], scr[:, 1536:2048], z.consts[:, 5:6],
                                               OcB[:, q0:q0 + 512], ALU.mult, ALU.add).then_inc(sV.h, 1); sV.n += 1
                tok["ozfree"] = (sV, sV.n)

        for (Oc, h) in ((OcA, hA), (OcB, hB)):
            @blk.scalar
            def _(s):
                wt(s, sV)
                s.activation(scr[:], Oc[:], AF.Square).then_inc(sA.h, 1); sA.n += 1
            tok_gsq = sA.n

            @blk.tensor
            def _(t):
                wt(t, sA, tok_gsq)
                s_, v_ = tok["stats_evd_b"] if "stats_evd_b" in tok else (sV, 0)
                wt(t, s_, v_)
                for j in range(4):
                    t.matmul(ps[:, ST6:ST6 + 512], z.ones128f[:], Oc[:, j * 512:(j + 1) * 512],
                             start=(j == 0), stop=(j == 3))
                t.drain().then_inc(sT.h, 1); sT.n += 1
            tok_gsum = sT.n

            @blk.vector
            def _(v):
                wt(v, sT, tok_gsum)
                v.tensor_reduce(gnmu[:, 0:1], ps[:, ST6:ST6 + 512], mybir.AxisListType.X, ALU.add)
                v.drain()
                v.tensor_scalar(gnmu[:, 0:1], gnmu[:, 0:1], 1.0 / 262144.0, None, ALU.mult)
                v.drain().then_inc(sV.h, 1); sV.n += 1
            tok_gsr = sV.n

            @blk.tensor
            def _(t):
                wt(t, sV, tok_gsr)
                for j in range(4):
                    t.matmul(ps[:, ST6:ST6 + 512], z.ones128f[:], scr[:, j * 512:(j + 1) * 512],
                             start=(j == 0), stop=(j == 3))
                t.drain().then_inc(sT.h, 1); sT.n += 1
            tok_gss = sT.n

            @blk.vector
            def _(v):
                wt(v, sT, tok_gss)
                v.tensor_reduce(gnmu[:, 1:2], ps[:, ST6:ST6 + 512], mybir.AxisListType.X, ALU.add)
                v.drain()
                v.tensor_scalar(gnmu[:, 1:2], gnmu[:, 1:2], 1.0 / 262144.0, None, ALU.mult)
                v.drain()
                v.tensor_tensor(scr[:, 0:1], gnmu[:, 0:1], gnmu[:, 0:1], ALU.mult)
                v.drain()
                v.tensor_tensor(gnmu[:, 1:2], gnmu[:, 1:2], scr[:, 0:1], ALU.subtract)
                v.drain().then_inc(sV.h, 1); sV.n += 1
            tok["stats_evd_b"] = (sV, sV.n)
            tok_gvar = sV.n

            @blk.scalar
            def _(s):
                wt(s, sV, tok_gvar)
                s.activation(gnmu[:, 1:2], gnmu[:, 1:2], AF.Sqrt,
                             bias=z.consts[:, 2:3]).then_inc(sA.h, 1); sA.n += 1
            tok_gstd = sA.n

            @blk.vector
            def _(v):
                wt(v, sA, tok_gstd)
                v.reciprocal(gnmu[:, 1:2], gnmu[:, 1:2])
                v.drain()
                v.tensor_scalar(gnmu[:, 1:2], gnmu[:, 1:2], 0.8, None, ALU.mult)
                v.drain()
                wt(v, sT)
                v.tensor_scalar(wvp[:], Oc[:], gnmu[:, 0:1], gnmu[:, 1:2], ALU.subtract, ALU.mult)
                v.drain().then_inc(sV.h, 1); sV.n += 1
            tok_gn = sV.n

            for b in range(2):
                @blk.tensor
                def _(t):
                    wt(t, sV, tok_gn)
                    s_, v_ = tok["bank7_ev"]
                    wt(t, s_, v_)
                    for q_lo in range(8):
                        ql = b * 8 + q_lo
                        t.transpose(psbf[:, q_lo * 128:(q_lo + 1) * 128],
                                    bass.AP(wvp, ql, [[2048, 128], [16, 128]]), z.ident[:])
                    t.drain().then_inc(sT.h, 1); sT.n += 1
                tok_tr = sT.n

                @blk.vector
                def _(v):
                    wt(v, sT, tok_tr)
                    v.tensor_copy(z.TT[:, h * 2048 + b * 1024: h * 2048 + (b + 1) * 1024], psbf[:])
                    v.drain().then_inc(sV.h, 1); sV.n += 1
                tok["bank7_ev"] = (sV, sV.n)
            tok["wvp_free"] = [(sT, sT.n), (sT, sT.n)]

    z.tok["phaseB_done"] = (sV, z.sV.n)
    return pes


def _phase_c(z, seq):
    nc, blk = z.nc, z.blk
    dL, dS, sT, sV, sA, sG = z.dL, z.dS, z.sT, z.sV, z.sA, z.sG
    W = z.weights
    ps = z.ps

    pes = contextlib.ExitStack()
    sb = lambda name, shape, dt=F32: pes.enter_context(nc.sbuf_tensor(name + z.sfx, shape, dt))
    wpj = sb("wpj", [128, 4096], BF16)
    av = sb("av", [128, 4096], F32)
    scr1 = sb("c1s", [128, 2048], F32)
    scr2 = sb("c2s", [128, 2048], F32)
    scr3 = sb("c3s", [128, 2048], F32)
    scr4 = sb("c4s", [128, 2048], F32)

    tok = z.tok
    tok["wpj_free"] = (sV, sV.n)
    tok["av_free"] = (sV, sV.n)
    tok["chain_free"] = (sV, sV.n)

    for j in range(8):
        par = j % 2
        dX = z.dX[par]

        @blk.sync
        def _(q):
            s_, v_ = tok["wpj_free"]
            wt(q, s_, v_)
            q.dma_start(out=bass.AP(wpj, 0, [[4096, 128], [256, 16], [1, 128]]),
                        in_=bass.AP(W["wproj_c"].tensor, j * 128, [[2048, 128], [128 * 2048, 16], [1, 128]])
                        ).then_inc(dX.h, 16); dX.n += 16
            q.dma_start(out=bass.AP(wpj, 128, [[4096, 128], [256, 16], [1, 128]]),
                        in_=bass.AP(W["wproj_c"].tensor, (j + 8) * 128, [[2048, 128], [128 * 2048, 16], [1, 128]])
                        ).then_inc(dX.h, 16); dX.n += 16
            s_, v_ = tok["chain_free"]
            wt(q, s_, v_)
            q.dma_start(out=scr3[:],
                        in_=bass.AP(z.xt.tensor, seq * 2097152 + j * 128,
                                    [[1024, 128], [131072, 16], [1, 128]])
                        ).then_inc(dX.h, 16); dX.n += 16
        tok_w = dX.n

        for half in range(2):
            @blk.tensor
            def _(t):
                wt(t, dX, tok_w)
                if half == 0:
                    s_, v_ = tok["av_free"]
                    wt(t, s_, v_)
                else:
                    wt(t, sV)
                for q4 in range(4):
                    for c in range(16):
                        t.matmul(ps[:, q4 * 512:(q4 + 1) * 512],
                                 wpj[:, c * 256 + half * 128:c * 256 + half * 128 + 128],
                                 z.TT[:, c * 2048 + q4 * 512:c * 2048 + (q4 + 1) * 512],
                                 start=(c == 0), stop=(c == 15))
                t.drain().then_inc(sT.h, 1); sT.n += 1
            tok_mm = sT.n

            @blk.vector
            def _(v):
                wt(v, sT, tok_mm)
                v.tensor_copy(av[:, half * 2048:(half + 1) * 2048], ps[:, 0:2048])
                v.drain().then_inc(sV.h, 1); sV.n += 1
        tok["av_evict"] = (sV, sV.n)

        @blk.vector
        def _(v):
            wt(v, dX, tok_w)
            v.tensor_scalar(scr2[:], av[:, 0:2048], -1.0, None, ALU.mult)
            v.drain()
            v.tensor_tensor(scr1[:], av[:, 0:2048], scr2[:], ALU.max)
            v.drain()
            v.tensor_scalar(scr1[:], scr1[:], 1e-30, None, ALU.max)
            v.drain()
            v.reciprocal(scr2[:], scr1[:])
            v.tensor_scalar(scr4[:], scr1[:], 1.0, None, ALU.is_gt)
            v.drain()
            v.tensor_tensor(scr1[:], scr1[:], scr2[:], ALU.min)
            v.drain().then_inc(sV.h, 1); sV.n += 1
        tok_pre = sV.n

        @blk.scalar
        def _(s):
            wt(s, sV, tok_pre)
            s.activation(scr2[:], scr1[:], AF.Arctan)
            s.drain()
            s.activation(scr1[:], av[:, 0:2048], AF.Sign)
            s.drain().then_inc(sA.h, 1); sA.n += 1
        tok_act = sA.n

        @blk.vector
        def _(v):
            wt(v, sA, tok_act)
            v.scalar_tensor_tensor(av[:, 0:2048], scr4[:], math.pi / 2, scr2[:], ALU.mult, ALU.add)
            v.drain()
            v.tensor_tensor(scr4[:], scr4[:], scr2[:], ALU.mult)
            v.drain()
            v.scalar_tensor_tensor(scr4[:], scr4[:], -2.0, av[:, 0:2048], ALU.mult, ALU.add)
            v.drain()
            v.tensor_tensor(scr2[:], scr4[:], scr1[:], ALU.mult)
            v.drain()
            v.tensor_scalar(scr2[:], scr2[:], z.consts[:, 6:7], z.consts[:, 7:8], ALU.mult, ALU.add)
            v.drain()
            v.tensor_tensor(scr1[:], scr2[:], av[:, 2048:4096], ALU.mult)
            v.drain()
            v.tensor_tensor(z.xs[:, j * 2048:(j + 1) * 2048], scr1[:], scr3[:], ALU.add)
            v.drain().then_inc(sV.h, 1); sV.n += 1
        tok["chain_free"] = (sV, sV.n)
        tok["av_free"] = (sV, sV.n)
        tok["wpj_free"] = (sV, sV.n)

    z.tok["phaseC_done"] = (sV, z.sV.n)
    return pes


def _phase_ffn(z, seq, tag):
    nc, blk = z.nc, z.blk
    dL, dS, sT, sV, sA, sG = z.dL, z.dS, z.sT, z.sV, z.sA, z.sG
    W = z.weights
    ps = z.ps
    QW = 512

    pes = contextlib.ExitStack()
    sb = lambda name, shape, dt=F32: pes.enter_context(nc.sbuf_tensor(name + z.sfx, shape, dt))
    hbuf = sb("hbuf" + tag, [128, 32 * QW], F32)
    k2b = sb("k2b" + tag, [128, 32 * QW], BF16)
    k1 = sb("k1" + tag, [128, 8 * QW], BF16)
    reps = sb("reps" + tag, [128, 3 * QW], F32)
    scr = sb("scrD" + tag, [128, 2 * QW], F32)
    scr2 = sb("scr2D" + tag, [128, 2 * QW], F32)
    sqb8 = sb("sqbD" + tag, [128, 4096], BF16)
    tch = sb("tchD" + tag, [128, 4096], BF16)

    MU, QR, A1 = 0, QW, 2 * QW
    rp = lambda a: reps[:, a:a + QW]
    tok = z.tok

    for tq in _BUILD_OPTS.get("ffn_tq", range(4)):
        t0c = tq * QW
        xsl = lambda fc: z.xs[:, fc * 2048 + t0c:fc * 2048 + t0c + QW]

        xs_v = bass.AP(z.xs, t0c, [[16384, 128], [2048, 8], [1, 512]])
        mu_v = bass.AP(reps, MU, [[1536, 128], [0, 8], [1, 512]])
        qr_v = bass.AP(reps, QR, [[1536, 128], [0, 8], [1, 512]])

        @blk.vector
        def _(v):
            wt(v, sT)
            wt(v, sA)
            v.tensor_copy(sqb8[:], xs_v).then_inc(sV.h, 1); sV.n += 1
        tok_cast = sV.n

        @blk.tensor
        def _(t):
            wt(t, sV, tok_cast)
            wt(t, sV)
            for fc in range(8):
                last = t.matmul(ps[:, 0:QW], z.ones128[:], sqb8[:, fc * QW:(fc + 1) * QW],
                                start=(fc == 0), stop=(fc == 7))
            last.then_inc(sT.h, 1); sT.n += 1
        tok_mm0 = sT.n

        @blk.vector
        def _(v):
            wt(v, sT, tok_mm0)
            v.tensor_scalar(rp(MU), ps[:, 0:QW], 1.0 / 1024.0, None, ALU.mult).then_inc(sV.h, 1); sV.n += 1
        tok_mu = sV.n

        @blk.scalar
        def _(s):
            wt(s, sT, tok_mm0)
            s.activation(sqb8[:], sqb8[:], AF.Square).then_inc(sA.h, 1); sA.n += 1
        tok_sq8 = sA.n

        @blk.tensor
        def _(t):
            wt(t, sA, tok_sq8)
            wt(t, sV, tok_mu)
            for fc in range(8):
                last = t.matmul(ps[:, 0:QW], z.ones128[:], sqb8[:, fc * QW:(fc + 1) * QW],
                                start=(fc == 0), stop=(fc == 7))
            last.then_inc(sT.h, 1); sT.n += 1
        tok_mm1 = sT.n

        @blk.vector
        def _(v):
            wt(v, sT, tok_mm1)
            v.tensor_scalar(scr2[:, 0:QW], ps[:, 0:QW], 1.0 / 1024.0, None, ALU.mult)
            v.drain()
            v.tensor_tensor(scr[:, 0:QW], rp(MU), rp(MU), ALU.mult)
            v.drain()
            v.tensor_tensor(scr[:, 0:QW], scr2[:, 0:QW], scr[:, 0:QW], ALU.subtract).then_inc(sV.h, 1); sV.n += 1
        tok_var = sV.n

        @blk.scalar
        def _(s):
            wt(s, sV, tok_var)
            s.activation(scr2[:, 0:QW], scr[:, 0:QW], AF.Sqrt, bias=z.consts[:, 2:3]).then_inc(sA.h, 1); sA.n += 1
        tok_std = sA.n

        @blk.vector
        def _(v):
            wt(v, sA, tok_std)
            v.reciprocal(rp(A1), scr2[:, 0:QW])
            v.drain()
            v.tensor_scalar(scr2[:, 0:QW], scr[:, 0:QW], 1e-5, None, ALU.add)
            v.drain()
            v.reciprocal(scr2[:, 0:QW], scr2[:, 0:QW])
            v.drain()
            v.tensor_tensor(scr2[:, 0:QW], scr[:, 0:QW], scr2[:, 0:QW], ALU.mult).then_inc(sV.h, 1); sV.n += 1
        tok_mu2 = sV.n

        @blk.scalar
        def _(s):
            wt(s, sV, tok_mu2)
            s.activation(scr2[:, 0:QW], scr2[:, 0:QW], AF.Sqrt, bias=z.consts[:, 2:3]).then_inc(sA.h, 1); sA.n += 1
        tok_rr = sA.n

        @blk.vector
        def _(v):
            wt(v, sA, tok_rr)
            v.reciprocal(scr2[:, 0:QW], scr2[:, 0:QW])
            v.tensor_reduce(rp(QR), bass.AP(z.xs, t0c, [[16384, 128], [1, 512], [2048, 8]]),
                            mybir.AxisListType.X, ALU.max)
            v.tensor_reduce(scr[:, 0:QW], bass.AP(z.xs, t0c, [[16384, 128], [1, 512], [2048, 8]]),
                            mybir.AxisListType.X, ALU.min)
            v.drain()
            v.tensor_tensor(rp(QR), rp(QR), rp(MU), ALU.subtract)
            v.tensor_tensor(scr[:, 0:QW], rp(MU), scr[:, 0:QW], ALU.subtract)
            v.drain()
            v.tensor_tensor(rp(QR), rp(QR), scr[:, 0:QW], ALU.max).then_inc(sV.h, 1); sV.n += 1
        tok_fold = sV.n

        @blk.tensor
        def _(t):
            wt(t, sV, tok_fold)
            wt(t, sV)
            for i in range(4):
                last = t.transpose(ps[:, 1024 + i * 128:1024 + (i + 1) * 128],
                                   reps[:, QW + i * 128:QW + (i + 1) * 128], z.identf[:])
            last.then_inc(sT.h, 1); sT.n += 1
        tok_amt = sT.n

        @blk.vector
        def _(v):
            wt(v, sT, tok_amt)
            for i in range(4):
                last = v.tensor_reduce(scr[:, QW + i:QW + i + 1], ps[:, 1024 + i * 128:1024 + (i + 1) * 128],
                                       mybir.AxisListType.X, ALU.max)
            last.then_inc(sV.h, 1); sV.n += 1
        tok_amr = sV.n

        @blk.gpsimd
        def _(g):
            wt(g, sV, tok_amr)
            wt(g, dL)
            for i in range(4):
                g.dma_start(out=z.bounce[i * 128:(i + 1) * 128],
                            in_=scr[:, QW + i:QW + i + 1]).then_inc(dS.h, 16); dS.n += 16
        @blk.sync
        def _(q):
            wt(q, dS)
            wt(q, dL)
            q.dma_start(out=scr[:, 0:QW],
                        in_=bass.AP(z.bounce.tensor, 0, [[0, 128], [1, 512]])).then_inc(dL.h, 16); dL.n += 16
        tok_am = dL.n

        @blk.vector
        def _(v):
            wt(v, dL, tok_am)
            v.tensor_tensor(scr[:, 0:QW], scr[:, 0:QW], rp(A1), ALU.mult)
            v.drain()
            v.tensor_tensor(scr[:, 0:QW], scr[:, 0:QW], scr2[:, 0:QW], ALU.mult)
            v.drain()
            v.tensor_scalar(scr[:, 0:QW], scr[:, 0:QW], 1e-5, None, ALU.max)
            v.drain()
            v.reciprocal(rp(QR), scr[:, 0:QW])
            v.drain()
            v.tensor_scalar(rp(QR), rp(QR), 127.0, None, ALU.mult)
            v.drain()
            v.tensor_tensor(rp(QR), rp(QR), scr2[:, 0:QW], ALU.mult)
            v.drain()
            v.tensor_tensor(rp(QR), rp(QR), rp(A1), ALU.mult)
            v.drain()
            v.tensor_scalar(rp(A1), scr[:, 0:QW], z.consts[:, 9:10], 1.0 / 127.0, ALU.mult, ALU.mult)
            v.drain().then_inc(sV.h, 1); sV.n += 1

        @blk.vector
        def _(v):
            wt(v, sT)
            v.tensor_tensor(hbuf[:, 0:4096], xs_v, mu_v, ALU.subtract)
            v.drain()
            v.tensor_tensor(hbuf[:, 0:4096], hbuf[:, 0:4096], qr_v, ALU.mult)
            v.drain()
            v.tensor_scalar(k1[:], hbuf[:, 0:4096], MAGIC, MAGIC, ALU.add, ALU.subtract)
            v.drain().then_inc(sV.h, 1); sV.n += 1
        tok_k1 = sV.n

        tch1tok = tok.setdefault("tch1_free" + tag, [(sV, 0), (sV, 0)])
        tch1tok[0] = (sT, sT.n)
        tch1tok[1] = (sT, sT.n)
        for hp in range(16):
            hf = hp * 2
            par1 = hp % 2
            dX = z.dX[par1]
            @blk.sync
            def _(q):
                s_, v_ = tch1tok[par1]
                wt(q, s_, v_)
                q.dma_start(out=tch[:, par1 * 2048:par1 * 2048 + 1024],
                            in_=bass.AP(W["t1_c"].tensor, hf * 128, [[4096, 128], [128 * 4096, 8], [1, 128]])
                            ).then_inc(dX.h, 16); dX.n += 16
                q.dma_start(out=tch[:, par1 * 2048 + 1024:par1 * 2048 + 2048],
                            in_=bass.AP(W["t1_c"].tensor, (hf + 1) * 128, [[4096, 128], [128 * 4096, 8], [1, 128]])
                            ).then_inc(dX.h, 16); dX.n += 16
            tok_t = dX.n

            pb1 = (hp % 2) * 1024
            ev1tok = tok.setdefault("ffn1_evp" + tag, [(sV, 0), (sV, 0)])
            @blk.tensor
            def _(t):
                wt(t, dX, tok_t)
                if hp == 0:
                    wt(t, sV, tok_k1)
                if hp >= 2:
                    s_, v_ = ev1tok[hp % 2]
                    wt(t, s_, v_)
                for c in range(8):
                    t.matmul(ps[:, pb1:pb1 + 512],
                             tch[:, par1 * 2048 + c * 128:par1 * 2048 + (c + 1) * 128],
                             k1[:, c * QW:(c + 1) * QW], start=(c == 0), stop=(c == 7))
                for c in range(8):
                    last = t.matmul(ps[:, pb1 + 512:pb1 + 1024],
                                    tch[:, par1 * 2048 + 1024 + c * 128:par1 * 2048 + 1024 + (c + 1) * 128],
                                    k1[:, c * QW:(c + 1) * QW], start=(c == 0), stop=(c == 7))
                last.then_inc(sT.h, 1); sT.n += 1
            tch1tok[par1] = (sT, sT.n)
            tok_pairmm = sT.n
            @blk.vector
            def _(v):
                wt(v, sT, tok_pairmm)
                etmp = scr[:, 0:1024] if hp % 2 == 0 else scr2[:, 0:1024]
                a1_v = bass.AP(reps, A1, [[1536, 128], [0, 2], [1, 512]])
                b1_v = bass.AP(z.b1, hf, [[32, 128], [1, 2], [0, 512]])
                v.tensor_tensor(etmp, ps[:, pb1:pb1 + 1024], a1_v, ALU.mult)
                v.drain()
                v.tensor_tensor(hbuf[:, hf * QW:(hf + 2) * QW], etmp,
                                b1_v, ALU.add).then_inc(sV.h, 1); sV.n += 1
            ev1tok[hp % 2] = (sV, sV.n)
            tok["ffn1_ev" + tag] = (sV, sV.n)

        @blk.scalar
        def _(s):
            s_, v_ = tok["ffn1_ev" + tag]
            wt(s, s_, v_)
            s.activation(hbuf[:], hbuf[:], AF.Gelu).then_inc(sA.h, 1); sA.n += 1
        tok_gelu = sA.n

        for g8 in range(8):
            sqoff = (g8 % 2) * 2048
            @blk.scalar
            def _(s):
                if g8 == 0:
                    wt(s, sA, tok_gelu)
                if g8 >= 2:
                    wt(s, sT)
                s.activation(sqb8[:, sqoff:sqoff + 2048], hbuf[:, g8 * 2048:(g8 + 1) * 2048],
                             AF.Square).then_inc(sA.h, 1); sA.n += 1
            tok_sqg = sA.n
            @blk.tensor
            def _(t):
                wt(t, sA, tok_sqg)
                if g8 == 0:
                    wt(t, sV)
                for c in range(4):
                    hf = g8 * 4 + c
                    last = t.matmul(ps[:, 0:QW], z.ones128[:], sqb8[:, sqoff + c * 512:sqoff + (c + 1) * 512],
                                    start=(hf == 0), stop=(hf == 31))
                last.then_inc(sT.h, 1); sT.n += 1
        tok_hsq = sT.n

        @blk.vector
        def _(v):
            wt(v, sT, tok_hsq)
            wt(v, sA)
            v.tensor_scalar(scr[:, 0:QW], ps[:, 0:QW], 1.0 / 4096.0, None, ALU.mult)
            v.tensor_reduce(scr2[:, 0:QW], bass.AP(hbuf, 0, [[16384, 128], [1, 512], [512, 32]]),
                            mybir.AxisListType.X, ALU.max)
            v.tensor_reduce(scr2[:, QW:2 * QW], bass.AP(hbuf, 0, [[16384, 128], [1, 512], [512, 32]]),
                            mybir.AxisListType.X, ALU.min)
            v.drain()
            v.tensor_scalar(scr2[:, QW:2 * QW], scr2[:, QW:2 * QW], -1.0, None, ALU.mult)
            v.drain()
            v.tensor_tensor(scr2[:, 0:QW], scr2[:, 0:QW], scr2[:, QW:2 * QW], ALU.max).then_inc(sV.h, 1); sV.n += 1
        tok_hf = sV.n

        @blk.tensor
        def _(t):
            wt(t, sV, tok_hf)
            for i in range(4):
                last = t.transpose(ps[:, 1024 + i * 128:1024 + (i + 1) * 128],
                                   scr2[:, i * 128:(i + 1) * 128], z.identf[:])
            last.then_inc(sT.h, 1); sT.n += 1
        tok_hamt = sT.n

        @blk.vector
        def _(v):
            wt(v, sT, tok_hamt)
            for i in range(4):
                last = v.tensor_reduce(reps[:, i:i + 1], ps[:, 1024 + i * 128:1024 + (i + 1) * 128],
                                       mybir.AxisListType.X, ALU.max)
            last.then_inc(sV.h, 1); sV.n += 1
        tok_hamr = sV.n

        @blk.gpsimd
        def _(g):
            wt(g, sV, tok_hamr)
            wt(g, dL)
            for i in range(4):
                g.dma_start(out=z.bounce[i * 128:(i + 1) * 128],
                            in_=reps[:, i:i + 1]).then_inc(dS.h, 16); dS.n += 16
        @blk.sync
        def _(q):
            wt(q, dS)
            wt(q, dL)
            q.dma_start(out=scr2[:, QW:2 * QW],
                        in_=bass.AP(z.bounce.tensor, 0, [[0, 128], [1, 512]])).then_inc(dL.h, 16); dL.n += 16
        tok_ham = dL.n

        @blk.scalar
        def _(s):
            wt(s, sV, tok_hf)
            s.activation(scr[:, QW:2 * QW], scr[:, 0:QW], AF.Sqrt, bias=z.consts[:, 2:3]).then_inc(sA.h, 1); sA.n += 1
        tok_hstd = sA.n

        @blk.vector
        def _(v):
            wt(v, sA, tok_hstd)
            wt(v, dL, tok_ham)
            v.reciprocal(scr[:, QW:2 * QW], scr[:, QW:2 * QW])
            v.drain()
            v.tensor_tensor(scr2[:, 0:QW], scr[:, QW:2 * QW], scr2[:, QW:2 * QW], ALU.mult)
            v.drain()
            v.tensor_scalar(scr2[:, 0:QW], scr2[:, 0:QW], 1e-5, None, ALU.max)
            v.drain()
            v.tensor_scalar(scr2[:, QW:2 * QW], scr2[:, 0:QW], z.consts[:, 10:11], 1.0 / 127.0,
                            ALU.mult, ALU.mult)
            v.drain()
            v.reciprocal(scr2[:, 0:QW], scr2[:, 0:QW])
            v.drain()
            v.tensor_scalar(scr2[:, 0:QW], scr2[:, 0:QW], 127.0, None, ALU.mult)
            v.drain()
            v.tensor_tensor(scr[:, QW:2 * QW], scr[:, QW:2 * QW], scr2[:, 0:QW], ALU.mult)
            v.drain().then_inc(sV.h, 1); sV.n += 1

        @blk.vector
        def _(v):
            wt(v, sT)
            q2r_v = bass.AP(scr, QW, [[1024, 128], [0, 32], [1, 512]])
            v.tensor_tensor(hbuf[:], hbuf[:], q2r_v, ALU.mult)
            v.drain()
            v.tensor_scalar(k2b[:], hbuf[:], MAGIC, MAGIC, ALU.add, ALU.subtract)
            v.drain().then_inc(sV.h, 1); sV.n += 1
        tok_k2 = sV.n

        tch2tok = tok.setdefault("tch2_free" + tag, [(sV, 0), (sV, 0)])
        tch2tok[0] = (sT, sT.n)
        tch2tok[1] = (sT, sT.n)
        ev2tok = tok.setdefault("ffn2_ev" + tag, [(sV, 0), (sV, 0)])
        for of in range(8):
            toks = []
            for sub in range(2):
                hidx = (of * 2 + sub) % 2
                dXs = z.dX[hidx]
                @blk.sync
                def _(q):
                    s_, v_ = tch2tok[hidx]
                    wt(q, s_, v_)
                    q.dma_start(out=tch[:, hidx * 2048:hidx * 2048 + 2048],
                                in_=bass.AP(W["t2_c"].tensor, of * 128 + sub * 16 * 128 * 1024,
                                            [[1024, 128], [128 * 1024, 16], [1, 128]])
                                ).then_inc(dXs.h, 16); dXs.n += 16
                toks.append((hidx, dXs, dXs.n))
            @blk.tensor
            def _(t):
                if of == 0:
                    wt(t, sV, tok_k2)
                    wt(t, sV)
                if of >= 2:
                    s_, v_ = ev2tok[of % 2]
                    wt(t, s_, v_)
                pb = (of % 2) * 512
                for sub in range(2):
                    hidx, dXs, dv = toks[sub]
                    wt(t, dXs, dv)
                    for c in range(16):
                        hc = sub * 16 + c
                        last = t.matmul(ps[:, 1024 + pb:1024 + pb + 512],
                                        tch[:, hidx * 2048 + c * 128:hidx * 2048 + (c + 1) * 128],
                                        k2b[:, hc * QW:(hc + 1) * QW], start=(hc == 0), stop=(hc == 31))
                last.then_inc(sT.h, 1); sT.n += 1
            for sub in range(2):
                hidx = (of * 2 + sub) % 2
                tch2tok[hidx] = (sT, sT.n)
            tok_m2 = sT.n

            @blk.vector
            def _(v):
                wt(v, sT, tok_m2)
                pb = (of % 2) * 512
                v.tensor_tensor(scr2[:, 0:QW], ps[:, 1024 + pb:1024 + pb + QW], scr2[:, QW:2 * QW], ALU.mult)
                v.drain()
                v.scalar_tensor_tensor(xsl(of), scr2[:, 0:QW], z.b2[:, of:of + 1],
                                       xsl(of), ALU.add, ALU.add)
                v.drain().then_inc(sV.h, 1); sV.n += 1
            ev2tok[of % 2] = (sV, sV.n)

    z.tok["ffn_done" + tag] = (sV, z.sV.n)
    return pes


def _ln_stats_rep(z, srcfn, sqb, dstmu, dstms, nfeat_chunks, width, inv_n):
    nc, blk = z.nc, z.blk
    sT, sV, sA = z.sT, z.sV, z.sA
    ps = z.ps
    nq = width // 512
    for pss in range(2):
        for fc in range(nfeat_chunks):
            @blk.vector
            def _(v):
                wt(v, sT)
                v.tensor_copy(sqb[:, 0:width], srcfn(fc))
                v.drain().then_inc(sV.h, 1); sV.n += 1
            tok_cast = sV.n
            if pss == 1:
                @blk.scalar
                def _(s):
                    wt(s, sV, tok_cast)
                    s.activation(sqb[:, 0:width], sqb[:, 0:width], AF.Square).then_inc(sA.h, 1); sA.n += 1
            @blk.tensor
            def _(t):
                wt(t, sV, tok_cast)
                if pss == 1:
                    wt(t, sA)
                if fc == 0:
                    wt(t, sV)
                for q4 in range(nq):
                    t.matmul(ps[:, q4 * 512:(q4 + 1) * 512], z.ones128[:],
                             sqb[:, q4 * 512:(q4 + 1) * 512], start=(fc == 0), stop=(fc == nfeat_chunks - 1))
                t.drain().then_inc(sT.h, 1); sT.n += 1
        tok_mm = z.sT.n
        @blk.vector
        def _(v):
            wt(v, sT, tok_mm)
            dst = dstmu if pss == 0 else dstms
            v.tensor_scalar(dst, ps[:, 0:width], inv_n, None, ALU.mult)
            v.drain().then_inc(sV.h, 1); sV.n += 1


def _rstd_from_stats(z, mu, ms, var_scr, out_rstd):
    nc, blk = z.nc, z.blk
    sV, sA = z.sV, z.sA
    @blk.vector
    def _(v):
        v.tensor_tensor(var_scr, mu, mu, ALU.mult)
        v.drain()
        v.tensor_tensor(var_scr, ms, var_scr, ALU.subtract)
        v.drain().then_inc(sV.h, 1); sV.n += 1
    tok_v = sV.n
    @blk.scalar
    def _(s):
        wt(s, sV, tok_v)
        s.activation(out_rstd, var_scr, AF.Sqrt, bias=z.consts[:, 2:3]).then_inc(sA.h, 1); sA.n += 1
    tok_s = sA.n
    @blk.vector
    def _(v):
        wt(v, sA, tok_s)
        v.reciprocal(out_rstd, out_rstd)
        v.drain().then_inc(sV.h, 1); sV.n += 1


def _phase_e(z, seq):
    nc, blk = z.nc, z.blk
    dL, dS, sT, sV, sA, sG = z.dL, z.dS, z.sT, z.sV, z.sA, z.sG
    W = z.weights
    ps, psbf = z.ps, z.psbf

    pesE = contextlib.ExitStack()
    sb = lambda name, shape, dt=F32: pesE.enter_context(nc.sbuf_tensor(name + z.sfx, shape, dt))
    oT = pesE.enter_context(nc.sbuf_tensor("oT" + z.sfx, [128, 8 * 2048], BF16, side="right"))
    repsE = sb("repsE", [128, 4096], F32)
    scrE = sb("scrE", [128, 2048], F32)
    scrE2 = sb("scrE2", [128, 2048], F32)
    sqbE = sb("sqbE", [128, 2048], BF16)

    tok = z.tok

    _ln_stats_rep(z, lambda fc: z.xs[:, fc * 2048:(fc + 1) * 2048], sqbE,
                  repsE[:, 0:2048], scrE2[:], 8, 2048, 1.0 / 1024.0)
    _rstd_from_stats(z, repsE[:, 0:2048], scrE2[:], scrE[:], repsE[:, 2048:4096])

    pes1 = contextlib.ExitStack()
    sb1 = lambda name, shape, dt=F32: pes1.enter_context(nc.sbuf_tensor(name + z.sfx, shape, dt))
    u3 = pes1.enter_context(nc.sbuf_tensor("u3" + z.sfx, [128, 8 * 2048], BF16, side="right"))
    wbv_b = sb1("wbv_b", [128, 1024], F32)

    @blk.sync
    def _(q):
        wt(q, dL)
        q.dma_start(out=wbv_b[:], in_=bass.AP(z.wbv_row.tensor, 0, [[0, 128], [1, E]])).then_inc(dL.h, 16); dL.n += 16
    tok_wbv = dL.n
    qkT = sb1("qkT", [128, 4096], BF16)
    vp = sb1("vp", [128, 16 * 128], BF16)
    expAB = sb1("expAB", [128, 2 * 2560], BF16)
    EbufE = sb1("EbufE", [128, 2048], BF16)
    wwch = sb1("wwch", [128, 1024], BF16)
    wwf = sb1("wwf", [128, 1024], F32)

    @blk.vector
    def _(v):
        for fc in range(8):
            v.tensor_tensor(scrE[:], z.xs[:, fc * 2048:(fc + 1) * 2048], repsE[:, 0:2048], ALU.subtract)
            v.drain()
            v.tensor_tensor(u3[:, fc * 2048:(fc + 1) * 2048], scrE[:], repsE[:, 2048:4096], ALU.mult)
            v.drain().then_inc(sV.h, 1); sV.n += 1
    tok_u3 = sV.n

    SA, SB = 0, 512
    OO = 1024
    ZA, ZB_ = 1536, 2048
    VB = 2560
    tok["expdoneE"] = (sA, sA.n)
    tok["ozfreeE"] = (sV, sV.n)
    tok["EbufE_free"] = (sT, sT.n)

    for pr in range(8):
        par = pr % 2
        dX = z.dX[par]

        @blk.sync
        def _(q):
            s_, v_ = tok.get("wwch_free", (sV, tok_u3))
            wt(q, s_, v_)
            q.dma_start(out=wwf[:],
                        in_=bass.AP(z.wdirect["wwqT"].tensor, pr * 128, [[1024, 128], [131072, 8], [1, 128]])
                        ).then_inc(dX.h, 16); dX.n += 16
            wt(q, dL)
            q.dma_start(out=expAB[:, 0:2560], in_=W["expb_c"][2 * pr]).then_inc(dL.h, 16); dL.n += 16
            q.dma_start(out=expAB[:, 2560:5120], in_=W["expb_c"][2 * pr + 1]).then_inc(dL.h, 16); dL.n += 16
        tok_wwf = dX.n
        tok_expab = dL.n
        @blk.vector
        def _(v):
            wt(v, dX, tok_wwf)
            wt(v, sT)
            v.tensor_copy(wwch[:], wwf[:]).then_inc(sV.h, 1); sV.n += 1
        tok_ww = sV.n

        for which in range(2):
            if which == 1:
                @blk.sync
                def _(q):
                    wt(q, sV)
                    wt(q, dX)
                    q.dma_start(out=wwf[:],
                                in_=bass.AP(z.wdirect["wwkT"].tensor, pr * 128, [[1024, 128], [131072, 8], [1, 128]])
                                ).then_inc(dX.h, 16); dX.n += 16
                tok_wwkf = dX.n
                @blk.vector
                def _(v):
                    wt(v, dX, tok_wwkf)
                    wt(v, sT)
                    v.tensor_copy(wwch[:], wwf[:]).then_inc(sV.h, 1); sV.n += 1
                tok_ww = sV.n
            @blk.tensor
            def _(t):
                wt(t, sV, tok_ww)
                wt(t, sV)
                for q4 in range(4):
                    for c in range(8):
                        t.matmul(ps[:, q4 * 512:(q4 + 1) * 512],
                                 wwch[:, c * 128:(c + 1) * 128],
                                 u3[:, c * 2048 + q4 * 512:c * 2048 + (q4 + 1) * 512],
                                 start=(c == 0), stop=(c == 7))
                t.drain().then_inc(sT.h, 1); sT.n += 1
            tok_qk = sT.n

            @blk.vector
            def _(v):
                wt(v, sT, tok_qk)
                bias = z.wbq if which == 0 else z.wbk
                if which == 0:
                    v.tensor_scalar(qkT[:, 0:2048], ps[:, 0:2048], bias[:, pr:pr + 1], 1.0 / 32.0, ALU.add, ALU.mult)
                else:
                    v.tensor_scalar(qkT[:, 2048:4096], ps[:, 0:2048], bias[:, pr:pr + 1], None, ALU.add)
                v.drain().then_inc(sV.h, 1); sV.n += 1
        tok_qkev = sV.n

        @blk.sync
        def _(q):
            wt(q, sV)
            wt(q, dX)
            q.dma_start(out=wwf[:],
                        in_=bass.AP(z.wdirect["wwvT"].tensor, pr * 128, [[1024, 128], [131072, 8], [1, 128]])
                        ).then_inc(dX.h, 16); dX.n += 16
        tok_wwvf = dX.n
        @blk.vector
        def _(v):
            wt(v, dX, tok_wwvf)
            wt(v, sT)
            v.tensor_copy(wwch[:], wwf[:]).then_inc(sV.h, 1); sV.n += 1
        tok_wwv = sV.n
        for tt in range(NT):
            vb = VB + (tt % 2) * 512
            @blk.tensor
            def _(t):
                if tt == 0:
                    wt(t, sV, tok_wwv)
                if tt >= 2:
                    s_, v_ = tok["vevE" + str(tt % 2)]
                    wt(t, s_, v_)
                for c in range(8):
                    t.matmul(ps[:, vb:vb + 128], u3[:, c * 2048 + tt * 128:c * 2048 + (tt + 1) * 128],
                             wwch[:, c * 128:(c + 1) * 128], start=(c == 0), stop=(c == 7))
                t.drain().then_inc(sT.h, 1); sT.n += 1
            tok_vm = sT.n
            @blk.vector
            def _(v):
                wt(v, sT, tok_vm)
                if tt == 0 and pr == 0:
                    wt(v, dL, tok_wbv)
                v.tensor_tensor(vp[:, tt * 128:(tt + 1) * 128], ps[:, vb:vb + 128],
                                wbv_b[:, pr * 128:(pr + 1) * 128], ALU.add)
                v.drain().then_inc(sV.h, 1); sV.n += 1
            tok["vevE" + str(tt % 2)] = (sV, sV.n)
        tok_vdone = sV.n

        for which in range(2):
            @blk.vector
            def _(v):
                if which == 0:
                    wt(v, sV, tok_vdone)
                v.tensor_tensor(sqbE[:], qkT[:, which * 2048:(which + 1) * 2048],
                                qkT[:, which * 2048:(which + 1) * 2048], ALU.mult)
                v.drain().then_inc(sV.h, 1); sV.n += 1
            tok_sqq = sV.n
            @blk.tensor
            def _(t):
                wt(t, sV, tok_sqq)
                if which == 0:
                    wt(t, sV)
                for q4 in range(4):
                    t.matmul(ps[:, q4 * 512:(q4 + 1) * 512], z.ones128[:],
                             sqbE[:, q4 * 512:(q4 + 1) * 512], start=True, stop=True)
                t.drain().then_inc(sT.h, 1); sT.n += 1
            tok_nmm = sT.n
            @blk.vector
            def _(v):
                wt(v, sT, tok_nmm)
                v.tensor_reduce(z.consts2[:, 4 + which:5 + which], ps[:, 0:2048],
                                mybir.AxisListType.X, ALU.max)
                v.drain().then_inc(sV.h, 1); sV.n += 1
        tok_bam = sV.n

        @blk.vector
        def _(v):
            v.tensor_tensor(z.consts2[:, 6:7], z.consts2[:, 4:5], z.consts2[:, 5:6], ALU.mult)
            v.drain().then_inc(sV.h, 1); sV.n += 1
        tok_bm2 = sV.n

        @blk.scalar
        def _(s):
            wt(s, sV, tok_bm2)
            s.activation(z.consts2[:, 6:7], z.consts2[:, 6:7], AF.Sqrt).then_inc(sA.h, 1); sA.n += 1
        tok_bsq = sA.n

        @blk.vector
        def _(v):
            wt(v, sA, tok_bsq)
            v.tensor_scalar(z.consts2[:, 6:7], z.consts2[:, 6:7], -1.0, None, ALU.mult)
            v.drain().then_inc(sV.h, 1); sV.n += 1
        tok_bb = sV.n

        for qb in range(4):
            q0 = qb * 512
            nkt = 4 * (qb + 1)
            sT0, sA0, sV0 = sT.n, sA.n, sV.n
            oz_s, oz_v = tok["ozfreeE"]

            def zov(t, kt, nkt):
                eb = (kt % 2) * 1024
                fl = dict(start=(kt == 0), stop=(kt == nkt - 1))
                t.matmul(ps[:, ZA:ZA + 512], z.ones128[:], EbufE[:, eb:eb + 512], **fl)
                t.matmul(ps[:, ZB_:ZB_ + 512], z.ones128[:], EbufE[:, eb + 512:eb + 1024], **fl)
                t.matmul(ps[0:64, OO:OO + 512], vp[:, kt * 128:kt * 128 + 64], EbufE[:, eb:eb + 512],
                         tile_position=(0, 0), **fl)
                return t.matmul(ps[64:128, OO:OO + 512], vp[:, kt * 128 + 64:kt * 128 + 128],
                                EbufE[:, eb + 512:eb + 1024],
                                tile_position=(0, 64), skip_group_check=True, **fl)

            @blk.tensor
            def _(t):
                for kt in range(nkt):
                    wt(t, sA, sA0 + kt)
                    if kt == 0:
                        wt(t, oz_s, oz_v)
                        wt(t, sV, tok_bb)
                    t.matmul(ps[:, SA:SA + 512], qkT[0:64, 2048 + kt * 128:2048 + (kt + 1) * 128],
                             qkT[0:64, q0:q0 + 512], start=True, stop=True, tile_position=(0, 0))
                    t.matmul(ps[:, SB:SB + 512], qkT[64:128, 2048 + kt * 128:2048 + (kt + 1) * 128],
                             qkT[64:128, q0:q0 + 512], start=True, stop=True,
                             tile_position=(64, 0)).then_inc(sT.h, 1); sT.n += 1
                    if kt > 0:
                        wt(t, sV, sV0 + kt)
                        zov(t, kt - 1, nkt)
                wt(t, sV, sV0 + nkt)
                zov(t, nkt - 1, nkt).then_inc(sT.h, 1); sT.n += 1

            @blk.scalar
            def _(s):
                for kt in range(nkt):
                    eb = (kt % 2) * 1024
                    wt(s, sT, sT0 + kt + 1)
                    s.activation(EbufE[:, eb:eb + 1024], ps[:, 0:1024], AF.Exp,
                                 bias=z.consts2[:, 6:7]).then_inc(sA.h, 1); sA.n += 1

            @blk.vector
            def _(v):
                wt(v, dL, tok_expab)
                for kt in range(nkt):
                    eb = (kt % 2) * 1024
                    co = q0 - kt * 128 + 512
                    wt(v, sA, sA0 + kt + 1)
                    v.tensor_tensor(EbufE[:, eb:eb + 512], EbufE[:, eb:eb + 512],
                                    expAB[:, co:co + 512], ALU.mult)
                    v.tensor_tensor(EbufE[:, eb + 512:eb + 1024], EbufE[:, eb + 512:eb + 1024],
                                    expAB[:, 2560 + co:2560 + co + 512], ALU.mult)
                    v.drain().then_inc(sV.h, 1); sV.n += 1
            tok_oz = sT.n

            @blk.vector
            def _(v):
                wt(v, sT, tok_oz)
                v.reciprocal(scrE[:, 0:512], ps[:, ZA:ZA + 512])
                v.reciprocal(scrE[:, 512:1024], ps[:, ZB_:ZB_ + 512])
                v.drain()
                v.tensor_tensor(oT[0:64, pr * 2048 + q0:pr * 2048 + q0 + 512],
                                ps[0:64, OO:OO + 512], scrE[0:64, 0:512], ALU.mult)
                v.tensor_tensor(oT[64:128, pr * 2048 + q0:pr * 2048 + q0 + 512],
                                ps[64:128, OO:OO + 512], scrE[64:128, 512:1024], ALU.mult)
                v.drain().then_inc(sV.h, 1); sV.n += 1
            tok["ozfreeE"] = (sV, sV.n)
            tok["expdoneE"] = (sA, sA.n)
        tok["wwch_free"] = (sT, sT.n)

    pes1.close()
    _quiesce(z)

    pes2 = contextlib.ExitStack()
    sb2 = lambda name, shape, dt=F32: pes2.enter_context(nc.sbuf_tensor(name + z.sfx, shape, dt))
    ybuf = sb2("ybuf", [128, 8 * 1024], F32)
    pwch = sb2("pwch", [128, 1024], BF16)
    pwf = sb2("pwf", [128, 1024], F32)

    for th in range(2):
        t0c = th * 1024

        @blk.vector
        def _(v):
            wt(v, sT)
            xs_h = bass.AP(z.xs, t0c, [[16384, 128], [2048, 8], [1, 1024]])
            mu_h = bass.AP(repsE, t0c, [[4096, 128], [0, 8], [1, 1024]])
            rs_h = bass.AP(repsE, 2048 + t0c, [[4096, 128], [0, 8], [1, 1024]])
            v.tensor_tensor(ybuf[:], xs_h, mu_h, ALU.subtract)
            v.drain()
            v.tensor_tensor(ybuf[:], ybuf[:], rs_h, ALU.mult)
            v.drain().then_inc(sV.h, 1); sV.n += 1
        tok_uh = sV.n

        for of in range(8):
            dX = z.dX[of % 2]
            @blk.sync
            def _(q):
                s_, v_ = tok.get("pwf_free", (sV, sV.n))
                wt(q, s_, v_)
                q.dma_start(out=pwf[:],
                            in_=bass.AP(z.wdirect["projwT"].tensor, of * 128, [[1024, 128], [131072, 8], [1, 128]])
                            ).then_inc(dX.h, 16); dX.n += 16
            tok_pwf = dX.n
            @blk.vector
            def _(v):
                wt(v, dX, tok_pwf)
                s_, v_ = tok.get("pwch_free", (sV, 0))
                wt(v, s_, v_)
                v.tensor_copy(pwch[:], pwf[:]).then_inc(sV.h, 1); sV.n += 1
            tok_pw = sV.n
            tok["pwf_free"] = (sV, tok_pw)

            for q4 in range(2):
                qq0 = t0c + q4 * 512
                @blk.tensor
                def _(t):
                    if q4 == 0:
                        wt(t, sV, tok_pw)
                    wt(t, sV)
                    for oc in range(8):
                        t.matmul(ps[:, (q4 % 2) * 512:(q4 % 2) * 512 + 512],
                                 pwch[:, oc * 128:(oc + 1) * 128],
                                 oT[:, oc * 2048 + qq0:oc * 2048 + qq0 + 512],
                                 start=(oc == 0), stop=(oc == 7))
                    t.drain().then_inc(sT.h, 1); sT.n += 1
                tok_pm = sT.n
                tok["pwch_free"] = (sT, sT.n)

                @blk.vector
                def _(v):
                    wt(v, sT, tok_pm)
                    if of == 0 and q4 == 0:
                        wt(v, sV, tok_uh)
                    yslc = ybuf[:, of * 1024 + q4 * 512:of * 1024 + (q4 + 1) * 512]
                    v.scalar_tensor_tensor(yslc, ps[:, (q4 % 2) * 512:(q4 % 2) * 512 + 512],
                                           z.pbias[:, of:of + 1], yslc, ALU.add, ALU.add)
                    v.drain().then_inc(sV.h, 1); sV.n += 1

        _ln_stats_rep(z, lambda fc: ybuf[:, fc * 1024:(fc + 1) * 1024], sqbE,
                      scrE[:, 0:1024], scrE2[:, 0:1024], 8, 1024, 1.0 / 1024.0)
        _rstd_from_stats(z, scrE[:, 0:1024], scrE2[:, 0:1024], scrE2[:, 1024:2048], scrE[:, 1024:2048])

        @blk.vector
        def _(v):
            mu2_h = bass.AP(scrE, 0, [[2048, 128], [0, 8], [1, 1024]])
            rs2_h = bass.AP(scrE, 1024, [[2048, 128], [0, 8], [1, 1024]])
            lng_h = bass.AP(z.lng, 0, [[8, 128], [1, 8], [0, 1024]])
            lnb_h = bass.AP(z.lnb, 0, [[8, 128], [1, 8], [0, 1024]])
            xs_h = bass.AP(z.xs, t0c, [[16384, 128], [2048, 8], [1, 1024]])
            v.tensor_tensor(ybuf[:], ybuf[:], mu2_h, ALU.subtract)
            v.drain()
            v.tensor_tensor(ybuf[:], ybuf[:], rs2_h, ALU.mult)
            v.drain()
            v.tensor_tensor(ybuf[:], ybuf[:], lng_h, ALU.mult)
            v.drain()
            v.tensor_tensor(ybuf[:], ybuf[:], lnb_h, ALU.add)
            v.drain()
            v.tensor_tensor(xs_h, xs_h, ybuf[:], ALU.add)
            v.drain().then_inc(sV.h, 1); sV.n += 1

    pes2.close()
    z.tok["phaseE_done"] = (sV, z.sV.n)
    return pesE
